# version 07
# baseline (speedup 1.0000x reference)
import numpy as np
import ml_dtypes
import concourse.bass as bass
import concourse.mybir as mybir
from concourse.bass_utils import run_bass_kernel_spmd
from contextlib import ExitStack

F32 = mybir.dt.float32
BF16 = mybir.dt.bfloat16
AF = mybir.ActivationFunctionType
ALU = mybir.AluOpType
AX = mybir.AxisListType
AP = bass.AP
BF = ml_dtypes.bfloat16

B, S, D = 4, 4096, 1024
DH, H, DK = 512, 8, 64
DFF, FFI = 4096, 2816
NR = 2048
TCH, NCH = 512, 8
EPS = 1e-5
ENGS = ("sync", "tensor", "vector", "scalar", "gpsimd")
QMIDS = (0.25, 0.75, 1.25, 1.75, 2.5, 3.5, 5.0)

DEBUG = ()


class Prog:
    def __init__(self, nc, stack):
        self.nc, self.stack = nc, stack
        self.ops = {e: [] for e in ENGS}
        self.cnt, self.sems = {}, {}

    def sem(self, name):
        s = self.stack.enter_context(self.nc.semaphore(name))
        self.cnt[s] = 0
        self.sems[name] = s
        return s

    def on(self, eng, fn):
        self.ops[eng].append(fn)

    def inc(self, eng, fn, sem, n):
        self.cnt[sem] += n
        self.ops[eng].append(lambda e, fn=fn, sem=sem, n=n: fn(e).then_inc(sem, n))
        return self.cnt[sem]

    def wait(self, eng, sem, v):
        self.ops[eng].append(lambda e, sem=sem, v=v: e.wait_ge(sem, v))


class K:
    def __init__(self):
        self.nc = bass.Bass("TRN2", target_bir_lowering=False,
                            detect_race_conditions=False)
        self.stack = ExitStack()
        self.p = Prog(self.nc, self.stack)
        self.d = {}
        self.sb = {}
        self.ps = {}

    def S(self, name):
        return self.p.sems[name]


def din(k, name, shape, dt=F32, kind="ExternalInput"):
    k.d[name] = k.nc.dram_tensor(name, shape, dt, kind=kind).ap()


def declare(k):
    din(k, "x", [S, D])
    din(k, "xq", [NR, D])
    din(k, "mask0", [1])
    for i in (1, 2):
        for w in ("wq", "wk", "wv", "wo"):
            din(k, f"{w}{i}", [DH, DH])
        for bn in ("bq", "bk", "bv", "bo"):
            din(k, f"{bn}{i}", [DH])
        din(k, f"g{i}", [DH]); din(k, f"be{i}", [DH])
    din(k, "wff1", [DFF, D]); din(k, "bff1", [DFF])
    din(k, "wff2", [D, DFF]); din(k, "bff2", [D])
    din(k, "gff", [D]); din(k, "beff", [D])
    din(k, "wgate", [FFI, D]); din(k, "wup", [FFI, D]); din(k, "wdown", [D, FFI])
    din(k, "idn", [128, 128])
    din(k, "idnb", [128, 128], BF16)
    din(k, "jmatb", [128, 128], BF16)
    din(k, "sel1a", [1, 128], BF16)
    din(k, "sel1b", [1, 128], BF16)
    din(k, "sel88", [88, 1408], BF16)
    din(k, "onesf", [1, 128])
    din(k, "hpad", [5120], BF16, "Internal")
    din(k, "hsD", [DH, S], BF16, "Internal")
    din(k, "wff1T", [D, DFF], BF16, "Internal")
    din(k, "wff2T", [DFF, D], BF16, "Internal")
    din(k, "wgqT", [D, FFI], BF16, "Internal")
    din(k, "wuqT", [D, FFI], BF16, "Internal")
    din(k, "wdqT", [FFI, D], BF16, "Internal")
    din(k, "hqTD", [FFI, NR], BF16, "Internal")
    k.out = k.nc.dram_tensor("out", [NR, D], F32, kind="ExternalOutput").ap()
    for nm in DEBUG:
        shp = {"xhq": [DH, NR], "hs": [DH, S], "x1": [NR, DH], "x2": [NR, DH],
               "kfm": [DH, S], "vtm": [S, DH], "qfm": [DH, NR], "o1": [DH, NR],
               "z3": [NR, D], "xres": [NR, D], "qx": [D, NR], "hq": [FFI, NR],
               "wdq": [FFI, D], "wgq": [D, FFI], "wuq": [D, FFI],
               }[nm]
        din(k, "dbg_" + nm, shp, F32, "ExternalOutput")


def alloc(k):
    E = k.stack.enter_context
    def sbt(name, cols, dt=BF16, parts=128):
        k.sb[name] = E(k.nc.sbuf_tensor("sb_" + name, [parts, cols], dt))
    sbt("idn", 128, F32); sbt("idnb", 128); sbt("jmatb", 128)
    sbt("sel1a", 128, BF16, 1); sbt("sel1b", 128, BF16, 33)
    sbt("sel88", 1408, BF16, 88)
    sbt("onesf", 128, F32, 1)
    sbt("onescol", 1, BF16)
    sbt("bias4", 32, F32)
    sbt("bff1c", 32, F32); sbt("bff2c", 8, F32)
    sbt("rowv", 512, F32, 1)
    sbt("maskc", 2, F32)
    sbt("s1", 16384)
    sbt("s2", 16384)
    sbt("s3", 14080)
    sbt("s4", 16384)
    sbt("s5", 8192)
    sbt("s6", 8192)
    sbt("s7", 8192)
    sbt("ptr", 2048)
    sbt("f32a", 2048, F32)
    sbt("f32b", 4160, F32)
    sbt("stats", 16, F32)
    k.ps["A"] = E(k.nc.psum_tensor("psA", [128, 1024], F32))
    for nm in ("C", "D", "Z0", "Z1", "Z2", "Z3"):
        k.ps[nm] = E(k.nc.psum_tensor("ps" + nm, [128, 512], F32))


def a(k, name, p0=0, p1=128, c0=0, c1=None):
    t = k.sb[name].ap()
    if c1 is None:
        c1 = t.shape[-1]
    return t[p0:p1, c0:c1]


def xqfm(k, fc, c0=0, c1=NR):
    return a(k, "s4", 0, 128, 2048 * fc + c0, 2048 * fc + c1)
def qfm(k, dc, c0=0, c1=NR):
    return a(k, "s4", 0, 128, 8192 + 2048 * dc + c0, 8192 + 2048 * dc + c1)
def kfm(k, dc, c0=0, c1=S):
    return a(k, "s1", 0, 128, 4096 * dc + c0, 4096 * dc + c1)
def vtm(k, kt, c0=0, c1=DH):
    return a(k, "s2", 0, 128, 512 * kt + c0, 512 * kt + c1)
def hso(k, fc, c0=0, c1=NR):
    return a(k, "s3", 0, 128, 2048 * fc + c0, 2048 * fc + c1)
def hsring(k, buf, fc, c0=0, c1=512):
    base = 8192 + 2048 * buf + 512 * fc
    return a(k, "s3", 0, 128, base + c0, base + c1)
def xtm_ring(k, buf, tt, c0=0, c1=512):
    base = 2048 * buf + 512 * tt
    return a(k, "s7", 0, 128, base + c0, base + c1)
def xfm_ring(k, buf, fc, c0=0, c1=512):
    base = 8192 + 2048 * buf + 512 * fc
    return a(k, "s3", 0, 128, base + c0, base + c1)
def wT(k, mi, m, ic, o0, o1):
    base = 2048 * m + 512 * ic
    return a(k, "s5", 0, 128, base + o0, base + o1)
def holtT(k, p0, p1, c0, c1):
    return a(k, "s6", p0, p1, c0, c1)
def x1t(k, tt, c0=0, c1=512):
    return a(k, "s6", 0, 128, 512 * tt + c0, 512 * tt + c1)
def x2t(k, tt, c0=0, c1=512):
    return a(k, "s7", 0, 128, 512 * tt + c0, 512 * tt + c1)
QXT = [("s6", 0), ("s6", 2048), ("s6", 4096), ("s6", 6144),
       ("s7", 0), ("s7", 2048), ("s7", 4096), ("s7", 6144)]
def yfm(k, fc, c0=0, c1=NR):
    return a(k, "s1", 0, 128, 2048 * fc + c0, 2048 * fc + c1)
def h1sb(k, fb, c0=0, c1=512):
    return a(k, "s2", 0, 128, 512 * fb + c0, 512 * fb + c1)
def z3t(k, tt, c0=0, c1=D):
    return a(k, "s4", 0, 128, 1024 * tt + c0, 1024 * tt + c1)
def qxfm(k, kc, c0=0, c1=NR):
    nm, base = QXT[kc]
    return a(k, nm, 0, 128, base + c0, base + c1)
def wgq_tile(k, kc, c0=0, c1=FFI):
    if kc < 5:
        return a(k, "s1", 0, 128, FFI * kc + c0, FFI * kc + c1)
    return a(k, "s3", 0, 128, FFI * (kc - 5) + c0, FFI * (kc - 5) + c1)
def wuq_tile(k, kc, c0=0, c1=FFI):
    if kc < 5:
        return a(k, "s2", 0, 128, FFI * kc + c0, FFI * kc + c1)
    if kc < 7:
        return a(k, "s3", 0, 128, 8448 + FFI * (kc - 5) + c0, 8448 + FFI * (kc - 5) + c1)
    return a(k, "s5", 0, 128, c0, c1)
def wdq_tile(k, kc, c0=0, c1=1024):
    if kc < 16:
        return a(k, "s1", 0, 128, 1024 * kc + c0, 1024 * kc + c1)
    return a(k, "s2", 0, 128, 1024 * (kc - 16) + c0, 1024 * (kc - 16) + c1)
def hqring(k, buf, kc, c0=0, c1=128):
    base = 2816 * buf + 128 * kc
    return a(k, "s3", 0, 128, base + c0, base + c1)
def hqstage(k, kc, c0=0, c1=128):
    base = 2816 + 128 * kc
    return a(k, "s5", 0, 128, base + c0, base + c1)


def gbe_ap(k, c0, c1):
    return k.sb["f32b"].ap().bitcast(BF16)[0:128, 5120 + c0:5120 + c1]
def bvb_ap(k, c0, c1):
    return k.sb["f32b"].ap().bitcast(BF16)[0:128, 7168 + c0:7168 + c1]


def barrier(k):
    p = k.p
    names = {"sync": "bsy", "tensor": "bte", "vector": "bve",
             "scalar": "bse", "gpsimd": "bgp"}
    vals = {}
    for eng in ENGS:
        s = p.sems[names[eng]]
        p.cnt[s] += 1
        vals[eng] = p.cnt[s]
        p.ops[eng].append(lambda e, s=s: e.nop().then_inc(s, 1))
    for eng in ENGS:
        for o in ENGS:
            if o != eng:
                p.wait(eng, p.sems[names[o]], vals[o])


def mm(k, out_ap, lhsT, rhs, start=True, stop=True, tp=None):
    return k.p.inc("tensor", lambda e, o=out_ap, l=lhsT, r=rhs, s1=start,
                   s2=stop, tp=tp: e.matmul(o, l, r, start=s1, stop=s2,
                                            tile_position=tp), k.S("te"), 1)


def trans(k, out_ap, in_ap, fp32=False):
    ident = a(k, "idn") if fp32 else a(k, "idnb")
    return k.p.inc("tensor", lambda e, o=out_ap, i=in_ap, d=ident:
                   e.transpose(o, i, d), k.S("te"), 1)


def psb(k, nm, c0, c1):
    return k.ps[nm].bitcast(BF16)[:, c0:c1]


def sdma(k, dst, src, eng="sync", slow=False):
    sem = k.S("ld") if eng == "sync" else k.S("gd")
    if slow:
        return k.p.inc(eng, lambda e, d=dst, s=src: e.dma_start(
            out=d, in_=s, allow_slow_non_contiguous=True), sem, 16)
    return k.p.inc(eng, lambda e, d=dst, s=src: e.dma_start(out=d, in_=s), sem, 16)


def stdma(k, dst, src):
    return k.p.inc("sync", lambda e, d=dst, s=src: e.dma_start(out=d, in_=s),
                   k.S("st"), 16)


def vop(k, fn):
    k.p.on("vector", fn)


def vinc(k, fn):
    return k.p.inc("vector", fn, k.S("ve"), 1)


def sinc(k, fn):
    return k.p.inc("scalar", fn, k.S("se"), 1)


def wait(k, eng, sem, v=None):
    if v is None:
        v = k.p.cnt[k.S(sem)]
    k.p.wait(eng, k.S(sem), v)


def phase_consts(k):
    d = k.d
    sdma(k, a(k, "idn"), d["idn"])
    sdma(k, a(k, "idnb"), d["idnb"])
    sdma(k, a(k, "jmatb"), d["jmatb"])
    sdma(k, a(k, "sel1a", 0, 1), d["sel1a"])
    sdma(k, a(k, "sel1b", 32, 33), d["sel1b"])
    sdma(k, a(k, "sel88", 0, 88), d["sel88"])
    sdma(k, a(k, "onesf", 0, 1), d["onesf"])
    names = [f"{w}{i}" for i in (1, 2) for w in ("bq", "bk", "bv", "bo")]
    for j, nm in enumerate(names):
        sdma(k, a(k, "bias4", 0, 128, 4 * j, 4 * j + 4),
             AP(tensor=d[nm].tensor, offset=0, ap=[[1, 128], [128, 4]]), slow=True)
    sdma(k, a(k, "bff1c"), AP(tensor=d["bff1"].tensor, offset=0, ap=[[1, 128], [128, 32]]), slow=True)
    sdma(k, a(k, "bff2c"), AP(tensor=d["bff2"].tensor, offset=0, ap=[[1, 128], [128, 8]]), slow=True)
    sdma(k, a(k, "rowv", 0, 1, 0, 1), d["mask0"].rearrange("(a b) -> a b", a=1))
    wait(k, "vector", "ld")
    vinc(k, lambda e: e.memset(a(k, "onescol"), 1.0))
    wait(k, "tensor", "ld")
    wait(k, "tensor", "ve")
    mm(k, k.ps["Z1"][:, 0:1], a(k, "onesf", 0, 1), a(k, "rowv", 0, 1, 0, 1))
    wait(k, "vector", "te")
    vmc = vinc(k, lambda e: e.tensor_copy(out=a(k, "maskc", 0, 128, 0, 1),
                                          in_=k.ps["Z1"][:, 0:1]))
    wait(k, "vector", "ve", vmc)
    vop(k, lambda e: e.tensor_scalar(out=a(k, "maskc", 0, 128, 1, 2),
                                     in0=a(k, "maskc", 0, 128, 0, 1),
                                     scalar1=-1.0, scalar2=1.0,
                                     op0=ALU.mult, op1=ALU.add))
    v = vinc(k, lambda e: e.engine_nop())
    wait(k, "tensor", "ve", v)
    wait(k, "sync", "ve", v)


def build_bvb(k):
    d = k.d
    for i in (1, 2):
        v = sdma(k, a(k, "rowv", 0, 1, 0, 512), d[f"bv{i}"].rearrange("(a b) -> a b", a=1))
        wait(k, "tensor", "ld", v)
        mm(k, k.ps["Z1"][:, 0:512], a(k, "onesf", 0, 1), a(k, "rowv", 0, 1, 0, 512))
        wait(k, "vector", "te")
        col = 512 * (i - 1)
        ve = vinc(k, lambda e, col=col: e.tensor_copy(
            out=bvb_ap(k, col, col + 512), in_=k.ps["Z1"][:, 0:512]))
        wait(k, "sync", "ve", ve)
        wait(k, "tensor", "ve", ve)


def build_gbe(k, gname, bename, width):
    d = k.d
    for idx, nm in ((0, gname), (1, bename)):
        for off in range(0, width, 512):
            w = min(512, width - off)
            v = sdma(k, a(k, "rowv", 0, 1, 0, w),
                     d[nm].rearrange("(a b) -> a b", a=1)[:, off:off + w])
            wait(k, "tensor", "ld", v)
            wait(k, "tensor", "ve")
            mm(k, k.ps["Z1"][:, 0:w], a(k, "onesf", 0, 1), a(k, "rowv", 0, 1, 0, w))
            wait(k, "vector", "te")
            col = idx * 1024 + off
            ve = vinc(k, lambda e, col=col, w=w: e.tensor_copy(
                out=gbe_ap(k, col, col + w), in_=k.ps["Z1"][:, 0:w]))
            wait(k, "sync", "ve", ve)
    wait(k, "tensor", "ve")


def phase_holt_tables(k):
    p, d = k.p, k.d
    hf = k.sb["f32b"].ap()
    hb = k.sb["s6"].ap()
    V2C, G2C, W2C = 80, 592, 1104

    def V(fn):
        p.on("vector", fn)

    def VE(fn):
        v = vinc(k, fn)
        wait(k, "vector", "ve", v)
        return v

    VE(lambda e: e.memset(hf[0:1, 6:7], 10.0))
    VE(lambda e: e.reciprocal(out=hf[0:1, 4:5], in_=hf[0:1, 6:7]))
    for _ in range(2):
        VE(lambda e: e.tensor_scalar(out=hf[0:1, 7:8], in0=hf[0:1, 4:5],
                                     scalar1=-10.0, scalar2=2.0,
                                     op0=ALU.mult, op1=ALU.add))
        VE(lambda e: e.tensor_tensor(out=hf[0:1, 4:5], in0=hf[0:1, 4:5],
                                     in1=hf[0:1, 7:8], op=ALU.mult))
    VE(lambda e: e.tensor_tensor(out=hf[0:1, 5:6], in0=hf[0:1, 4:5],
                                 in1=hf[0:1, 4:5], op=ALU.mult))
    VE(lambda e: e.tensor_scalar(out=hf[0:1, 0:1], in0=hf[0:1, 4:5],
                                 scalar1=-1.0, scalar2=1.0, op0=ALU.mult, op1=ALU.add))
    VE(lambda e: e.tensor_copy(out=hf[0:1, 1:2], in_=hf[0:1, 0:1]))
    VE(lambda e: e.tensor_scalar(out=hf[0:1, 2:3], in0=hf[0:1, 5:6],
                                 scalar1=-1.0, scalar2=None, op0=ALU.mult))
    VE(lambda e: e.tensor_scalar(out=hf[0:1, 3:4], in0=hf[0:1, 5:6],
                                 scalar1=-1.0, scalar2=1.0, op0=ALU.mult, op1=ALU.add))
    VE(lambda e: e.tensor_copy(out=hf[0:1, 8:9], in_=hf[0:1, 0:1]))
    VE(lambda e: e.tensor_copy(out=hf[0:1, 9:10], in_=hf[0:1, 2:3]))
    VE(lambda e: e.tensor_copy(out=hf[0:1, 10:11], in_=hf[0:1, 1:2]))
    VE(lambda e: e.tensor_copy(out=hf[0:1, 11:12], in_=hf[0:1, 3:4]))
    vrow = vinc(k, lambda e: e.engine_nop())

    wait(k, "sync", "ve", vrow)
    sdma(k, hf[0:2, 16:18], hf[0:1, 0:4], slow=True)
    sdma(k, hf[0:2, 48:50], hf[0:1, 8:12], slow=True)
    sdma(k, hf[0:2, W2C:W2C + 1], hf[0:1, 4:6], slow=True)
    sdma(k, hf[0:2, V2C:V2C + 1], hf[0:1, 0:2], slow=True)
    vst = p.cnt[k.S("ld")]
    wait(k, "vector", "ld", vst)
    VE(lambda e: e.memset(hf[0:2, G2C:G2C + 1], 0.0))
    VE(lambda e: e.memset(hf[0:1, G2C:G2C + 1], 1.0))
    v0 = vinc(k, lambda e: e.engine_nop())
    wait(k, "tensor", "ve", v0)
    wait(k, "tensor", "ld", vst)

    ps = k.ps["C"]
    smb = k.sb["f32a"].ap().bitcast(BF16)
    def ev(dst, src):
        wait(k, "vector", "te")
        ve = vinc(k, lambda e, dst=dst, src=src: e.tensor_copy(out=dst, in_=src))
        wait(k, "tensor", "ve", ve)

    m, stage = 1, 0
    m512_stage = None
    while m < 4096:
        if m == 512:
            m512_stage = stage
        Pm = hf[0:2, 16 + 2 * stage:18 + 2 * stage]
        Mm = hf[0:2, 48 + 2 * stage:50 + 2 * stage]
        step = min(m, 4096 - m)
        for off in range(0, step, 512):
            w = min(512, step - off)
            mm(k, ps[0:2, 0:w], Mm, hf[0:2, W2C + off:W2C + off + w])
            if m + off < 2048:
                ev(hf[0:2, W2C + m + off:W2C + m + off + w], ps[0:2, 0:w])
            else:
                wait(k, "vector", "te")
                ve = vinc(k, lambda e, w=w: e.tensor_copy(
                    out=smb[0:2, 3520:3520 + w][0:2, 0:w], in_=ps[0:2, 0:w]))
                wait(k, "tensor", "ve", ve)
                wait(k, "gpsimd", "ve", ve)
                sdma(k, d["hpad"][512 + m + off:512 + m + off + w],
                     smb[0:1, 3520:3520 + w], "gpsimd")
                wait(k, "vector", "gd")
        if m < 512:
            stepv = min(m, 512 - m)
            for cc, lhs in ((G2C, Mm), (V2C, Pm)):
                mm(k, ps[0:2, 0:stepv], lhs, hf[0:2, cc:cc + stepv])
                ev(hf[0:2, cc + m:cc + m + stepv], ps[0:2, 0:stepv])
        mm(k, ps[0:2, 0:2], Mm, Pm)
        ev(hf[0:2, 18 + 2 * stage:20 + 2 * stage], ps[0:2, 0:2])
        mm(k, ps[0:2, 0:2], Pm, Mm)
        ev(hf[0:2, 50 + 2 * stage:52 + 2 * stage], ps[0:2, 0:2])
        m *= 2
        stage += 1
    c512 = 48 + 2 * (m512_stage + 1)
    V(lambda e: e.tensor_copy(out=hb[0:2, 4624:4626], in_=hf[0:2, c512:c512 + 2]))
    V(lambda e: e.tensor_copy(out=hb[0:2, 4112:4624], in_=hf[0:2, V2C:V2C + 512]))
    zr = k.sb["f32a"].ap()
    V(lambda e: e.memset(zr[0:1, 0:512], 0.0))
    vz = vinc(k, lambda e: e.engine_nop())
    wait(k, "gpsimd", "ve", vz)
    sdma(k, d["hpad"][0:512], zr[0:1, 0:512], "gpsimd")
    sdma(k, d["hpad"][4608:5120], zr[0:1, 0:512], "gpsimd")
    sdma(k, d["hpad"][512:2560], hf[0:1, W2C:W2C + 2048], "gpsimd")
    wait(k, "sync", "gd")

    hkb = k.sb["ptr"].ap()
    for dst_base in (0, 2048):
        for jsub in range(4):
            off = 512 - 127 - 128 * jsub
            v = sdma(k, hkb[0:128, 0:512],
                     AP(tensor=d["hpad"].tensor, offset=off, ap=[[1, 128], [1, 512]]))
            wait(k, "tensor", "ld", v)
            for tsub in range(4):
                mm(k, k.ps["C"][:, 128 * tsub:128 * tsub + 128], a(k, "jmatb"),
                   hkb[0:128, 128 * tsub:128 * tsub + 128])
            wait(k, "vector", "te")
            col = dst_base + 512 * jsub
            ve = vinc(k, lambda e, col=col: e.tensor_copy(
                out=hb[0:128, col:col + 512], in_=k.ps["C"][:, 0:512]))
            wait(k, "tensor", "ve", ve)
            wait(k, "sync", "ve", ve)
    V(lambda e: e.tensor_copy(out=hb[0:1, 2048:2560], in_=hf[0:1, G2C:G2C + 512]))
    for t in range(4):
        srcc = 128 * (3 - t)
        k.p.inc("tensor", lambda e, srcc=srcc: e.transpose(
            k.ps["C"][:, 0:2], hf[0:2, W2C + srcc:W2C + srcc + 128],
            a(k, "idn", 0, 2, 0, 2)), k.S("te"), 1)
        wait(k, "vector", "te")
        ve = vinc(k, lambda e: e.tensor_copy(out=hkb[0:128, 1024:1026],
                                             in_=k.ps["C"][:, 0:2]))
        wait(k, "tensor", "ve", ve)
        mm(k, k.ps["C"][:, 0:2], a(k, "jmatb"), hkb[0:128, 1024:1026])
        wait(k, "vector", "te")
        for dst0 in (4096, 4104):
            vinc(k, lambda e, dst0=dst0, t=t: e.tensor_copy(
                out=hb[0:128, dst0 + 2 * t:dst0 + 2 * t + 2], in_=k.ps["C"][:, 0:2]))
        wait(k, "tensor", "ve")
    wait(k, "gpsimd", "ve")
    sdma(k, hb[0:1, 4104:4105], hf[0:1, G2C + 511:G2C + 512], "gpsimd", slow=True)
    sdma(k, hb[0:1, 4105:4106], hf[1:2, G2C + 511:G2C + 512], "gpsimd", slow=True)
    wait(k, "vector", "gd")
    wait(k, "tensor", "gd")
    V(lambda e: e.memset(hb[0:2, 4704:5728], 0.0))
    v = vinc(k, lambda e: e.engine_nop())
    wait(k, "tensor", "ve", v)


def sprev(k, c0=0, c1=512):
    return a(k, "s6", 0, 2, 4704 + c0, 4704 + c1)
def snew(k, c0=0, c1=512):
    return a(k, "s6", 0, 2, 5216 + c0, 5216 + c1)


def phase_attn_weights(k, mi):
    d = k.d
    stage = k.sb["ptr"].ap()
    for m, w in enumerate(("wq", "wk", "wv", "wo")):
        wd = d[f"{w}{mi}"]
        for oc in range(4):
            vg = sdma(k, stage[0:128, 0:512], wd[128 * oc:128 * oc + 128, :], "gpsimd")
            wait(k, "tensor", "gd", vg)
            for ic in range(4):
                trans(k, psb(k, "D", 128 * ic, 128 * ic + 128),
                      stage[0:128, 128 * ic:128 * ic + 128])
            wait(k, "vector", "te")
            for ic in range(4):
                dst = wT(k, mi, m, ic, 128 * oc, 128 * oc + 128)
                vinc(k, lambda e, dst=dst, ic=ic: e.tensor_copy(
                    out=dst, in_=psb(k, "D", 128 * ic, 128 * ic + 128)))
            wait(k, "tensor", "ve")
            wait(k, "gpsimd", "ve")


def ln_tile(k, full_src, apply_src, out_bf, width, sq_ap=None):
    st = k.sb["stats"].ap()
    if sq_ap is None:
        sq_ap = k.sb["f32b"].ap()[0:128, 0:width]
    vsq = sinc(k, lambda e: e.activation(out=sq_ap, in_=full_src,
                                         func=AF.Square, accum_out=st[0:128, 2:3]))
    wait(k, "vector", "se", vsq)
    va = vinc(k, lambda e: e.tensor_reduce(out=st[0:128, 3:4], in_=full_src,
                                           axis=AX.X, op=ALU.add))
    vop(k, lambda e: e.tensor_scalar(out=st[0:128, 2:3], in0=st[0:128, 2:3],
                                     scalar1=1.0 / width, scalar2=None, op0=ALU.mult))
    wait(k, "vector", "ve", va)
    vc = vinc(k, lambda e: e.tensor_scalar(out=st[0:128, 0:1], in0=st[0:128, 3:4],
                                           scalar1=1.0 / width, scalar2=None,
                                           op0=ALU.mult))
    wait(k, "vector", "ve", vc)
    vd = vinc(k, lambda e: e.tensor_tensor(out=st[0:128, 3:4], in0=st[0:128, 0:1],
                                           in1=st[0:128, 0:1], op=ALU.mult))
    wait(k, "vector", "ve", vd)
    vv = vinc(k, lambda e: e.scalar_tensor_tensor(
        out=st[0:128, 2:3], in0=st[0:128, 2:3], scalar=EPS, in1=st[0:128, 3:4],
        op0=ALU.add, op1=ALU.subtract))
    wait(k, "scalar", "ve", vv)
    vsr = sinc(k, lambda e: e.activation(out=st[0:128, 3:4], in_=st[0:128, 2:3],
                                         func=AF.Sqrt))
    wait(k, "vector", "se", vsr)
    vh = vinc(k, lambda e: e.reciprocal(out=st[0:128, 1:2], in_=st[0:128, 3:4]))
    wait(k, "vector", "ve", vh)
    vij = vinc(k, lambda e: e.scalar_tensor_tensor(
        out=st[0:128, 3:4], in0=st[0:128, 1:2], scalar=st[0:128, 1:2],
        in1=st[0:128, 2:3], op0=ALU.mult, op1=ALU.mult))
    wait(k, "vector", "ve", vij)
    vk = vinc(k, lambda e: e.tensor_scalar(out=st[0:128, 3:4], in0=st[0:128, 3:4],
                                           scalar1=-0.5, scalar2=1.5,
                                           op0=ALU.mult, op1=ALU.add))
    wait(k, "vector", "ve", vk)
    vl = vinc(k, lambda e: e.tensor_tensor(out=st[0:128, 1:2], in0=st[0:128, 1:2],
                                           in1=st[0:128, 3:4], op=ALU.mult))
    wait(k, "vector", "ve", vl)
    vm = vinc(k, lambda e: e.scalar_tensor_tensor(
        out=st[0:128, 4:5], in0=st[0:128, 0:1], scalar=-1.0, in1=st[0:128, 1:2],
        op0=ALU.mult, op1=ALU.mult))
    wait(k, "scalar", "ve", vm)
    return sinc(k, lambda e: e.activation(
        out=out_bf, in_=apply_src, func=AF.Identity,
        scale=st[0:128, 1:2], bias=st[0:128, 4:5]))


def phase_xq(k):
    d = k.d
    xf = k.sb["f32a"].ap()
    tmb = k.sb["ptr"].ap()
    done = {}
    for tt in range(16):
        buf = tt % 2
        if tt >= 2:
            wait(k, "sync", "se", done[tt - 2][0])
            wait(k, "sync", "se", done[tt - 2][1])
        v = sdma(k, xf[0:128, 1024 * buf:1024 * buf + 1024],
                 d["xq"][128 * tt:128 * tt + 128, :])
        wait(k, "scalar", "ld", v)
        wait(k, "vector", "ld", v)
        vxh = ln_tile(k, xf[0:128, 1024 * buf:1024 * buf + 1024],
                      xf[0:128, 1024 * buf:1024 * buf + 512],
                      tmb[0:128, 512 * buf:512 * buf + 512], D)
        wait(k, "tensor", "se", vxh)
        for fc in range(4):
            trans(k, psb(k, "D", 128 * fc, 128 * fc + 128),
                  tmb[0:128, 512 * buf + 128 * fc:512 * buf + 128 * fc + 128])
        wait(k, "scalar", "te")
        for fc in range(4):
            dst = xqfm(k, fc, 128 * tt, 128 * tt + 128)
            sinc(k, lambda e, dst=dst, fc=fc: e.copy(
                out=dst, in_=psb(k, "D", 128 * fc, 128 * fc + 128)))
        wait(k, "tensor", "se")
        done[tt] = (vxh, k.p.cnt[k.S("se")])
    wait(k, "tensor", "se")
    proj_fm(k, 1, 0, lambda fc, c0, c1: xqfm(k, fc, c0, c1), NR, lambda dc, c0, c1: qfm(k, dc, c0, c1))


def proj_fm(k, mi, m, src_fn, ntok, dst_fn):
    for dc in range(4):
        for t0 in range(0, ntok, 512):
            ps = k.ps["C"] if (t0 // 512) % 2 == 0 else k.ps["D"]
            for fc in range(4):
                mm(k, ps[:, 0:512], wT(k, mi, m, fc, 128 * dc, 128 * dc + 128),
                   src_fn(fc, t0, t0 + 512), start=(fc == 0), stop=(fc == 3))
            wait(k, "scalar", "te")
            bj = 4 * (4 * (mi - 1) + m) + dc
            dst = dst_fn(dc, t0, t0 + 512)
            vse = sinc(k, lambda e, dst=dst, ps=ps, bj=bj: e.activation(
                out=dst, in_=ps[:, 0:512], func=AF.Identity,
                bias=a(k, "bias4", 0, 128, bj, bj + 1)))
            wait(k, "tensor", "se", vse)


def phase_stream_x(k):
    p, d = k.p, k.d
    xf = k.sb["f32a"].ap()
    hb = k.sb["s6"].ap()
    hst = k.sb["f32b"].ap().bitcast(BF16)

    for c in range(NCH):
        blk = c % 2
        for tt in range(4):
            ti = 4 * c + tt
            buf = ti % 2
            v = sdma(k, xf[0:128, 1024 * buf:1024 * buf + 1024],
                     d["x"][128 * ti:128 * ti + 128, :])
            wait(k, "scalar", "ld", v)
            wait(k, "vector", "ld", v)
            vxh = ln_tile(k, xf[0:128, 1024 * buf:1024 * buf + 1024],
                          xf[0:128, 1024 * buf:1024 * buf + 512],
                          xtm_ring(k, blk, tt), D)
            wait(k, "tensor", "se", vxh)
            for fc in range(4):
                trans(k, psb(k, "Z2", 128 * fc, 128 * fc + 128),
                      xtm_ring(k, blk, tt, 128 * fc, 128 * fc + 128))
            wait(k, "scalar", "te")
            for fc in range(4):
                dst = xfm_ring(k, blk, fc, 128 * tt, 128 * tt + 128)
                sinc(k, lambda e, dst=dst, fc=fc: e.copy(
                    out=dst, in_=psb(k, "Z2", 128 * fc, 128 * fc + 128)))
            wait(k, "tensor", "se")
            wait(k, "sync", "se")
        for dc in range(4):
            ps = k.ps["C"] if dc % 2 == 0 else k.ps["D"]
            for fc in range(4):
                mm(k, ps[:, 0:512], wT(k, 1, 1, fc, 128 * dc, 128 * dc + 128),
                   xfm_ring(k, blk, fc, 0, 512), start=(fc == 0), stop=(fc == 3))
            wait(k, "scalar", "te")
            bj = 4 * 1 + dc
            dst = kfm(k, dc, 512 * c, 512 * c + 512)
            vse = sinc(k, lambda e, dst=dst, ps=ps, bj=bj: e.activation(
                out=dst, in_=ps[:, 0:512], func=AF.Identity,
                bias=a(k, "bias4", 0, 128, bj, bj + 1)))
            wait(k, "tensor", "se", vse)
        for tt in range(4):
            ps = k.ps["C"] if tt % 2 == 0 else k.ps["D"]
            for fc in range(4):
                mm(k, ps[:, 0:512], xfm_ring(k, blk, fc, 128 * tt, 128 * tt + 128),
                   wT(k, 1, 2, fc, 0, 512), start=(fc == 0), stop=(fc == 3))
            wait(k, "vector", "te")
            dst = vtm(k, 4 * c + tt)
            vve = vinc(k, lambda e, dst=dst, ps=ps: e.scalar_tensor_tensor(
                out=dst, in0=ps[:, 0:512], scalar=1.0,
                in1=bvb_ap(k, 0, 512), op0=ALU.mult, op1=ALU.add))
            wait(k, "tensor", "ve", vve)
        ut0 = 4104 if c == 0 else 4096
        for js in range(4):
            mm(k, k.ps["Z2"][0:2, 0:512], hb[0:128, ut0 + 2 * js:ut0 + 2 * js + 2],
               xtm_ring(k, blk, js, 0, 512), start=(js == 0), stop=False)
        mm(k, k.ps["Z2"][0:2, 0:512], hb[0:2, 4624:4626], sprev(k),
           start=False, stop=True)
        wait(k, "vector", "te")
        vs = vinc(k, lambda e: e.tensor_copy(out=snew(k), in_=k.ps["Z2"][0:2, 0:512]))
        wait(k, "tensor", "ve", vs)
        mt0 = 2048 if c == 0 else 0
        for fcq in range(4):
            ps = k.ps["Z0"] if fcq % 2 == 0 else k.ps["Z1"]
            for js in range(4):
                mm(k, ps[:, 0:512],
                   xtm_ring(k, blk, js, 128 * fcq, 128 * fcq + 128),
                   hb[0:128, mt0 + 512 * js:mt0 + 512 * js + 512],
                   start=(js == 0), stop=False)
            mm(k, ps[:, 0:512], sprev(k, 128 * fcq, 128 * fcq + 128),
               hb[0:2, 4112:4624], start=False, stop=True)
            wait(k, "scalar", "te")
            vse = sinc(k, lambda e, ps=ps, fcq=fcq: e.copy(
                out=hst[0:128, 4096 + 512 * fcq:4096 + 512 * fcq + 512],
                in_=ps[:, 0:512]))
            wait(k, "sync", "se", vse)
            sdma(k, d["hsD"][128 * fcq:128 * fcq + 128, 512 * c:512 * c + 512],
                 hst[0:128, 4096 + 512 * fcq:4096 + 512 * fcq + 512])
            wait(k, "tensor", "se", vse)
        wait(k, "vector", "te")
        vsp = vinc(k, lambda e: e.tensor_copy(out=sprev(k), in_=snew(k)))
        wait(k, "tensor", "ve", vsp)
        wait(k, "sync", "ve")
        wait(k, "sync", "ld")
        wait(k, "scalar", "ld")
    wait(k, "sync", "ld")


def phase_attention(k, mi):
    p = k.p
    build_gbe(k, f"g{mi}", f"be{mi}", 512)
    pt = k.sb["ptr"].ap()
    smf = k.sb["f32a"].ap()
    smb = k.sb["f32a"].ap().bitcast(BF16)
    bcst = k.sb["f32b"].ap().bitcast(BF16)
    res_fn = (lambda dc, c0, c1: xqfm(k, dc, c0, c1)) if mi == 1 else \
             (lambda dc, c0, c1: hso(k, dc, c0, c1))
    xdst = x1t if mi == 1 else x2t
    zst = k.sb["f32b"].ap().bitcast(BF16)
    ost = lambda hp, c0=0, c1=512: bcst[0:128, 2048 + 512 * hp + c0:2048 + 512 * hp + c1]

    cut = ATTCUT if mi == ATTMI else None
    nqc = 1 if cut else 4
    nhp = 1 if cut in ("S", "AV", "SUMS", "BCAST") else 4
    for qc in range(nqc):
        vo_prev = None
        for hp in range(nhp):
            exp_se = {}
            avsum_te = {}
            S_te = {}

            def psS(kt):
                if kt % 2 == 0:
                    return k.ps["A"][:, 0:512], k.ps["A"][:, 512:1024]
                return k.ps["C"][:, 0:512], k.ps["D"][:, 0:512]

            def emit_S(kt, hp=hp, qc=qc):
                if kt >= 2:
                    wait(k, "tensor", "se", exp_se[kt - 2][1])
                pa, pb_ = psS(kt)
                mm(k, pa, kfm(k, hp, 128 * kt, 128 * kt + 128)[0:64, :],
                   qfm(k, hp, 512 * qc, 512 * qc + 512)[0:64, :],
                   start=True, stop=True, tp=(0, 0))
                mm(k, pb_, kfm(k, hp, 128 * kt, 128 * kt + 128)[64:128, :],
                   qfm(k, hp, 512 * qc, 512 * qc + 512)[64:128, :],
                   start=True, stop=True, tp=(64, 0))
                S_te[kt] = p.cnt[k.S("te")]

            def emit_exp(kt):
                pa, pb_ = psS(kt)
                pb = kt % 2
                wait(k, "scalar", "te", S_te[kt])
                if kt >= 2 and (kt - 2) in avsum_te:
                    wait(k, "scalar", "te", avsum_te[kt - 2])
                v0 = sinc(k, lambda e, pa=pa, pb=pb: e.activation(
                    out=pt[0:128, 1024 * pb:1024 * pb + 512], in_=pa,
                    func=AF.Exp, scale=0.125))
                v1 = sinc(k, lambda e, pb_=pb_, pb=pb: e.activation(
                    out=pt[0:128, 1024 * pb + 512:1024 * pb + 1024], in_=pb_,
                    func=AF.Exp, scale=0.125))
                exp_se[kt] = (v0, v1)

            def emit_AV(kt, hp=hp):
                pb = kt % 2
                if kt == 0 and vo_prev is not None:
                    wait(k, "tensor", "ve", vo_prev)
                wait(k, "tensor", "se", exp_se[kt][0])
                mm(k, k.ps["Z0"][0:64, :], vtm(k, kt, 128 * hp, 128 * hp + 64),
                   pt[0:128, 1024 * pb:1024 * pb + 512],
                   start=(kt == 0), stop=(kt == 31), tp=(0, 0))
                mm(k, k.ps["Z2"][0:1, :], a(k, "onescol"),
                   pt[0:128, 1024 * pb:1024 * pb + 512],
                   start=(kt == 0), stop=(kt == 31), tp=(0, 0))
                wait(k, "tensor", "se", exp_se[kt][1])
                mm(k, k.ps["Z1"][64:128, :], vtm(k, kt, 128 * hp + 64, 128 * hp + 128),
                   pt[0:128, 1024 * pb + 512:1024 * pb + 1024],
                   start=(kt == 0), stop=(kt == 31), tp=(0, 64))
                mm(k, k.ps["Z3"][0:1, :], a(k, "onescol"),
                   pt[0:128, 1024 * pb + 512:1024 * pb + 1024],
                   start=(kt == 0), stop=(kt == 31), tp=(0, 0))
                avsum_te[kt] = p.cnt[k.S("te")]

            emit_S(0)
            emit_exp(0)
            emit_S(1)
            emit_exp(1)
            for kt in range(2, 32):
                emit_S(kt)
                if cut != "S":
                    emit_AV(kt - 2)
                emit_exp(kt)
            if cut != "S":
                emit_AV(30)
                emit_AV(31)
            if cut in ("S", "AV"):
                return
            wait(k, "vector", "te")
            fbf = k.sb["f32b"].ap()
            vinc(k, lambda e: e.tensor_copy(out=fbf[0:1, 0:512],
                                            in_=k.ps["Z2"][0:1, :]))
            vinc(k, lambda e: e.tensor_copy(out=fbf[0:1, 512:1024],
                                            in_=k.ps["Z3"][0:1, :]))
            vop(k, lambda e: e.reciprocal(out=smf[0:1, 0:1024],
                                          in_=fbf[0:1, 0:1024]))
            vop(k, lambda e: e.tensor_tensor(
                out=smf[0:1, 1024:2048], in0=fbf[0:1, 0:1024],
                in1=smf[0:1, 0:1024], op=ALU.mult))
            vop(k, lambda e: e.tensor_scalar(
                out=smf[0:1, 1024:2048], in0=smf[0:1, 1024:2048],
                scalar1=-1.0, scalar2=2.0, op0=ALU.mult, op1=ALU.add))
            vop(k, lambda e: e.tensor_tensor(
                out=bcst[0:1, 0:1024], in0=smf[0:1, 0:1024],
                in1=smf[0:1, 1024:2048], op=ALU.mult))
            vrec = vinc(k, lambda e: e.engine_nop())
            if cut == "SUMS":
                return
            wait(k, "tensor", "ve", vrec)
            mm(k, k.ps["Z2"][0:64, 0:512], a(k, "sel1a", 0, 1, 0, 64),
               bcst[0:1, 0:512], start=True, stop=True, tp=(0, 0))
            mm(k, k.ps["Z2"][64:128, 0:512], a(k, "sel1a", 0, 1, 0, 64),
               bcst[0:1, 512:1024], start=True, stop=True, tp=(0, 64))
            wait(k, "vector", "te")
            vb = vinc(k, lambda e: e.tensor_copy(out=bcst[0:128, 4096:4608],
                                                 in_=k.ps["Z2"][:, 0:512]))
            vinc(k, lambda e, hp=hp: e.tensor_tensor(
                out=ost(hp)[0:64, :], in0=k.ps["Z0"][0:64, :],
                in1=bcst[0:64, 4096:4608], op=ALU.mult))
            vo = vinc(k, lambda e, hp=hp: e.tensor_tensor(
                out=ost(hp)[64:128, :], in0=k.ps["Z1"][64:128, :],
                in1=bcst[64:128, 4096:4608], op=ALU.mult))
            vo_prev = vo
        if cut == "BCAST":
            return
        if cut == "HP":
            continue
        wait(k, "tensor", "ve", vo_prev)
        for dc in range(4):
            ps = k.ps["C"] if dc % 2 == 0 else k.ps["D"]
            for hp in range(4):
                mm(k, ps[:, 0:512], wT(k, mi, 3, hp, 128 * dc, 128 * dc + 128),
                   ost(hp), start=(hp == 0), stop=(hp == 3))
            wait(k, "vector", "te")
            bj = 4 * (4 * (mi - 1) + 3) + dc
            res = res_fn(dc, 512 * qc, 512 * qc + 512)
            vz = vinc(k, lambda e, ps=ps, bj=bj, res=res, dc=dc: e.scalar_tensor_tensor(
                out=zst[0:128, 512 * dc:512 * dc + 512], in0=ps[:, 0:512],
                scalar=a(k, "bias4", 0, 128, bj, bj + 1), in1=res,
                op0=ALU.add, op1=ALU.add))
            wait(k, "tensor", "ve", vz)
        if cut == "OPROJ":
            return
        for sub in range(4):
            tq = 4 * qc + sub
            for dc in range(4):
                trans(k, psb(k, "C", 128 * dc, 128 * dc + 128),
                      zst[0:128, 512 * dc + 128 * sub:512 * dc + 128 * sub + 128])
            wait(k, "vector", "te")
            wait(k, "scalar", "te")
            vap = ln_tile(k, psb(k, "C", 0, 512), psb(k, "C", 0, 512),
                          bcst[0:128, 4608:5120], 512,
                          sq_ap=k.sb["f32a"].ap()[0:128, 0:512])
            wait(k, "vector", "se", vap)
            vop(k, lambda e: e.tensor_tensor(
                out=bcst[0:128, 4608:5120], in0=bcst[0:128, 4608:5120],
                in1=gbe_ap(k, 0, 512), op=ALU.mult))
            dst = xdst(k, tq)
            vx = vinc(k, lambda e, dst=dst: e.tensor_tensor(
                out=dst, in0=bcst[0:128, 4608:5120],
                in1=gbe_ap(k, 1024, 1536), op=ALU.add))
            wait(k, "tensor", "ve", vx)
            wait(k, "tensor", "se", vap)


def phase_merge_hso(k):
    d = k.d
    hm = k.sb["f32b"].ap().bitcast(BF16)
    for fc in range(4):
        v0 = sdma(k, hm[0:128, 0:2048],
                  d["hsD"][128 * fc:128 * fc + 128, 0:2048])
        v1 = sdma(k, hm[0:128, 2048:4096],
                  d["hsD"][128 * fc:128 * fc + 128, 2048:4096])
        wait(k, "vector", "ld", v1)
        vop(k, lambda e, fc=fc: e.tensor_scalar(
            out=hso(k, fc), in0=hm[0:128, 2048:4096],
            scalar1=a(k, "maskc", 0, 128, 1, 2), scalar2=None, op0=ALU.mult))
        vm = vinc(k, lambda e, fc=fc: e.scalar_tensor_tensor(
            out=hso(k, fc), in0=hm[0:128, 0:2048],
            scalar=a(k, "maskc", 0, 128, 0, 1), in1=hso(k, fc),
            op0=ALU.mult, op1=ALU.add))
        wait(k, "sync", "ve", vm)


def phase_proj2(k):
    p, d = k.p, k.d
    for c in range(NCH):
        buf = c % 2
        vs = []
        for fc in range(4):
            vs.append(sdma(k, hsring(k, buf, fc),
                           d["hsD"][128 * fc:128 * fc + 128, 512 * c:512 * c + 512]))
        wait(k, "tensor", "ld", vs[-1])
        for dc in range(4):
            ps = k.ps["C"] if dc % 2 == 0 else k.ps["D"]
            for fc in range(4):
                mm(k, ps[:, 0:512], wT(k, 2, 1, fc, 128 * dc, 128 * dc + 128),
                   hsring(k, buf, fc), start=(fc == 0), stop=(fc == 3))
            wait(k, "scalar", "te")
            bj = 4 * 5 + dc
            dst = kfm(k, dc, 512 * c, 512 * c + 512)
            vse = sinc(k, lambda e, dst=dst, ps=ps, bj=bj: e.activation(
                out=dst, in_=ps[:, 0:512], func=AF.Identity,
                bias=a(k, "bias4", 0, 128, bj, bj + 1)))
            wait(k, "tensor", "se", vse)
        for tt in range(4):
            ps = k.ps["C"] if tt % 2 == 0 else k.ps["D"]
            for fc in range(4):
                mm(k, ps[:, 0:512], hsring(k, buf, fc, 128 * tt, 128 * tt + 128),
                   wT(k, 2, 2, fc, 0, 512), start=(fc == 0), stop=(fc == 3))
            wait(k, "vector", "te")
            dst = vtm(k, 4 * c + tt)
            vve = vinc(k, lambda e, dst=dst, ps=ps: e.scalar_tensor_tensor(
                out=dst, in0=ps[:, 0:512], scalar=1.0,
                in1=bvb_ap(k, 512, 1024), op0=ALU.mult, op1=ALU.add))
            wait(k, "tensor", "ve", vve)
        wait(k, "sync", "ve")
        wait(k, "sync", "se")
    proj_fm(k, 2, 0, lambda fc, c0, c1: hso(k, fc, c0, c1), NR, lambda dc, c0, c1: qfm(k, dc, c0, c1))


def phase_ffw_weights(k):
    d = k.d
    stage = k.sb["ptr"].ap()
    for wname, tname, nrow, ncol in (("wff1", "wff1T", DFF, D),
                                     ("wff2", "wff2T", D, DFF)):
        for r in range(nrow // 128):
            for cg in range(ncol // 1024):
                vg = sdma(k, stage[0:128, 0:1024],
                          d[wname][128 * r:128 * r + 128, 1024 * cg:1024 * cg + 1024],
                          "gpsimd")
                wait(k, "tensor", "gd", vg)
                for ic in range(8):
                    trans(k, psb(k, "D", 128 * (ic % 4), 128 * (ic % 4) + 128),
                          stage[0:128, 128 * ic:128 * ic + 128])
                    if ic % 4 == 3:
                        wait(k, "vector", "te")
                        for j in range(4):
                            icj = ic - 3 + j
                            vinc(k, lambda e, icj=icj, j=j: e.tensor_copy(
                                out=stage[0:128, 1024 + 128 * icj:1152 + 128 * icj],
                                in_=psb(k, "D", 128 * j, 128 * j + 128)))
                        wait(k, "tensor", "ve")
                ve = k.p.cnt[k.S("ve")]
                wait(k, "sync", "ve", ve)
                dst = AP(tensor=d[tname].tensor, offset=128 * r + 1024 * cg * nrow,
                         ap=[[nrow, 128], [128 * nrow, 8], [1, 128]])
                sdma(k, dst, stage[0:128, 1024:2048])
                wait(k, "gpsimd", "ld")
    wait(k, "sync", "ld")


def phase_attn_ffn(k):
    p, d = k.p, k.d
    build_gbe(k, "gff", "beff", 1024)
    for tq in range(16):
        for half, xsrc in ((0, x1t), (1, x2t)):
            for f in range(4):
                trans(k, psb(k, "D", 128 * f, 128 * f + 128),
                      xsrc(k, tq, 128 * f, 128 * f + 128))
            wait(k, "scalar", "te")
            for f in range(4):
                fc = 4 * half + f
                dst = yfm(k, fc, 128 * tq, 128 * tq + 128)
                sinc(k, lambda e, dst=dst, f=f: e.copy(
                    out=dst, in_=psb(k, "D", 128 * f, 128 * f + 128)))
            wait(k, "tensor", "se")
    wait(k, "vector", "se")
    wait(k, "sync", "se")

    w1r = k.sb["s3"].ap()
    w2r = k.sb["s3"].ap()
    hst = k.sb["f32b"].ap().bitcast(BF16)
    xf = k.sb["f32a"].ap()
    for tb in range(4):
        h1_te = {}
        for fb in range(32):
            buf = fb % 2
            if fb >= 2:
                wait(k, "sync", "te", h1_te[fb - 2])
            src = AP(tensor=d["wff1T"].tensor, offset=128 * fb,
                     ap=[[DFF, 128], [128 * DFF, 8], [1, 128]])
            v = sdma(k, w1r[0:128, 1024 * buf:1024 * buf + 1024], src)
            wait(k, "tensor", "ld", v)
            ps = k.ps["C"] if fb % 2 == 0 else k.ps["D"]
            for yc in range(8):
                mm(k, ps[:, 0:512], w1r[0:128, 1024 * buf + 128 * yc:1024 * buf + 128 * yc + 128],
                   yfm(k, yc, 512 * tb, 512 * tb + 512),
                   start=(yc == 0), stop=(yc == 7))
            h1_te[fb] = k.p.cnt[k.S("te")]
            wait(k, "scalar", "te")
            dst = h1sb(k, fb)
            vse = sinc(k, lambda e, dst=dst, ps=ps, fb=fb: e.activation(
                out=dst, in_=ps[:, 0:512], func=AF.Relu,
                bias=a(k, "bff1c", 0, 128, fb, fb + 1)))
            wait(k, "tensor", "se", vse)
            wait(k, "sync", "se", vse)
        h2_te = {}
        for oc in range(8):
            buf = oc % 2
            if oc >= 2:
                wait(k, "sync", "te", h2_te[oc - 2])
            src = AP(tensor=d["wff2T"].tensor, offset=128 * oc,
                     ap=[[D, 128], [128 * D, 32], [1, 128]])
            v = sdma(k, w2r[0:128, 2048 + 4096 * buf:2048 + 4096 * buf + 4096], src)
            wait(k, "tensor", "ld", v)
            ps = k.ps["C"] if oc % 2 == 0 else k.ps["D"]
            for ffc in range(32):
                mm(k, ps[:, 0:512],
                   w2r[0:128, 2048 + 4096 * buf + 128 * ffc:2048 + 4096 * buf + 128 * ffc + 128],
                   h1sb(k, ffc), start=(ffc == 0), stop=(ffc == 31))
            h2_te[oc] = k.p.cnt[k.S("te")]
            wait(k, "scalar", "te")
            vse = sinc(k, lambda e, ps=ps, oc=oc: e.activation(
                out=hst[0:128, 4096:4608], in_=ps[:, 0:512], func=AF.Identity,
                bias=a(k, "bff2c", 0, 128, oc, oc + 1)))
            wait(k, "tensor", "se", vse)
            for sub in range(4):
                trans(k, psb(k, "Z0", 128 * sub, 128 * sub + 128),
                      hst[0:128, 4096 + 128 * sub:4096 + 128 * sub + 128])
            wait(k, "vector", "te")
            for sub in range(4):
                tq = 4 * tb + sub
                xs = x1t(k, tq, 128 * oc, 128 * oc + 128) if oc < 4 else \
                     x2t(k, tq, 128 * (oc - 4), 128 * (oc - 4) + 128)
                dst = z3t(k, tq, 128 * oc, 128 * oc + 128)
                vinc(k, lambda e, dst=dst, xs=xs, sub=sub: e.tensor_tensor(
                    out=dst, in0=psb(k, "Z0", 128 * sub, 128 * sub + 128),
                    in1=xs, op=ALU.add))
            wait(k, "tensor", "ve")
            wait(k, "scalar", "ve")
    for tq in range(16):
        v = sdma(k, xf[0:128, 0:1024], d["xq"][128 * tq:128 * tq + 128, :])
        vap = ln_tile(k, z3t(k, tq), z3t(k, tq), hst[0:128, 4096:5120], D)
        wait(k, "vector", "se", vap)
        vop(k, lambda e: e.tensor_tensor(
            out=hst[0:128, 4096:5120], in0=hst[0:128, 4096:5120],
            in1=gbe_ap(k, 0, 1024), op=ALU.mult))
        vop(k, lambda e: e.tensor_tensor(
            out=hst[0:128, 4096:5120], in0=hst[0:128, 4096:5120],
            in1=gbe_ap(k, 1024, 2048), op=ALU.add))
        wait(k, "vector", "ld", v)
        dst = z3t(k, tq)
        vx = vinc(k, lambda e, dst=dst: e.tensor_tensor(
            out=dst, in0=hst[0:128, 4096:5120], in1=xf[0:128, 0:1024], op=ALU.add))
        wait(k, "sync", "ve", vx)
        wait(k, "scalar", "ve", vx)


def quant_ladder(k, zsrc_f32, bca_ps, bcr_ps, width, out_bf):
    p = k.p
    fbap = k.sb["f32b"].ap()
    fbb = k.sb["f32b"].ap().bitcast(BF16)
    z = fbap[0:128, 2304:2304 + width]
    u = fbap[0:128, 2816:2816 + width]
    sg = fbb[0:128, 0:width]
    gslot = lambda j: fbb[0:128, 512 * (j + 1):512 * (j + 1) + width]
    t2 = fbb[0:128, 4096:4096 + width]
    vz = vinc(k, lambda e: e.tensor_tensor(out=z, in0=zsrc_f32, in1=bca_ps,
                                           op=ALU.mult))
    wait(k, "scalar", "ve", vz)
    sinc(k, lambda e: e.activation(out=u, in_=z, func=AF.Abs))
    vsg = sinc(k, lambda e: e.activation(out=sg, in_=z, func=AF.Sign))
    wait(k, "vector", "se", vsg)
    for j in range(7):
        vg = vinc(k, lambda e, j=j: e.tensor_scalar(
            out=gslot(j), in0=u, scalar1=QMIDS[j], scalar2=None, op0=ALU.is_gt))
    wait(k, "gpsimd", "ve", vg)
    G = lambda fn: p.on("gpsimd", fn)
    G(lambda e: e.tensor_tensor(out=t2, in0=gslot(0), in1=gslot(1), op=ALU.add))
    G(lambda e: e.tensor_tensor(out=gslot(0), in0=gslot(4), in1=gslot(5),
                                op=ALU.add))
    G(lambda e: e.tensor_tensor(out=t2, in0=t2, in1=gslot(2), op=ALU.add))
    G(lambda e: e.tensor_scalar(out=gslot(1), in0=gslot(0), scalar1=2.0,
                                scalar2=None, op0=ALU.mult))
    G(lambda e: e.tensor_tensor(out=t2, in0=t2, in1=gslot(3), op=ALU.add))
    G(lambda e: e.tensor_scalar(out=gslot(2), in0=gslot(6), scalar1=4.0,
                                scalar2=None, op0=ALU.mult))
    G(lambda e: e.tensor_tensor(out=t2, in0=t2, in1=gslot(1), op=ALU.add))
    G(lambda e: e.tensor_tensor(out=t2, in0=t2, in1=gslot(2), op=ALU.add))
    vgp = k.p.inc("gpsimd", lambda e: e.tensor_tensor(
        out=t2, in0=t2, in1=sg, op=ALU.mult), k.S("gp"), 1)
    wait(k, "vector", "gp", vgp)
    return vinc(k, lambda e: e.tensor_tensor(out=out_bf, in0=t2, in1=bcr_ps,
                                             op=ALU.mult))


def quant_scales(k, src_f32, width, nblk):
    p = k.p
    sm = k.sb["f32a"].ap()
    bm = sm[0:128, 1408:1408 + nblk]
    rr = sm[0:128, 1496:1496 + nblk]
    rh = sm[0:128, 1584:1584 + nblk]
    tmp = sm[0:128, 1672:1672 + nblk]
    stg = k.sb["ptr"].ap()
    rhiS = stg[0:128, 1536:1536 + nblk]
    rhhiS = stg[0:128, 1664:1664 + nblk]
    rloS = stg[0:128, 1792:1792 + nblk]
    rhloS = stg[0:128, 1920:1920 + nblk]
    v1 = vinc(k, lambda e: e.tensor_reduce(
        out=bm, in_=src_f32.rearrange("p (g w) -> p g w", w=16),
        axis=AX.X, op=ALU.max, apply_absolute_value=True))
    wait(k, "vector", "ve", v1)
    v2 = vinc(k, lambda e: e.tensor_scalar(out=bm, in0=bm, scalar1=6e-12,
                                           scalar2=None, op0=ALU.max))
    wait(k, "vector", "ve", v2)
    v3 = vinc(k, lambda e: e.reciprocal(out=rr, in_=bm))
    vop(k, lambda e: e.tensor_scalar(out=rh, in0=bm, scalar1=1.0 / 12.0,
                                     scalar2=None, op0=ALU.mult))
    wait(k, "vector", "ve", v3)
    v4 = vinc(k, lambda e: e.tensor_tensor(out=tmp, in0=rr, in1=bm, op=ALU.mult))
    wait(k, "vector", "ve", v4)
    v5 = vinc(k, lambda e: e.tensor_scalar(out=tmp, in0=tmp, scalar1=-1.0,
                                           scalar2=2.0, op0=ALU.mult, op1=ALU.add))
    wait(k, "vector", "ve", v5)
    v6 = vinc(k, lambda e: e.tensor_tensor(out=rr, in0=rr, in1=tmp, op=ALU.mult))
    wait(k, "vector", "ve", v6)
    v7 = vinc(k, lambda e: e.tensor_scalar(out=rr, in0=rr, scalar1=6.0,
                                           scalar2=None, op0=ALU.mult))
    vop(k, lambda e: e.tensor_copy(out=rhhiS, in_=rh))
    wait(k, "vector", "ve", v7)
    v8 = vinc(k, lambda e: e.tensor_copy(out=rhiS, in_=rr))
    vop(k, lambda e: e.tensor_tensor(out=bm, in0=rh, in1=rhhiS, op=ALU.subtract))
    wait(k, "vector", "ve", v8)
    v9 = vinc(k, lambda e: e.tensor_tensor(out=tmp, in0=rr, in1=rhiS,
                                           op=ALU.subtract))
    vop(k, lambda e: e.tensor_copy(out=rhloS, in_=bm))
    wait(k, "vector", "ve", v9)
    vr = vinc(k, lambda e: e.tensor_copy(out=rloS, in_=tmp))
    wait(k, "tensor", "ve", vr)
    pD = k.ps["D"].bitcast(BF16)
    trans(k, pD[0:nblk, 0:128], rhiS)
    trans(k, pD[0:nblk, 128:256], rhhiS)
    trans(k, pD[0:nblk, 256:384], rloS)
    trans(k, pD[0:nblk, 384:512], rhloS)
    wait(k, "vector", "te")
    smb = k.sb["f32a"].ap().bitcast(BF16)
    vinc(k, lambda e: e.tensor_copy(out=smb[0:nblk, 3776:3904][0:nblk, 0:128],
                                    in_=pD[0:nblk, 0:128]))
    vinc(k, lambda e: e.tensor_copy(out=smb[0:nblk, 3904:4032][0:nblk, 0:128],
                                    in_=pD[0:nblk, 128:256]))
    vinc(k, lambda e: e.tensor_copy(out=smb[0:nblk, 3520:3648][0:nblk, 0:128],
                                    in_=pD[0:nblk, 256:384]))
    v = vinc(k, lambda e: e.tensor_copy(out=smb[0:nblk, 3648:3776][0:nblk, 0:128],
                                        in_=pD[0:nblk, 384:512]))
    wait(k, "tensor", "ve", v)
    return v


def phase_ffn_weights_quant(k):
    p, d = k.p, k.d
    fa = k.sb["f32a"].ap()
    fb = k.sb["f32b"].ap()
    stage_in = fb[0:128, 0:1024]
    smb = k.sb["f32a"].ap().bitcast(BF16)
    outb = k.sb["ptr"].ap()
    for wname, tname in (("wgate", "wgqT"), ("wup", "wuqT")):
        for r in range(FFI // 128):
            v = sdma(k, stage_in, d[wname][128 * r:128 * r + 128, :])
            wait(k, "vector", "ld", v)
            wait(k, "scalar", "ld", v)
            vr = quant_scales(k, stage_in, 1024, 64)
            wait(k, "tensor", "ve", vr)
            for ic in range(8):
                trans(k, k.ps["Z2"][:, 128 * (ic % 4):128 * (ic % 4) + 128],
                      stage_in[:, 128 * ic:128 * ic + 128], fp32=True)
                if ic % 4 == 3:
                    wait(k, "vector", "te")
                    for j in range(4):
                        icj = ic - 3 + j
                        vinc(k, lambda e, icj=icj, j=j: e.tensor_copy(
                            out=fa[0:128, 128 * icj:128 * icj + 128],
                            in_=k.ps["Z2"][:, 128 * j:128 * j + 128]))
                    wait(k, "tensor", "ve")
            for ic in range(8):
                mm(k, k.ps["A"][:, 128 * ic:128 * ic + 128],
                   a(k, "sel88", 0, 64, 128 * ic, 128 * ic + 128),
                   smb[0:64, 3776:3904][0:64, 0:128], start=True, stop=False)
                mm(k, k.ps["A"][:, 128 * ic:128 * ic + 128],
                   a(k, "sel88", 0, 64, 128 * ic, 128 * ic + 128),
                   smb[0:64, 3520:3648][0:64, 0:128], start=False, stop=True)
                psr = k.ps["Z0"] if ic < 4 else k.ps["Z1"]
                mm(k, psr[:, 128 * (ic % 4):128 * (ic % 4) + 128],
                   a(k, "sel88", 0, 64, 128 * ic, 128 * ic + 128),
                   smb[0:64, 3904:4032][0:64, 0:128], start=True, stop=False)
                mm(k, psr[:, 128 * (ic % 4):128 * (ic % 4) + 128],
                   a(k, "sel88", 0, 64, 128 * ic, 128 * ic + 128),
                   smb[0:64, 3648:3776][0:64, 0:128], start=False, stop=True)
            wait(k, "vector", "te")
            quant_ladder(k, fa[0:128, 0:512], k.ps["A"][:, 0:512],
                         k.ps["Z0"][:, 0:512], 512, outb[0:128, 0:512])
            vq = quant_ladder(k, fa[0:128, 512:1024], k.ps["A"][:, 512:1024],
                              k.ps["Z1"][:, 0:512], 512, outb[0:128, 512:1024])
            wait(k, "sync", "ve", vq)
            wait(k, "tensor", "ve", vq)
            dst = AP(tensor=d[tname].tensor, offset=128 * r,
                     ap=[[FFI, 128], [128 * FFI, 8], [1, 128]])
            sdma(k, dst, outb[0:128, 0:1024])
            wait(k, "vector", "ld")
            wait(k, "scalar", "ld")
        wait(k, "sync", "ld")
    for r in range(D // 128):
        for hf2 in range(2):
            v = sdma(k, fb[0:128, 0:1408],
                     d["wdown"][128 * r:128 * r + 128, 1408 * hf2:1408 * hf2 + 1408])
            wait(k, "vector", "ld", v)
            wait(k, "scalar", "ld", v)
            vr = quant_scales(k, fb[0:128, 0:1408], 1408, 88)
            wait(k, "tensor", "ve", vr)
            for ic in range(11):
                trans(k, k.ps["Z2"][:, 128 * (ic % 4):128 * (ic % 4) + 128],
                      fb[0:128, 128 * ic:128 * ic + 128], fp32=True)
                if ic % 4 == 3 or ic == 10:
                    wait(k, "vector", "te")
                    for j in range(ic % 4 + 1):
                        icj = ic - (ic % 4) + j
                        vinc(k, lambda e, icj=icj, j=j: e.tensor_copy(
                            out=fa[0:128, 128 * icj:128 * icj + 128],
                            in_=k.ps["Z2"][:, 128 * j:128 * j + 128]))
                    wait(k, "tensor", "ve")
            for ic in range(11):
                if ic < 8:
                    psx, col = k.ps["A"], 128 * ic
                    psy, coly = (k.ps["Z0"], 128 * ic) if ic < 4 else (k.ps["Z1"], 128 * (ic - 4))
                else:
                    psx, col = k.ps["C"], 128 * (ic - 8)
                    psy, coly = k.ps["Z2"], 128 * (ic - 8)
                mm(k, psx[:, col:col + 128],
                   a(k, "sel88", 0, 88, 128 * ic, 128 * ic + 128),
                   smb[0:88, 3776:3904][0:88, 0:128], start=True, stop=False)
                mm(k, psx[:, col:col + 128],
                   a(k, "sel88", 0, 88, 128 * ic, 128 * ic + 128),
                   smb[0:88, 3520:3648][0:88, 0:128], start=False, stop=True)
                mm(k, psy[:, coly:coly + 128],
                   a(k, "sel88", 0, 88, 128 * ic, 128 * ic + 128),
                   smb[0:88, 3904:4032][0:88, 0:128], start=True, stop=False)
                mm(k, psy[:, coly:coly + 128],
                   a(k, "sel88", 0, 88, 128 * ic, 128 * ic + 128),
                   smb[0:88, 3648:3776][0:88, 0:128], start=False, stop=True)
            wait(k, "vector", "te")
            quant_ladder(k, fa[0:128, 0:512], k.ps["A"][:, 0:512],
                         k.ps["Z0"][:, 0:512], 512, outb[0:128, 0:512])
            quant_ladder(k, fa[0:128, 512:1024], k.ps["A"][:, 512:1024],
                         k.ps["Z1"][:, 0:512], 512, outb[0:128, 512:1024])
            vq2 = quant_ladder(k, fa[0:128, 1024:1408], k.ps["C"][:, 0:384],
                               k.ps["Z2"][:, 0:384], 384, outb[0:128, 1024:1408])
            wait(k, "sync", "ve", vq2)
            wait(k, "tensor", "ve", vq2)
            dst = AP(tensor=d["wdqT"].tensor, offset=128 * r + 1408 * hf2 * D,
                     ap=[[D, 128], [128 * D, 11], [1, 128]])
            sdma(k, dst, outb[0:128, 0:1408])
            wait(k, "vector", "ld")
            wait(k, "scalar", "ld")
    wait(k, "sync", "ld")


def phase_ffn_block(k):
    p, d = k.p, k.d
    fa = k.sb["f32a"].ap()
    fb = k.sb["f32b"].ap()
    smb = k.sb["f32a"].ap().bitcast(BF16)
    ptb = k.sb["ptr"].ap()
    for kc in range(8):
        sdma(k, wgq_tile(k, kc), d["wgqT"][128 * kc:128 * kc + 128, :])
        sdma(k, wuq_tile(k, kc), d["wuqT"][128 * kc:128 * kc + 128, :], "gpsimd")
    for tq in range(16):
        vap = ln_tile(k, z3t(k, tq), z3t(k, tq), fa[0:128, 0:1024], D,
                      sq_ap=fb[0:128, 2048:3072])
        wait(k, "vector", "se", vap)
        vr = quant_scales(k, fa[0:128, 0:1024], 1024, 64)
        wait(k, "tensor", "ve", vr)
        for ic in range(8):
            trans(k, k.ps["Z2"][:, 128 * (ic % 4):128 * (ic % 4) + 128],
                  fa[0:128, 128 * ic:128 * ic + 128], fp32=True)
            if ic % 4 == 3:
                wait(k, "scalar", "te")
                for j in range(4):
                    icj = ic - 3 + j
                    sinc(k, lambda e, icj=icj, j=j: e.copy(
                        out=fa[0:128, 128 * icj:128 * icj + 128],
                        in_=k.ps["Z2"][:, 128 * j:128 * j + 128]))
                wait(k, "tensor", "se")
        for ic in range(8):
            mm(k, k.ps["A"][:, 128 * ic:128 * ic + 128],
               a(k, "sel88", 0, 64, 128 * ic, 128 * ic + 128),
               smb[0:64, 3776:3904][0:64, 0:128], start=True, stop=False)
            mm(k, k.ps["A"][:, 128 * ic:128 * ic + 128],
               a(k, "sel88", 0, 64, 128 * ic, 128 * ic + 128),
               smb[0:64, 3520:3648][0:64, 0:128], start=False, stop=True)
            psr = k.ps["Z0"] if ic < 4 else k.ps["Z1"]
            mm(k, psr[:, 128 * (ic % 4):128 * (ic % 4) + 128],
               a(k, "sel88", 0, 64, 128 * ic, 128 * ic + 128),
               smb[0:64, 3904:4032][0:64, 0:128], start=True, stop=False)
            mm(k, psr[:, 128 * (ic % 4):128 * (ic % 4) + 128],
               a(k, "sel88", 0, 64, 128 * ic, 128 * ic + 128),
               smb[0:64, 3648:3776][0:64, 0:128], start=False, stop=True)
        wait(k, "vector", "te")
        wait(k, "vector", "se")
        quant_ladder(k, fa[0:128, 0:512], k.ps["A"][:, 0:512],
                     k.ps["Z0"][:, 0:512], 512, ptb[0:128, 0:512])
        vq = quant_ladder(k, fa[0:128, 512:1024], k.ps["A"][:, 512:1024],
                          k.ps["Z1"][:, 0:512], 512, ptb[0:128, 512:1024])
        wait(k, "scalar", "ve", vq)
        for ic in range(8):
            dst = qxfm(k, ic, 128 * tq, 128 * tq + 128)
            sinc(k, lambda e, dst=dst, ic=ic: e.copy(
                out=dst, in_=ptb[0:128, 128 * ic:128 * ic + 128]))
        wait(k, "tensor", "se")
        wait(k, "vector", "se")
    wait(k, "tensor", "gd")
    wait(k, "tensor", "ld")
    for ts in range(16):
        for hf2 in range(2):
            for ci, w in ((0, 512), (1, 512), (2, 384)):
                fo0 = 1408 * hf2 + 512 * ci
                psg, psu = k.ps["C"], k.ps["D"]
                for kc in range(8):
                    mm(k, psg[:, 0:w], qxfm(k, kc, 128 * ts, 128 * ts + 128),
                       wgq_tile(k, kc, fo0, fo0 + w), start=(kc == 0), stop=(kc == 7))
                for kc in range(8):
                    mm(k, psu[:, 0:w], qxfm(k, kc, 128 * ts, 128 * ts + 128),
                       wuq_tile(k, kc, fo0, fo0 + w), start=(kc == 0), stop=(kc == 7))
                wait(k, "scalar", "te")
                vsg = sinc(k, lambda e, w=w: e.activation(
                    out=ptb[0:128, 1024:1024 + w], in_=psg[:, 0:w], func=AF.Silu))
                wait(k, "vector", "se", vsg)
                vh = vinc(k, lambda e, w=w, ci=ci: e.tensor_tensor(
                    out=fa[0:128, 512 * ci:512 * ci + w], in0=psu[:, 0:w],
                    in1=ptb[0:128, 1024:1024 + w], op=ALU.mult))
                wait(k, "tensor", "ve", vh)
                wait(k, "scalar", "ve", vh)
            vr = quant_scales(k, fa[0:128, 0:1408], 1408, 88)
            wait(k, "tensor", "ve", vr)
            for ic in range(11):
                trans(k, k.ps["Z2"][:, 128 * (ic % 4):128 * (ic % 4) + 128],
                      fa[0:128, 128 * ic:128 * ic + 128], fp32=True)
                if ic % 4 == 3 or ic == 10:
                    wait(k, "scalar", "te")
                    for j in range(ic % 4 + 1):
                        icj = ic - (ic % 4) + j
                        sinc(k, lambda e, icj=icj, j=j: e.copy(
                            out=fa[0:128, 128 * icj:128 * icj + 128],
                            in_=k.ps["Z2"][:, 128 * j:128 * j + 128]))
                    wait(k, "tensor", "se")
            for ic in range(11):
                if ic < 8:
                    psx, col = k.ps["A"], 128 * ic
                    psy, coly = (k.ps["Z0"], 128 * ic) if ic < 4 else (k.ps["Z1"], 128 * (ic - 4))
                else:
                    psx, col = k.ps["C"], 128 * (ic - 8)
                    psy, coly = k.ps["Z2"], 128 * (ic - 8)
                mm(k, psx[:, col:col + 128],
                   a(k, "sel88", 0, 88, 128 * ic, 128 * ic + 128),
                   smb[0:88, 3776:3904][0:88, 0:128], start=True, stop=False)
                mm(k, psx[:, col:col + 128],
                   a(k, "sel88", 0, 88, 128 * ic, 128 * ic + 128),
                   smb[0:88, 3520:3648][0:88, 0:128], start=False, stop=True)
                mm(k, psy[:, coly:coly + 128],
                   a(k, "sel88", 0, 88, 128 * ic, 128 * ic + 128),
                   smb[0:88, 3904:4032][0:88, 0:128], start=True, stop=False)
                mm(k, psy[:, coly:coly + 128],
                   a(k, "sel88", 0, 88, 128 * ic, 128 * ic + 128),
                   smb[0:88, 3648:3776][0:88, 0:128], start=False, stop=True)
            wait(k, "vector", "te")
            wait(k, "vector", "se")
            hq0 = 2816 + 1408 * hf2
            quant_ladder(k, fa[0:128, 0:512], k.ps["A"][:, 0:512],
                         k.ps["Z0"][:, 0:512], 512,
                         a(k, "s5", 0, 128, hq0, hq0 + 512))
            quant_ladder(k, fa[0:128, 512:1024], k.ps["A"][:, 512:1024],
                         k.ps["Z1"][:, 0:512], 512,
                         a(k, "s5", 0, 128, hq0 + 512, hq0 + 1024))
            vq2 = quant_ladder(k, fa[0:128, 1024:1408], k.ps["C"][:, 0:384],
                               k.ps["Z2"][:, 0:384], 384,
                               a(k, "s5", 0, 128, hq0 + 1024, hq0 + 1408))
            wait(k, "tensor", "ve", vq2)
        vhq = p.cnt[k.S("ve")]
        wait(k, "sync", "ve", vhq)
        dst = AP(tensor=d["hqTD"].tensor, offset=128 * ts,
                 ap=[[NR, 128], [128 * NR, 22], [1, 128]])
        sdma(k, dst, a(k, "s5", 0, 128, 2816, 5632))
        wait(k, "vector", "ld")
        wait(k, "gpsimd", "ld")
    wait(k, "sync", "ld")
    barrier(k)
    for kc in range(22):
        nm = "s1" if kc < 16 else "s2"
        sdma(k, wdq_tile(k, kc), d["wdqT"][128 * kc:128 * kc + 128, :])
    wait(k, "tensor", "ld")
    g2_te = {}
    for ts in range(16):
        buf = ts % 2
        if ts >= 2:
            wait(k, "gpsimd", "te", g2_te[ts - 2])
        v = sdma(k, a(k, "s3", 0, 128, 2816 * buf, 2816 * buf + 2816),
                 AP(tensor=d["hqTD"].tensor, offset=128 * ts,
                    ap=[[NR, 128], [128 * NR, 22], [1, 128]]), "gpsimd")
        wait(k, "tensor", "gd", v)
        for oh in range(2):
            ps = k.ps["C"] if oh == 0 else k.ps["D"]
            for kc in range(22):
                mm(k, ps[:, 0:512], hqring(k, buf, kc),
                   wdq_tile(k, kc, 512 * oh, 512 * oh + 512),
                   start=(kc == 0), stop=(kc == 21))
            wait(k, "vector", "te")
            col = 1024 + 512 * oh
            vf = vinc(k, lambda e, ps=ps, col=col, ts=ts, oh=oh: e.scalar_tensor_tensor(
                out=fa[0:128, col:col + 512], in0=ps[:, 0:512], scalar=1.0,
                in1=z3t(k, ts, 512 * oh, 512 * oh + 512),
                op0=ALU.mult, op1=ALU.add))
            wait(k, "sync", "ve", vf)
            stdma(k, k.out[128 * ts:128 * ts + 128, 512 * oh:512 * oh + 512],
                  fa[0:128, col:col + 512])
            wait(k, "tensor", "ve", vf)
        g2_te[ts] = k.p.cnt[k.S("te")]
    if "qx" in DEBUG:
        wait(k, "gpsimd", "te")
        for kc in range(8):
            sdma(k, d["dbg_qx"][128 * kc:128 * kc + 128, :], qxfm(k, kc), "gpsimd")
    if "hq" in DEBUG:
        wait(k, "gpsimd", "ld")
        for rr in range(4):
            sdma(k, d["dbg_hq"][704 * rr:704 * rr + 704, :],
                 d["hqTD"][704 * rr:704 * rr + 704, :], "gpsimd")
    for wnm in ("wdq", "wgq", "wuq"):
        if wnm in DEBUG:
            wait(k, "gpsimd", "ld")
            for rr in range(4):
                nrow = k.d[wnm + "T"].shape[0] // 4
                sdma(k, d["dbg_" + wnm][nrow * rr:nrow * rr + nrow, :],
                     d[wnm + "T"][nrow * rr:nrow * rr + nrow, :], "gpsimd")
    if DEBUG:
        wait(k, "sync", "gd")
    wait(k, "sync", "st")


STAGE = "full"
ATTCUT = None
ATTMI = 1
DBG = False


def emit_dbg(k):
    if not DBG:
        return
    if STAGE == "P2":
        for dc in range(4):
            sdma(k, k.out[128 * dc:128 * dc + 128, 0:1024],
                 qfm(k, dc, 0, 1024), "gpsimd")
            sdma(k, k.out[512 + 128 * dc:512 + 128 * dc + 128, 0:1024],
                 kfm(k, dc, 0, 1024), "gpsimd")
        for t in range(8):
            sdma(k, k.out[1024 + 128 * t:1152 + 128 * t, 0:512],
                 vtm(k, t), "gpsimd")
        for fc in range(4):
            sdma(k, k.out[1024 + 128 * fc:1152 + 128 * fc, 512:1024],
                 xqfm(k, fc, 0, 512), "gpsimd")
    elif STAGE == "ATT1":
        for t in range(16):
            sdma(k, k.out[128 * t:128 * t + 128, 0:512], x1t(k, t), "gpsimd")
    elif STAGE == "ATT2":
        for t in range(16):
            sdma(k, k.out[128 * t:128 * t + 128, 0:512], x1t(k, t), "gpsimd")
            sdma(k, k.out[128 * t:128 * t + 128, 512:1024], x2t(k, t), "gpsimd")
    elif STAGE == "FFN1":
        for t in range(16):
            sdma(k, k.out[128 * t:128 * t + 128, 0:1024], z3t(k, t), "gpsimd")
    wait(k, "sync", "gd")


def emit_program(k):
    phase_consts(k)
    phase_holt_tables(k)
    build_bvb(k)
    phase_attn_weights(k, 1)
    barrier(k)
    if STAGE == "TABLES":
        return
    phase_xq(k)
    barrier(k)
    phase_stream_x(k)
    barrier(k)
    if STAGE == "P2":
        emit_dbg(k)
        return
    phase_attention(k, 1)
    barrier(k)
    if STAGE == "ATT1":
        emit_dbg(k)
        return
    phase_attn_weights(k, 2)
    phase_merge_hso(k)
    barrier(k)
    if STAGE == "MERGE":
        return
    phase_proj2(k)
    barrier(k)
    if STAGE == "PROJ2":
        return
    phase_attention(k, 2)
    barrier(k)
    if STAGE == "ATT2":
        emit_dbg(k)
        return
    phase_ffw_weights(k)
    barrier(k)
    phase_attn_ffn(k)
    barrier(k)
    if STAGE == "FFN1":
        emit_dbg(k)
        return
    phase_ffn_weights_quant(k)
    barrier(k)
    phase_ffn_block(k)


def build():
    k = K()
    declare(k)
    with k.stack:
        alloc(k)
        for s in ("ld", "gd", "gp", "te", "ve", "se", "st",
                  "bsy", "bte", "bve", "bse", "bgp"):
            k.p.sem(s)
        emit_program(k)
        blk = k.stack.enter_context(k.nc.Block())
        for eng_name, meth in (("sync", blk.sync), ("tensor", blk.tensor),
                               ("vector", blk.vector), ("scalar", blk.scalar),
                               ("gpsimd", blk.gpsimd)):
            def body(e, _ops=k.p.ops[eng_name]):
                for fn in _ops:
                    fn(e)
            meth(body)
    return k


_cache = {}


def kernel(**inputs):
    if "k" not in _cache:
        _cache["k"] = build()
    k = _cache["k"]
    f32 = np.float32
    j_np = np.eye(128, dtype=f32)[::-1].copy()
    sel1a = np.zeros((1, 128), f32); sel1a[0, 0:64] = 1
    sel1b = np.zeros((1, 128), f32); sel1b[0, 64:128] = 1
    def selmat(nblk, ncol):
        s = np.zeros((nblk, ncol), f32)
        for c in range(ncol // 128):
            for pp in range(128):
                g = 8 * c + pp // 16
                if g < nblk:
                    s[g, 128 * c + pp] = 1
        return s
    consts = {
        "idn": np.eye(128, dtype=f32),
        "idnb": np.eye(128, dtype=f32).astype(BF),
        "jmatb": j_np.astype(BF),
        "sel1a": sel1a.astype(BF), "sel1b": sel1b.astype(BF),
        "sel88": selmat(88, 1408).astype(BF),
        "onesf": np.ones((1, 128), f32),
    }
    x_full = np.ascontiguousarray(inputs["L0_attn_x"], dtype=f32)
    wmap = {
        "wq1": "L0_attn_wq1", "bq1": "L0_attn_bq1", "wk1": "L0_attn_wk1",
        "bk1": "L0_attn_bk1", "wv1": "L0_attn_wv1", "bv1": "L0_attn_bv1",
        "wo1": "L0_attn_wo1", "bo1": "L0_attn_bo1",
        "wq2": "L0_attn_wq2", "bq2": "L0_attn_bq2", "wk2": "L0_attn_wk2",
        "bk2": "L0_attn_bk2", "wv2": "L0_attn_wv2", "bv2": "L0_attn_bv2",
        "wo2": "L0_attn_wo2", "bo2": "L0_attn_bo2",
        "g1": "L0_attn_g1", "be1": "L0_attn_be1",
        "g2": "L0_attn_g2", "be2": "L0_attn_be2",
        "wff1": "L0_attn_w_ff1", "bff1": "L0_attn_b_ff1",
        "wff2": "L0_attn_w_ff2", "bff2": "L0_attn_b_ff2",
        "gff": "L0_attn_g_ff", "beff": "L0_attn_be_ff",
        "wgate": "L0_ffn_w_gate", "wup": "L0_ffn_w_up", "wdown": "L0_ffn_w_down",
    }
    shared = {kk: np.ascontiguousarray(inputs[v], dtype=f32) for kk, v in wmap.items()}
    in_maps = []
    for c in range(8):
        bb, half = c // 2, c % 2
        m = dict(consts)
        m.update(shared)
        m["x"] = x_full[bb]
        m["xq"] = np.ascontiguousarray(x_full[bb, half * NR:(half + 1) * NR, :])
        m["mask0"] = np.asarray([1.0 - half], f32)
        in_maps.append(m)
    res = run_bass_kernel_spmd(k.nc, in_maps, core_ids=list(range(8)))
    out = np.empty((B, S, D), f32)
    for c in range(8):
        bb, half = c // 2, c % 2
        out[bb, half * NR:(half + 1) * NR, :] = np.asarray(res.results[c]["out"])
    return out


# version 08
# speedup vs baseline: 1.4458x; 1.4458x over previous
import numpy as np
import ml_dtypes
import concourse.bass as bass
import concourse.mybir as mybir
from concourse.bass_utils import run_bass_kernel_spmd
from contextlib import ExitStack

F32 = mybir.dt.float32
BF16 = mybir.dt.bfloat16
AF = mybir.ActivationFunctionType
ALU = mybir.AluOpType
AX = mybir.AxisListType
AP = bass.AP
BF = ml_dtypes.bfloat16

B, S, D = 4, 4096, 1024
DH, H, DK = 512, 8, 64
DFF, FFI = 4096, 2816
NR = 2048
TCH, NCH = 512, 8
EPS = 1e-5
ENGS = ("sync", "tensor", "vector", "scalar", "gpsimd")
QMIDS = (0.25, 0.75, 1.25, 1.75, 2.5, 3.5, 5.0)

DEBUG = ()


class Prog:
    def __init__(self, nc, stack):
        self.nc, self.stack = nc, stack
        self.ops = {e: [] for e in ENGS}
        self.cnt, self.sems = {}, {}

    def sem(self, name):
        s = self.stack.enter_context(self.nc.semaphore(name))
        self.cnt[s] = 0
        self.sems[name] = s
        return s

    def on(self, eng, fn):
        self.ops[eng].append(fn)

    def inc(self, eng, fn, sem, n):
        self.cnt[sem] += n
        self.ops[eng].append(lambda e, fn=fn, sem=sem, n=n: fn(e).then_inc(sem, n))
        return self.cnt[sem]

    def wait(self, eng, sem, v):
        self.ops[eng].append(lambda e, sem=sem, v=v: e.wait_ge(sem, v))


class K:
    def __init__(self):
        self.nc = bass.Bass("TRN2", target_bir_lowering=False,
                            detect_race_conditions=False)
        self.stack = ExitStack()
        self.p = Prog(self.nc, self.stack)
        self.d = {}
        self.sb = {}
        self.ps = {}

    def S(self, name):
        return self.p.sems[name]


def din(k, name, shape, dt=F32, kind="ExternalInput"):
    k.d[name] = k.nc.dram_tensor(name, shape, dt, kind=kind).ap()


def declare(k):
    din(k, "x", [S, D])
    din(k, "xq", [NR, D])
    din(k, "mask0", [1])
    for i in (1, 2):
        for w in ("wq", "wk", "wv", "wo"):
            din(k, f"{w}{i}", [DH, DH])
        for bn in ("bq", "bk", "bv", "bo"):
            din(k, f"{bn}{i}", [DH])
        din(k, f"g{i}", [DH]); din(k, f"be{i}", [DH])
    din(k, "wff1", [DFF, D]); din(k, "bff1", [DFF])
    din(k, "wff2", [D, DFF]); din(k, "bff2", [D])
    din(k, "gff", [D]); din(k, "beff", [D])
    din(k, "wgate", [FFI, D]); din(k, "wup", [FFI, D]); din(k, "wdown", [D, FFI])
    din(k, "idn", [128, 128])
    din(k, "idnb", [128, 128], BF16)
    din(k, "jmatb", [128, 128], BF16)
    din(k, "sel1a", [1, 128], BF16)
    din(k, "sel1b", [1, 128], BF16)
    din(k, "sel88", [88, 1408], BF16)
    din(k, "onesf", [1, 128])
    din(k, "hpad", [5120], BF16, "Internal")
    din(k, "hsD", [DH, S], BF16, "Internal")
    din(k, "wff1T", [D, DFF], BF16, "Internal")
    din(k, "wff2T", [DFF, D], BF16, "Internal")
    din(k, "wgqT", [D, FFI], BF16, "Internal")
    din(k, "wuqT", [D, FFI], BF16, "Internal")
    din(k, "wdqT", [FFI, D], BF16, "Internal")
    din(k, "hqTD", [FFI, NR], BF16, "Internal")
    k.out = k.nc.dram_tensor("out", [NR, D], F32, kind="ExternalOutput").ap()
    for nm in DEBUG:
        shp = {"xhq": [DH, NR], "hs": [DH, S], "x1": [NR, DH], "x2": [NR, DH],
               "kfm": [DH, S], "vtm": [S, DH], "qfm": [DH, NR], "o1": [DH, NR],
               "z3": [NR, D], "xres": [NR, D], "qx": [D, NR], "hq": [FFI, NR],
               "wdq": [FFI, D], "wgq": [D, FFI], "wuq": [D, FFI],
               }[nm]
        din(k, "dbg_" + nm, shp, F32, "ExternalOutput")


def alloc(k):
    E = k.stack.enter_context
    def sbt(name, cols, dt=BF16, parts=128):
        k.sb[name] = E(k.nc.sbuf_tensor("sb_" + name, [parts, cols], dt))
    sbt("idn", 128, F32); sbt("idnb", 128); sbt("jmatb", 128)
    sbt("sel1a", 128, BF16, 1); sbt("sel1b", 128, BF16, 33)
    sbt("sel88", 1408, BF16, 88)
    sbt("onesf", 128, F32, 1)
    sbt("onescol", 1, BF16)
    sbt("bias4", 32, F32)
    sbt("bff1c", 32, F32); sbt("bff2c", 8, F32)
    sbt("rowv", 512, F32, 1)
    sbt("maskc", 2, F32)
    sbt("s1", 16384)
    sbt("s2", 16384)
    sbt("s3", 14080)
    sbt("s4", 16384)
    sbt("s5", 8192)
    sbt("s6", 8192)
    sbt("s7", 8192)
    sbt("ptr", 2048)
    sbt("f32a", 2048, F32)
    sbt("f32b", 4160, F32)
    sbt("stats", 16, F32)
    k.ps["A"] = E(k.nc.psum_tensor("psA", [128, 1024], F32))
    for nm in ("C", "D", "Z0", "Z1", "Z2", "Z3"):
        k.ps[nm] = E(k.nc.psum_tensor("ps" + nm, [128, 512], F32))


def a(k, name, p0=0, p1=128, c0=0, c1=None):
    t = k.sb[name].ap()
    if c1 is None:
        c1 = t.shape[-1]
    return t[p0:p1, c0:c1]


def xqfm(k, fc, c0=0, c1=NR):
    return a(k, "s4", 0, 128, 2048 * fc + c0, 2048 * fc + c1)
def qfm(k, dc, c0=0, c1=NR):
    return a(k, "s4", 0, 128, 8192 + 2048 * dc + c0, 8192 + 2048 * dc + c1)
def kfm(k, dc, c0=0, c1=S):
    return a(k, "s1", 0, 128, 4096 * dc + c0, 4096 * dc + c1)
def vtm(k, kt, c0=0, c1=DH):
    return a(k, "s2", 0, 128, 512 * kt + c0, 512 * kt + c1)
def hso(k, fc, c0=0, c1=NR):
    return a(k, "s3", 0, 128, 2048 * fc + c0, 2048 * fc + c1)
def hsring(k, buf, fc, c0=0, c1=512):
    base = 8192 + 2048 * buf + 512 * fc
    return a(k, "s3", 0, 128, base + c0, base + c1)
def xtm_ring(k, buf, tt, c0=0, c1=512):
    base = 2048 * buf + 512 * tt
    return a(k, "s7", 0, 128, base + c0, base + c1)
def xfm_ring(k, buf, fc, c0=0, c1=512):
    base = 8192 + 2048 * buf + 512 * fc
    return a(k, "s3", 0, 128, base + c0, base + c1)
def wT(k, mi, m, ic, o0, o1):
    base = 2048 * m + 512 * ic
    return a(k, "s5", 0, 128, base + o0, base + o1)
def holtT(k, p0, p1, c0, c1):
    return a(k, "s6", p0, p1, c0, c1)
def x1t(k, tt, c0=0, c1=512):
    return a(k, "s6", 0, 128, 512 * tt + c0, 512 * tt + c1)
def x2t(k, tt, c0=0, c1=512):
    return a(k, "s7", 0, 128, 512 * tt + c0, 512 * tt + c1)
QXT = [("s6", 0), ("s6", 2048), ("s6", 4096), ("s6", 6144),
       ("s7", 0), ("s7", 2048), ("s7", 4096), ("s7", 6144)]
def yfm(k, fc, c0=0, c1=NR):
    return a(k, "s1", 0, 128, 2048 * fc + c0, 2048 * fc + c1)
def h1sb(k, fb, c0=0, c1=512):
    return a(k, "s2", 0, 128, 512 * fb + c0, 512 * fb + c1)
def z3t(k, tt, c0=0, c1=D):
    return a(k, "s4", 0, 128, 1024 * tt + c0, 1024 * tt + c1)
def qxfm(k, kc, c0=0, c1=NR):
    nm, base = QXT[kc]
    return a(k, nm, 0, 128, base + c0, base + c1)
def wgq_tile(k, kc, c0=0, c1=FFI):
    if kc < 5:
        return a(k, "s1", 0, 128, FFI * kc + c0, FFI * kc + c1)
    return a(k, "s3", 0, 128, FFI * (kc - 5) + c0, FFI * (kc - 5) + c1)
def wuq_tile(k, kc, c0=0, c1=FFI):
    if kc < 5:
        return a(k, "s2", 0, 128, FFI * kc + c0, FFI * kc + c1)
    if kc < 7:
        return a(k, "s3", 0, 128, 8448 + FFI * (kc - 5) + c0, 8448 + FFI * (kc - 5) + c1)
    return a(k, "s5", 0, 128, c0, c1)
def wdq_tile(k, kc, c0=0, c1=1024):
    if kc < 16:
        return a(k, "s1", 0, 128, 1024 * kc + c0, 1024 * kc + c1)
    return a(k, "s2", 0, 128, 1024 * (kc - 16) + c0, 1024 * (kc - 16) + c1)
def hqring(k, buf, kc, c0=0, c1=128):
    base = 2816 * buf + 128 * kc
    return a(k, "s3", 0, 128, base + c0, base + c1)
def hqstage(k, kc, c0=0, c1=128):
    base = 2816 + 128 * kc
    return a(k, "s5", 0, 128, base + c0, base + c1)


def gbe_ap(k, c0, c1):
    return k.sb["f32b"].ap().bitcast(BF16)[0:128, 5120 + c0:5120 + c1]
def bvb_ap(k, c0, c1):
    return k.sb["f32b"].ap().bitcast(BF16)[0:128, 7168 + c0:7168 + c1]


def barrier(k):
    p = k.p
    names = {"sync": "bsy", "tensor": "bte", "vector": "bve",
             "scalar": "bse", "gpsimd": "bgp"}
    vals = {}
    for eng in ENGS:
        s = p.sems[names[eng]]
        p.cnt[s] += 1
        vals[eng] = p.cnt[s]
        p.ops[eng].append(lambda e, s=s: e.nop().then_inc(s, 1))
    for eng in ENGS:
        for o in ENGS:
            if o != eng:
                p.wait(eng, p.sems[names[o]], vals[o])


def mm(k, out_ap, lhsT, rhs, start=True, stop=True, tp=None):
    return k.p.inc("tensor", lambda e, o=out_ap, l=lhsT, r=rhs, s1=start,
                   s2=stop, tp=tp: e.matmul(o, l, r, start=s1, stop=s2,
                                            tile_position=tp), k.S("te"), 1)


def trans(k, out_ap, in_ap, fp32=False):
    ident = a(k, "idn") if fp32 else a(k, "idnb")
    return k.p.inc("tensor", lambda e, o=out_ap, i=in_ap, d=ident:
                   e.transpose(o, i, d), k.S("te"), 1)


def psb(k, nm, c0, c1):
    return k.ps[nm].bitcast(BF16)[:, c0:c1]


def sdma(k, dst, src, eng="sync", slow=False):
    sem = k.S("ld") if eng == "sync" else k.S("gd")
    if slow:
        return k.p.inc(eng, lambda e, d=dst, s=src: e.dma_start(
            out=d, in_=s, allow_slow_non_contiguous=True), sem, 16)
    return k.p.inc(eng, lambda e, d=dst, s=src: e.dma_start(out=d, in_=s), sem, 16)


def stdma(k, dst, src):
    return k.p.inc("sync", lambda e, d=dst, s=src: e.dma_start(out=d, in_=s),
                   k.S("st"), 16)


def vop(k, fn):
    k.p.on("vector", fn)


def vinc(k, fn):
    return k.p.inc("vector", fn, k.S("ve"), 1)


def sinc(k, fn):
    return k.p.inc("scalar", fn, k.S("se"), 1)


def wait(k, eng, sem, v=None):
    if v is None:
        v = k.p.cnt[k.S(sem)]
    k.p.wait(eng, k.S(sem), v)


def phase_consts(k):
    d = k.d
    sdma(k, a(k, "idn"), d["idn"])
    sdma(k, a(k, "idnb"), d["idnb"])
    sdma(k, a(k, "jmatb"), d["jmatb"])
    sdma(k, a(k, "sel1a", 0, 1), d["sel1a"])
    sdma(k, a(k, "sel1b", 32, 33), d["sel1b"])
    sdma(k, a(k, "sel88", 0, 88), d["sel88"])
    sdma(k, a(k, "onesf", 0, 1), d["onesf"])
    names = [f"{w}{i}" for i in (1, 2) for w in ("bq", "bk", "bv", "bo")]
    for j, nm in enumerate(names):
        sdma(k, a(k, "bias4", 0, 128, 4 * j, 4 * j + 4),
             AP(tensor=d[nm].tensor, offset=0, ap=[[1, 128], [128, 4]]), slow=True)
    sdma(k, a(k, "bff1c"), AP(tensor=d["bff1"].tensor, offset=0, ap=[[1, 128], [128, 32]]), slow=True)
    sdma(k, a(k, "bff2c"), AP(tensor=d["bff2"].tensor, offset=0, ap=[[1, 128], [128, 8]]), slow=True)
    sdma(k, a(k, "rowv", 0, 1, 0, 1), d["mask0"].rearrange("(a b) -> a b", a=1))
    wait(k, "vector", "ld")
    vinc(k, lambda e: e.memset(a(k, "onescol"), 1.0))
    wait(k, "tensor", "ld")
    wait(k, "tensor", "ve")
    mm(k, k.ps["Z1"][:, 0:1], a(k, "onesf", 0, 1), a(k, "rowv", 0, 1, 0, 1))
    wait(k, "vector", "te")
    vmc = vinc(k, lambda e: e.tensor_copy(out=a(k, "maskc", 0, 128, 0, 1),
                                          in_=k.ps["Z1"][:, 0:1]))
    wait(k, "vector", "ve", vmc)
    vop(k, lambda e: e.tensor_scalar(out=a(k, "maskc", 0, 128, 1, 2),
                                     in0=a(k, "maskc", 0, 128, 0, 1),
                                     scalar1=-1.0, scalar2=1.0,
                                     op0=ALU.mult, op1=ALU.add))
    v = vinc(k, lambda e: e.engine_nop())
    wait(k, "tensor", "ve", v)
    wait(k, "sync", "ve", v)


def build_bvb(k):
    d = k.d
    for i in (1, 2):
        v = sdma(k, a(k, "rowv", 0, 1, 0, 512), d[f"bv{i}"].rearrange("(a b) -> a b", a=1))
        wait(k, "tensor", "ld", v)
        mm(k, k.ps["Z1"][:, 0:512], a(k, "onesf", 0, 1), a(k, "rowv", 0, 1, 0, 512))
        wait(k, "vector", "te")
        col = 512 * (i - 1)
        ve = vinc(k, lambda e, col=col: e.tensor_copy(
            out=bvb_ap(k, col, col + 512), in_=k.ps["Z1"][:, 0:512]))
        wait(k, "sync", "ve", ve)
        wait(k, "tensor", "ve", ve)


def build_gbe(k, gname, bename, width):
    d = k.d
    for idx, nm in ((0, gname), (1, bename)):
        for off in range(0, width, 512):
            w = min(512, width - off)
            v = sdma(k, a(k, "rowv", 0, 1, 0, w),
                     d[nm].rearrange("(a b) -> a b", a=1)[:, off:off + w])
            wait(k, "tensor", "ld", v)
            wait(k, "tensor", "ve")
            mm(k, k.ps["Z1"][:, 0:w], a(k, "onesf", 0, 1), a(k, "rowv", 0, 1, 0, w))
            wait(k, "vector", "te")
            col = idx * 1024 + off
            ve = vinc(k, lambda e, col=col, w=w: e.tensor_copy(
                out=gbe_ap(k, col, col + w), in_=k.ps["Z1"][:, 0:w]))
            wait(k, "sync", "ve", ve)
    wait(k, "tensor", "ve")


def phase_holt_tables(k):
    p, d = k.p, k.d
    hf = k.sb["f32b"].ap()
    hb = k.sb["s6"].ap()
    V2C, G2C, W2C = 80, 592, 1104

    def V(fn):
        p.on("vector", fn)

    def VE(fn):
        v = vinc(k, fn)
        wait(k, "vector", "ve", v)
        return v

    VE(lambda e: e.memset(hf[0:1, 6:7], 10.0))
    VE(lambda e: e.reciprocal(out=hf[0:1, 4:5], in_=hf[0:1, 6:7]))
    for _ in range(2):
        VE(lambda e: e.tensor_scalar(out=hf[0:1, 7:8], in0=hf[0:1, 4:5],
                                     scalar1=-10.0, scalar2=2.0,
                                     op0=ALU.mult, op1=ALU.add))
        VE(lambda e: e.tensor_tensor(out=hf[0:1, 4:5], in0=hf[0:1, 4:5],
                                     in1=hf[0:1, 7:8], op=ALU.mult))
    VE(lambda e: e.tensor_tensor(out=hf[0:1, 5:6], in0=hf[0:1, 4:5],
                                 in1=hf[0:1, 4:5], op=ALU.mult))
    VE(lambda e: e.tensor_scalar(out=hf[0:1, 0:1], in0=hf[0:1, 4:5],
                                 scalar1=-1.0, scalar2=1.0, op0=ALU.mult, op1=ALU.add))
    VE(lambda e: e.tensor_copy(out=hf[0:1, 1:2], in_=hf[0:1, 0:1]))
    VE(lambda e: e.tensor_scalar(out=hf[0:1, 2:3], in0=hf[0:1, 5:6],
                                 scalar1=-1.0, scalar2=None, op0=ALU.mult))
    VE(lambda e: e.tensor_scalar(out=hf[0:1, 3:4], in0=hf[0:1, 5:6],
                                 scalar1=-1.0, scalar2=1.0, op0=ALU.mult, op1=ALU.add))
    VE(lambda e: e.tensor_copy(out=hf[0:1, 8:9], in_=hf[0:1, 0:1]))
    VE(lambda e: e.tensor_copy(out=hf[0:1, 9:10], in_=hf[0:1, 2:3]))
    VE(lambda e: e.tensor_copy(out=hf[0:1, 10:11], in_=hf[0:1, 1:2]))
    VE(lambda e: e.tensor_copy(out=hf[0:1, 11:12], in_=hf[0:1, 3:4]))
    vrow = vinc(k, lambda e: e.engine_nop())

    wait(k, "sync", "ve", vrow)
    sdma(k, hf[0:2, 16:18], hf[0:1, 0:4], slow=True)
    sdma(k, hf[0:2, 48:50], hf[0:1, 8:12], slow=True)
    sdma(k, hf[0:2, W2C:W2C + 1], hf[0:1, 4:6], slow=True)
    sdma(k, hf[0:2, V2C:V2C + 1], hf[0:1, 0:2], slow=True)
    vst = p.cnt[k.S("ld")]
    wait(k, "vector", "ld", vst)
    VE(lambda e: e.memset(hf[0:2, G2C:G2C + 1], 0.0))
    VE(lambda e: e.memset(hf[0:1, G2C:G2C + 1], 1.0))
    v0 = vinc(k, lambda e: e.engine_nop())
    wait(k, "tensor", "ve", v0)
    wait(k, "tensor", "ld", vst)

    ps = k.ps["C"]
    smb = k.sb["f32a"].ap().bitcast(BF16)
    def ev(dst, src):
        wait(k, "vector", "te")
        ve = vinc(k, lambda e, dst=dst, src=src: e.tensor_copy(out=dst, in_=src))
        wait(k, "tensor", "ve", ve)

    m, stage = 1, 0
    m512_stage = None
    while m < 4096:
        if m == 512:
            m512_stage = stage
        Pm = hf[0:2, 16 + 2 * stage:18 + 2 * stage]
        Mm = hf[0:2, 48 + 2 * stage:50 + 2 * stage]
        step = min(m, 4096 - m)
        for off in range(0, step, 512):
            w = min(512, step - off)
            mm(k, ps[0:2, 0:w], Mm, hf[0:2, W2C + off:W2C + off + w])
            if m + off < 2048:
                ev(hf[0:2, W2C + m + off:W2C + m + off + w], ps[0:2, 0:w])
            else:
                wait(k, "vector", "te")
                ve = vinc(k, lambda e, w=w: e.tensor_copy(
                    out=smb[0:2, 3520:3520 + w][0:2, 0:w], in_=ps[0:2, 0:w]))
                wait(k, "tensor", "ve", ve)
                wait(k, "gpsimd", "ve", ve)
                sdma(k, d["hpad"][512 + m + off:512 + m + off + w],
                     smb[0:1, 3520:3520 + w], "gpsimd")
                wait(k, "vector", "gd")
        if m < 512:
            stepv = min(m, 512 - m)
            for cc, lhs in ((G2C, Mm), (V2C, Pm)):
                mm(k, ps[0:2, 0:stepv], lhs, hf[0:2, cc:cc + stepv])
                ev(hf[0:2, cc + m:cc + m + stepv], ps[0:2, 0:stepv])
        mm(k, ps[0:2, 0:2], Mm, Pm)
        ev(hf[0:2, 18 + 2 * stage:20 + 2 * stage], ps[0:2, 0:2])
        mm(k, ps[0:2, 0:2], Pm, Mm)
        ev(hf[0:2, 50 + 2 * stage:52 + 2 * stage], ps[0:2, 0:2])
        m *= 2
        stage += 1
    c512 = 48 + 2 * (m512_stage + 1)
    V(lambda e: e.tensor_copy(out=hb[0:2, 4624:4626], in_=hf[0:2, c512:c512 + 2]))
    V(lambda e: e.tensor_copy(out=hb[0:2, 4112:4624], in_=hf[0:2, V2C:V2C + 512]))
    zr = k.sb["f32a"].ap()
    V(lambda e: e.memset(zr[0:1, 0:512], 0.0))
    vz = vinc(k, lambda e: e.engine_nop())
    wait(k, "gpsimd", "ve", vz)
    sdma(k, d["hpad"][0:512], zr[0:1, 0:512], "gpsimd")
    sdma(k, d["hpad"][4608:5120], zr[0:1, 0:512], "gpsimd")
    sdma(k, d["hpad"][512:2560], hf[0:1, W2C:W2C + 2048], "gpsimd")
    wait(k, "sync", "gd")

    hkb = k.sb["ptr"].ap()
    for dst_base in (0, 2048):
        for jsub in range(4):
            off = 512 - 127 - 128 * jsub
            v = sdma(k, hkb[0:128, 0:512],
                     AP(tensor=d["hpad"].tensor, offset=off, ap=[[1, 128], [1, 512]]))
            wait(k, "tensor", "ld", v)
            for tsub in range(4):
                mm(k, k.ps["C"][:, 128 * tsub:128 * tsub + 128], a(k, "jmatb"),
                   hkb[0:128, 128 * tsub:128 * tsub + 128])
            wait(k, "vector", "te")
            col = dst_base + 512 * jsub
            ve = vinc(k, lambda e, col=col: e.tensor_copy(
                out=hb[0:128, col:col + 512], in_=k.ps["C"][:, 0:512]))
            wait(k, "tensor", "ve", ve)
            wait(k, "sync", "ve", ve)
    V(lambda e: e.tensor_copy(out=hb[0:1, 2048:2560], in_=hf[0:1, G2C:G2C + 512]))
    for t in range(4):
        srcc = 128 * (3 - t)
        k.p.inc("tensor", lambda e, srcc=srcc: e.transpose(
            k.ps["C"][:, 0:2], hf[0:2, W2C + srcc:W2C + srcc + 128],
            a(k, "idn", 0, 2, 0, 2)), k.S("te"), 1)
        wait(k, "vector", "te")
        ve = vinc(k, lambda e: e.tensor_copy(out=hkb[0:128, 1024:1026],
                                             in_=k.ps["C"][:, 0:2]))
        wait(k, "tensor", "ve", ve)
        mm(k, k.ps["C"][:, 0:2], a(k, "jmatb"), hkb[0:128, 1024:1026])
        wait(k, "vector", "te")
        for dst0 in (4096, 4104):
            vinc(k, lambda e, dst0=dst0, t=t: e.tensor_copy(
                out=hb[0:128, dst0 + 2 * t:dst0 + 2 * t + 2], in_=k.ps["C"][:, 0:2]))
        wait(k, "tensor", "ve")
    wait(k, "gpsimd", "ve")
    sdma(k, hb[0:1, 4104:4105], hf[0:1, G2C + 511:G2C + 512], "gpsimd", slow=True)
    sdma(k, hb[0:1, 4105:4106], hf[1:2, G2C + 511:G2C + 512], "gpsimd", slow=True)
    wait(k, "vector", "gd")
    wait(k, "tensor", "gd")
    V(lambda e: e.memset(hb[0:2, 4704:5728], 0.0))
    v = vinc(k, lambda e: e.engine_nop())
    wait(k, "tensor", "ve", v)


def sprev(k, c0=0, c1=512):
    return a(k, "s6", 0, 2, 4704 + c0, 4704 + c1)
def snew(k, c0=0, c1=512):
    return a(k, "s6", 0, 2, 5216 + c0, 5216 + c1)


def phase_attn_weights(k, mi):
    d = k.d
    stage = k.sb["ptr"].ap()
    for m, w in enumerate(("wq", "wk", "wv", "wo")):
        wd = d[f"{w}{mi}"]
        for oc in range(4):
            vg = sdma(k, stage[0:128, 0:512], wd[128 * oc:128 * oc + 128, :], "gpsimd")
            wait(k, "tensor", "gd", vg)
            for ic in range(4):
                trans(k, psb(k, "D", 128 * ic, 128 * ic + 128),
                      stage[0:128, 128 * ic:128 * ic + 128])
            wait(k, "vector", "te")
            for ic in range(4):
                dst = wT(k, mi, m, ic, 128 * oc, 128 * oc + 128)
                vinc(k, lambda e, dst=dst, ic=ic: e.tensor_copy(
                    out=dst, in_=psb(k, "D", 128 * ic, 128 * ic + 128)))
            wait(k, "tensor", "ve")
            wait(k, "gpsimd", "ve")


def ln_tile(k, full_src, apply_src, out_bf, width, sq_ap=None):
    st = k.sb["stats"].ap()
    if sq_ap is None:
        sq_ap = k.sb["f32b"].ap()[0:128, 0:width]
    vsq = sinc(k, lambda e: e.activation(out=sq_ap, in_=full_src,
                                         func=AF.Square, accum_out=st[0:128, 2:3]))
    wait(k, "vector", "se", vsq)
    va = vinc(k, lambda e: e.tensor_reduce(out=st[0:128, 3:4], in_=full_src,
                                           axis=AX.X, op=ALU.add))
    vop(k, lambda e: e.tensor_scalar(out=st[0:128, 2:3], in0=st[0:128, 2:3],
                                     scalar1=1.0 / width, scalar2=None, op0=ALU.mult))
    wait(k, "vector", "ve", va)
    vc = vinc(k, lambda e: e.tensor_scalar(out=st[0:128, 0:1], in0=st[0:128, 3:4],
                                           scalar1=1.0 / width, scalar2=None,
                                           op0=ALU.mult))
    wait(k, "vector", "ve", vc)
    vd = vinc(k, lambda e: e.tensor_tensor(out=st[0:128, 3:4], in0=st[0:128, 0:1],
                                           in1=st[0:128, 0:1], op=ALU.mult))
    wait(k, "vector", "ve", vd)
    vv = vinc(k, lambda e: e.scalar_tensor_tensor(
        out=st[0:128, 2:3], in0=st[0:128, 2:3], scalar=EPS, in1=st[0:128, 3:4],
        op0=ALU.add, op1=ALU.subtract))
    wait(k, "scalar", "ve", vv)
    vsr = sinc(k, lambda e: e.activation(out=st[0:128, 3:4], in_=st[0:128, 2:3],
                                         func=AF.Sqrt))
    wait(k, "vector", "se", vsr)
    vh = vinc(k, lambda e: e.reciprocal(out=st[0:128, 1:2], in_=st[0:128, 3:4]))
    wait(k, "vector", "ve", vh)
    vij = vinc(k, lambda e: e.scalar_tensor_tensor(
        out=st[0:128, 3:4], in0=st[0:128, 1:2], scalar=st[0:128, 1:2],
        in1=st[0:128, 2:3], op0=ALU.mult, op1=ALU.mult))
    wait(k, "vector", "ve", vij)
    vk = vinc(k, lambda e: e.tensor_scalar(out=st[0:128, 3:4], in0=st[0:128, 3:4],
                                           scalar1=-0.5, scalar2=1.5,
                                           op0=ALU.mult, op1=ALU.add))
    wait(k, "vector", "ve", vk)
    vl = vinc(k, lambda e: e.tensor_tensor(out=st[0:128, 1:2], in0=st[0:128, 1:2],
                                           in1=st[0:128, 3:4], op=ALU.mult))
    wait(k, "vector", "ve", vl)
    vm = vinc(k, lambda e: e.scalar_tensor_tensor(
        out=st[0:128, 4:5], in0=st[0:128, 0:1], scalar=-1.0, in1=st[0:128, 1:2],
        op0=ALU.mult, op1=ALU.mult))
    wait(k, "scalar", "ve", vm)
    return sinc(k, lambda e: e.activation(
        out=out_bf, in_=apply_src, func=AF.Identity,
        scale=st[0:128, 1:2], bias=st[0:128, 4:5]))


def phase_xq(k):
    d = k.d
    xf = k.sb["f32a"].ap()
    tmb = k.sb["ptr"].ap()
    done = {}
    for tt in range(16):
        buf = tt % 2
        if tt >= 2:
            wait(k, "sync", "se", done[tt - 2][0])
            wait(k, "sync", "se", done[tt - 2][1])
        v = sdma(k, xf[0:128, 1024 * buf:1024 * buf + 1024],
                 d["xq"][128 * tt:128 * tt + 128, :])
        wait(k, "scalar", "ld", v)
        wait(k, "vector", "ld", v)
        vxh = ln_tile(k, xf[0:128, 1024 * buf:1024 * buf + 1024],
                      xf[0:128, 1024 * buf:1024 * buf + 512],
                      tmb[0:128, 512 * buf:512 * buf + 512], D)
        wait(k, "tensor", "se", vxh)
        for fc in range(4):
            trans(k, psb(k, "D", 128 * fc, 128 * fc + 128),
                  tmb[0:128, 512 * buf + 128 * fc:512 * buf + 128 * fc + 128])
        wait(k, "scalar", "te")
        for fc in range(4):
            dst = xqfm(k, fc, 128 * tt, 128 * tt + 128)
            sinc(k, lambda e, dst=dst, fc=fc: e.copy(
                out=dst, in_=psb(k, "D", 128 * fc, 128 * fc + 128)))
        wait(k, "tensor", "se")
        done[tt] = (vxh, k.p.cnt[k.S("se")])
    wait(k, "tensor", "se")
    proj_fm(k, 1, 0, lambda fc, c0, c1: xqfm(k, fc, c0, c1), NR, lambda dc, c0, c1: qfm(k, dc, c0, c1))


def proj_fm(k, mi, m, src_fn, ntok, dst_fn):
    for dc in range(4):
        for t0 in range(0, ntok, 512):
            ps = k.ps["C"] if (t0 // 512) % 2 == 0 else k.ps["D"]
            for fc in range(4):
                mm(k, ps[:, 0:512], wT(k, mi, m, fc, 128 * dc, 128 * dc + 128),
                   src_fn(fc, t0, t0 + 512), start=(fc == 0), stop=(fc == 3))
            wait(k, "scalar", "te")
            bj = 4 * (4 * (mi - 1) + m) + dc
            dst = dst_fn(dc, t0, t0 + 512)
            vse = sinc(k, lambda e, dst=dst, ps=ps, bj=bj: e.activation(
                out=dst, in_=ps[:, 0:512], func=AF.Identity,
                bias=a(k, "bias4", 0, 128, bj, bj + 1)))
            wait(k, "tensor", "se", vse)


def phase_stream_x(k):
    p, d = k.p, k.d
    xf = k.sb["f32a"].ap()
    hb = k.sb["s6"].ap()
    hst = k.sb["f32b"].ap().bitcast(BF16)

    for c in range(NCH):
        blk = c % 2
        for tt in range(4):
            ti = 4 * c + tt
            buf = ti % 2
            v = sdma(k, xf[0:128, 1024 * buf:1024 * buf + 1024],
                     d["x"][128 * ti:128 * ti + 128, :])
            wait(k, "scalar", "ld", v)
            wait(k, "vector", "ld", v)
            vxh = ln_tile(k, xf[0:128, 1024 * buf:1024 * buf + 1024],
                          xf[0:128, 1024 * buf:1024 * buf + 512],
                          xtm_ring(k, blk, tt), D)
            wait(k, "tensor", "se", vxh)
            for fc in range(4):
                trans(k, psb(k, "Z2", 128 * fc, 128 * fc + 128),
                      xtm_ring(k, blk, tt, 128 * fc, 128 * fc + 128))
            wait(k, "scalar", "te")
            for fc in range(4):
                dst = xfm_ring(k, blk, fc, 128 * tt, 128 * tt + 128)
                sinc(k, lambda e, dst=dst, fc=fc: e.copy(
                    out=dst, in_=psb(k, "Z2", 128 * fc, 128 * fc + 128)))
            wait(k, "tensor", "se")
            wait(k, "sync", "se")
        for dc in range(4):
            ps = k.ps["C"] if dc % 2 == 0 else k.ps["D"]
            for fc in range(4):
                mm(k, ps[:, 0:512], wT(k, 1, 1, fc, 128 * dc, 128 * dc + 128),
                   xfm_ring(k, blk, fc, 0, 512), start=(fc == 0), stop=(fc == 3))
            wait(k, "scalar", "te")
            bj = 4 * 1 + dc
            dst = kfm(k, dc, 512 * c, 512 * c + 512)
            vse = sinc(k, lambda e, dst=dst, ps=ps, bj=bj: e.activation(
                out=dst, in_=ps[:, 0:512], func=AF.Identity,
                bias=a(k, "bias4", 0, 128, bj, bj + 1)))
            wait(k, "tensor", "se", vse)
        for tt in range(4):
            ps = k.ps["C"] if tt % 2 == 0 else k.ps["D"]
            for fc in range(4):
                mm(k, ps[:, 0:512], xfm_ring(k, blk, fc, 128 * tt, 128 * tt + 128),
                   wT(k, 1, 2, fc, 0, 512), start=(fc == 0), stop=(fc == 3))
            wait(k, "vector", "te")
            dst = vtm(k, 4 * c + tt)
            vve = vinc(k, lambda e, dst=dst, ps=ps: e.scalar_tensor_tensor(
                out=dst, in0=ps[:, 0:512], scalar=1.0,
                in1=bvb_ap(k, 0, 512), op0=ALU.mult, op1=ALU.add))
            wait(k, "tensor", "ve", vve)
        ut0 = 4104 if c == 0 else 4096
        for js in range(4):
            mm(k, k.ps["Z2"][0:2, 0:512], hb[0:128, ut0 + 2 * js:ut0 + 2 * js + 2],
               xtm_ring(k, blk, js, 0, 512), start=(js == 0), stop=False)
        mm(k, k.ps["Z2"][0:2, 0:512], hb[0:2, 4624:4626], sprev(k),
           start=False, stop=True)
        wait(k, "vector", "te")
        vs = vinc(k, lambda e: e.tensor_copy(out=snew(k), in_=k.ps["Z2"][0:2, 0:512]))
        wait(k, "tensor", "ve", vs)
        mt0 = 2048 if c == 0 else 0
        for fcq in range(4):
            ps = k.ps["Z0"] if fcq % 2 == 0 else k.ps["Z1"]
            for js in range(4):
                mm(k, ps[:, 0:512],
                   xtm_ring(k, blk, js, 128 * fcq, 128 * fcq + 128),
                   hb[0:128, mt0 + 512 * js:mt0 + 512 * js + 512],
                   start=(js == 0), stop=False)
            mm(k, ps[:, 0:512], sprev(k, 128 * fcq, 128 * fcq + 128),
               hb[0:2, 4112:4624], start=False, stop=True)
            wait(k, "scalar", "te")
            vse = sinc(k, lambda e, ps=ps, fcq=fcq: e.copy(
                out=hst[0:128, 4096 + 512 * fcq:4096 + 512 * fcq + 512],
                in_=ps[:, 0:512]))
            wait(k, "sync", "se", vse)
            sdma(k, d["hsD"][128 * fcq:128 * fcq + 128, 512 * c:512 * c + 512],
                 hst[0:128, 4096 + 512 * fcq:4096 + 512 * fcq + 512])
            wait(k, "tensor", "se", vse)
        wait(k, "vector", "te")
        vsp = vinc(k, lambda e: e.tensor_copy(out=sprev(k), in_=snew(k)))
        wait(k, "tensor", "ve", vsp)
        wait(k, "sync", "ve")
        wait(k, "sync", "ld")
        wait(k, "scalar", "ld")
    wait(k, "sync", "ld")


def phase_attention(k, mi):
    p = k.p
    build_gbe(k, f"g{mi}", f"be{mi}", 512)
    pt = k.sb["ptr"].ap()
    smf = k.sb["f32a"].ap()
    smb = k.sb["f32a"].ap().bitcast(BF16)
    bcst = k.sb["f32b"].ap().bitcast(BF16)
    res_fn = (lambda dc, c0, c1: xqfm(k, dc, c0, c1)) if mi == 1 else \
             (lambda dc, c0, c1: hso(k, dc, c0, c1))
    xdst = x1t if mi == 1 else x2t
    zst = k.sb["f32b"].ap().bitcast(BF16)
    ost = lambda hp, c0=0, c1=512: bcst[0:128, 2048 + 512 * hp + c0:2048 + 512 * hp + c1]

    cut = ATTCUT if mi == ATTMI else None
    nqc = 1 if cut else 4
    nhp = 1 if cut in ("S", "AV", "SUMS", "BCAST") else 4
    for qc in range(nqc):
        vo_prev = None
        for hp in range(nhp):
            exp_se = {}
            avsum_te = {}
            S_te = {}

            def psS(kt):
                if kt % 2 == 0:
                    return k.ps["A"][:, 0:512], k.ps["A"][:, 512:1024]
                return k.ps["C"][:, 0:512], k.ps["D"][:, 0:512]

            def emit_S(kt, hp=hp, qc=qc):
                if kt >= 2:
                    wait(k, "tensor", "se", exp_se[kt - 2][1])
                pa, pb_ = psS(kt)
                mm(k, pa, kfm(k, hp, 128 * kt, 128 * kt + 128)[0:64, :],
                   qfm(k, hp, 512 * qc, 512 * qc + 512)[0:64, :],
                   start=True, stop=True, tp=(0, 0))
                mm(k, pb_, kfm(k, hp, 128 * kt, 128 * kt + 128)[64:128, :],
                   qfm(k, hp, 512 * qc, 512 * qc + 512)[64:128, :],
                   start=True, stop=True, tp=(64, 0))
                S_te[kt] = p.cnt[k.S("te")]

            def emit_exp(kt):
                pa, pb_ = psS(kt)
                pb = kt % 2
                wait(k, "scalar", "te", S_te[kt])
                if kt >= 2 and (kt - 2) in avsum_te:
                    wait(k, "scalar", "te", avsum_te[kt - 2])
                v0 = sinc(k, lambda e, pa=pa, pb=pb: e.activation(
                    out=pt[0:128, 1024 * pb:1024 * pb + 512], in_=pa,
                    func=AF.Exp, scale=0.125))
                v1 = sinc(k, lambda e, pb_=pb_, pb=pb: e.activation(
                    out=pt[0:128, 1024 * pb + 512:1024 * pb + 1024], in_=pb_,
                    func=AF.Exp, scale=0.125))
                exp_se[kt] = (v0, v1)

            def emit_AV(kt, hp=hp):
                pb = kt % 2
                if kt == 0 and vo_prev is not None:
                    wait(k, "tensor", "ve", vo_prev)
                wait(k, "tensor", "se", exp_se[kt][0])
                mm(k, k.ps["Z0"][0:64, :], vtm(k, kt, 128 * hp, 128 * hp + 64),
                   pt[0:128, 1024 * pb:1024 * pb + 512],
                   start=(kt == 0), stop=(kt == 31), tp=(0, 0))
                mm(k, k.ps["Z2"][0:1, :], a(k, "onescol"),
                   pt[0:128, 1024 * pb:1024 * pb + 512],
                   start=(kt == 0), stop=(kt == 31), tp=(0, 0))
                wait(k, "tensor", "se", exp_se[kt][1])
                mm(k, k.ps["Z1"][64:128, :], vtm(k, kt, 128 * hp + 64, 128 * hp + 128),
                   pt[0:128, 1024 * pb + 512:1024 * pb + 1024],
                   start=(kt == 0), stop=(kt == 31), tp=(0, 64))
                mm(k, k.ps["Z3"][0:1, :], a(k, "onescol"),
                   pt[0:128, 1024 * pb + 512:1024 * pb + 1024],
                   start=(kt == 0), stop=(kt == 31), tp=(0, 0))
                avsum_te[kt] = p.cnt[k.S("te")]

            emit_S(0)
            emit_exp(0)
            emit_S(1)
            emit_exp(1)
            for kt in range(2, 32):
                emit_S(kt)
                if cut != "S":
                    emit_AV(kt - 2)
                emit_exp(kt)
            if cut != "S":
                emit_AV(30)
                emit_AV(31)
            if cut in ("S", "AV"):
                return
            wait(k, "vector", "te")
            fbf = k.sb["f32b"].ap()
            vinc(k, lambda e: e.tensor_copy(out=fbf[0:1, 0:512],
                                            in_=k.ps["Z2"][0:1, :]))
            vinc(k, lambda e: e.tensor_copy(out=fbf[0:1, 512:1024],
                                            in_=k.ps["Z3"][0:1, :]))
            vop(k, lambda e: e.reciprocal(out=smf[0:1, 0:1024],
                                          in_=fbf[0:1, 0:1024]))
            vop(k, lambda e: e.tensor_tensor(
                out=smf[0:1, 1024:2048], in0=fbf[0:1, 0:1024],
                in1=smf[0:1, 0:1024], op=ALU.mult))
            vop(k, lambda e: e.tensor_scalar(
                out=smf[0:1, 1024:2048], in0=smf[0:1, 1024:2048],
                scalar1=-1.0, scalar2=2.0, op0=ALU.mult, op1=ALU.add))
            vop(k, lambda e: e.tensor_tensor(
                out=bcst[0:1, 0:1024], in0=smf[0:1, 0:1024],
                in1=smf[0:1, 1024:2048], op=ALU.mult))
            vrec = vinc(k, lambda e: e.engine_nop())
            if cut == "SUMS":
                return
            wait(k, "tensor", "ve", vrec)
            mm(k, k.ps["Z2"][0:64, 0:512], a(k, "sel1a", 0, 1, 0, 64),
               bcst[0:1, 0:512], start=True, stop=True, tp=(0, 0))
            mm(k, k.ps["Z2"][64:128, 0:512], a(k, "sel1a", 0, 1, 0, 64),
               bcst[0:1, 512:1024], start=True, stop=True, tp=(0, 64))
            wait(k, "vector", "te")
            vb = vinc(k, lambda e: e.tensor_copy(out=bcst[0:128, 4096:4608],
                                                 in_=k.ps["Z2"][:, 0:512]))
            vinc(k, lambda e, hp=hp: e.tensor_tensor(
                out=ost(hp)[0:64, :], in0=k.ps["Z0"][0:64, :],
                in1=bcst[0:64, 4096:4608], op=ALU.mult))
            vo = vinc(k, lambda e, hp=hp: e.tensor_tensor(
                out=ost(hp)[64:128, :], in0=k.ps["Z1"][64:128, :],
                in1=bcst[64:128, 4096:4608], op=ALU.mult))
            vo_prev = vo
        if cut == "BCAST":
            return
        if cut == "HP":
            continue
        wait(k, "tensor", "ve", vo_prev)
        for dc in range(4):
            ps = k.ps["C"] if dc % 2 == 0 else k.ps["D"]
            for hp in range(4):
                mm(k, ps[:, 0:512], wT(k, mi, 3, hp, 128 * dc, 128 * dc + 128),
                   ost(hp), start=(hp == 0), stop=(hp == 3))
            wait(k, "vector", "te")
            bj = 4 * (4 * (mi - 1) + 3) + dc
            res = res_fn(dc, 512 * qc, 512 * qc + 512)
            vz = vinc(k, lambda e, ps=ps, bj=bj, res=res, dc=dc: e.scalar_tensor_tensor(
                out=zst[0:128, 512 * dc:512 * dc + 512], in0=ps[:, 0:512],
                scalar=a(k, "bias4", 0, 128, bj, bj + 1), in1=res,
                op0=ALU.add, op1=ALU.add))
            wait(k, "tensor", "ve", vz)
        if cut == "OPROJ":
            return
        for sub in range(4):
            tq = 4 * qc + sub
            for dc in range(4):
                trans(k, psb(k, "C", 128 * dc, 128 * dc + 128),
                      zst[0:128, 512 * dc + 128 * sub:512 * dc + 128 * sub + 128])
            wait(k, "vector", "te")
            wait(k, "scalar", "te")
            vap = ln_tile(k, psb(k, "C", 0, 512), psb(k, "C", 0, 512),
                          bcst[0:128, 4608:5120], 512,
                          sq_ap=k.sb["f32a"].ap()[0:128, 0:512])
            wait(k, "vector", "se", vap)
            vop(k, lambda e: e.tensor_tensor(
                out=bcst[0:128, 4608:5120], in0=bcst[0:128, 4608:5120],
                in1=gbe_ap(k, 0, 512), op=ALU.mult))
            dst = xdst(k, tq)
            vx = vinc(k, lambda e, dst=dst: e.tensor_tensor(
                out=dst, in0=bcst[0:128, 4608:5120],
                in1=gbe_ap(k, 1024, 1536), op=ALU.add))
            wait(k, "tensor", "ve", vx)
            wait(k, "tensor", "se", vap)


def phase_merge_hso(k):
    d = k.d
    hm = k.sb["f32b"].ap().bitcast(BF16)
    for fc in range(4):
        v0 = sdma(k, hm[0:128, 0:2048],
                  d["hsD"][128 * fc:128 * fc + 128, 0:2048])
        v1 = sdma(k, hm[0:128, 2048:4096],
                  d["hsD"][128 * fc:128 * fc + 128, 2048:4096])
        wait(k, "vector", "ld", v1)
        vop(k, lambda e, fc=fc: e.tensor_scalar(
            out=hso(k, fc), in0=hm[0:128, 2048:4096],
            scalar1=a(k, "maskc", 0, 128, 1, 2), scalar2=None, op0=ALU.mult))
        vm = vinc(k, lambda e, fc=fc: e.scalar_tensor_tensor(
            out=hso(k, fc), in0=hm[0:128, 0:2048],
            scalar=a(k, "maskc", 0, 128, 0, 1), in1=hso(k, fc),
            op0=ALU.mult, op1=ALU.add))
        wait(k, "sync", "ve", vm)


def phase_proj2(k):
    p, d = k.p, k.d
    for c in range(NCH):
        buf = c % 2
        vs = []
        for fc in range(4):
            vs.append(sdma(k, hsring(k, buf, fc),
                           d["hsD"][128 * fc:128 * fc + 128, 512 * c:512 * c + 512]))
        wait(k, "tensor", "ld", vs[-1])
        for dc in range(4):
            ps = k.ps["C"] if dc % 2 == 0 else k.ps["D"]
            for fc in range(4):
                mm(k, ps[:, 0:512], wT(k, 2, 1, fc, 128 * dc, 128 * dc + 128),
                   hsring(k, buf, fc), start=(fc == 0), stop=(fc == 3))
            wait(k, "scalar", "te")
            bj = 4 * 5 + dc
            dst = kfm(k, dc, 512 * c, 512 * c + 512)
            vse = sinc(k, lambda e, dst=dst, ps=ps, bj=bj: e.activation(
                out=dst, in_=ps[:, 0:512], func=AF.Identity,
                bias=a(k, "bias4", 0, 128, bj, bj + 1)))
            wait(k, "tensor", "se", vse)
        for tt in range(4):
            ps = k.ps["C"] if tt % 2 == 0 else k.ps["D"]
            for fc in range(4):
                mm(k, ps[:, 0:512], hsring(k, buf, fc, 128 * tt, 128 * tt + 128),
                   wT(k, 2, 2, fc, 0, 512), start=(fc == 0), stop=(fc == 3))
            wait(k, "vector", "te")
            dst = vtm(k, 4 * c + tt)
            vve = vinc(k, lambda e, dst=dst, ps=ps: e.scalar_tensor_tensor(
                out=dst, in0=ps[:, 0:512], scalar=1.0,
                in1=bvb_ap(k, 512, 1024), op0=ALU.mult, op1=ALU.add))
            wait(k, "tensor", "ve", vve)
        wait(k, "sync", "ve")
        wait(k, "sync", "se")
    proj_fm(k, 2, 0, lambda fc, c0, c1: hso(k, fc, c0, c1), NR, lambda dc, c0, c1: qfm(k, dc, c0, c1))


def phase_ffw_weights(k):
    d = k.d
    stage = k.sb["ptr"].ap()
    for wname, tname, nrow, ncol in (("wff1", "wff1T", DFF, D),
                                     ("wff2", "wff2T", D, DFF)):
        for r in range(nrow // 128):
            for cg in range(ncol // 1024):
                vg = sdma(k, stage[0:128, 0:1024],
                          d[wname][128 * r:128 * r + 128, 1024 * cg:1024 * cg + 1024],
                          "gpsimd")
                wait(k, "tensor", "gd", vg)
                for ic in range(8):
                    trans(k, psb(k, "D", 128 * (ic % 4), 128 * (ic % 4) + 128),
                          stage[0:128, 128 * ic:128 * ic + 128])
                    if ic % 4 == 3:
                        wait(k, "vector", "te")
                        for j in range(4):
                            icj = ic - 3 + j
                            vinc(k, lambda e, icj=icj, j=j: e.tensor_copy(
                                out=stage[0:128, 1024 + 128 * icj:1152 + 128 * icj],
                                in_=psb(k, "D", 128 * j, 128 * j + 128)))
                        wait(k, "tensor", "ve")
                ve = k.p.cnt[k.S("ve")]
                wait(k, "sync", "ve", ve)
                dst = AP(tensor=d[tname].tensor, offset=128 * r + 1024 * cg * nrow,
                         ap=[[nrow, 128], [128 * nrow, 8], [1, 128]])
                sdma(k, dst, stage[0:128, 1024:2048])
                wait(k, "gpsimd", "ld")
    wait(k, "sync", "ld")


def phase_attn_ffn(k):
    p, d = k.p, k.d
    build_gbe(k, "gff", "beff", 1024)
    for tq in range(16):
        for half, xsrc in ((0, x1t), (1, x2t)):
            for f in range(4):
                trans(k, psb(k, "D", 128 * f, 128 * f + 128),
                      xsrc(k, tq, 128 * f, 128 * f + 128))
            wait(k, "scalar", "te")
            for f in range(4):
                fc = 4 * half + f
                dst = yfm(k, fc, 128 * tq, 128 * tq + 128)
                sinc(k, lambda e, dst=dst, f=f: e.copy(
                    out=dst, in_=psb(k, "D", 128 * f, 128 * f + 128)))
            wait(k, "tensor", "se")
    wait(k, "vector", "se")
    wait(k, "sync", "se")

    w1r = k.sb["s3"].ap()
    w2r = k.sb["s3"].ap()
    hst = k.sb["f32b"].ap().bitcast(BF16)
    xf = k.sb["f32a"].ap()
    for tb in range(4):
        h1_te = {}
        for fb in range(32):
            buf = fb % 2
            if fb >= 2:
                wait(k, "sync", "te", h1_te[fb - 2])
            src = AP(tensor=d["wff1T"].tensor, offset=128 * fb,
                     ap=[[DFF, 128], [128 * DFF, 8], [1, 128]])
            v = sdma(k, w1r[0:128, 1024 * buf:1024 * buf + 1024], src)
            wait(k, "tensor", "ld", v)
            ps = k.ps["C"] if fb % 2 == 0 else k.ps["D"]
            for yc in range(8):
                mm(k, ps[:, 0:512], w1r[0:128, 1024 * buf + 128 * yc:1024 * buf + 128 * yc + 128],
                   yfm(k, yc, 512 * tb, 512 * tb + 512),
                   start=(yc == 0), stop=(yc == 7))
            h1_te[fb] = k.p.cnt[k.S("te")]
            wait(k, "scalar", "te")
            dst = h1sb(k, fb)
            vse = sinc(k, lambda e, dst=dst, ps=ps, fb=fb: e.activation(
                out=dst, in_=ps[:, 0:512], func=AF.Relu,
                bias=a(k, "bff1c", 0, 128, fb, fb + 1)))
            wait(k, "tensor", "se", vse)
            wait(k, "sync", "se", vse)
        h2_te = {}
        for oc in range(8):
            buf = oc % 2
            if oc >= 2:
                wait(k, "sync", "te", h2_te[oc - 2])
            src = AP(tensor=d["wff2T"].tensor, offset=128 * oc,
                     ap=[[D, 128], [128 * D, 32], [1, 128]])
            v = sdma(k, w2r[0:128, 2048 + 4096 * buf:2048 + 4096 * buf + 4096], src)
            wait(k, "tensor", "ld", v)
            ps = k.ps["C"] if oc % 2 == 0 else k.ps["D"]
            for ffc in range(32):
                mm(k, ps[:, 0:512],
                   w2r[0:128, 2048 + 4096 * buf + 128 * ffc:2048 + 4096 * buf + 128 * ffc + 128],
                   h1sb(k, ffc), start=(ffc == 0), stop=(ffc == 31))
            h2_te[oc] = k.p.cnt[k.S("te")]
            wait(k, "scalar", "te")
            vse = sinc(k, lambda e, ps=ps, oc=oc: e.activation(
                out=hst[0:128, 4096:4608], in_=ps[:, 0:512], func=AF.Identity,
                bias=a(k, "bff2c", 0, 128, oc, oc + 1)))
            wait(k, "tensor", "se", vse)
            for sub in range(4):
                trans(k, psb(k, "Z0", 128 * sub, 128 * sub + 128),
                      hst[0:128, 4096 + 128 * sub:4096 + 128 * sub + 128])
            wait(k, "vector", "te")
            for sub in range(4):
                tq = 4 * tb + sub
                xs = x1t(k, tq, 128 * oc, 128 * oc + 128) if oc < 4 else \
                     x2t(k, tq, 128 * (oc - 4), 128 * (oc - 4) + 128)
                dst = z3t(k, tq, 128 * oc, 128 * oc + 128)
                vinc(k, lambda e, dst=dst, xs=xs, sub=sub: e.tensor_tensor(
                    out=dst, in0=psb(k, "Z0", 128 * sub, 128 * sub + 128),
                    in1=xs, op=ALU.add))
            wait(k, "tensor", "ve")
            wait(k, "scalar", "ve")
    for tq in range(16):
        v = sdma(k, xf[0:128, 0:1024], d["xq"][128 * tq:128 * tq + 128, :])
        vap = ln_tile(k, z3t(k, tq), z3t(k, tq), hst[0:128, 4096:5120], D)
        wait(k, "vector", "se", vap)
        vop(k, lambda e: e.tensor_tensor(
            out=hst[0:128, 4096:5120], in0=hst[0:128, 4096:5120],
            in1=gbe_ap(k, 0, 1024), op=ALU.mult))
        vop(k, lambda e: e.tensor_tensor(
            out=hst[0:128, 4096:5120], in0=hst[0:128, 4096:5120],
            in1=gbe_ap(k, 1024, 2048), op=ALU.add))
        wait(k, "vector", "ld", v)
        dst = z3t(k, tq)
        vx = vinc(k, lambda e, dst=dst: e.tensor_tensor(
            out=dst, in0=hst[0:128, 4096:5120], in1=xf[0:128, 0:1024], op=ALU.add))
        wait(k, "sync", "ve", vx)
        wait(k, "scalar", "ve", vx)


def quant_ladder(k, zsrc_f32, bca_ps, bcr_ps, width, out_bf):
    p = k.p
    fbap = k.sb["f32b"].ap()
    fbb = k.sb["f32b"].ap().bitcast(BF16)
    z = fbap[0:128, 2304:2304 + width]
    u = fbap[0:128, 2816:2816 + width]
    sg = fbb[0:128, 0:width]
    gslot = lambda j: fbb[0:128, 512 * (j + 1):512 * (j + 1) + width]
    t2 = fbb[0:128, 4096:4096 + width]
    vz = vinc(k, lambda e: e.tensor_tensor(out=z, in0=zsrc_f32, in1=bca_ps,
                                           op=ALU.mult))
    wait(k, "scalar", "ve", vz)
    sinc(k, lambda e: e.activation(out=u, in_=z, func=AF.Abs))
    vsg = sinc(k, lambda e: e.activation(out=sg, in_=z, func=AF.Sign))
    wait(k, "vector", "se", vsg)
    t0, t1 = gslot(0), gslot(1)
    vop(k, lambda e: e.tensor_scalar(out=t0, in0=u, scalar1=QMIDS[0], scalar2=None,
                                     op0=ALU.is_gt))
    for i, sc in ((1, None), (2, None), (3, None), (4, 2.0), (5, 2.0), (6, 4.0)):
        vop(k, lambda e, i=i: e.tensor_scalar(out=t1, in0=u, scalar1=QMIDS[i],
                                              scalar2=None, op0=ALU.is_gt))
        if sc is None:
            vop(k, lambda e: e.tensor_tensor(out=t0, in0=t0, in1=t1, op=ALU.add))
        else:
            vop(k, lambda e, sc=sc: e.scalar_tensor_tensor(
                out=t0, in0=t1, scalar=sc, in1=t0, op0=ALU.mult, op1=ALU.add))
    vop(k, lambda e: e.tensor_tensor(out=t0, in0=t0, in1=sg, op=ALU.mult))
    return vinc(k, lambda e: e.tensor_tensor(out=out_bf, in0=t0, in1=bcr_ps,
                                             op=ALU.mult))


def quant_scales(k, src_f32, width, nblk):
    p = k.p
    sm = k.sb["f32a"].ap()
    bm = sm[0:128, 1408:1408 + nblk]
    rr = sm[0:128, 1496:1496 + nblk]
    rh = sm[0:128, 1584:1584 + nblk]
    tmp = sm[0:128, 1672:1672 + nblk]
    stg = k.sb["ptr"].ap()
    rhiS = stg[0:128, 1536:1536 + nblk]
    rhhiS = stg[0:128, 1664:1664 + nblk]
    rloS = stg[0:128, 1792:1792 + nblk]
    rhloS = stg[0:128, 1920:1920 + nblk]
    v1 = vinc(k, lambda e: e.tensor_reduce(
        out=bm, in_=src_f32.rearrange("p (g w) -> p g w", w=16),
        axis=AX.X, op=ALU.max, apply_absolute_value=True))
    wait(k, "vector", "ve", v1)
    v2 = vinc(k, lambda e: e.tensor_scalar(out=bm, in0=bm, scalar1=6e-12,
                                           scalar2=None, op0=ALU.max))
    wait(k, "vector", "ve", v2)
    v3 = vinc(k, lambda e: e.reciprocal(out=rr, in_=bm))
    vop(k, lambda e: e.tensor_scalar(out=rh, in0=bm, scalar1=1.0 / 12.0,
                                     scalar2=None, op0=ALU.mult))
    wait(k, "vector", "ve", v3)
    v4 = vinc(k, lambda e: e.tensor_tensor(out=tmp, in0=rr, in1=bm, op=ALU.mult))
    wait(k, "vector", "ve", v4)
    v5 = vinc(k, lambda e: e.tensor_scalar(out=tmp, in0=tmp, scalar1=-1.0,
                                           scalar2=2.0, op0=ALU.mult, op1=ALU.add))
    wait(k, "vector", "ve", v5)
    v6 = vinc(k, lambda e: e.tensor_tensor(out=rr, in0=rr, in1=tmp, op=ALU.mult))
    wait(k, "vector", "ve", v6)
    v7 = vinc(k, lambda e: e.tensor_scalar(out=rr, in0=rr, scalar1=6.0,
                                           scalar2=None, op0=ALU.mult))
    vop(k, lambda e: e.tensor_copy(out=rhhiS, in_=rh))
    wait(k, "vector", "ve", v7)
    v8 = vinc(k, lambda e: e.tensor_copy(out=rhiS, in_=rr))
    vop(k, lambda e: e.tensor_tensor(out=bm, in0=rh, in1=rhhiS, op=ALU.subtract))
    wait(k, "vector", "ve", v8)
    v9 = vinc(k, lambda e: e.tensor_tensor(out=tmp, in0=rr, in1=rhiS,
                                           op=ALU.subtract))
    vop(k, lambda e: e.tensor_copy(out=rhloS, in_=bm))
    wait(k, "vector", "ve", v9)
    vr = vinc(k, lambda e: e.tensor_copy(out=rloS, in_=tmp))
    wait(k, "tensor", "ve", vr)
    pD = k.ps["D"].bitcast(BF16)
    trans(k, pD[0:nblk, 0:128], rhiS)
    trans(k, pD[0:nblk, 128:256], rhhiS)
    trans(k, pD[0:nblk, 256:384], rloS)
    trans(k, pD[0:nblk, 384:512], rhloS)
    wait(k, "vector", "te")
    smb = k.sb["f32a"].ap().bitcast(BF16)
    vinc(k, lambda e: e.tensor_copy(out=smb[0:nblk, 3776:3904][0:nblk, 0:128],
                                    in_=pD[0:nblk, 0:128]))
    vinc(k, lambda e: e.tensor_copy(out=smb[0:nblk, 3904:4032][0:nblk, 0:128],
                                    in_=pD[0:nblk, 128:256]))
    vinc(k, lambda e: e.tensor_copy(out=smb[0:nblk, 3520:3648][0:nblk, 0:128],
                                    in_=pD[0:nblk, 256:384]))
    v = vinc(k, lambda e: e.tensor_copy(out=smb[0:nblk, 3648:3776][0:nblk, 0:128],
                                        in_=pD[0:nblk, 384:512]))
    wait(k, "tensor", "ve", v)
    return v


def phase_ffn_weights_quant(k):
    p, d = k.p, k.d
    fa = k.sb["f32a"].ap()
    fb = k.sb["f32b"].ap()
    stage_in = fb[0:128, 0:1024]
    smb = k.sb["f32a"].ap().bitcast(BF16)
    outb = k.sb["ptr"].ap()
    for wname, tname in (("wgate", "wgqT"), ("wup", "wuqT")):
        for r in range(FFI // 128):
            v = sdma(k, stage_in, d[wname][128 * r:128 * r + 128, :])
            wait(k, "vector", "ld", v)
            wait(k, "scalar", "ld", v)
            vr = quant_scales(k, stage_in, 1024, 64)
            wait(k, "tensor", "ve", vr)
            for ic in range(8):
                trans(k, k.ps["Z2"][:, 128 * (ic % 4):128 * (ic % 4) + 128],
                      stage_in[:, 128 * ic:128 * ic + 128], fp32=True)
                if ic % 4 == 3:
                    wait(k, "vector", "te")
                    for j in range(4):
                        icj = ic - 3 + j
                        vinc(k, lambda e, icj=icj, j=j: e.tensor_copy(
                            out=fa[0:128, 128 * icj:128 * icj + 128],
                            in_=k.ps["Z2"][:, 128 * j:128 * j + 128]))
                    wait(k, "tensor", "ve")
            for ic in range(8):
                mm(k, k.ps["A"][:, 128 * ic:128 * ic + 128],
                   a(k, "sel88", 0, 64, 128 * ic, 128 * ic + 128),
                   smb[0:64, 3776:3904][0:64, 0:128], start=True, stop=False)
                mm(k, k.ps["A"][:, 128 * ic:128 * ic + 128],
                   a(k, "sel88", 0, 64, 128 * ic, 128 * ic + 128),
                   smb[0:64, 3520:3648][0:64, 0:128], start=False, stop=True)
                psr = k.ps["Z0"] if ic < 4 else k.ps["Z1"]
                mm(k, psr[:, 128 * (ic % 4):128 * (ic % 4) + 128],
                   a(k, "sel88", 0, 64, 128 * ic, 128 * ic + 128),
                   smb[0:64, 3904:4032][0:64, 0:128], start=True, stop=False)
                mm(k, psr[:, 128 * (ic % 4):128 * (ic % 4) + 128],
                   a(k, "sel88", 0, 64, 128 * ic, 128 * ic + 128),
                   smb[0:64, 3648:3776][0:64, 0:128], start=False, stop=True)
            wait(k, "vector", "te")
            quant_ladder(k, fa[0:128, 0:512], k.ps["A"][:, 0:512],
                         k.ps["Z0"][:, 0:512], 512, outb[0:128, 0:512])
            vq = quant_ladder(k, fa[0:128, 512:1024], k.ps["A"][:, 512:1024],
                              k.ps["Z1"][:, 0:512], 512, outb[0:128, 512:1024])
            wait(k, "sync", "ve", vq)
            wait(k, "tensor", "ve", vq)
            dst = AP(tensor=d[tname].tensor, offset=128 * r,
                     ap=[[FFI, 128], [128 * FFI, 8], [1, 128]])
            sdma(k, dst, outb[0:128, 0:1024])
            wait(k, "vector", "ld")
            wait(k, "scalar", "ld")
        wait(k, "sync", "ld")
    for r in range(D // 128):
        for hf2 in range(2):
            v = sdma(k, fb[0:128, 0:1408],
                     d["wdown"][128 * r:128 * r + 128, 1408 * hf2:1408 * hf2 + 1408])
            wait(k, "vector", "ld", v)
            wait(k, "scalar", "ld", v)
            vr = quant_scales(k, fb[0:128, 0:1408], 1408, 88)
            wait(k, "tensor", "ve", vr)
            for ic in range(11):
                trans(k, k.ps["Z2"][:, 128 * (ic % 4):128 * (ic % 4) + 128],
                      fb[0:128, 128 * ic:128 * ic + 128], fp32=True)
                if ic % 4 == 3 or ic == 10:
                    wait(k, "vector", "te")
                    for j in range(ic % 4 + 1):
                        icj = ic - (ic % 4) + j
                        vinc(k, lambda e, icj=icj, j=j: e.tensor_copy(
                            out=fa[0:128, 128 * icj:128 * icj + 128],
                            in_=k.ps["Z2"][:, 128 * j:128 * j + 128]))
                    wait(k, "tensor", "ve")
            for ic in range(11):
                if ic < 8:
                    psx, col = k.ps["A"], 128 * ic
                    psy, coly = (k.ps["Z0"], 128 * ic) if ic < 4 else (k.ps["Z1"], 128 * (ic - 4))
                else:
                    psx, col = k.ps["C"], 128 * (ic - 8)
                    psy, coly = k.ps["Z2"], 128 * (ic - 8)
                mm(k, psx[:, col:col + 128],
                   a(k, "sel88", 0, 88, 128 * ic, 128 * ic + 128),
                   smb[0:88, 3776:3904][0:88, 0:128], start=True, stop=False)
                mm(k, psx[:, col:col + 128],
                   a(k, "sel88", 0, 88, 128 * ic, 128 * ic + 128),
                   smb[0:88, 3520:3648][0:88, 0:128], start=False, stop=True)
                mm(k, psy[:, coly:coly + 128],
                   a(k, "sel88", 0, 88, 128 * ic, 128 * ic + 128),
                   smb[0:88, 3904:4032][0:88, 0:128], start=True, stop=False)
                mm(k, psy[:, coly:coly + 128],
                   a(k, "sel88", 0, 88, 128 * ic, 128 * ic + 128),
                   smb[0:88, 3648:3776][0:88, 0:128], start=False, stop=True)
            wait(k, "vector", "te")
            quant_ladder(k, fa[0:128, 0:512], k.ps["A"][:, 0:512],
                         k.ps["Z0"][:, 0:512], 512, outb[0:128, 0:512])
            quant_ladder(k, fa[0:128, 512:1024], k.ps["A"][:, 512:1024],
                         k.ps["Z1"][:, 0:512], 512, outb[0:128, 512:1024])
            vq2 = quant_ladder(k, fa[0:128, 1024:1408], k.ps["C"][:, 0:384],
                               k.ps["Z2"][:, 0:384], 384, outb[0:128, 1024:1408])
            wait(k, "sync", "ve", vq2)
            wait(k, "tensor", "ve", vq2)
            dst = AP(tensor=d["wdqT"].tensor, offset=128 * r + 1408 * hf2 * D,
                     ap=[[D, 128], [128 * D, 11], [1, 128]])
            sdma(k, dst, outb[0:128, 0:1408])
            wait(k, "vector", "ld")
            wait(k, "scalar", "ld")
    wait(k, "sync", "ld")


def phase_ffn_block(k):
    p, d = k.p, k.d
    fa = k.sb["f32a"].ap()
    fb = k.sb["f32b"].ap()
    smb = k.sb["f32a"].ap().bitcast(BF16)
    ptb = k.sb["ptr"].ap()
    for kc in range(8):
        sdma(k, wgq_tile(k, kc), d["wgqT"][128 * kc:128 * kc + 128, :])
        sdma(k, wuq_tile(k, kc), d["wuqT"][128 * kc:128 * kc + 128, :], "gpsimd")
    for tq in range(16):
        vap = ln_tile(k, z3t(k, tq), z3t(k, tq), fa[0:128, 0:1024], D,
                      sq_ap=fb[0:128, 2048:3072])
        wait(k, "vector", "se", vap)
        vr = quant_scales(k, fa[0:128, 0:1024], 1024, 64)
        wait(k, "tensor", "ve", vr)
        for ic in range(8):
            trans(k, k.ps["Z2"][:, 128 * (ic % 4):128 * (ic % 4) + 128],
                  fa[0:128, 128 * ic:128 * ic + 128], fp32=True)
            if ic % 4 == 3:
                wait(k, "scalar", "te")
                for j in range(4):
                    icj = ic - 3 + j
                    sinc(k, lambda e, icj=icj, j=j: e.copy(
                        out=fa[0:128, 128 * icj:128 * icj + 128],
                        in_=k.ps["Z2"][:, 128 * j:128 * j + 128]))
                wait(k, "tensor", "se")
        for ic in range(8):
            mm(k, k.ps["A"][:, 128 * ic:128 * ic + 128],
               a(k, "sel88", 0, 64, 128 * ic, 128 * ic + 128),
               smb[0:64, 3776:3904][0:64, 0:128], start=True, stop=False)
            mm(k, k.ps["A"][:, 128 * ic:128 * ic + 128],
               a(k, "sel88", 0, 64, 128 * ic, 128 * ic + 128),
               smb[0:64, 3520:3648][0:64, 0:128], start=False, stop=True)
            psr = k.ps["Z0"] if ic < 4 else k.ps["Z1"]
            mm(k, psr[:, 128 * (ic % 4):128 * (ic % 4) + 128],
               a(k, "sel88", 0, 64, 128 * ic, 128 * ic + 128),
               smb[0:64, 3904:4032][0:64, 0:128], start=True, stop=False)
            mm(k, psr[:, 128 * (ic % 4):128 * (ic % 4) + 128],
               a(k, "sel88", 0, 64, 128 * ic, 128 * ic + 128),
               smb[0:64, 3648:3776][0:64, 0:128], start=False, stop=True)
        wait(k, "vector", "te")
        wait(k, "vector", "se")
        quant_ladder(k, fa[0:128, 0:512], k.ps["A"][:, 0:512],
                     k.ps["Z0"][:, 0:512], 512, ptb[0:128, 0:512])
        vq = quant_ladder(k, fa[0:128, 512:1024], k.ps["A"][:, 512:1024],
                          k.ps["Z1"][:, 0:512], 512, ptb[0:128, 512:1024])
        wait(k, "scalar", "ve", vq)
        for ic in range(8):
            dst = qxfm(k, ic, 128 * tq, 128 * tq + 128)
            sinc(k, lambda e, dst=dst, ic=ic: e.copy(
                out=dst, in_=ptb[0:128, 128 * ic:128 * ic + 128]))
        wait(k, "tensor", "se")
        wait(k, "vector", "se")
    wait(k, "tensor", "gd")
    wait(k, "tensor", "ld")
    for ts in range(16):
        for hf2 in range(2):
            for ci, w in ((0, 512), (1, 512), (2, 384)):
                fo0 = 1408 * hf2 + 512 * ci
                psg, psu = k.ps["C"], k.ps["D"]
                for kc in range(8):
                    mm(k, psg[:, 0:w], qxfm(k, kc, 128 * ts, 128 * ts + 128),
                       wgq_tile(k, kc, fo0, fo0 + w), start=(kc == 0), stop=(kc == 7))
                for kc in range(8):
                    mm(k, psu[:, 0:w], qxfm(k, kc, 128 * ts, 128 * ts + 128),
                       wuq_tile(k, kc, fo0, fo0 + w), start=(kc == 0), stop=(kc == 7))
                wait(k, "scalar", "te")
                vsg = sinc(k, lambda e, w=w: e.activation(
                    out=ptb[0:128, 1024:1024 + w], in_=psg[:, 0:w], func=AF.Silu))
                wait(k, "vector", "se", vsg)
                vh = vinc(k, lambda e, w=w, ci=ci: e.tensor_tensor(
                    out=fa[0:128, 512 * ci:512 * ci + w], in0=psu[:, 0:w],
                    in1=ptb[0:128, 1024:1024 + w], op=ALU.mult))
                wait(k, "tensor", "ve", vh)
                wait(k, "scalar", "ve", vh)
            vr = quant_scales(k, fa[0:128, 0:1408], 1408, 88)
            wait(k, "tensor", "ve", vr)
            for ic in range(11):
                trans(k, k.ps["Z2"][:, 128 * (ic % 4):128 * (ic % 4) + 128],
                      fa[0:128, 128 * ic:128 * ic + 128], fp32=True)
                if ic % 4 == 3 or ic == 10:
                    wait(k, "scalar", "te")
                    for j in range(ic % 4 + 1):
                        icj = ic - (ic % 4) + j
                        sinc(k, lambda e, icj=icj, j=j: e.copy(
                            out=fa[0:128, 128 * icj:128 * icj + 128],
                            in_=k.ps["Z2"][:, 128 * j:128 * j + 128]))
                    wait(k, "tensor", "se")
            for ic in range(11):
                if ic < 8:
                    psx, col = k.ps["A"], 128 * ic
                    psy, coly = (k.ps["Z0"], 128 * ic) if ic < 4 else (k.ps["Z1"], 128 * (ic - 4))
                else:
                    psx, col = k.ps["C"], 128 * (ic - 8)
                    psy, coly = k.ps["Z2"], 128 * (ic - 8)
                mm(k, psx[:, col:col + 128],
                   a(k, "sel88", 0, 88, 128 * ic, 128 * ic + 128),
                   smb[0:88, 3776:3904][0:88, 0:128], start=True, stop=False)
                mm(k, psx[:, col:col + 128],
                   a(k, "sel88", 0, 88, 128 * ic, 128 * ic + 128),
                   smb[0:88, 3520:3648][0:88, 0:128], start=False, stop=True)
                mm(k, psy[:, coly:coly + 128],
                   a(k, "sel88", 0, 88, 128 * ic, 128 * ic + 128),
                   smb[0:88, 3904:4032][0:88, 0:128], start=True, stop=False)
                mm(k, psy[:, coly:coly + 128],
                   a(k, "sel88", 0, 88, 128 * ic, 128 * ic + 128),
                   smb[0:88, 3648:3776][0:88, 0:128], start=False, stop=True)
            wait(k, "vector", "te")
            wait(k, "vector", "se")
            hq0 = 2816 + 1408 * hf2
            quant_ladder(k, fa[0:128, 0:512], k.ps["A"][:, 0:512],
                         k.ps["Z0"][:, 0:512], 512,
                         a(k, "s5", 0, 128, hq0, hq0 + 512))
            quant_ladder(k, fa[0:128, 512:1024], k.ps["A"][:, 512:1024],
                         k.ps["Z1"][:, 0:512], 512,
                         a(k, "s5", 0, 128, hq0 + 512, hq0 + 1024))
            vq2 = quant_ladder(k, fa[0:128, 1024:1408], k.ps["C"][:, 0:384],
                               k.ps["Z2"][:, 0:384], 384,
                               a(k, "s5", 0, 128, hq0 + 1024, hq0 + 1408))
            wait(k, "tensor", "ve", vq2)
        vhq = p.cnt[k.S("ve")]
        wait(k, "sync", "ve", vhq)
        dst = AP(tensor=d["hqTD"].tensor, offset=128 * ts,
                 ap=[[NR, 128], [128 * NR, 22], [1, 128]])
        sdma(k, dst, a(k, "s5", 0, 128, 2816, 5632))
        wait(k, "vector", "ld")
        wait(k, "gpsimd", "ld")
    wait(k, "sync", "ld")
    barrier(k)
    for kc in range(22):
        nm = "s1" if kc < 16 else "s2"
        sdma(k, wdq_tile(k, kc), d["wdqT"][128 * kc:128 * kc + 128, :])
    wait(k, "tensor", "ld")
    g2_te = {}
    for ts in range(16):
        buf = ts % 2
        if ts >= 2:
            wait(k, "gpsimd", "te", g2_te[ts - 2])
        v = sdma(k, a(k, "s3", 0, 128, 2816 * buf, 2816 * buf + 2816),
                 AP(tensor=d["hqTD"].tensor, offset=128 * ts,
                    ap=[[NR, 128], [128 * NR, 22], [1, 128]]), "gpsimd")
        wait(k, "tensor", "gd", v)
        for oh in range(2):
            ps = k.ps["C"] if oh == 0 else k.ps["D"]
            for kc in range(22):
                mm(k, ps[:, 0:512], hqring(k, buf, kc),
                   wdq_tile(k, kc, 512 * oh, 512 * oh + 512),
                   start=(kc == 0), stop=(kc == 21))
            wait(k, "vector", "te")
            col = 1024 + 512 * oh
            vf = vinc(k, lambda e, ps=ps, col=col, ts=ts, oh=oh: e.scalar_tensor_tensor(
                out=fa[0:128, col:col + 512], in0=ps[:, 0:512], scalar=1.0,
                in1=z3t(k, ts, 512 * oh, 512 * oh + 512),
                op0=ALU.mult, op1=ALU.add))
            wait(k, "sync", "ve", vf)
            stdma(k, k.out[128 * ts:128 * ts + 128, 512 * oh:512 * oh + 512],
                  fa[0:128, col:col + 512])
            wait(k, "tensor", "ve", vf)
        g2_te[ts] = k.p.cnt[k.S("te")]
    if "qx" in DEBUG:
        wait(k, "gpsimd", "te")
        for kc in range(8):
            sdma(k, d["dbg_qx"][128 * kc:128 * kc + 128, :], qxfm(k, kc), "gpsimd")
    if "hq" in DEBUG:
        wait(k, "gpsimd", "ld")
        for rr in range(4):
            sdma(k, d["dbg_hq"][704 * rr:704 * rr + 704, :],
                 d["hqTD"][704 * rr:704 * rr + 704, :], "gpsimd")
    for wnm in ("wdq", "wgq", "wuq"):
        if wnm in DEBUG:
            wait(k, "gpsimd", "ld")
            for rr in range(4):
                nrow = k.d[wnm + "T"].shape[0] // 4
                sdma(k, d["dbg_" + wnm][nrow * rr:nrow * rr + nrow, :],
                     d[wnm + "T"][nrow * rr:nrow * rr + nrow, :], "gpsimd")
    if DEBUG:
        wait(k, "sync", "gd")
    wait(k, "sync", "st")


STAGE = "full"
ATTCUT = None
ATTMI = 1
DBG = False


def emit_dbg(k):
    if not DBG:
        return
    if STAGE == "P2":
        for dc in range(4):
            sdma(k, k.out[128 * dc:128 * dc + 128, 0:1024],
                 qfm(k, dc, 0, 1024), "gpsimd")
            sdma(k, k.out[512 + 128 * dc:512 + 128 * dc + 128, 0:1024],
                 kfm(k, dc, 0, 1024), "gpsimd")
        for t in range(8):
            sdma(k, k.out[1024 + 128 * t:1152 + 128 * t, 0:512],
                 vtm(k, t), "gpsimd")
        for fc in range(4):
            sdma(k, k.out[1024 + 128 * fc:1152 + 128 * fc, 512:1024],
                 xqfm(k, fc, 0, 512), "gpsimd")
    elif STAGE == "ATT1":
        for t in range(16):
            sdma(k, k.out[128 * t:128 * t + 128, 0:512], x1t(k, t), "gpsimd")
    elif STAGE == "ATT2":
        for t in range(16):
            sdma(k, k.out[128 * t:128 * t + 128, 0:512], x1t(k, t), "gpsimd")
            sdma(k, k.out[128 * t:128 * t + 128, 512:1024], x2t(k, t), "gpsimd")
    elif STAGE == "FFN1":
        for t in range(16):
            sdma(k, k.out[128 * t:128 * t + 128, 0:1024], z3t(k, t), "gpsimd")
    wait(k, "sync", "gd")


def emit_program(k):
    phase_consts(k)
    phase_holt_tables(k)
    build_bvb(k)
    phase_attn_weights(k, 1)
    barrier(k)
    if STAGE == "TABLES":
        return
    phase_xq(k)
    barrier(k)
    phase_stream_x(k)
    barrier(k)
    if STAGE == "P2":
        emit_dbg(k)
        return
    phase_attention(k, 1)
    barrier(k)
    if STAGE == "ATT1":
        emit_dbg(k)
        return
    phase_attn_weights(k, 2)
    phase_merge_hso(k)
    barrier(k)
    if STAGE == "MERGE":
        return
    phase_proj2(k)
    barrier(k)
    if STAGE == "PROJ2":
        return
    phase_attention(k, 2)
    barrier(k)
    if STAGE == "ATT2":
        emit_dbg(k)
        return
    phase_ffw_weights(k)
    barrier(k)
    phase_attn_ffn(k)
    barrier(k)
    if STAGE == "FFN1":
        emit_dbg(k)
        return
    phase_ffn_weights_quant(k)
    barrier(k)
    phase_ffn_block(k)


def build():
    k = K()
    declare(k)
    with k.stack:
        alloc(k)
        for s in ("ld", "gd", "gp", "te", "ve", "se", "st",
                  "bsy", "bte", "bve", "bse", "bgp"):
            k.p.sem(s)
        emit_program(k)
        blk = k.stack.enter_context(k.nc.Block())
        for eng_name, meth in (("sync", blk.sync), ("tensor", blk.tensor),
                               ("vector", blk.vector), ("scalar", blk.scalar),
                               ("gpsimd", blk.gpsimd)):
            def body(e, _ops=k.p.ops[eng_name]):
                for fn in _ops:
                    fn(e)
            meth(body)
    return k


_cache = {}


def kernel(**inputs):
    if "k" not in _cache:
        _cache["k"] = build()
    k = _cache["k"]
    f32 = np.float32
    j_np = np.eye(128, dtype=f32)[::-1].copy()
    sel1a = np.zeros((1, 128), f32); sel1a[0, 0:64] = 1
    sel1b = np.zeros((1, 128), f32); sel1b[0, 64:128] = 1
    def selmat(nblk, ncol):
        s = np.zeros((nblk, ncol), f32)
        for c in range(ncol // 128):
            for pp in range(128):
                g = 8 * c + pp // 16
                if g < nblk:
                    s[g, 128 * c + pp] = 1
        return s
    consts = {
        "idn": np.eye(128, dtype=f32),
        "idnb": np.eye(128, dtype=f32).astype(BF),
        "jmatb": j_np.astype(BF),
        "sel1a": sel1a.astype(BF), "sel1b": sel1b.astype(BF),
        "sel88": selmat(88, 1408).astype(BF),
        "onesf": np.ones((1, 128), f32),
    }
    x_full = np.ascontiguousarray(inputs["L0_attn_x"], dtype=f32)
    wmap = {
        "wq1": "L0_attn_wq1", "bq1": "L0_attn_bq1", "wk1": "L0_attn_wk1",
        "bk1": "L0_attn_bk1", "wv1": "L0_attn_wv1", "bv1": "L0_attn_bv1",
        "wo1": "L0_attn_wo1", "bo1": "L0_attn_bo1",
        "wq2": "L0_attn_wq2", "bq2": "L0_attn_bq2", "wk2": "L0_attn_wk2",
        "bk2": "L0_attn_bk2", "wv2": "L0_attn_wv2", "bv2": "L0_attn_bv2",
        "wo2": "L0_attn_wo2", "bo2": "L0_attn_bo2",
        "g1": "L0_attn_g1", "be1": "L0_attn_be1",
        "g2": "L0_attn_g2", "be2": "L0_attn_be2",
        "wff1": "L0_attn_w_ff1", "bff1": "L0_attn_b_ff1",
        "wff2": "L0_attn_w_ff2", "bff2": "L0_attn_b_ff2",
        "gff": "L0_attn_g_ff", "beff": "L0_attn_be_ff",
        "wgate": "L0_ffn_w_gate", "wup": "L0_ffn_w_up", "wdown": "L0_ffn_w_down",
    }
    shared = {kk: np.ascontiguousarray(inputs[v], dtype=f32) for kk, v in wmap.items()}
    in_maps = []
    for c in range(8):
        bb, half = c // 2, c % 2
        m = dict(consts)
        m.update(shared)
        m["x"] = x_full[bb]
        m["xq"] = np.ascontiguousarray(x_full[bb, half * NR:(half + 1) * NR, :])
        m["mask0"] = np.asarray([1.0 - half], f32)
        in_maps.append(m)
    res = run_bass_kernel_spmd(k.nc, in_maps, core_ids=list(range(8)))
    out = np.empty((B, S, D), f32)
    for c in range(8):
        bb, half = c // 2, c % 2
        out[bb, half * NR:(half + 1) * NR, :] = np.asarray(res.results[c]["out"])
    return out


# version 09
# speedup vs baseline: 1.4858x; 1.0276x over previous
import numpy as np
import ml_dtypes
import concourse.bass as bass
import concourse.mybir as mybir
from concourse.bass_utils import run_bass_kernel_spmd
from contextlib import ExitStack

F32 = mybir.dt.float32
BF16 = mybir.dt.bfloat16
AF = mybir.ActivationFunctionType
ALU = mybir.AluOpType
AX = mybir.AxisListType
AP = bass.AP
BF = ml_dtypes.bfloat16

B, S, D = 4, 4096, 1024
DH, H, DK = 512, 8, 64
DFF, FFI = 4096, 2816
NR = 2048
TCH, NCH = 512, 8
EPS = 1e-5
ENGS = ("sync", "tensor", "vector", "scalar", "gpsimd")
QMIDS = (0.25, 0.75, 1.25, 1.75, 2.5, 3.5, 5.0)

DEBUG = ()


class Prog:
    def __init__(self, nc, stack):
        self.nc, self.stack = nc, stack
        self.ops = {e: [] for e in ENGS}
        self.cnt, self.sems = {}, {}

    def sem(self, name):
        s = self.stack.enter_context(self.nc.semaphore(name))
        self.cnt[s] = 0
        self.sems[name] = s
        return s

    def on(self, eng, fn):
        self.ops[eng].append(fn)

    def inc(self, eng, fn, sem, n):
        self.cnt[sem] += n
        self.ops[eng].append(lambda e, fn=fn, sem=sem, n=n: fn(e).then_inc(sem, n))
        return self.cnt[sem]

    def wait(self, eng, sem, v):
        self.ops[eng].append(lambda e, sem=sem, v=v: e.wait_ge(sem, v))


class K:
    def __init__(self):
        self.nc = bass.Bass("TRN2", target_bir_lowering=False,
                            detect_race_conditions=False)
        self.stack = ExitStack()
        self.p = Prog(self.nc, self.stack)
        self.d = {}
        self.sb = {}
        self.ps = {}

    def S(self, name):
        return self.p.sems[name]


def din(k, name, shape, dt=F32, kind="ExternalInput"):
    k.d[name] = k.nc.dram_tensor(name, shape, dt, kind=kind).ap()


def declare(k):
    din(k, "x", [S, D])
    din(k, "xq", [NR, D])
    din(k, "mask0", [1])
    for i in (1, 2):
        for w in ("wq", "wk", "wv", "wo"):
            din(k, f"{w}{i}", [DH, DH])
        for bn in ("bq", "bk", "bv", "bo"):
            din(k, f"{bn}{i}", [DH])
        din(k, f"g{i}", [DH]); din(k, f"be{i}", [DH])
    din(k, "wff1", [DFF, D]); din(k, "bff1", [DFF])
    din(k, "wff2", [D, DFF]); din(k, "bff2", [D])
    din(k, "gff", [D]); din(k, "beff", [D])
    din(k, "wgate", [FFI, D]); din(k, "wup", [FFI, D]); din(k, "wdown", [D, FFI])
    din(k, "idn", [128, 128])
    din(k, "idnb", [128, 128], BF16)
    din(k, "jmatb", [128, 128], BF16)
    din(k, "sel1a", [1, 128], BF16)
    din(k, "sel1b", [1, 128], BF16)
    din(k, "sel88", [88, 1408], BF16)
    din(k, "onesf", [1, 128])
    din(k, "hpad", [5120], BF16, "Internal")
    din(k, "hsD", [DH, S], BF16, "Internal")
    din(k, "wff1T", [D, DFF], BF16, "Internal")
    din(k, "wff2T", [DFF, D], BF16, "Internal")
    din(k, "wgqT", [D, FFI], BF16, "Internal")
    din(k, "wuqT", [D, FFI], BF16, "Internal")
    din(k, "wdqT", [FFI, D], BF16, "Internal")
    din(k, "hqTD", [NR, FFI], BF16, "Internal")
    k.out = k.nc.dram_tensor("out", [NR, D], F32, kind="ExternalOutput").ap()
    for nm in DEBUG:
        shp = {"xhq": [DH, NR], "hs": [DH, S], "x1": [NR, DH], "x2": [NR, DH],
               "kfm": [DH, S], "vtm": [S, DH], "qfm": [DH, NR], "o1": [DH, NR],
               "z3": [NR, D], "xres": [NR, D], "qx": [D, NR], "hq": [NR, FFI],
               "wdq": [FFI, D], "wgq": [D, FFI], "wuq": [D, FFI],
               }[nm]
        din(k, "dbg_" + nm, shp, F32, "ExternalOutput")


def alloc(k):
    E = k.stack.enter_context
    def sbt(name, cols, dt=BF16, parts=128):
        k.sb[name] = E(k.nc.sbuf_tensor("sb_" + name, [parts, cols], dt))
    sbt("idn", 128, F32); sbt("idnb", 128); sbt("jmatb", 128)
    sbt("sel1a", 128, BF16, 1); sbt("sel1b", 128, BF16, 33)
    sbt("sel88", 1408, BF16, 88)
    sbt("onesf", 128, F32, 1)
    sbt("onescol", 1, BF16)
    sbt("bias4", 32, F32)
    sbt("bff1c", 32, F32); sbt("bff2c", 8, F32)
    sbt("rowv", 512, F32, 1)
    sbt("maskc", 2, F32)
    sbt("s1", 16384)
    sbt("s2", 16384)
    sbt("s3", 14080)
    sbt("s4", 16384)
    sbt("s5", 8192)
    sbt("s6", 8192)
    sbt("s7", 8192)
    sbt("ptr", 2048)
    sbt("f32a", 2048, F32)
    sbt("f32b", 4160, F32)
    sbt("stats", 16, F32)
    k.ps["A"] = E(k.nc.psum_tensor("psA", [128, 1024], F32))
    for nm in ("C", "D", "Z0", "Z1", "Z2", "Z3"):
        k.ps[nm] = E(k.nc.psum_tensor("ps" + nm, [128, 512], F32))


def a(k, name, p0=0, p1=128, c0=0, c1=None):
    t = k.sb[name].ap()
    if c1 is None:
        c1 = t.shape[-1]
    return t[p0:p1, c0:c1]


def xqfm(k, fc, c0=0, c1=NR):
    return a(k, "s4", 0, 128, 2048 * fc + c0, 2048 * fc + c1)
def qfm(k, dc, c0=0, c1=NR):
    return a(k, "s4", 0, 128, 8192 + 2048 * dc + c0, 8192 + 2048 * dc + c1)
def kfm(k, dc, c0=0, c1=S):
    return a(k, "s1", 0, 128, 4096 * dc + c0, 4096 * dc + c1)
def vtm(k, kt, c0=0, c1=DH):
    return a(k, "s2", 0, 128, 512 * kt + c0, 512 * kt + c1)
def hso(k, fc, c0=0, c1=NR):
    return a(k, "s3", 0, 128, 2048 * fc + c0, 2048 * fc + c1)
def hsring(k, buf, fc, c0=0, c1=512):
    base = 8192 + 2048 * buf + 512 * fc
    return a(k, "s3", 0, 128, base + c0, base + c1)
def xtm_ring(k, buf, tt, c0=0, c1=512):
    base = 2048 * buf + 512 * tt
    return a(k, "s7", 0, 128, base + c0, base + c1)
def xfm_ring(k, buf, fc, c0=0, c1=512):
    base = 8192 + 2048 * buf + 512 * fc
    return a(k, "s3", 0, 128, base + c0, base + c1)
def wT(k, mi, m, ic, o0, o1):
    base = 2048 * m + 512 * ic
    return a(k, "s5", 0, 128, base + o0, base + o1)
def holtT(k, p0, p1, c0, c1):
    return a(k, "s6", p0, p1, c0, c1)
def x1t(k, tt, c0=0, c1=512):
    return a(k, "s6", 0, 128, 512 * tt + c0, 512 * tt + c1)
def x2t(k, tt, c0=0, c1=512):
    return a(k, "s7", 0, 128, 512 * tt + c0, 512 * tt + c1)
QXT = [("s6", 0), ("s6", 2048), ("s6", 4096), ("s6", 6144),
       ("s7", 0), ("s7", 2048), ("s7", 4096), ("s7", 6144)]
def yfm(k, fc, c0=0, c1=NR):
    return a(k, "s1", 0, 128, 2048 * fc + c0, 2048 * fc + c1)
def h1sb(k, fb, c0=0, c1=512):
    return a(k, "s2", 0, 128, 512 * fb + c0, 512 * fb + c1)
def z3t(k, tt, c0=0, c1=D):
    return a(k, "s4", 0, 128, 1024 * tt + c0, 1024 * tt + c1)
def qxfm(k, kc, c0=0, c1=NR):
    nm, base = QXT[kc]
    return a(k, nm, 0, 128, base + c0, base + c1)
def wgq_tile(k, kc, c0=0, c1=FFI):
    if kc < 5:
        return a(k, "s1", 0, 128, FFI * kc + c0, FFI * kc + c1)
    return a(k, "s3", 0, 128, FFI * (kc - 5) + c0, FFI * (kc - 5) + c1)
def wuq_tile(k, kc, c0=0, c1=FFI):
    if kc < 5:
        return a(k, "s2", 0, 128, FFI * kc + c0, FFI * kc + c1)
    if kc < 7:
        return a(k, "s3", 0, 128, 8448 + FFI * (kc - 5) + c0, 8448 + FFI * (kc - 5) + c1)
    return a(k, "s5", 0, 128, c0, c1)
def wdq_tile(k, kc, c0=0, c1=1024):
    if kc < 16:
        return a(k, "s1", 0, 128, 1024 * kc + c0, 1024 * kc + c1)
    return a(k, "s2", 0, 128, 1024 * (kc - 16) + c0, 1024 * (kc - 16) + c1)
def hqring(k, buf, kc, c0=0, c1=128):
    base = 2816 * buf + 128 * kc
    return a(k, "s3", 0, 128, base + c0, base + c1)
def hqstage(k, kc, c0=0, c1=128):
    base = 2816 + 128 * kc
    return a(k, "s5", 0, 128, base + c0, base + c1)


def gbe_ap(k, c0, c1):
    return k.sb["f32b"].ap().bitcast(BF16)[0:128, 5120 + c0:5120 + c1]
def bvb_ap(k, c0, c1):
    return k.sb["f32b"].ap().bitcast(BF16)[0:128, 7168 + c0:7168 + c1]


def barrier(k):
    p = k.p
    names = {"sync": "bsy", "tensor": "bte", "vector": "bve",
             "scalar": "bse", "gpsimd": "bgp"}
    vals = {}
    for eng in ENGS:
        s = p.sems[names[eng]]
        p.cnt[s] += 1
        vals[eng] = p.cnt[s]
        p.ops[eng].append(lambda e, s=s: e.nop().then_inc(s, 1))
    for eng in ENGS:
        for o in ENGS:
            if o != eng:
                p.wait(eng, p.sems[names[o]], vals[o])


def mm(k, out_ap, lhsT, rhs, start=True, stop=True, tp=None):
    return k.p.inc("tensor", lambda e, o=out_ap, l=lhsT, r=rhs, s1=start,
                   s2=stop, tp=tp: e.matmul(o, l, r, start=s1, stop=s2,
                                            tile_position=tp), k.S("te"), 1)


def trans(k, out_ap, in_ap, fp32=False):
    ident = a(k, "idn") if fp32 else a(k, "idnb")
    return k.p.inc("tensor", lambda e, o=out_ap, i=in_ap, d=ident:
                   e.transpose(o, i, d), k.S("te"), 1)


def psb(k, nm, c0, c1):
    return k.ps[nm].bitcast(BF16)[:, c0:c1]


def sdma(k, dst, src, eng="sync", slow=False):
    sem = k.S("ld") if eng == "sync" else k.S("gd")
    if slow:
        return k.p.inc(eng, lambda e, d=dst, s=src: e.dma_start(
            out=d, in_=s, allow_slow_non_contiguous=True), sem, 16)
    return k.p.inc(eng, lambda e, d=dst, s=src: e.dma_start(out=d, in_=s), sem, 16)


def stdma(k, dst, src):
    return k.p.inc("sync", lambda e, d=dst, s=src: e.dma_start(out=d, in_=s),
                   k.S("st"), 16)


def vop(k, fn):
    k.p.on("vector", fn)


def vinc(k, fn):
    return k.p.inc("vector", fn, k.S("ve"), 1)


def sinc(k, fn):
    return k.p.inc("scalar", fn, k.S("se"), 1)


def wait(k, eng, sem, v=None):
    if v is None:
        v = k.p.cnt[k.S(sem)]
    k.p.wait(eng, k.S(sem), v)


def phase_consts(k):
    d = k.d
    sdma(k, a(k, "idn"), d["idn"])
    sdma(k, a(k, "idnb"), d["idnb"])
    sdma(k, a(k, "jmatb"), d["jmatb"])
    sdma(k, a(k, "sel1a", 0, 1), d["sel1a"])
    sdma(k, a(k, "sel1b", 32, 33), d["sel1b"])
    sdma(k, a(k, "sel88", 0, 88), d["sel88"])
    sdma(k, a(k, "onesf", 0, 1), d["onesf"])
    names = [f"{w}{i}" for i in (1, 2) for w in ("bq", "bk", "bv", "bo")]
    for j, nm in enumerate(names):
        sdma(k, a(k, "bias4", 0, 128, 4 * j, 4 * j + 4),
             AP(tensor=d[nm].tensor, offset=0, ap=[[1, 128], [128, 4]]), slow=True)
    sdma(k, a(k, "bff1c"), AP(tensor=d["bff1"].tensor, offset=0, ap=[[1, 128], [128, 32]]), slow=True)
    sdma(k, a(k, "bff2c"), AP(tensor=d["bff2"].tensor, offset=0, ap=[[1, 128], [128, 8]]), slow=True)
    sdma(k, a(k, "rowv", 0, 1, 0, 1), d["mask0"].rearrange("(a b) -> a b", a=1))
    wait(k, "vector", "ld")
    vinc(k, lambda e: e.memset(a(k, "onescol"), 1.0))
    wait(k, "tensor", "ld")
    wait(k, "tensor", "ve")
    mm(k, k.ps["Z1"][:, 0:1], a(k, "onesf", 0, 1), a(k, "rowv", 0, 1, 0, 1))
    wait(k, "vector", "te")
    vmc = vinc(k, lambda e: e.tensor_copy(out=a(k, "maskc", 0, 128, 0, 1),
                                          in_=k.ps["Z1"][:, 0:1]))
    wait(k, "vector", "ve", vmc)
    vop(k, lambda e: e.tensor_scalar(out=a(k, "maskc", 0, 128, 1, 2),
                                     in0=a(k, "maskc", 0, 128, 0, 1),
                                     scalar1=-1.0, scalar2=1.0,
                                     op0=ALU.mult, op1=ALU.add))
    v = vinc(k, lambda e: e.engine_nop())
    wait(k, "tensor", "ve", v)
    wait(k, "sync", "ve", v)


def build_bvb(k):
    d = k.d
    for i in (1, 2):
        v = sdma(k, a(k, "rowv", 0, 1, 0, 512), d[f"bv{i}"].rearrange("(a b) -> a b", a=1))
        wait(k, "tensor", "ld", v)
        mm(k, k.ps["Z1"][:, 0:512], a(k, "onesf", 0, 1), a(k, "rowv", 0, 1, 0, 512))
        wait(k, "vector", "te")
        col = 512 * (i - 1)
        ve = vinc(k, lambda e, col=col: e.tensor_copy(
            out=bvb_ap(k, col, col + 512), in_=k.ps["Z1"][:, 0:512]))
        wait(k, "sync", "ve", ve)
        wait(k, "tensor", "ve", ve)


def build_gbe(k, gname, bename, width):
    d = k.d
    for idx, nm in ((0, gname), (1, bename)):
        for off in range(0, width, 512):
            w = min(512, width - off)
            v = sdma(k, a(k, "rowv", 0, 1, 0, w),
                     d[nm].rearrange("(a b) -> a b", a=1)[:, off:off + w])
            wait(k, "tensor", "ld", v)
            wait(k, "tensor", "ve")
            mm(k, k.ps["Z1"][:, 0:w], a(k, "onesf", 0, 1), a(k, "rowv", 0, 1, 0, w))
            wait(k, "vector", "te")
            col = idx * 1024 + off
            ve = vinc(k, lambda e, col=col, w=w: e.tensor_copy(
                out=gbe_ap(k, col, col + w), in_=k.ps["Z1"][:, 0:w]))
            wait(k, "sync", "ve", ve)
    wait(k, "tensor", "ve")


def phase_holt_tables(k):
    p, d = k.p, k.d
    hf = k.sb["f32b"].ap()
    hb = k.sb["s6"].ap()
    V2C, G2C, W2C = 80, 592, 1104

    def V(fn):
        p.on("vector", fn)

    def VE(fn):
        v = vinc(k, fn)
        wait(k, "vector", "ve", v)
        return v

    VE(lambda e: e.memset(hf[0:1, 6:7], 10.0))
    VE(lambda e: e.reciprocal(out=hf[0:1, 4:5], in_=hf[0:1, 6:7]))
    for _ in range(2):
        VE(lambda e: e.tensor_scalar(out=hf[0:1, 7:8], in0=hf[0:1, 4:5],
                                     scalar1=-10.0, scalar2=2.0,
                                     op0=ALU.mult, op1=ALU.add))
        VE(lambda e: e.tensor_tensor(out=hf[0:1, 4:5], in0=hf[0:1, 4:5],
                                     in1=hf[0:1, 7:8], op=ALU.mult))
    VE(lambda e: e.tensor_tensor(out=hf[0:1, 5:6], in0=hf[0:1, 4:5],
                                 in1=hf[0:1, 4:5], op=ALU.mult))
    VE(lambda e: e.tensor_scalar(out=hf[0:1, 0:1], in0=hf[0:1, 4:5],
                                 scalar1=-1.0, scalar2=1.0, op0=ALU.mult, op1=ALU.add))
    VE(lambda e: e.tensor_copy(out=hf[0:1, 1:2], in_=hf[0:1, 0:1]))
    VE(lambda e: e.tensor_scalar(out=hf[0:1, 2:3], in0=hf[0:1, 5:6],
                                 scalar1=-1.0, scalar2=None, op0=ALU.mult))
    VE(lambda e: e.tensor_scalar(out=hf[0:1, 3:4], in0=hf[0:1, 5:6],
                                 scalar1=-1.0, scalar2=1.0, op0=ALU.mult, op1=ALU.add))
    VE(lambda e: e.tensor_copy(out=hf[0:1, 8:9], in_=hf[0:1, 0:1]))
    VE(lambda e: e.tensor_copy(out=hf[0:1, 9:10], in_=hf[0:1, 2:3]))
    VE(lambda e: e.tensor_copy(out=hf[0:1, 10:11], in_=hf[0:1, 1:2]))
    VE(lambda e: e.tensor_copy(out=hf[0:1, 11:12], in_=hf[0:1, 3:4]))
    vrow = vinc(k, lambda e: e.engine_nop())

    wait(k, "sync", "ve", vrow)
    sdma(k, hf[0:2, 16:18], hf[0:1, 0:4], slow=True)
    sdma(k, hf[0:2, 48:50], hf[0:1, 8:12], slow=True)
    sdma(k, hf[0:2, W2C:W2C + 1], hf[0:1, 4:6], slow=True)
    sdma(k, hf[0:2, V2C:V2C + 1], hf[0:1, 0:2], slow=True)
    vst = p.cnt[k.S("ld")]
    wait(k, "vector", "ld", vst)
    VE(lambda e: e.memset(hf[0:2, G2C:G2C + 1], 0.0))
    VE(lambda e: e.memset(hf[0:1, G2C:G2C + 1], 1.0))
    v0 = vinc(k, lambda e: e.engine_nop())
    wait(k, "tensor", "ve", v0)
    wait(k, "tensor", "ld", vst)

    ps = k.ps["C"]
    smb = k.sb["f32a"].ap().bitcast(BF16)
    def ev(dst, src):
        wait(k, "vector", "te")
        ve = vinc(k, lambda e, dst=dst, src=src: e.tensor_copy(out=dst, in_=src))
        wait(k, "tensor", "ve", ve)

    m, stage = 1, 0
    m512_stage = None
    while m < 4096:
        if m == 512:
            m512_stage = stage
        Pm = hf[0:2, 16 + 2 * stage:18 + 2 * stage]
        Mm = hf[0:2, 48 + 2 * stage:50 + 2 * stage]
        step = min(m, 4096 - m)
        for off in range(0, step, 512):
            w = min(512, step - off)
            mm(k, ps[0:2, 0:w], Mm, hf[0:2, W2C + off:W2C + off + w])
            if m + off < 2048:
                ev(hf[0:2, W2C + m + off:W2C + m + off + w], ps[0:2, 0:w])
            else:
                wait(k, "vector", "te")
                ve = vinc(k, lambda e, w=w: e.tensor_copy(
                    out=smb[0:2, 3520:3520 + w][0:2, 0:w], in_=ps[0:2, 0:w]))
                wait(k, "tensor", "ve", ve)
                wait(k, "gpsimd", "ve", ve)
                sdma(k, d["hpad"][512 + m + off:512 + m + off + w],
                     smb[0:1, 3520:3520 + w], "gpsimd")
                wait(k, "vector", "gd")
        if m < 512:
            stepv = min(m, 512 - m)
            for cc, lhs in ((G2C, Mm), (V2C, Pm)):
                mm(k, ps[0:2, 0:stepv], lhs, hf[0:2, cc:cc + stepv])
                ev(hf[0:2, cc + m:cc + m + stepv], ps[0:2, 0:stepv])
        mm(k, ps[0:2, 0:2], Mm, Pm)
        ev(hf[0:2, 18 + 2 * stage:20 + 2 * stage], ps[0:2, 0:2])
        mm(k, ps[0:2, 0:2], Pm, Mm)
        ev(hf[0:2, 50 + 2 * stage:52 + 2 * stage], ps[0:2, 0:2])
        m *= 2
        stage += 1
    c512 = 48 + 2 * (m512_stage + 1)
    V(lambda e: e.tensor_copy(out=hb[0:2, 4624:4626], in_=hf[0:2, c512:c512 + 2]))
    V(lambda e: e.tensor_copy(out=hb[0:2, 4112:4624], in_=hf[0:2, V2C:V2C + 512]))
    zr = k.sb["f32a"].ap()
    V(lambda e: e.memset(zr[0:1, 0:512], 0.0))
    vz = vinc(k, lambda e: e.engine_nop())
    wait(k, "gpsimd", "ve", vz)
    sdma(k, d["hpad"][0:512], zr[0:1, 0:512], "gpsimd")
    sdma(k, d["hpad"][4608:5120], zr[0:1, 0:512], "gpsimd")
    sdma(k, d["hpad"][512:2560], hf[0:1, W2C:W2C + 2048], "gpsimd")
    wait(k, "sync", "gd")

    hkb = k.sb["ptr"].ap()
    for dst_base in (0, 2048):
        for jsub in range(4):
            off = 512 - 127 - 128 * jsub
            v = sdma(k, hkb[0:128, 0:512],
                     AP(tensor=d["hpad"].tensor, offset=off, ap=[[1, 128], [1, 512]]))
            wait(k, "tensor", "ld", v)
            for tsub in range(4):
                mm(k, k.ps["C"][:, 128 * tsub:128 * tsub + 128], a(k, "jmatb"),
                   hkb[0:128, 128 * tsub:128 * tsub + 128])
            wait(k, "vector", "te")
            col = dst_base + 512 * jsub
            ve = vinc(k, lambda e, col=col: e.tensor_copy(
                out=hb[0:128, col:col + 512], in_=k.ps["C"][:, 0:512]))
            wait(k, "tensor", "ve", ve)
            wait(k, "sync", "ve", ve)
    V(lambda e: e.tensor_copy(out=hb[0:1, 2048:2560], in_=hf[0:1, G2C:G2C + 512]))
    for t in range(4):
        srcc = 128 * (3 - t)
        k.p.inc("tensor", lambda e, srcc=srcc: e.transpose(
            k.ps["C"][:, 0:2], hf[0:2, W2C + srcc:W2C + srcc + 128],
            a(k, "idn", 0, 2, 0, 2)), k.S("te"), 1)
        wait(k, "vector", "te")
        ve = vinc(k, lambda e: e.tensor_copy(out=hkb[0:128, 1024:1026],
                                             in_=k.ps["C"][:, 0:2]))
        wait(k, "tensor", "ve", ve)
        mm(k, k.ps["C"][:, 0:2], a(k, "jmatb"), hkb[0:128, 1024:1026])
        wait(k, "vector", "te")
        for dst0 in (4096, 4104):
            vinc(k, lambda e, dst0=dst0, t=t: e.tensor_copy(
                out=hb[0:128, dst0 + 2 * t:dst0 + 2 * t + 2], in_=k.ps["C"][:, 0:2]))
        wait(k, "tensor", "ve")
    wait(k, "gpsimd", "ve")
    sdma(k, hb[0:1, 4104:4105], hf[0:1, G2C + 511:G2C + 512], "gpsimd", slow=True)
    sdma(k, hb[0:1, 4105:4106], hf[1:2, G2C + 511:G2C + 512], "gpsimd", slow=True)
    wait(k, "vector", "gd")
    wait(k, "tensor", "gd")
    V(lambda e: e.memset(hb[0:2, 4704:5728], 0.0))
    v = vinc(k, lambda e: e.engine_nop())
    wait(k, "tensor", "ve", v)


def sprev(k, c0=0, c1=512):
    return a(k, "s6", 0, 2, 4704 + c0, 4704 + c1)
def snew(k, c0=0, c1=512):
    return a(k, "s6", 0, 2, 5216 + c0, 5216 + c1)


def phase_attn_weights(k, mi):
    d = k.d
    stage = k.sb["ptr"].ap()
    for m, w in enumerate(("wq", "wk", "wv", "wo")):
        wd = d[f"{w}{mi}"]
        for oc in range(4):
            vg = sdma(k, stage[0:128, 0:512], wd[128 * oc:128 * oc + 128, :], "gpsimd")
            wait(k, "tensor", "gd", vg)
            for ic in range(4):
                trans(k, psb(k, "D", 128 * ic, 128 * ic + 128),
                      stage[0:128, 128 * ic:128 * ic + 128])
            wait(k, "vector", "te")
            for ic in range(4):
                dst = wT(k, mi, m, ic, 128 * oc, 128 * oc + 128)
                vinc(k, lambda e, dst=dst, ic=ic: e.tensor_copy(
                    out=dst, in_=psb(k, "D", 128 * ic, 128 * ic + 128)))
            wait(k, "tensor", "ve")
            wait(k, "gpsimd", "ve")


def ln_tile(k, full_src, apply_src, out_bf, width, sq_ap=None):
    st = k.sb["stats"].ap()
    if sq_ap is None:
        sq_ap = k.sb["f32b"].ap()[0:128, 0:width]
    vsq = sinc(k, lambda e: e.activation(out=sq_ap, in_=full_src,
                                         func=AF.Square, accum_out=st[0:128, 2:3]))
    wait(k, "vector", "se", vsq)
    va = vinc(k, lambda e: e.tensor_reduce(out=st[0:128, 3:4], in_=full_src,
                                           axis=AX.X, op=ALU.add))
    vop(k, lambda e: e.tensor_scalar(out=st[0:128, 2:3], in0=st[0:128, 2:3],
                                     scalar1=1.0 / width, scalar2=None, op0=ALU.mult))
    wait(k, "vector", "ve", va)
    vc = vinc(k, lambda e: e.tensor_scalar(out=st[0:128, 0:1], in0=st[0:128, 3:4],
                                           scalar1=1.0 / width, scalar2=None,
                                           op0=ALU.mult))
    wait(k, "vector", "ve", vc)
    vd = vinc(k, lambda e: e.tensor_tensor(out=st[0:128, 3:4], in0=st[0:128, 0:1],
                                           in1=st[0:128, 0:1], op=ALU.mult))
    wait(k, "vector", "ve", vd)
    vv = vinc(k, lambda e: e.scalar_tensor_tensor(
        out=st[0:128, 2:3], in0=st[0:128, 2:3], scalar=EPS, in1=st[0:128, 3:4],
        op0=ALU.add, op1=ALU.subtract))
    wait(k, "scalar", "ve", vv)
    vsr = sinc(k, lambda e: e.activation(out=st[0:128, 3:4], in_=st[0:128, 2:3],
                                         func=AF.Sqrt))
    wait(k, "vector", "se", vsr)
    vh = vinc(k, lambda e: e.reciprocal(out=st[0:128, 1:2], in_=st[0:128, 3:4]))
    wait(k, "vector", "ve", vh)
    vij = vinc(k, lambda e: e.scalar_tensor_tensor(
        out=st[0:128, 3:4], in0=st[0:128, 1:2], scalar=st[0:128, 1:2],
        in1=st[0:128, 2:3], op0=ALU.mult, op1=ALU.mult))
    wait(k, "vector", "ve", vij)
    vk = vinc(k, lambda e: e.tensor_scalar(out=st[0:128, 3:4], in0=st[0:128, 3:4],
                                           scalar1=-0.5, scalar2=1.5,
                                           op0=ALU.mult, op1=ALU.add))
    wait(k, "vector", "ve", vk)
    vl = vinc(k, lambda e: e.tensor_tensor(out=st[0:128, 1:2], in0=st[0:128, 1:2],
                                           in1=st[0:128, 3:4], op=ALU.mult))
    wait(k, "vector", "ve", vl)
    vm = vinc(k, lambda e: e.scalar_tensor_tensor(
        out=st[0:128, 4:5], in0=st[0:128, 0:1], scalar=-1.0, in1=st[0:128, 1:2],
        op0=ALU.mult, op1=ALU.mult))
    wait(k, "scalar", "ve", vm)
    return sinc(k, lambda e: e.activation(
        out=out_bf, in_=apply_src, func=AF.Identity,
        scale=st[0:128, 1:2], bias=st[0:128, 4:5]))


def phase_xq(k):
    d = k.d
    xf = k.sb["f32a"].ap()
    tmb = k.sb["ptr"].ap()
    done = {}
    for tt in range(16):
        buf = tt % 2
        if tt >= 2:
            wait(k, "sync", "se", done[tt - 2][0])
            wait(k, "sync", "se", done[tt - 2][1])
        v = sdma(k, xf[0:128, 1024 * buf:1024 * buf + 1024],
                 d["xq"][128 * tt:128 * tt + 128, :])
        wait(k, "scalar", "ld", v)
        wait(k, "vector", "ld", v)
        vxh = ln_tile(k, xf[0:128, 1024 * buf:1024 * buf + 1024],
                      xf[0:128, 1024 * buf:1024 * buf + 512],
                      tmb[0:128, 512 * buf:512 * buf + 512], D)
        wait(k, "tensor", "se", vxh)
        for fc in range(4):
            trans(k, psb(k, "D", 128 * fc, 128 * fc + 128),
                  tmb[0:128, 512 * buf + 128 * fc:512 * buf + 128 * fc + 128])
        wait(k, "scalar", "te")
        for fc in range(4):
            dst = xqfm(k, fc, 128 * tt, 128 * tt + 128)
            sinc(k, lambda e, dst=dst, fc=fc: e.copy(
                out=dst, in_=psb(k, "D", 128 * fc, 128 * fc + 128)))
        wait(k, "tensor", "se")
        done[tt] = (vxh, k.p.cnt[k.S("se")])
    wait(k, "tensor", "se")
    proj_fm(k, 1, 0, lambda fc, c0, c1: xqfm(k, fc, c0, c1), NR, lambda dc, c0, c1: qfm(k, dc, c0, c1))


def proj_fm(k, mi, m, src_fn, ntok, dst_fn):
    for dc in range(4):
        for t0 in range(0, ntok, 512):
            ps = k.ps["C"] if (t0 // 512) % 2 == 0 else k.ps["D"]
            for fc in range(4):
                mm(k, ps[:, 0:512], wT(k, mi, m, fc, 128 * dc, 128 * dc + 128),
                   src_fn(fc, t0, t0 + 512), start=(fc == 0), stop=(fc == 3))
            wait(k, "scalar", "te")
            bj = 4 * (4 * (mi - 1) + m) + dc
            dst = dst_fn(dc, t0, t0 + 512)
            vse = sinc(k, lambda e, dst=dst, ps=ps, bj=bj: e.activation(
                out=dst, in_=ps[:, 0:512], func=AF.Identity,
                bias=a(k, "bias4", 0, 128, bj, bj + 1)))
            wait(k, "tensor", "se", vse)


def phase_stream_x(k):
    p, d = k.p, k.d
    xf = k.sb["f32a"].ap()
    hb = k.sb["s6"].ap()
    hst = k.sb["f32b"].ap().bitcast(BF16)

    for c in range(NCH):
        blk = c % 2
        for tt in range(4):
            ti = 4 * c + tt
            buf = ti % 2
            v = sdma(k, xf[0:128, 1024 * buf:1024 * buf + 1024],
                     d["x"][128 * ti:128 * ti + 128, :])
            wait(k, "scalar", "ld", v)
            wait(k, "vector", "ld", v)
            vxh = ln_tile(k, xf[0:128, 1024 * buf:1024 * buf + 1024],
                          xf[0:128, 1024 * buf:1024 * buf + 512],
                          xtm_ring(k, blk, tt), D)
            wait(k, "tensor", "se", vxh)
            for fc in range(4):
                trans(k, psb(k, "Z2", 128 * fc, 128 * fc + 128),
                      xtm_ring(k, blk, tt, 128 * fc, 128 * fc + 128))
            wait(k, "scalar", "te")
            for fc in range(4):
                dst = xfm_ring(k, blk, fc, 128 * tt, 128 * tt + 128)
                sinc(k, lambda e, dst=dst, fc=fc: e.copy(
                    out=dst, in_=psb(k, "Z2", 128 * fc, 128 * fc + 128)))
            wait(k, "tensor", "se")
            wait(k, "sync", "se")
        for dc in range(4):
            ps = k.ps["C"] if dc % 2 == 0 else k.ps["D"]
            for fc in range(4):
                mm(k, ps[:, 0:512], wT(k, 1, 1, fc, 128 * dc, 128 * dc + 128),
                   xfm_ring(k, blk, fc, 0, 512), start=(fc == 0), stop=(fc == 3))
            wait(k, "scalar", "te")
            bj = 4 * 1 + dc
            dst = kfm(k, dc, 512 * c, 512 * c + 512)
            vse = sinc(k, lambda e, dst=dst, ps=ps, bj=bj: e.activation(
                out=dst, in_=ps[:, 0:512], func=AF.Identity,
                bias=a(k, "bias4", 0, 128, bj, bj + 1)))
            wait(k, "tensor", "se", vse)
        for tt in range(4):
            ps = k.ps["C"] if tt % 2 == 0 else k.ps["D"]
            for fc in range(4):
                mm(k, ps[:, 0:512], xfm_ring(k, blk, fc, 128 * tt, 128 * tt + 128),
                   wT(k, 1, 2, fc, 0, 512), start=(fc == 0), stop=(fc == 3))
            wait(k, "vector", "te")
            dst = vtm(k, 4 * c + tt)
            vve = vinc(k, lambda e, dst=dst, ps=ps: e.scalar_tensor_tensor(
                out=dst, in0=ps[:, 0:512], scalar=1.0,
                in1=bvb_ap(k, 0, 512), op0=ALU.mult, op1=ALU.add))
            wait(k, "tensor", "ve", vve)
        ut0 = 4104 if c == 0 else 4096
        for js in range(4):
            mm(k, k.ps["Z2"][0:2, 0:512], hb[0:128, ut0 + 2 * js:ut0 + 2 * js + 2],
               xtm_ring(k, blk, js, 0, 512), start=(js == 0), stop=False)
        mm(k, k.ps["Z2"][0:2, 0:512], hb[0:2, 4624:4626], sprev(k),
           start=False, stop=True)
        wait(k, "vector", "te")
        vs = vinc(k, lambda e: e.tensor_copy(out=snew(k), in_=k.ps["Z2"][0:2, 0:512]))
        wait(k, "tensor", "ve", vs)
        mt0 = 2048 if c == 0 else 0
        for fcq in range(4):
            ps = k.ps["Z0"] if fcq % 2 == 0 else k.ps["Z1"]
            for js in range(4):
                mm(k, ps[:, 0:512],
                   xtm_ring(k, blk, js, 128 * fcq, 128 * fcq + 128),
                   hb[0:128, mt0 + 512 * js:mt0 + 512 * js + 512],
                   start=(js == 0), stop=False)
            mm(k, ps[:, 0:512], sprev(k, 128 * fcq, 128 * fcq + 128),
               hb[0:2, 4112:4624], start=False, stop=True)
            wait(k, "scalar", "te")
            vse = sinc(k, lambda e, ps=ps, fcq=fcq: e.copy(
                out=hst[0:128, 4096 + 512 * fcq:4096 + 512 * fcq + 512],
                in_=ps[:, 0:512]))
            wait(k, "sync", "se", vse)
            sdma(k, d["hsD"][128 * fcq:128 * fcq + 128, 512 * c:512 * c + 512],
                 hst[0:128, 4096 + 512 * fcq:4096 + 512 * fcq + 512])
            wait(k, "tensor", "se", vse)
        wait(k, "vector", "te")
        vsp = vinc(k, lambda e: e.tensor_copy(out=sprev(k), in_=snew(k)))
        wait(k, "tensor", "ve", vsp)
        wait(k, "sync", "ve")
        wait(k, "sync", "ld")
        wait(k, "scalar", "ld")
    wait(k, "sync", "ld")


def phase_attention(k, mi):
    p = k.p
    build_gbe(k, f"g{mi}", f"be{mi}", 512)
    pt = k.sb["ptr"].ap()
    smf = k.sb["f32a"].ap()
    smb = k.sb["f32a"].ap().bitcast(BF16)
    bcst = k.sb["f32b"].ap().bitcast(BF16)
    res_fn = (lambda dc, c0, c1: xqfm(k, dc, c0, c1)) if mi == 1 else \
             (lambda dc, c0, c1: hso(k, dc, c0, c1))
    xdst = x1t if mi == 1 else x2t
    zst = k.sb["f32b"].ap().bitcast(BF16)
    ost = lambda hp, c0=0, c1=512: bcst[0:128, 2048 + 512 * hp + c0:2048 + 512 * hp + c1]

    cut = ATTCUT if mi == ATTMI else None
    nqc = 1 if cut else 4
    nhp = 1 if cut in ("S", "AV", "SUMS", "BCAST") else 4
    for qc in range(nqc):
        vo_prev = None
        for hp in range(nhp):
            exp_se = {}
            avsum_te = {}
            S_te = {}

            def psS(kt):
                if kt % 2 == 0:
                    return k.ps["A"][:, 0:512], k.ps["A"][:, 512:1024]
                return k.ps["C"][:, 0:512], k.ps["D"][:, 0:512]

            def emit_S(kt, hp=hp, qc=qc):
                if kt >= 2:
                    wait(k, "tensor", "se", exp_se[kt - 2][1])
                pa, pb_ = psS(kt)
                mm(k, pa, kfm(k, hp, 128 * kt, 128 * kt + 128)[0:64, :],
                   qfm(k, hp, 512 * qc, 512 * qc + 512)[0:64, :],
                   start=True, stop=True, tp=(0, 0))
                mm(k, pb_, kfm(k, hp, 128 * kt, 128 * kt + 128)[64:128, :],
                   qfm(k, hp, 512 * qc, 512 * qc + 512)[64:128, :],
                   start=True, stop=True, tp=(64, 0))
                S_te[kt] = p.cnt[k.S("te")]

            def emit_exp(kt):
                pa, pb_ = psS(kt)
                pb = kt % 2
                wait(k, "scalar", "te", S_te[kt])
                if kt >= 2 and (kt - 2) in avsum_te:
                    wait(k, "scalar", "te", avsum_te[kt - 2])
                v0 = sinc(k, lambda e, pa=pa, pb=pb: e.activation(
                    out=pt[0:128, 1024 * pb:1024 * pb + 512], in_=pa,
                    func=AF.Exp, scale=0.125))
                v1 = sinc(k, lambda e, pb_=pb_, pb=pb: e.activation(
                    out=pt[0:128, 1024 * pb + 512:1024 * pb + 1024], in_=pb_,
                    func=AF.Exp, scale=0.125))
                exp_se[kt] = (v0, v1)

            def emit_AV(kt, hp=hp):
                pb = kt % 2
                if kt == 0 and vo_prev is not None:
                    wait(k, "tensor", "ve", vo_prev)
                wait(k, "tensor", "se", exp_se[kt][0])
                mm(k, k.ps["Z0"][0:64, :], vtm(k, kt, 128 * hp, 128 * hp + 64),
                   pt[0:128, 1024 * pb:1024 * pb + 512],
                   start=(kt == 0), stop=(kt == 31), tp=(0, 0))
                wait(k, "tensor", "se", exp_se[kt][1])
                mm(k, k.ps["Z1"][64:128, :], vtm(k, kt, 128 * hp + 64, 128 * hp + 128),
                   pt[0:128, 1024 * pb + 512:1024 * pb + 1024],
                   start=(kt == 0), stop=(kt == 31), tp=(0, 64))
                mm(k, k.ps["Z2"][0:1, :], a(k, "onescol"),
                   pt[0:128, 1024 * pb:1024 * pb + 512],
                   start=(kt == 0), stop=(kt == 31), tp=(0, 0))
                mm(k, k.ps["Z3"][0:1, :], a(k, "onescol"),
                   pt[0:128, 1024 * pb + 512:1024 * pb + 1024],
                   start=(kt == 0), stop=(kt == 31), tp=(0, 0))
                avsum_te[kt] = p.cnt[k.S("te")]

            emit_S(0)
            emit_exp(0)
            emit_S(1)
            emit_exp(1)
            for kt in range(2, 32):
                emit_S(kt)
                if cut != "S":
                    emit_AV(kt - 2)
                emit_exp(kt)
            if cut != "S":
                emit_AV(30)
                emit_AV(31)
            if cut in ("S", "AV"):
                return
            wait(k, "vector", "te")
            fbf = k.sb["f32b"].ap()
            vinc(k, lambda e: e.tensor_copy(out=fbf[0:1, 0:512],
                                            in_=k.ps["Z2"][0:1, :]))
            vinc(k, lambda e: e.tensor_copy(out=fbf[0:1, 512:1024],
                                            in_=k.ps["Z3"][0:1, :]))
            vop(k, lambda e: e.reciprocal(out=smf[0:1, 0:1024],
                                          in_=fbf[0:1, 0:1024]))
            vop(k, lambda e: e.tensor_tensor(
                out=smf[0:1, 1024:2048], in0=fbf[0:1, 0:1024],
                in1=smf[0:1, 0:1024], op=ALU.mult))
            vop(k, lambda e: e.tensor_scalar(
                out=smf[0:1, 1024:2048], in0=smf[0:1, 1024:2048],
                scalar1=-1.0, scalar2=2.0, op0=ALU.mult, op1=ALU.add))
            vop(k, lambda e: e.tensor_tensor(
                out=bcst[0:1, 0:1024], in0=smf[0:1, 0:1024],
                in1=smf[0:1, 1024:2048], op=ALU.mult))
            vrec = vinc(k, lambda e: e.engine_nop())
            if cut == "SUMS":
                return
            wait(k, "tensor", "ve", vrec)
            mm(k, k.ps["Z2"][0:64, 0:512], a(k, "sel1a", 0, 1, 0, 64),
               bcst[0:1, 0:512], start=True, stop=True, tp=(0, 0))
            mm(k, k.ps["Z2"][64:128, 0:512], a(k, "sel1a", 0, 1, 0, 64),
               bcst[0:1, 512:1024], start=True, stop=True, tp=(0, 64))
            wait(k, "vector", "te")
            vb = vinc(k, lambda e: e.tensor_copy(out=bcst[0:128, 4096:4608],
                                                 in_=k.ps["Z2"][:, 0:512]))
            vinc(k, lambda e, hp=hp: e.tensor_tensor(
                out=ost(hp)[0:64, :], in0=k.ps["Z0"][0:64, :],
                in1=bcst[0:64, 4096:4608], op=ALU.mult))
            vo = vinc(k, lambda e, hp=hp: e.tensor_tensor(
                out=ost(hp)[64:128, :], in0=k.ps["Z1"][64:128, :],
                in1=bcst[64:128, 4096:4608], op=ALU.mult))
            vo_prev = vo
        if cut == "BCAST":
            return
        if cut == "HP":
            continue
        wait(k, "tensor", "ve", vo_prev)
        for dc in range(4):
            ps = k.ps["C"] if dc % 2 == 0 else k.ps["D"]
            for hp in range(4):
                mm(k, ps[:, 0:512], wT(k, mi, 3, hp, 128 * dc, 128 * dc + 128),
                   ost(hp), start=(hp == 0), stop=(hp == 3))
            wait(k, "vector", "te")
            bj = 4 * (4 * (mi - 1) + 3) + dc
            res = res_fn(dc, 512 * qc, 512 * qc + 512)
            vz = vinc(k, lambda e, ps=ps, bj=bj, res=res, dc=dc: e.scalar_tensor_tensor(
                out=zst[0:128, 512 * dc:512 * dc + 512], in0=ps[:, 0:512],
                scalar=a(k, "bias4", 0, 128, bj, bj + 1), in1=res,
                op0=ALU.add, op1=ALU.add))
            wait(k, "tensor", "ve", vz)
        if cut == "OPROJ":
            return
        for sub in range(4):
            tq = 4 * qc + sub
            for dc in range(4):
                trans(k, psb(k, "C", 128 * dc, 128 * dc + 128),
                      zst[0:128, 512 * dc + 128 * sub:512 * dc + 128 * sub + 128])
            wait(k, "vector", "te")
            wait(k, "scalar", "te")
            vap = ln_tile(k, psb(k, "C", 0, 512), psb(k, "C", 0, 512),
                          bcst[0:128, 4608:5120], 512,
                          sq_ap=k.sb["f32a"].ap()[0:128, 0:512])
            wait(k, "vector", "se", vap)
            vop(k, lambda e: e.tensor_tensor(
                out=bcst[0:128, 4608:5120], in0=bcst[0:128, 4608:5120],
                in1=gbe_ap(k, 0, 512), op=ALU.mult))
            dst = xdst(k, tq)
            vx = vinc(k, lambda e, dst=dst: e.tensor_tensor(
                out=dst, in0=bcst[0:128, 4608:5120],
                in1=gbe_ap(k, 1024, 1536), op=ALU.add))
            wait(k, "tensor", "ve", vx)
            wait(k, "tensor", "se", vap)


def phase_merge_hso(k):
    d = k.d
    hm = k.sb["f32b"].ap().bitcast(BF16)
    for fc in range(4):
        v0 = sdma(k, hm[0:128, 0:2048],
                  d["hsD"][128 * fc:128 * fc + 128, 0:2048])
        v1 = sdma(k, hm[0:128, 2048:4096],
                  d["hsD"][128 * fc:128 * fc + 128, 2048:4096])
        wait(k, "vector", "ld", v1)
        vop(k, lambda e, fc=fc: e.tensor_scalar(
            out=hso(k, fc), in0=hm[0:128, 2048:4096],
            scalar1=a(k, "maskc", 0, 128, 1, 2), scalar2=None, op0=ALU.mult))
        vm = vinc(k, lambda e, fc=fc: e.scalar_tensor_tensor(
            out=hso(k, fc), in0=hm[0:128, 0:2048],
            scalar=a(k, "maskc", 0, 128, 0, 1), in1=hso(k, fc),
            op0=ALU.mult, op1=ALU.add))
        wait(k, "sync", "ve", vm)


def phase_proj2(k):
    p, d = k.p, k.d
    for c in range(NCH):
        buf = c % 2
        vs = []
        for fc in range(4):
            vs.append(sdma(k, hsring(k, buf, fc),
                           d["hsD"][128 * fc:128 * fc + 128, 512 * c:512 * c + 512]))
        wait(k, "tensor", "ld", vs[-1])
        for dc in range(4):
            ps = k.ps["C"] if dc % 2 == 0 else k.ps["D"]
            for fc in range(4):
                mm(k, ps[:, 0:512], wT(k, 2, 1, fc, 128 * dc, 128 * dc + 128),
                   hsring(k, buf, fc), start=(fc == 0), stop=(fc == 3))
            wait(k, "scalar", "te")
            bj = 4 * 5 + dc
            dst = kfm(k, dc, 512 * c, 512 * c + 512)
            vse = sinc(k, lambda e, dst=dst, ps=ps, bj=bj: e.activation(
                out=dst, in_=ps[:, 0:512], func=AF.Identity,
                bias=a(k, "bias4", 0, 128, bj, bj + 1)))
            wait(k, "tensor", "se", vse)
        for tt in range(4):
            ps = k.ps["C"] if tt % 2 == 0 else k.ps["D"]
            for fc in range(4):
                mm(k, ps[:, 0:512], hsring(k, buf, fc, 128 * tt, 128 * tt + 128),
                   wT(k, 2, 2, fc, 0, 512), start=(fc == 0), stop=(fc == 3))
            wait(k, "vector", "te")
            dst = vtm(k, 4 * c + tt)
            vve = vinc(k, lambda e, dst=dst, ps=ps: e.scalar_tensor_tensor(
                out=dst, in0=ps[:, 0:512], scalar=1.0,
                in1=bvb_ap(k, 512, 1024), op0=ALU.mult, op1=ALU.add))
            wait(k, "tensor", "ve", vve)
        wait(k, "sync", "ve")
        wait(k, "sync", "se")
    proj_fm(k, 2, 0, lambda fc, c0, c1: hso(k, fc, c0, c1), NR, lambda dc, c0, c1: qfm(k, dc, c0, c1))


def phase_ffw_weights(k):
    d = k.d
    stage = k.sb["ptr"].ap()
    for wname, tname, nrow, ncol in (("wff1", "wff1T", DFF, D),
                                     ("wff2", "wff2T", D, DFF)):
        for r in range(nrow // 128):
            for cg in range(ncol // 1024):
                vg = sdma(k, stage[0:128, 0:1024],
                          d[wname][128 * r:128 * r + 128, 1024 * cg:1024 * cg + 1024],
                          "gpsimd")
                wait(k, "tensor", "gd", vg)
                for ic in range(8):
                    trans(k, psb(k, "D", 128 * (ic % 4), 128 * (ic % 4) + 128),
                          stage[0:128, 128 * ic:128 * ic + 128])
                    if ic % 4 == 3:
                        wait(k, "vector", "te")
                        for j in range(4):
                            icj = ic - 3 + j
                            vinc(k, lambda e, icj=icj, j=j: e.tensor_copy(
                                out=stage[0:128, 1024 + 128 * icj:1152 + 128 * icj],
                                in_=psb(k, "D", 128 * j, 128 * j + 128)))
                        wait(k, "tensor", "ve")
                ve = k.p.cnt[k.S("ve")]
                wait(k, "sync", "ve", ve)
                dst = AP(tensor=d[tname].tensor, offset=128 * r + 1024 * cg * nrow,
                         ap=[[nrow, 128], [128 * nrow, 8], [1, 128]])
                sdma(k, dst, stage[0:128, 1024:2048])
                wait(k, "gpsimd", "ld")
    wait(k, "sync", "ld")


def phase_attn_ffn(k):
    p, d = k.p, k.d
    build_gbe(k, "gff", "beff", 1024)
    for tq in range(16):
        for half, xsrc in ((0, x1t), (1, x2t)):
            for f in range(4):
                trans(k, psb(k, "D", 128 * f, 128 * f + 128),
                      xsrc(k, tq, 128 * f, 128 * f + 128))
            wait(k, "scalar", "te")
            for f in range(4):
                fc = 4 * half + f
                dst = yfm(k, fc, 128 * tq, 128 * tq + 128)
                sinc(k, lambda e, dst=dst, f=f: e.copy(
                    out=dst, in_=psb(k, "D", 128 * f, 128 * f + 128)))
            wait(k, "tensor", "se")
    wait(k, "vector", "se")
    wait(k, "sync", "se")

    w1r = k.sb["s3"].ap()
    w2r = k.sb["s3"].ap()
    hst = k.sb["f32b"].ap().bitcast(BF16)
    xf = k.sb["f32a"].ap()
    for tb in range(4):
        h1_te = {}
        for fb in range(32):
            buf = fb % 2
            if fb >= 2:
                wait(k, "sync", "te", h1_te[fb - 2])
            src = AP(tensor=d["wff1T"].tensor, offset=128 * fb,
                     ap=[[DFF, 128], [128 * DFF, 8], [1, 128]])
            v = sdma(k, w1r[0:128, 1024 * buf:1024 * buf + 1024], src)
            wait(k, "tensor", "ld", v)
            ps = k.ps["C"] if fb % 2 == 0 else k.ps["D"]
            for yc in range(8):
                mm(k, ps[:, 0:512], w1r[0:128, 1024 * buf + 128 * yc:1024 * buf + 128 * yc + 128],
                   yfm(k, yc, 512 * tb, 512 * tb + 512),
                   start=(yc == 0), stop=(yc == 7))
            h1_te[fb] = k.p.cnt[k.S("te")]
            wait(k, "scalar", "te")
            dst = h1sb(k, fb)
            vse = sinc(k, lambda e, dst=dst, ps=ps, fb=fb: e.activation(
                out=dst, in_=ps[:, 0:512], func=AF.Relu,
                bias=a(k, "bff1c", 0, 128, fb, fb + 1)))
            wait(k, "tensor", "se", vse)
            wait(k, "sync", "se", vse)
        h2_te = {}
        for oc in range(8):
            buf = oc % 2
            if oc >= 2:
                wait(k, "sync", "te", h2_te[oc - 2])
            src = AP(tensor=d["wff2T"].tensor, offset=128 * oc,
                     ap=[[D, 128], [128 * D, 32], [1, 128]])
            v = sdma(k, w2r[0:128, 2048 + 4096 * buf:2048 + 4096 * buf + 4096], src)
            wait(k, "tensor", "ld", v)
            ps = k.ps["C"] if oc % 2 == 0 else k.ps["D"]
            for ffc in range(32):
                mm(k, ps[:, 0:512],
                   w2r[0:128, 2048 + 4096 * buf + 128 * ffc:2048 + 4096 * buf + 128 * ffc + 128],
                   h1sb(k, ffc), start=(ffc == 0), stop=(ffc == 31))
            h2_te[oc] = k.p.cnt[k.S("te")]
            wait(k, "scalar", "te")
            vse = sinc(k, lambda e, ps=ps, oc=oc: e.activation(
                out=hst[0:128, 4096:4608], in_=ps[:, 0:512], func=AF.Identity,
                bias=a(k, "bff2c", 0, 128, oc, oc + 1)))
            wait(k, "tensor", "se", vse)
            for sub in range(4):
                trans(k, psb(k, "Z0", 128 * sub, 128 * sub + 128),
                      hst[0:128, 4096 + 128 * sub:4096 + 128 * sub + 128])
            wait(k, "vector", "te")
            for sub in range(4):
                tq = 4 * tb + sub
                xs = x1t(k, tq, 128 * oc, 128 * oc + 128) if oc < 4 else \
                     x2t(k, tq, 128 * (oc - 4), 128 * (oc - 4) + 128)
                dst = z3t(k, tq, 128 * oc, 128 * oc + 128)
                vinc(k, lambda e, dst=dst, xs=xs, sub=sub: e.tensor_tensor(
                    out=dst, in0=psb(k, "Z0", 128 * sub, 128 * sub + 128),
                    in1=xs, op=ALU.add))
            wait(k, "tensor", "ve")
            wait(k, "scalar", "ve")
    for tq in range(16):
        v = sdma(k, xf[0:128, 0:1024], d["xq"][128 * tq:128 * tq + 128, :])
        vap = ln_tile(k, z3t(k, tq), z3t(k, tq), hst[0:128, 4096:5120], D)
        wait(k, "vector", "se", vap)
        vop(k, lambda e: e.tensor_tensor(
            out=hst[0:128, 4096:5120], in0=hst[0:128, 4096:5120],
            in1=gbe_ap(k, 0, 1024), op=ALU.mult))
        vop(k, lambda e: e.tensor_tensor(
            out=hst[0:128, 4096:5120], in0=hst[0:128, 4096:5120],
            in1=gbe_ap(k, 1024, 2048), op=ALU.add))
        wait(k, "vector", "ld", v)
        dst = z3t(k, tq)
        vx = vinc(k, lambda e, dst=dst: e.tensor_tensor(
            out=dst, in0=hst[0:128, 4096:5120], in1=xf[0:128, 0:1024], op=ALU.add))
        wait(k, "sync", "ve", vx)
        wait(k, "scalar", "ve", vx)


def quant_ladder(k, zsrc_f32, bca_ps, bcr_ps, width, out_bf):
    p = k.p
    fbap = k.sb["f32b"].ap()
    fbb = k.sb["f32b"].ap().bitcast(BF16)
    z = fbap[0:128, 2304:2304 + width]
    u = fbap[0:128, 2816:2816 + width]
    sg = fbb[0:128, 0:width]
    gslot = lambda j: fbb[0:128, 512 * (j + 1):512 * (j + 1) + width]
    t2 = fbb[0:128, 4096:4096 + width]
    vz = vinc(k, lambda e: e.tensor_tensor(out=z, in0=zsrc_f32, in1=bca_ps,
                                           op=ALU.mult))
    wait(k, "scalar", "ve", vz)
    sinc(k, lambda e: e.activation(out=u, in_=z, func=AF.Abs))
    vsg = sinc(k, lambda e: e.activation(out=sg, in_=z, func=AF.Sign))
    wait(k, "vector", "se", vsg)
    t0, tb = gslot(0), gslot(1)
    vop(k, lambda e: e.tensor_scalar(out=t0, in0=u, scalar1=QMIDS[0], scalar2=None,
                                     op0=ALU.is_gt))
    for i in (1, 2, 3):
        vop(k, lambda e, i=i: e.scalar_tensor_tensor(
            out=t0, in0=u, scalar=QMIDS[i], in1=t0, op0=ALU.is_gt, op1=ALU.add))
    vop(k, lambda e: e.tensor_scalar(out=tb, in0=u, scalar1=QMIDS[6], scalar2=None,
                                     op0=ALU.is_gt))
    for i in (6, 5, 4):
        vop(k, lambda e, i=i: e.scalar_tensor_tensor(
            out=tb, in0=u, scalar=QMIDS[i], in1=tb, op0=ALU.is_gt, op1=ALU.add))
    vop(k, lambda e: e.scalar_tensor_tensor(
        out=t0, in0=tb, scalar=2.0, in1=t0, op0=ALU.mult, op1=ALU.add))
    vop(k, lambda e: e.tensor_tensor(out=t0, in0=t0, in1=sg, op=ALU.mult))
    return vinc(k, lambda e: e.tensor_tensor(out=out_bf, in0=t0, in1=bcr_ps,
                                             op=ALU.mult))


def quant_scales(k, src_f32, width, nblk):
    p = k.p
    sm = k.sb["f32a"].ap()
    bm = sm[0:128, 1408:1408 + nblk]
    rr = sm[0:128, 1496:1496 + nblk]
    rh = sm[0:128, 1584:1584 + nblk]
    tmp = sm[0:128, 1672:1672 + nblk]
    stg = k.sb["ptr"].ap()
    rhiS = stg[0:128, 1536:1536 + nblk]
    rhhiS = stg[0:128, 1664:1664 + nblk]
    rloS = stg[0:128, 1792:1792 + nblk]
    rhloS = stg[0:128, 1920:1920 + nblk]
    v1 = vinc(k, lambda e: e.tensor_reduce(
        out=bm, in_=src_f32.rearrange("p (g w) -> p g w", w=16),
        axis=AX.X, op=ALU.max, apply_absolute_value=True))
    wait(k, "vector", "ve", v1)
    v2 = vinc(k, lambda e: e.tensor_scalar(out=bm, in0=bm, scalar1=6e-12,
                                           scalar2=None, op0=ALU.max))
    wait(k, "vector", "ve", v2)
    v3 = vinc(k, lambda e: e.reciprocal(out=rr, in_=bm))
    vop(k, lambda e: e.tensor_scalar(out=rh, in0=bm, scalar1=1.0 / 12.0,
                                     scalar2=None, op0=ALU.mult))
    wait(k, "vector", "ve", v3)
    v4 = vinc(k, lambda e: e.tensor_tensor(out=tmp, in0=rr, in1=bm, op=ALU.mult))
    wait(k, "vector", "ve", v4)
    v5 = vinc(k, lambda e: e.tensor_scalar(out=tmp, in0=tmp, scalar1=-1.0,
                                           scalar2=2.0, op0=ALU.mult, op1=ALU.add))
    wait(k, "vector", "ve", v5)
    v6 = vinc(k, lambda e: e.tensor_tensor(out=rr, in0=rr, in1=tmp, op=ALU.mult))
    wait(k, "vector", "ve", v6)
    v7 = vinc(k, lambda e: e.tensor_scalar(out=rr, in0=rr, scalar1=6.0,
                                           scalar2=None, op0=ALU.mult))
    vop(k, lambda e: e.tensor_copy(out=rhhiS, in_=rh))
    wait(k, "vector", "ve", v7)
    v8 = vinc(k, lambda e: e.tensor_copy(out=rhiS, in_=rr))
    vop(k, lambda e: e.tensor_tensor(out=bm, in0=rh, in1=rhhiS, op=ALU.subtract))
    wait(k, "vector", "ve", v8)
    v9 = vinc(k, lambda e: e.tensor_tensor(out=tmp, in0=rr, in1=rhiS,
                                           op=ALU.subtract))
    vop(k, lambda e: e.tensor_copy(out=rhloS, in_=bm))
    wait(k, "vector", "ve", v9)
    vr = vinc(k, lambda e: e.tensor_copy(out=rloS, in_=tmp))
    wait(k, "tensor", "ve", vr)
    pD = k.ps["D"].bitcast(BF16)
    trans(k, pD[0:nblk, 0:128], rhiS)
    trans(k, pD[0:nblk, 128:256], rhhiS)
    trans(k, pD[0:nblk, 256:384], rloS)
    trans(k, pD[0:nblk, 384:512], rhloS)
    wait(k, "vector", "te")
    smb = k.sb["f32a"].ap().bitcast(BF16)
    vinc(k, lambda e: e.tensor_copy(out=smb[0:nblk, 3776:3904][0:nblk, 0:128],
                                    in_=pD[0:nblk, 0:128]))
    vinc(k, lambda e: e.tensor_copy(out=smb[0:nblk, 3904:4032][0:nblk, 0:128],
                                    in_=pD[0:nblk, 128:256]))
    vinc(k, lambda e: e.tensor_copy(out=smb[0:nblk, 3520:3648][0:nblk, 0:128],
                                    in_=pD[0:nblk, 256:384]))
    v = vinc(k, lambda e: e.tensor_copy(out=smb[0:nblk, 3648:3776][0:nblk, 0:128],
                                        in_=pD[0:nblk, 384:512]))
    wait(k, "tensor", "ve", v)
    return v


def phase_ffn_weights_quant(k):
    p, d = k.p, k.d
    fa = k.sb["f32a"].ap()
    fb = k.sb["f32b"].ap()
    stage_in = fb[0:128, 0:1024]
    smb = k.sb["f32a"].ap().bitcast(BF16)
    outb = k.sb["ptr"].ap()
    for wname, tname in (("wgate", "wgqT"), ("wup", "wuqT")):
        for r in range(FFI // 128):
            v = sdma(k, stage_in, d[wname][128 * r:128 * r + 128, :])
            wait(k, "vector", "ld", v)
            wait(k, "scalar", "ld", v)
            vr = quant_scales(k, stage_in, 1024, 64)
            wait(k, "tensor", "ve", vr)
            for ic in range(8):
                trans(k, k.ps["Z2"][:, 128 * (ic % 4):128 * (ic % 4) + 128],
                      stage_in[:, 128 * ic:128 * ic + 128], fp32=True)
                if ic % 4 == 3:
                    wait(k, "vector", "te")
                    for j in range(4):
                        icj = ic - 3 + j
                        vinc(k, lambda e, icj=icj, j=j: e.tensor_copy(
                            out=fa[0:128, 128 * icj:128 * icj + 128],
                            in_=k.ps["Z2"][:, 128 * j:128 * j + 128]))
                    wait(k, "tensor", "ve")
            for ic in range(8):
                mm(k, k.ps["A"][:, 128 * ic:128 * ic + 128],
                   a(k, "sel88", 0, 64, 128 * ic, 128 * ic + 128),
                   smb[0:64, 3776:3904][0:64, 0:128], start=True, stop=False)
                mm(k, k.ps["A"][:, 128 * ic:128 * ic + 128],
                   a(k, "sel88", 0, 64, 128 * ic, 128 * ic + 128),
                   smb[0:64, 3520:3648][0:64, 0:128], start=False, stop=True)
                psr = k.ps["Z0"] if ic < 4 else k.ps["Z1"]
                mm(k, psr[:, 128 * (ic % 4):128 * (ic % 4) + 128],
                   a(k, "sel88", 0, 64, 128 * ic, 128 * ic + 128),
                   smb[0:64, 3904:4032][0:64, 0:128], start=True, stop=False)
                mm(k, psr[:, 128 * (ic % 4):128 * (ic % 4) + 128],
                   a(k, "sel88", 0, 64, 128 * ic, 128 * ic + 128),
                   smb[0:64, 3648:3776][0:64, 0:128], start=False, stop=True)
            wait(k, "vector", "te")
            quant_ladder(k, fa[0:128, 0:512], k.ps["A"][:, 0:512],
                         k.ps["Z0"][:, 0:512], 512, outb[0:128, 0:512])
            vq = quant_ladder(k, fa[0:128, 512:1024], k.ps["A"][:, 512:1024],
                              k.ps["Z1"][:, 0:512], 512, outb[0:128, 512:1024])
            wait(k, "sync", "ve", vq)
            wait(k, "tensor", "ve", vq)
            dst = AP(tensor=d[tname].tensor, offset=128 * r,
                     ap=[[FFI, 128], [128 * FFI, 8], [1, 128]])
            sdma(k, dst, outb[0:128, 0:1024])
            wait(k, "vector", "ld")
            wait(k, "scalar", "ld")
        wait(k, "sync", "ld")
    for r in range(D // 128):
        for hf2 in range(2):
            v = sdma(k, fb[0:128, 0:1408],
                     d["wdown"][128 * r:128 * r + 128, 1408 * hf2:1408 * hf2 + 1408])
            wait(k, "vector", "ld", v)
            wait(k, "scalar", "ld", v)
            vr = quant_scales(k, fb[0:128, 0:1408], 1408, 88)
            wait(k, "tensor", "ve", vr)
            for ic in range(11):
                trans(k, k.ps["Z2"][:, 128 * (ic % 4):128 * (ic % 4) + 128],
                      fb[0:128, 128 * ic:128 * ic + 128], fp32=True)
                if ic % 4 == 3 or ic == 10:
                    wait(k, "vector", "te")
                    for j in range(ic % 4 + 1):
                        icj = ic - (ic % 4) + j
                        vinc(k, lambda e, icj=icj, j=j: e.tensor_copy(
                            out=fa[0:128, 128 * icj:128 * icj + 128],
                            in_=k.ps["Z2"][:, 128 * j:128 * j + 128]))
                    wait(k, "tensor", "ve")
            for ic in range(11):
                if ic < 8:
                    psx, col = k.ps["A"], 128 * ic
                    psy, coly = (k.ps["Z0"], 128 * ic) if ic < 4 else (k.ps["Z1"], 128 * (ic - 4))
                else:
                    psx, col = k.ps["C"], 128 * (ic - 8)
                    psy, coly = k.ps["Z2"], 128 * (ic - 8)
                mm(k, psx[:, col:col + 128],
                   a(k, "sel88", 0, 88, 128 * ic, 128 * ic + 128),
                   smb[0:88, 3776:3904][0:88, 0:128], start=True, stop=False)
                mm(k, psx[:, col:col + 128],
                   a(k, "sel88", 0, 88, 128 * ic, 128 * ic + 128),
                   smb[0:88, 3520:3648][0:88, 0:128], start=False, stop=True)
                mm(k, psy[:, coly:coly + 128],
                   a(k, "sel88", 0, 88, 128 * ic, 128 * ic + 128),
                   smb[0:88, 3904:4032][0:88, 0:128], start=True, stop=False)
                mm(k, psy[:, coly:coly + 128],
                   a(k, "sel88", 0, 88, 128 * ic, 128 * ic + 128),
                   smb[0:88, 3648:3776][0:88, 0:128], start=False, stop=True)
            wait(k, "vector", "te")
            quant_ladder(k, fa[0:128, 0:512], k.ps["A"][:, 0:512],
                         k.ps["Z0"][:, 0:512], 512, outb[0:128, 0:512])
            quant_ladder(k, fa[0:128, 512:1024], k.ps["A"][:, 512:1024],
                         k.ps["Z1"][:, 0:512], 512, outb[0:128, 512:1024])
            vq2 = quant_ladder(k, fa[0:128, 1024:1408], k.ps["C"][:, 0:384],
                               k.ps["Z2"][:, 0:384], 384, outb[0:128, 1024:1408])
            wait(k, "sync", "ve", vq2)
            wait(k, "tensor", "ve", vq2)
            dst = AP(tensor=d["wdqT"].tensor, offset=128 * r + 1408 * hf2 * D,
                     ap=[[D, 128], [128 * D, 11], [1, 128]])
            sdma(k, dst, outb[0:128, 0:1408])
            wait(k, "vector", "ld")
            wait(k, "scalar", "ld")
    wait(k, "sync", "ld")


def phase_ffn_block(k):
    p, d = k.p, k.d
    fa = k.sb["f32a"].ap()
    fb = k.sb["f32b"].ap()
    smb = k.sb["f32a"].ap().bitcast(BF16)
    ptb = k.sb["ptr"].ap()
    for kc in range(8):
        sdma(k, wgq_tile(k, kc), d["wgqT"][128 * kc:128 * kc + 128, :])
        sdma(k, wuq_tile(k, kc), d["wuqT"][128 * kc:128 * kc + 128, :], "gpsimd")
    for tq in range(16):
        vap = ln_tile(k, z3t(k, tq), z3t(k, tq), fa[0:128, 0:1024], D,
                      sq_ap=fb[0:128, 2048:3072])
        wait(k, "vector", "se", vap)
        vr = quant_scales(k, fa[0:128, 0:1024], 1024, 64)
        wait(k, "tensor", "ve", vr)
        for ic in range(8):
            trans(k, k.ps["Z2"][:, 128 * (ic % 4):128 * (ic % 4) + 128],
                  fa[0:128, 128 * ic:128 * ic + 128], fp32=True)
            if ic % 4 == 3:
                wait(k, "scalar", "te")
                for j in range(4):
                    icj = ic - 3 + j
                    sinc(k, lambda e, icj=icj, j=j: e.copy(
                        out=fa[0:128, 128 * icj:128 * icj + 128],
                        in_=k.ps["Z2"][:, 128 * j:128 * j + 128]))
                wait(k, "tensor", "se")
        for ic in range(8):
            mm(k, k.ps["A"][:, 128 * ic:128 * ic + 128],
               a(k, "sel88", 0, 64, 128 * ic, 128 * ic + 128),
               smb[0:64, 3776:3904][0:64, 0:128], start=True, stop=False)
            mm(k, k.ps["A"][:, 128 * ic:128 * ic + 128],
               a(k, "sel88", 0, 64, 128 * ic, 128 * ic + 128),
               smb[0:64, 3520:3648][0:64, 0:128], start=False, stop=True)
            psr = k.ps["Z0"] if ic < 4 else k.ps["Z1"]
            mm(k, psr[:, 128 * (ic % 4):128 * (ic % 4) + 128],
               a(k, "sel88", 0, 64, 128 * ic, 128 * ic + 128),
               smb[0:64, 3904:4032][0:64, 0:128], start=True, stop=False)
            mm(k, psr[:, 128 * (ic % 4):128 * (ic % 4) + 128],
               a(k, "sel88", 0, 64, 128 * ic, 128 * ic + 128),
               smb[0:64, 3648:3776][0:64, 0:128], start=False, stop=True)
        wait(k, "vector", "te")
        wait(k, "vector", "se")
        quant_ladder(k, fa[0:128, 0:512], k.ps["A"][:, 0:512],
                     k.ps["Z0"][:, 0:512], 512, ptb[0:128, 0:512])
        vq = quant_ladder(k, fa[0:128, 512:1024], k.ps["A"][:, 512:1024],
                          k.ps["Z1"][:, 0:512], 512, ptb[0:128, 512:1024])
        wait(k, "scalar", "ve", vq)
        for ic in range(8):
            dst = qxfm(k, ic, 128 * tq, 128 * tq + 128)
            sinc(k, lambda e, dst=dst, ic=ic: e.copy(
                out=dst, in_=ptb[0:128, 128 * ic:128 * ic + 128]))
        wait(k, "tensor", "se")
        wait(k, "vector", "se")
    wait(k, "tensor", "gd")
    wait(k, "tensor", "ld")
    for ts in range(16):
        for hf2 in range(2):
            for ci, w in ((0, 512), (1, 512), (2, 384)):
                fo0 = 1408 * hf2 + 512 * ci
                psg, psu = k.ps["C"], k.ps["D"]
                for kc in range(8):
                    mm(k, psg[:, 0:w], qxfm(k, kc, 128 * ts, 128 * ts + 128),
                       wgq_tile(k, kc, fo0, fo0 + w), start=(kc == 0), stop=(kc == 7))
                for kc in range(8):
                    mm(k, psu[:, 0:w], qxfm(k, kc, 128 * ts, 128 * ts + 128),
                       wuq_tile(k, kc, fo0, fo0 + w), start=(kc == 0), stop=(kc == 7))
                wait(k, "scalar", "te")
                vsg = sinc(k, lambda e, w=w: e.activation(
                    out=ptb[0:128, 1024:1024 + w], in_=psg[:, 0:w], func=AF.Silu))
                wait(k, "vector", "se", vsg)
                vh = vinc(k, lambda e, w=w, ci=ci: e.tensor_tensor(
                    out=fa[0:128, 512 * ci:512 * ci + w], in0=psu[:, 0:w],
                    in1=ptb[0:128, 1024:1024 + w], op=ALU.mult))
                wait(k, "tensor", "ve", vh)
                wait(k, "scalar", "ve", vh)
            vr = quant_scales(k, fa[0:128, 0:1408], 1408, 88)
            wait(k, "tensor", "ve", vr)
            for ic in range(11):
                trans(k, k.ps["Z2"][:, 128 * (ic % 4):128 * (ic % 4) + 128],
                      fa[0:128, 128 * ic:128 * ic + 128], fp32=True)
                if ic % 4 == 3 or ic == 10:
                    wait(k, "scalar", "te")
                    for j in range(ic % 4 + 1):
                        icj = ic - (ic % 4) + j
                        sinc(k, lambda e, icj=icj, j=j: e.copy(
                            out=fa[0:128, 128 * icj:128 * icj + 128],
                            in_=k.ps["Z2"][:, 128 * j:128 * j + 128]))
                    wait(k, "tensor", "se")
            for ic in range(11):
                if ic < 8:
                    psx, col = k.ps["A"], 128 * ic
                    psy, coly = (k.ps["Z0"], 128 * ic) if ic < 4 else (k.ps["Z1"], 128 * (ic - 4))
                else:
                    psx, col = k.ps["C"], 128 * (ic - 8)
                    psy, coly = k.ps["Z2"], 128 * (ic - 8)
                mm(k, psx[:, col:col + 128],
                   a(k, "sel88", 0, 88, 128 * ic, 128 * ic + 128),
                   smb[0:88, 3776:3904][0:88, 0:128], start=True, stop=False)
                mm(k, psx[:, col:col + 128],
                   a(k, "sel88", 0, 88, 128 * ic, 128 * ic + 128),
                   smb[0:88, 3520:3648][0:88, 0:128], start=False, stop=True)
                mm(k, psy[:, coly:coly + 128],
                   a(k, "sel88", 0, 88, 128 * ic, 128 * ic + 128),
                   smb[0:88, 3904:4032][0:88, 0:128], start=True, stop=False)
                mm(k, psy[:, coly:coly + 128],
                   a(k, "sel88", 0, 88, 128 * ic, 128 * ic + 128),
                   smb[0:88, 3648:3776][0:88, 0:128], start=False, stop=True)
            wait(k, "vector", "te")
            wait(k, "vector", "se")
            hq0 = 2816 + 1408 * hf2
            quant_ladder(k, fa[0:128, 0:512], k.ps["A"][:, 0:512],
                         k.ps["Z0"][:, 0:512], 512,
                         a(k, "s5", 0, 128, hq0, hq0 + 512))
            quant_ladder(k, fa[0:128, 512:1024], k.ps["A"][:, 512:1024],
                         k.ps["Z1"][:, 0:512], 512,
                         a(k, "s5", 0, 128, hq0 + 512, hq0 + 1024))
            vq2 = quant_ladder(k, fa[0:128, 1024:1408], k.ps["C"][:, 0:384],
                               k.ps["Z2"][:, 0:384], 384,
                               a(k, "s5", 0, 128, hq0 + 1024, hq0 + 1408))
            wait(k, "tensor", "ve", vq2)
        vhq = p.cnt[k.S("ve")]
        wait(k, "sync", "ve", vhq)
        sdma(k, d["hqTD"][128 * ts:128 * ts + 128, :],
             a(k, "s5", 0, 128, 2816, 5632))
        wait(k, "vector", "ld")
    wait(k, "sync", "ld")
    barrier(k)
    for kc in range(22):
        nm = "s1" if kc < 16 else "s2"
        sdma(k, wdq_tile(k, kc), d["wdqT"][128 * kc:128 * kc + 128, :])
    wait(k, "tensor", "ld")
    g2_te = {}
    for ts in range(16):
        buf = ts % 2
        if ts >= 2:
            wait(k, "gpsimd", "te", g2_te[ts - 2])
        v = sdma(k, a(k, "s3", 0, 128, 2816 * buf, 2816 * buf + 2816),
                 d["hqTD"][128 * ts:128 * ts + 128, :], "gpsimd")
        wait(k, "tensor", "gd", v)
        for oh in range(2):
            ps = k.ps["C"] if oh == 0 else k.ps["D"]
            for kc in range(22):
                mm(k, ps[:, 0:512], hqring(k, buf, kc),
                   wdq_tile(k, kc, 512 * oh, 512 * oh + 512),
                   start=(kc == 0), stop=(kc == 21))
            wait(k, "vector", "te")
            col = 1024 + 512 * oh
            vf = vinc(k, lambda e, ps=ps, col=col, ts=ts, oh=oh: e.scalar_tensor_tensor(
                out=fa[0:128, col:col + 512], in0=ps[:, 0:512], scalar=1.0,
                in1=z3t(k, ts, 512 * oh, 512 * oh + 512),
                op0=ALU.mult, op1=ALU.add))
            wait(k, "sync", "ve", vf)
            stdma(k, k.out[128 * ts:128 * ts + 128, 512 * oh:512 * oh + 512],
                  fa[0:128, col:col + 512])
            wait(k, "tensor", "ve", vf)
        g2_te[ts] = k.p.cnt[k.S("te")]
    if "qx" in DEBUG:
        wait(k, "gpsimd", "te")
        for kc in range(8):
            sdma(k, d["dbg_qx"][128 * kc:128 * kc + 128, :], qxfm(k, kc), "gpsimd")
    if "hq" in DEBUG:
        wait(k, "gpsimd", "ld")
        for rr in range(4):
            sdma(k, d["dbg_hq"][512 * rr:512 * rr + 512, :],
                 d["hqTD"][512 * rr:512 * rr + 512, :], "gpsimd")
    for wnm in ("wdq", "wgq", "wuq"):
        if wnm in DEBUG:
            wait(k, "gpsimd", "ld")
            for rr in range(4):
                nrow = k.d[wnm + "T"].shape[0] // 4
                sdma(k, d["dbg_" + wnm][nrow * rr:nrow * rr + nrow, :],
                     d[wnm + "T"][nrow * rr:nrow * rr + nrow, :], "gpsimd")
    if DEBUG:
        wait(k, "sync", "gd")
    wait(k, "sync", "st")


STAGE = "full"
ATTCUT = None
ATTMI = 1
DBG = False


def emit_dbg(k):
    if not DBG:
        return
    if STAGE == "P2":
        for dc in range(4):
            sdma(k, k.out[128 * dc:128 * dc + 128, 0:1024],
                 qfm(k, dc, 0, 1024), "gpsimd")
            sdma(k, k.out[512 + 128 * dc:512 + 128 * dc + 128, 0:1024],
                 kfm(k, dc, 0, 1024), "gpsimd")
        for t in range(8):
            sdma(k, k.out[1024 + 128 * t:1152 + 128 * t, 0:512],
                 vtm(k, t), "gpsimd")
        for fc in range(4):
            sdma(k, k.out[1024 + 128 * fc:1152 + 128 * fc, 512:1024],
                 xqfm(k, fc, 0, 512), "gpsimd")
    elif STAGE == "ATT1":
        for t in range(16):
            sdma(k, k.out[128 * t:128 * t + 128, 0:512], x1t(k, t), "gpsimd")
    elif STAGE == "ATT2":
        for t in range(16):
            sdma(k, k.out[128 * t:128 * t + 128, 0:512], x1t(k, t), "gpsimd")
            sdma(k, k.out[128 * t:128 * t + 128, 512:1024], x2t(k, t), "gpsimd")
    elif STAGE == "FFN1":
        for t in range(16):
            sdma(k, k.out[128 * t:128 * t + 128, 0:1024], z3t(k, t), "gpsimd")
    wait(k, "sync", "gd")


def emit_program(k):
    phase_consts(k)
    phase_holt_tables(k)
    build_bvb(k)
    phase_attn_weights(k, 1)
    barrier(k)
    if STAGE == "TABLES":
        return
    phase_xq(k)
    barrier(k)
    phase_stream_x(k)
    barrier(k)
    if STAGE == "P2":
        emit_dbg(k)
        return
    phase_attention(k, 1)
    barrier(k)
    if STAGE == "ATT1":
        emit_dbg(k)
        return
    phase_attn_weights(k, 2)
    phase_merge_hso(k)
    barrier(k)
    if STAGE == "MERGE":
        return
    phase_proj2(k)
    barrier(k)
    if STAGE == "PROJ2":
        return
    phase_attention(k, 2)
    barrier(k)
    if STAGE == "ATT2":
        emit_dbg(k)
        return
    phase_ffw_weights(k)
    barrier(k)
    phase_attn_ffn(k)
    barrier(k)
    if STAGE == "FFN1":
        emit_dbg(k)
        return
    phase_ffn_weights_quant(k)
    barrier(k)
    phase_ffn_block(k)


def build():
    k = K()
    declare(k)
    with k.stack:
        alloc(k)
        for s in ("ld", "gd", "gp", "te", "ve", "se", "st",
                  "bsy", "bte", "bve", "bse", "bgp"):
            k.p.sem(s)
        emit_program(k)
        blk = k.stack.enter_context(k.nc.Block())
        for eng_name, meth in (("sync", blk.sync), ("tensor", blk.tensor),
                               ("vector", blk.vector), ("scalar", blk.scalar),
                               ("gpsimd", blk.gpsimd)):
            def body(e, _ops=k.p.ops[eng_name]):
                for fn in _ops:
                    fn(e)
            meth(body)
    return k


_cache = {}


def kernel(**inputs):
    if "k" not in _cache:
        _cache["k"] = build()
    k = _cache["k"]
    f32 = np.float32
    j_np = np.eye(128, dtype=f32)[::-1].copy()
    sel1a = np.zeros((1, 128), f32); sel1a[0, 0:64] = 1
    sel1b = np.zeros((1, 128), f32); sel1b[0, 64:128] = 1
    def selmat(nblk, ncol):
        s = np.zeros((nblk, ncol), f32)
        for c in range(ncol // 128):
            for pp in range(128):
                g = 8 * c + pp // 16
                if g < nblk:
                    s[g, 128 * c + pp] = 1
        return s
    consts = {
        "idn": np.eye(128, dtype=f32),
        "idnb": np.eye(128, dtype=f32).astype(BF),
        "jmatb": j_np.astype(BF),
        "sel1a": sel1a.astype(BF), "sel1b": sel1b.astype(BF),
        "sel88": selmat(88, 1408).astype(BF),
        "onesf": np.ones((1, 128), f32),
    }
    x_full = np.ascontiguousarray(inputs["L0_attn_x"], dtype=f32)
    wmap = {
        "wq1": "L0_attn_wq1", "bq1": "L0_attn_bq1", "wk1": "L0_attn_wk1",
        "bk1": "L0_attn_bk1", "wv1": "L0_attn_wv1", "bv1": "L0_attn_bv1",
        "wo1": "L0_attn_wo1", "bo1": "L0_attn_bo1",
        "wq2": "L0_attn_wq2", "bq2": "L0_attn_bq2", "wk2": "L0_attn_wk2",
        "bk2": "L0_attn_bk2", "wv2": "L0_attn_wv2", "bv2": "L0_attn_bv2",
        "wo2": "L0_attn_wo2", "bo2": "L0_attn_bo2",
        "g1": "L0_attn_g1", "be1": "L0_attn_be1",
        "g2": "L0_attn_g2", "be2": "L0_attn_be2",
        "wff1": "L0_attn_w_ff1", "bff1": "L0_attn_b_ff1",
        "wff2": "L0_attn_w_ff2", "bff2": "L0_attn_b_ff2",
        "gff": "L0_attn_g_ff", "beff": "L0_attn_be_ff",
        "wgate": "L0_ffn_w_gate", "wup": "L0_ffn_w_up", "wdown": "L0_ffn_w_down",
    }
    shared = {kk: np.ascontiguousarray(inputs[v], dtype=f32) for kk, v in wmap.items()}
    in_maps = []
    for c in range(8):
        bb, half = c // 2, c % 2
        m = dict(consts)
        m.update(shared)
        m["x"] = x_full[bb]
        m["xq"] = np.ascontiguousarray(x_full[bb, half * NR:(half + 1) * NR, :])
        m["mask0"] = np.asarray([1.0 - half], f32)
        in_maps.append(m)
    res = run_bass_kernel_spmd(k.nc, in_maps, core_ids=list(range(8)))
    out = np.empty((B, S, D), f32)
    for c in range(8):
        bb, half = c // 2, c % 2
        out[bb, half * NR:(half + 1) * NR, :] = np.asarray(res.results[c]["out"])
    return out


# version 10
# speedup vs baseline: 1.5101x; 1.0163x over previous
import numpy as np
import ml_dtypes
import concourse.bass as bass
import concourse.mybir as mybir
from concourse.bass_utils import run_bass_kernel_spmd
from contextlib import ExitStack

F32 = mybir.dt.float32
BF16 = mybir.dt.bfloat16
AF = mybir.ActivationFunctionType
ALU = mybir.AluOpType
AX = mybir.AxisListType
AP = bass.AP
BF = ml_dtypes.bfloat16

B, S, D = 4, 4096, 1024
DH, H, DK = 512, 8, 64
DFF, FFI = 4096, 2816
NR = 2048
TCH, NCH = 512, 8
EPS = 1e-5
ENGS = ("sync", "tensor", "vector", "scalar", "gpsimd")
QMIDS = (0.25, 0.75, 1.25, 1.75, 2.5, 3.5, 5.0)

DEBUG = ()


class Prog:
    def __init__(self, nc, stack):
        self.nc, self.stack = nc, stack
        self.ops = {e: [] for e in ENGS}
        self.cnt, self.sems = {}, {}

    def sem(self, name):
        s = self.stack.enter_context(self.nc.semaphore(name))
        self.cnt[s] = 0
        self.sems[name] = s
        return s

    def on(self, eng, fn):
        self.ops[eng].append(fn)

    def inc(self, eng, fn, sem, n):
        self.cnt[sem] += n
        self.ops[eng].append(lambda e, fn=fn, sem=sem, n=n: fn(e).then_inc(sem, n))
        return self.cnt[sem]

    def wait(self, eng, sem, v):
        self.ops[eng].append(lambda e, sem=sem, v=v: e.wait_ge(sem, v))


class K:
    def __init__(self):
        self.nc = bass.Bass("TRN2", target_bir_lowering=False,
                            detect_race_conditions=False)
        self.stack = ExitStack()
        self.p = Prog(self.nc, self.stack)
        self.d = {}
        self.sb = {}
        self.ps = {}

    def S(self, name):
        return self.p.sems[name]


def din(k, name, shape, dt=F32, kind="ExternalInput"):
    k.d[name] = k.nc.dram_tensor(name, shape, dt, kind=kind).ap()


def declare(k):
    din(k, "x", [S, D])
    din(k, "xq", [NR, D])
    din(k, "mask0", [1])
    for i in (1, 2):
        for w in ("wq", "wk", "wv", "wo"):
            din(k, f"{w}{i}", [DH, DH])
        for bn in ("bq", "bk", "bv", "bo"):
            din(k, f"{bn}{i}", [DH])
        din(k, f"g{i}", [DH]); din(k, f"be{i}", [DH])
    din(k, "wff1", [DFF, D]); din(k, "bff1", [DFF])
    din(k, "wff2", [D, DFF]); din(k, "bff2", [D])
    din(k, "gff", [D]); din(k, "beff", [D])
    din(k, "wgate", [FFI, D]); din(k, "wup", [FFI, D]); din(k, "wdown", [D, FFI])
    din(k, "idn", [128, 128])
    din(k, "idnb", [128, 128], BF16)
    din(k, "jmatb", [128, 128], BF16)
    din(k, "sel1a", [1, 128], BF16)
    din(k, "sel1b", [1, 128], BF16)
    din(k, "sel88", [88, 1408], BF16)
    din(k, "onesf", [1, 128])
    din(k, "hpad", [5120], BF16, "Internal")
    din(k, "hsD", [DH, S], BF16, "Internal")
    din(k, "wff1T", [DFF, D], BF16, "Internal")
    din(k, "wff2T", [D, DFF], BF16, "Internal")
    din(k, "hqTD", [NR, FFI], BF16, "Internal")
    k.out = k.nc.dram_tensor("out", [NR, D], F32, kind="ExternalOutput").ap()
    for nm in DEBUG:
        shp = {"xhq": [DH, NR], "hs": [DH, S], "x1": [NR, DH], "x2": [NR, DH],
               "kfm": [DH, S], "vtm": [S, DH], "qfm": [DH, NR], "o1": [DH, NR],
               "z3": [NR, D], "xres": [NR, D], "qx": [D, NR], "hq": [NR, FFI],
               "wdq": [FFI, D], "wgq": [D, FFI], "wuq": [D, FFI],
               }[nm]
        din(k, "dbg_" + nm, shp, F32, "ExternalOutput")


def alloc(k):
    E = k.stack.enter_context
    def sbt(name, cols, dt=BF16, parts=128):
        k.sb[name] = E(k.nc.sbuf_tensor("sb_" + name, [parts, cols], dt))
    sbt("idn", 128, F32); sbt("idnb", 128); sbt("jmatb", 128)
    sbt("sel1a", 128, BF16, 1); sbt("sel1b", 128, BF16, 33)
    sbt("sel88", 1408, BF16, 88)
    sbt("onesf", 128, F32, 1)
    sbt("onescol", 1, BF16)
    sbt("bias4", 32, F32)
    sbt("bff1c", 32, F32); sbt("bff2c", 8, F32)
    sbt("rowv", 512, F32, 1)
    sbt("maskc", 2, F32)
    sbt("s1", 16384)
    sbt("s2", 16384)
    sbt("s3", 14080)
    sbt("s4", 16384)
    sbt("s5", 8192)
    sbt("s6", 8192)
    sbt("s7", 8192)
    sbt("ptr", 2048)
    sbt("f32a", 2048, F32)
    sbt("f32b", 4160, F32)
    sbt("stats", 16, F32)
    k.ps["A"] = E(k.nc.psum_tensor("psA", [128, 1024], F32))
    for nm in ("C", "D", "Z0", "Z1", "Z2", "Z3"):
        k.ps[nm] = E(k.nc.psum_tensor("ps" + nm, [128, 512], F32))


def a(k, name, p0=0, p1=128, c0=0, c1=None):
    t = k.sb[name].ap()
    if c1 is None:
        c1 = t.shape[-1]
    return t[p0:p1, c0:c1]


def xqfm(k, fc, c0=0, c1=NR):
    return a(k, "s4", 0, 128, 2048 * fc + c0, 2048 * fc + c1)
def qfm(k, dc, c0=0, c1=NR):
    return a(k, "s4", 0, 128, 8192 + 2048 * dc + c0, 8192 + 2048 * dc + c1)
def kfm(k, dc, c0=0, c1=S):
    return a(k, "s1", 0, 128, 4096 * dc + c0, 4096 * dc + c1)
def vtm(k, kt, c0=0, c1=DH):
    return a(k, "s2", 0, 128, 512 * kt + c0, 512 * kt + c1)
def hso(k, fc, c0=0, c1=NR):
    return a(k, "s3", 0, 128, 2048 * fc + c0, 2048 * fc + c1)
def hsring(k, buf, fc, c0=0, c1=512):
    base = 8192 + 2048 * buf + 512 * fc
    return a(k, "s3", 0, 128, base + c0, base + c1)
def xtm_ring(k, buf, tt, c0=0, c1=512):
    base = 2048 * buf + 512 * tt
    return a(k, "s7", 0, 128, base + c0, base + c1)
def xfm_ring(k, buf, fc, c0=0, c1=512):
    base = 8192 + 2048 * buf + 512 * fc
    return a(k, "s3", 0, 128, base + c0, base + c1)
def wT(k, mi, m, ic, o0, o1):
    base = 2048 * m + 512 * ic
    return a(k, "s5", 0, 128, base + o0, base + o1)
def holtT(k, p0, p1, c0, c1):
    return a(k, "s6", p0, p1, c0, c1)
def x1t(k, tt, c0=0, c1=512):
    return a(k, "s6", 0, 128, 512 * tt + c0, 512 * tt + c1)
def x2t(k, tt, c0=0, c1=512):
    return a(k, "s7", 0, 128, 512 * tt + c0, 512 * tt + c1)
QXT = [("s6", 0), ("s6", 2048), ("s6", 4096), ("s6", 6144),
       ("s7", 0), ("s7", 2048), ("s7", 4096), ("s7", 6144)]
def yfm(k, fc, c0=0, c1=NR):
    return a(k, "s1", 0, 128, 2048 * fc + c0, 2048 * fc + c1)
def h1sb(k, fb, c0=0, c1=512):
    return a(k, "s2", 0, 128, 512 * fb + c0, 512 * fb + c1)
def z3t(k, tt, c0=0, c1=D):
    return a(k, "s4", 0, 128, 1024 * tt + c0, 1024 * tt + c1)
def qxfm(k, kc, c0=0, c1=NR):
    nm, base = QXT[kc]
    return a(k, nm, 0, 128, base + c0, base + c1)
def wgq_tile(k, kc, c0=0, c1=FFI):
    if kc < 5:
        return a(k, "s1", 0, 128, FFI * kc + c0, FFI * kc + c1)
    return a(k, "s3", 0, 128, FFI * (kc - 5) + c0, FFI * (kc - 5) + c1)
def wuq_tile(k, kc, c0=0, c1=FFI):
    if kc < 5:
        return a(k, "s2", 0, 128, FFI * kc + c0, FFI * kc + c1)
    if kc < 7:
        return a(k, "s3", 0, 128, 8448 + FFI * (kc - 5) + c0, 8448 + FFI * (kc - 5) + c1)
    return a(k, "s5", 0, 128, c0, c1)
def wdq_tile(k, kc, c0=0, c1=1024):
    if kc < 16:
        return a(k, "s1", 0, 128, 1024 * kc + c0, 1024 * kc + c1)
    return a(k, "s2", 0, 128, 1024 * (kc - 16) + c0, 1024 * (kc - 16) + c1)
def hqring(k, buf, kc, c0=0, c1=128):
    base = 2816 * buf + 128 * kc
    return a(k, "s3", 0, 128, base + c0, base + c1)
def hqstage(k, kc, c0=0, c1=128):
    base = 2816 + 128 * kc
    return a(k, "s5", 0, 128, base + c0, base + c1)


def gbe_ap(k, c0, c1):
    return k.sb["f32b"].ap().bitcast(BF16)[0:128, 5120 + c0:5120 + c1]
def bvb_ap(k, c0, c1):
    return k.sb["f32b"].ap().bitcast(BF16)[0:128, 7168 + c0:7168 + c1]


def barrier(k):
    p = k.p
    names = {"sync": "bsy", "tensor": "bte", "vector": "bve",
             "scalar": "bse", "gpsimd": "bgp"}
    vals = {}
    for eng in ENGS:
        s = p.sems[names[eng]]
        p.cnt[s] += 1
        vals[eng] = p.cnt[s]
        p.ops[eng].append(lambda e, s=s: e.nop().then_inc(s, 1))
    for eng in ENGS:
        for o in ENGS:
            if o != eng:
                p.wait(eng, p.sems[names[o]], vals[o])


def mm(k, out_ap, lhsT, rhs, start=True, stop=True, tp=None):
    return k.p.inc("tensor", lambda e, o=out_ap, l=lhsT, r=rhs, s1=start,
                   s2=stop, tp=tp: e.matmul(o, l, r, start=s1, stop=s2,
                                            tile_position=tp), k.S("te"), 1)


def trans(k, out_ap, in_ap, fp32=False):
    ident = a(k, "idn") if fp32 else a(k, "idnb")
    return k.p.inc("tensor", lambda e, o=out_ap, i=in_ap, d=ident:
                   e.transpose(o, i, d), k.S("te"), 1)


def psb(k, nm, c0, c1):
    return k.ps[nm].bitcast(BF16)[:, c0:c1]


def sdma(k, dst, src, eng="sync", slow=False):
    sem = k.S("ld") if eng == "sync" else k.S("gd")
    if slow:
        return k.p.inc(eng, lambda e, d=dst, s=src: e.dma_start(
            out=d, in_=s, allow_slow_non_contiguous=True), sem, 16)
    return k.p.inc(eng, lambda e, d=dst, s=src: e.dma_start(out=d, in_=s), sem, 16)


def stdma(k, dst, src):
    return k.p.inc("sync", lambda e, d=dst, s=src: e.dma_start(out=d, in_=s),
                   k.S("st"), 16)


def vop(k, fn):
    k.p.on("vector", fn)


def vinc(k, fn):
    return k.p.inc("vector", fn, k.S("ve"), 1)


def sinc(k, fn):
    return k.p.inc("scalar", fn, k.S("se"), 1)


def wait(k, eng, sem, v=None):
    if v is None:
        v = k.p.cnt[k.S(sem)]
    k.p.wait(eng, k.S(sem), v)


def phase_consts(k):
    d = k.d
    sdma(k, a(k, "idn"), d["idn"])
    sdma(k, a(k, "idnb"), d["idnb"])
    sdma(k, a(k, "jmatb"), d["jmatb"])
    sdma(k, a(k, "sel1a", 0, 1), d["sel1a"])
    sdma(k, a(k, "sel1b", 32, 33), d["sel1b"])
    sdma(k, a(k, "sel88", 0, 88), d["sel88"])
    sdma(k, a(k, "onesf", 0, 1), d["onesf"])
    names = [f"{w}{i}" for i in (1, 2) for w in ("bq", "bk", "bv", "bo")]
    for j, nm in enumerate(names):
        sdma(k, a(k, "bias4", 0, 128, 4 * j, 4 * j + 4),
             AP(tensor=d[nm].tensor, offset=0, ap=[[1, 128], [128, 4]]), slow=True)
    sdma(k, a(k, "bff1c"), AP(tensor=d["bff1"].tensor, offset=0, ap=[[1, 128], [128, 32]]), slow=True)
    sdma(k, a(k, "bff2c"), AP(tensor=d["bff2"].tensor, offset=0, ap=[[1, 128], [128, 8]]), slow=True)
    sdma(k, a(k, "rowv", 0, 1, 0, 1), d["mask0"].rearrange("(a b) -> a b", a=1))
    wait(k, "vector", "ld")
    vinc(k, lambda e: e.memset(a(k, "onescol"), 1.0))
    wait(k, "tensor", "ld")
    wait(k, "tensor", "ve")
    mm(k, k.ps["Z1"][:, 0:1], a(k, "onesf", 0, 1), a(k, "rowv", 0, 1, 0, 1))
    wait(k, "vector", "te")
    vmc = vinc(k, lambda e: e.tensor_copy(out=a(k, "maskc", 0, 128, 0, 1),
                                          in_=k.ps["Z1"][:, 0:1]))
    wait(k, "vector", "ve", vmc)
    vop(k, lambda e: e.tensor_scalar(out=a(k, "maskc", 0, 128, 1, 2),
                                     in0=a(k, "maskc", 0, 128, 0, 1),
                                     scalar1=-1.0, scalar2=1.0,
                                     op0=ALU.mult, op1=ALU.add))
    v = vinc(k, lambda e: e.engine_nop())
    wait(k, "tensor", "ve", v)
    wait(k, "sync", "ve", v)


def build_bvb(k):
    d = k.d
    for i in (1, 2):
        v = sdma(k, a(k, "rowv", 0, 1, 0, 512), d[f"bv{i}"].rearrange("(a b) -> a b", a=1))
        wait(k, "tensor", "ld", v)
        mm(k, k.ps["Z1"][:, 0:512], a(k, "onesf", 0, 1), a(k, "rowv", 0, 1, 0, 512))
        wait(k, "vector", "te")
        col = 512 * (i - 1)
        ve = vinc(k, lambda e, col=col: e.tensor_copy(
            out=bvb_ap(k, col, col + 512), in_=k.ps["Z1"][:, 0:512]))
        wait(k, "sync", "ve", ve)
        wait(k, "tensor", "ve", ve)


def build_gbe(k, gname, bename, width):
    d = k.d
    for idx, nm in ((0, gname), (1, bename)):
        for off in range(0, width, 512):
            w = min(512, width - off)
            v = sdma(k, a(k, "rowv", 0, 1, 0, w),
                     d[nm].rearrange("(a b) -> a b", a=1)[:, off:off + w])
            wait(k, "tensor", "ld", v)
            wait(k, "tensor", "ve")
            mm(k, k.ps["Z1"][:, 0:w], a(k, "onesf", 0, 1), a(k, "rowv", 0, 1, 0, w))
            wait(k, "vector", "te")
            col = idx * 1024 + off
            ve = vinc(k, lambda e, col=col, w=w: e.tensor_copy(
                out=gbe_ap(k, col, col + w), in_=k.ps["Z1"][:, 0:w]))
            wait(k, "sync", "ve", ve)
    wait(k, "tensor", "ve")


def phase_holt_tables(k):
    p, d = k.p, k.d
    hf = k.sb["f32b"].ap()
    hb = k.sb["s6"].ap()
    V2C, G2C, W2C = 80, 592, 1104

    def V(fn):
        p.on("vector", fn)

    def VE(fn):
        v = vinc(k, fn)
        wait(k, "vector", "ve", v)
        return v

    VE(lambda e: e.memset(hf[0:1, 6:7], 10.0))
    VE(lambda e: e.reciprocal(out=hf[0:1, 4:5], in_=hf[0:1, 6:7]))
    for _ in range(2):
        VE(lambda e: e.tensor_scalar(out=hf[0:1, 7:8], in0=hf[0:1, 4:5],
                                     scalar1=-10.0, scalar2=2.0,
                                     op0=ALU.mult, op1=ALU.add))
        VE(lambda e: e.tensor_tensor(out=hf[0:1, 4:5], in0=hf[0:1, 4:5],
                                     in1=hf[0:1, 7:8], op=ALU.mult))
    VE(lambda e: e.tensor_tensor(out=hf[0:1, 5:6], in0=hf[0:1, 4:5],
                                 in1=hf[0:1, 4:5], op=ALU.mult))
    VE(lambda e: e.tensor_scalar(out=hf[0:1, 0:1], in0=hf[0:1, 4:5],
                                 scalar1=-1.0, scalar2=1.0, op0=ALU.mult, op1=ALU.add))
    VE(lambda e: e.tensor_copy(out=hf[0:1, 1:2], in_=hf[0:1, 0:1]))
    VE(lambda e: e.tensor_scalar(out=hf[0:1, 2:3], in0=hf[0:1, 5:6],
                                 scalar1=-1.0, scalar2=None, op0=ALU.mult))
    VE(lambda e: e.tensor_scalar(out=hf[0:1, 3:4], in0=hf[0:1, 5:6],
                                 scalar1=-1.0, scalar2=1.0, op0=ALU.mult, op1=ALU.add))
    VE(lambda e: e.tensor_copy(out=hf[0:1, 8:9], in_=hf[0:1, 0:1]))
    VE(lambda e: e.tensor_copy(out=hf[0:1, 9:10], in_=hf[0:1, 2:3]))
    VE(lambda e: e.tensor_copy(out=hf[0:1, 10:11], in_=hf[0:1, 1:2]))
    VE(lambda e: e.tensor_copy(out=hf[0:1, 11:12], in_=hf[0:1, 3:4]))
    vrow = vinc(k, lambda e: e.engine_nop())

    wait(k, "sync", "ve", vrow)
    sdma(k, hf[0:2, 16:18], hf[0:1, 0:4], slow=True)
    sdma(k, hf[0:2, 48:50], hf[0:1, 8:12], slow=True)
    sdma(k, hf[0:2, W2C:W2C + 1], hf[0:1, 4:6], slow=True)
    sdma(k, hf[0:2, V2C:V2C + 1], hf[0:1, 0:2], slow=True)
    vst = p.cnt[k.S("ld")]
    wait(k, "vector", "ld", vst)
    VE(lambda e: e.memset(hf[0:2, G2C:G2C + 1], 0.0))
    VE(lambda e: e.memset(hf[0:1, G2C:G2C + 1], 1.0))
    v0 = vinc(k, lambda e: e.engine_nop())
    wait(k, "tensor", "ve", v0)
    wait(k, "tensor", "ld", vst)

    ps = k.ps["C"]
    smb = k.sb["f32a"].ap().bitcast(BF16)
    def ev(dst, src):
        wait(k, "vector", "te")
        ve = vinc(k, lambda e, dst=dst, src=src: e.tensor_copy(out=dst, in_=src))
        wait(k, "tensor", "ve", ve)

    m, stage = 1, 0
    m512_stage = None
    while m < 4096:
        if m == 512:
            m512_stage = stage
        Pm = hf[0:2, 16 + 2 * stage:18 + 2 * stage]
        Mm = hf[0:2, 48 + 2 * stage:50 + 2 * stage]
        step = min(m, 4096 - m)
        for off in range(0, step, 512):
            w = min(512, step - off)
            mm(k, ps[0:2, 0:w], Mm, hf[0:2, W2C + off:W2C + off + w])
            if m + off < 2048:
                ev(hf[0:2, W2C + m + off:W2C + m + off + w], ps[0:2, 0:w])
            else:
                wait(k, "vector", "te")
                ve = vinc(k, lambda e, w=w: e.tensor_copy(
                    out=smb[0:2, 3520:3520 + w][0:2, 0:w], in_=ps[0:2, 0:w]))
                wait(k, "tensor", "ve", ve)
                wait(k, "gpsimd", "ve", ve)
                sdma(k, d["hpad"][512 + m + off:512 + m + off + w],
                     smb[0:1, 3520:3520 + w], "gpsimd")
                wait(k, "vector", "gd")
        if m < 512:
            stepv = min(m, 512 - m)
            for cc, lhs in ((G2C, Mm), (V2C, Pm)):
                mm(k, ps[0:2, 0:stepv], lhs, hf[0:2, cc:cc + stepv])
                ev(hf[0:2, cc + m:cc + m + stepv], ps[0:2, 0:stepv])
        mm(k, ps[0:2, 0:2], Mm, Pm)
        ev(hf[0:2, 18 + 2 * stage:20 + 2 * stage], ps[0:2, 0:2])
        mm(k, ps[0:2, 0:2], Pm, Mm)
        ev(hf[0:2, 50 + 2 * stage:52 + 2 * stage], ps[0:2, 0:2])
        m *= 2
        stage += 1
    c512 = 48 + 2 * (m512_stage + 1)
    V(lambda e: e.tensor_copy(out=hb[0:2, 4624:4626], in_=hf[0:2, c512:c512 + 2]))
    V(lambda e: e.tensor_copy(out=hb[0:2, 4112:4624], in_=hf[0:2, V2C:V2C + 512]))
    zr = k.sb["f32a"].ap()
    V(lambda e: e.memset(zr[0:1, 0:512], 0.0))
    vz = vinc(k, lambda e: e.engine_nop())
    wait(k, "gpsimd", "ve", vz)
    sdma(k, d["hpad"][0:512], zr[0:1, 0:512], "gpsimd")
    sdma(k, d["hpad"][4608:5120], zr[0:1, 0:512], "gpsimd")
    sdma(k, d["hpad"][512:2560], hf[0:1, W2C:W2C + 2048], "gpsimd")
    wait(k, "sync", "gd")

    hkb = k.sb["ptr"].ap()
    for dst_base in (0, 2048):
        for jsub in range(4):
            off = 512 - 127 - 128 * jsub
            v = sdma(k, hkb[0:128, 0:512],
                     AP(tensor=d["hpad"].tensor, offset=off, ap=[[1, 128], [1, 512]]))
            wait(k, "tensor", "ld", v)
            for tsub in range(4):
                mm(k, k.ps["C"][:, 128 * tsub:128 * tsub + 128], a(k, "jmatb"),
                   hkb[0:128, 128 * tsub:128 * tsub + 128])
            wait(k, "vector", "te")
            col = dst_base + 512 * jsub
            ve = vinc(k, lambda e, col=col: e.tensor_copy(
                out=hb[0:128, col:col + 512], in_=k.ps["C"][:, 0:512]))
            wait(k, "tensor", "ve", ve)
            wait(k, "sync", "ve", ve)
    V(lambda e: e.tensor_copy(out=hb[0:1, 2048:2560], in_=hf[0:1, G2C:G2C + 512]))
    for t in range(4):
        srcc = 128 * (3 - t)
        k.p.inc("tensor", lambda e, srcc=srcc: e.transpose(
            k.ps["C"][:, 0:2], hf[0:2, W2C + srcc:W2C + srcc + 128],
            a(k, "idn", 0, 2, 0, 2)), k.S("te"), 1)
        wait(k, "vector", "te")
        ve = vinc(k, lambda e: e.tensor_copy(out=hkb[0:128, 1024:1026],
                                             in_=k.ps["C"][:, 0:2]))
        wait(k, "tensor", "ve", ve)
        mm(k, k.ps["C"][:, 0:2], a(k, "jmatb"), hkb[0:128, 1024:1026])
        wait(k, "vector", "te")
        for dst0 in (4096, 4104):
            vinc(k, lambda e, dst0=dst0, t=t: e.tensor_copy(
                out=hb[0:128, dst0 + 2 * t:dst0 + 2 * t + 2], in_=k.ps["C"][:, 0:2]))
        wait(k, "tensor", "ve")
    wait(k, "gpsimd", "ve")
    sdma(k, hb[0:1, 4104:4105], hf[0:1, G2C + 511:G2C + 512], "gpsimd", slow=True)
    sdma(k, hb[0:1, 4105:4106], hf[1:2, G2C + 511:G2C + 512], "gpsimd", slow=True)
    wait(k, "vector", "gd")
    wait(k, "tensor", "gd")
    V(lambda e: e.memset(hb[0:2, 4704:5728], 0.0))
    v = vinc(k, lambda e: e.engine_nop())
    wait(k, "tensor", "ve", v)


def sprev(k, c0=0, c1=512):
    return a(k, "s6", 0, 2, 4704 + c0, 4704 + c1)
def snew(k, c0=0, c1=512):
    return a(k, "s6", 0, 2, 5216 + c0, 5216 + c1)


def phase_attn_weights(k, mi):
    d = k.d
    stage = k.sb["ptr"].ap()
    for m, w in enumerate(("wq", "wk", "wv", "wo")):
        wd = d[f"{w}{mi}"]
        for oc in range(4):
            vg = sdma(k, stage[0:128, 0:512], wd[128 * oc:128 * oc + 128, :], "gpsimd")
            wait(k, "tensor", "gd", vg)
            for ic in range(4):
                trans(k, psb(k, "D", 128 * ic, 128 * ic + 128),
                      stage[0:128, 128 * ic:128 * ic + 128])
            wait(k, "vector", "te")
            for ic in range(4):
                dst = wT(k, mi, m, ic, 128 * oc, 128 * oc + 128)
                vinc(k, lambda e, dst=dst, ic=ic: e.tensor_copy(
                    out=dst, in_=psb(k, "D", 128 * ic, 128 * ic + 128)))
            wait(k, "tensor", "ve")
            wait(k, "gpsimd", "ve")


def ln_tile(k, full_src, apply_src, out_bf, width, sq_ap=None):
    st = k.sb["stats"].ap()
    if sq_ap is None:
        sq_ap = k.sb["f32b"].ap()[0:128, 0:width]
    vsq = sinc(k, lambda e: e.activation(out=sq_ap, in_=full_src,
                                         func=AF.Square, accum_out=st[0:128, 2:3]))
    wait(k, "vector", "se", vsq)
    va = vinc(k, lambda e: e.tensor_reduce(out=st[0:128, 3:4], in_=full_src,
                                           axis=AX.X, op=ALU.add))
    vop(k, lambda e: e.tensor_scalar(out=st[0:128, 2:3], in0=st[0:128, 2:3],
                                     scalar1=1.0 / width, scalar2=None, op0=ALU.mult))
    wait(k, "vector", "ve", va)
    vc = vinc(k, lambda e: e.tensor_scalar(out=st[0:128, 0:1], in0=st[0:128, 3:4],
                                           scalar1=1.0 / width, scalar2=None,
                                           op0=ALU.mult))
    wait(k, "vector", "ve", vc)
    vd = vinc(k, lambda e: e.tensor_tensor(out=st[0:128, 3:4], in0=st[0:128, 0:1],
                                           in1=st[0:128, 0:1], op=ALU.mult))
    wait(k, "vector", "ve", vd)
    vv = vinc(k, lambda e: e.scalar_tensor_tensor(
        out=st[0:128, 2:3], in0=st[0:128, 2:3], scalar=EPS, in1=st[0:128, 3:4],
        op0=ALU.add, op1=ALU.subtract))
    wait(k, "scalar", "ve", vv)
    vsr = sinc(k, lambda e: e.activation(out=st[0:128, 3:4], in_=st[0:128, 2:3],
                                         func=AF.Sqrt))
    wait(k, "vector", "se", vsr)
    vh = vinc(k, lambda e: e.reciprocal(out=st[0:128, 1:2], in_=st[0:128, 3:4]))
    wait(k, "vector", "ve", vh)
    vij = vinc(k, lambda e: e.scalar_tensor_tensor(
        out=st[0:128, 3:4], in0=st[0:128, 1:2], scalar=st[0:128, 1:2],
        in1=st[0:128, 2:3], op0=ALU.mult, op1=ALU.mult))
    wait(k, "vector", "ve", vij)
    vk = vinc(k, lambda e: e.tensor_scalar(out=st[0:128, 3:4], in0=st[0:128, 3:4],
                                           scalar1=-0.5, scalar2=1.5,
                                           op0=ALU.mult, op1=ALU.add))
    wait(k, "vector", "ve", vk)
    vl = vinc(k, lambda e: e.tensor_tensor(out=st[0:128, 1:2], in0=st[0:128, 1:2],
                                           in1=st[0:128, 3:4], op=ALU.mult))
    wait(k, "vector", "ve", vl)
    vm = vinc(k, lambda e: e.scalar_tensor_tensor(
        out=st[0:128, 4:5], in0=st[0:128, 0:1], scalar=-1.0, in1=st[0:128, 1:2],
        op0=ALU.mult, op1=ALU.mult))
    wait(k, "scalar", "ve", vm)
    return sinc(k, lambda e: e.activation(
        out=out_bf, in_=apply_src, func=AF.Identity,
        scale=st[0:128, 1:2], bias=st[0:128, 4:5]))


def phase_xq(k):
    d = k.d
    xf = k.sb["f32a"].ap()
    tmb = k.sb["ptr"].ap()
    done = {}
    for tt in range(16):
        buf = tt % 2
        if tt >= 2:
            wait(k, "sync", "se", done[tt - 2][0])
            wait(k, "sync", "se", done[tt - 2][1])
        v = sdma(k, xf[0:128, 1024 * buf:1024 * buf + 1024],
                 d["xq"][128 * tt:128 * tt + 128, :])
        wait(k, "scalar", "ld", v)
        wait(k, "vector", "ld", v)
        vxh = ln_tile(k, xf[0:128, 1024 * buf:1024 * buf + 1024],
                      xf[0:128, 1024 * buf:1024 * buf + 512],
                      tmb[0:128, 512 * buf:512 * buf + 512], D)
        wait(k, "tensor", "se", vxh)
        for fc in range(4):
            trans(k, psb(k, "D", 128 * fc, 128 * fc + 128),
                  tmb[0:128, 512 * buf + 128 * fc:512 * buf + 128 * fc + 128])
        wait(k, "scalar", "te")
        for fc in range(4):
            dst = xqfm(k, fc, 128 * tt, 128 * tt + 128)
            sinc(k, lambda e, dst=dst, fc=fc: e.copy(
                out=dst, in_=psb(k, "D", 128 * fc, 128 * fc + 128)))
        wait(k, "tensor", "se")
        done[tt] = (vxh, k.p.cnt[k.S("se")])
    wait(k, "tensor", "se")
    proj_fm(k, 1, 0, lambda fc, c0, c1: xqfm(k, fc, c0, c1), NR, lambda dc, c0, c1: qfm(k, dc, c0, c1))


def proj_fm(k, mi, m, src_fn, ntok, dst_fn):
    for dc in range(4):
        for t0 in range(0, ntok, 512):
            ps = k.ps["C"] if (t0 // 512) % 2 == 0 else k.ps["D"]
            for fc in range(4):
                mm(k, ps[:, 0:512], wT(k, mi, m, fc, 128 * dc, 128 * dc + 128),
                   src_fn(fc, t0, t0 + 512), start=(fc == 0), stop=(fc == 3))
            wait(k, "scalar", "te")
            bj = 4 * (4 * (mi - 1) + m) + dc
            dst = dst_fn(dc, t0, t0 + 512)
            vse = sinc(k, lambda e, dst=dst, ps=ps, bj=bj: e.activation(
                out=dst, in_=ps[:, 0:512], func=AF.Identity,
                bias=a(k, "bias4", 0, 128, bj, bj + 1)))
            wait(k, "tensor", "se", vse)


def phase_stream_x(k):
    p, d = k.p, k.d
    xf = k.sb["f32a"].ap()
    hb = k.sb["s6"].ap()
    hst = k.sb["f32b"].ap().bitcast(BF16)

    for c in range(NCH):
        blk = c % 2
        for tt in range(4):
            ti = 4 * c + tt
            buf = ti % 2
            v = sdma(k, xf[0:128, 1024 * buf:1024 * buf + 1024],
                     d["x"][128 * ti:128 * ti + 128, :])
            wait(k, "scalar", "ld", v)
            wait(k, "vector", "ld", v)
            vxh = ln_tile(k, xf[0:128, 1024 * buf:1024 * buf + 1024],
                          xf[0:128, 1024 * buf:1024 * buf + 512],
                          xtm_ring(k, blk, tt), D)
            wait(k, "tensor", "se", vxh)
            for fc in range(4):
                trans(k, psb(k, "Z2", 128 * fc, 128 * fc + 128),
                      xtm_ring(k, blk, tt, 128 * fc, 128 * fc + 128))
            wait(k, "scalar", "te")
            for fc in range(4):
                dst = xfm_ring(k, blk, fc, 128 * tt, 128 * tt + 128)
                sinc(k, lambda e, dst=dst, fc=fc: e.copy(
                    out=dst, in_=psb(k, "Z2", 128 * fc, 128 * fc + 128)))
            wait(k, "tensor", "se")
            wait(k, "sync", "se")
        for dc in range(4):
            ps = k.ps["C"] if dc % 2 == 0 else k.ps["D"]
            for fc in range(4):
                mm(k, ps[:, 0:512], wT(k, 1, 1, fc, 128 * dc, 128 * dc + 128),
                   xfm_ring(k, blk, fc, 0, 512), start=(fc == 0), stop=(fc == 3))
            wait(k, "scalar", "te")
            bj = 4 * 1 + dc
            dst = kfm(k, dc, 512 * c, 512 * c + 512)
            vse = sinc(k, lambda e, dst=dst, ps=ps, bj=bj: e.activation(
                out=dst, in_=ps[:, 0:512], func=AF.Identity,
                bias=a(k, "bias4", 0, 128, bj, bj + 1)))
            wait(k, "tensor", "se", vse)
        for tt in range(4):
            ps = k.ps["C"] if tt % 2 == 0 else k.ps["D"]
            for fc in range(4):
                mm(k, ps[:, 0:512], xfm_ring(k, blk, fc, 128 * tt, 128 * tt + 128),
                   wT(k, 1, 2, fc, 0, 512), start=(fc == 0), stop=(fc == 3))
            wait(k, "vector", "te")
            dst = vtm(k, 4 * c + tt)
            vve = vinc(k, lambda e, dst=dst, ps=ps: e.scalar_tensor_tensor(
                out=dst, in0=ps[:, 0:512], scalar=1.0,
                in1=bvb_ap(k, 0, 512), op0=ALU.mult, op1=ALU.add))
            wait(k, "tensor", "ve", vve)
        ut0 = 4104 if c == 0 else 4096
        for js in range(4):
            mm(k, k.ps["Z2"][0:2, 0:512], hb[0:128, ut0 + 2 * js:ut0 + 2 * js + 2],
               xtm_ring(k, blk, js, 0, 512), start=(js == 0), stop=False)
        mm(k, k.ps["Z2"][0:2, 0:512], hb[0:2, 4624:4626], sprev(k),
           start=False, stop=True)
        wait(k, "vector", "te")
        vs = vinc(k, lambda e: e.tensor_copy(out=snew(k), in_=k.ps["Z2"][0:2, 0:512]))
        wait(k, "tensor", "ve", vs)
        mt0 = 2048 if c == 0 else 0
        for fcq in range(4):
            ps = k.ps["Z0"] if fcq % 2 == 0 else k.ps["Z1"]
            for js in range(4):
                mm(k, ps[:, 0:512],
                   xtm_ring(k, blk, js, 128 * fcq, 128 * fcq + 128),
                   hb[0:128, mt0 + 512 * js:mt0 + 512 * js + 512],
                   start=(js == 0), stop=False)
            mm(k, ps[:, 0:512], sprev(k, 128 * fcq, 128 * fcq + 128),
               hb[0:2, 4112:4624], start=False, stop=True)
            wait(k, "scalar", "te")
            vse = sinc(k, lambda e, ps=ps, fcq=fcq: e.copy(
                out=hst[0:128, 4096 + 512 * fcq:4096 + 512 * fcq + 512],
                in_=ps[:, 0:512]))
            wait(k, "sync", "se", vse)
            sdma(k, d["hsD"][128 * fcq:128 * fcq + 128, 512 * c:512 * c + 512],
                 hst[0:128, 4096 + 512 * fcq:4096 + 512 * fcq + 512])
            wait(k, "tensor", "se", vse)
        wait(k, "vector", "te")
        vsp = vinc(k, lambda e: e.tensor_copy(out=sprev(k), in_=snew(k)))
        wait(k, "tensor", "ve", vsp)
        wait(k, "sync", "ve")
        wait(k, "sync", "ld")
        wait(k, "scalar", "ld")
    wait(k, "sync", "ld")


def phase_attention(k, mi):
    p = k.p
    build_gbe(k, f"g{mi}", f"be{mi}", 512)
    pt = k.sb["ptr"].ap()
    smf = k.sb["f32a"].ap()
    smb = k.sb["f32a"].ap().bitcast(BF16)
    bcst = k.sb["f32b"].ap().bitcast(BF16)
    res_fn = (lambda dc, c0, c1: xqfm(k, dc, c0, c1)) if mi == 1 else \
             (lambda dc, c0, c1: hso(k, dc, c0, c1))
    xdst = x1t if mi == 1 else x2t
    zst = k.sb["f32b"].ap().bitcast(BF16)
    ost = lambda hp, c0=0, c1=512: bcst[0:128, 2048 + 512 * hp + c0:2048 + 512 * hp + c1]

    cut = ATTCUT if mi == ATTMI else None
    nqc = 1 if cut else 4
    nhp = 1 if cut in ("S", "AV", "SUMS", "BCAST") else 4
    for qc in range(nqc):
        vo_prev = None
        for hp in range(nhp):
            exp_se = {}
            avsum_te = {}
            S_te = {}

            def psS(kt):
                if kt % 2 == 0:
                    return k.ps["A"][:, 0:512], k.ps["A"][:, 512:1024]
                return k.ps["C"][:, 0:512], k.ps["D"][:, 0:512]

            def emit_S(kt, hp=hp, qc=qc):
                if kt >= 2:
                    wait(k, "tensor", "se", exp_se[kt - 2][1])
                pa, pb_ = psS(kt)
                mm(k, pa, kfm(k, hp, 128 * kt, 128 * kt + 128)[0:64, :],
                   qfm(k, hp, 512 * qc, 512 * qc + 512)[0:64, :],
                   start=True, stop=True, tp=(0, 0))
                mm(k, pb_, kfm(k, hp, 128 * kt, 128 * kt + 128)[64:128, :],
                   qfm(k, hp, 512 * qc, 512 * qc + 512)[64:128, :],
                   start=True, stop=True, tp=(64, 0))
                S_te[kt] = p.cnt[k.S("te")]

            def emit_exp(kt):
                pa, pb_ = psS(kt)
                pb = kt % 2
                wait(k, "scalar", "te", S_te[kt])
                if kt >= 2 and (kt - 2) in avsum_te:
                    wait(k, "scalar", "te", avsum_te[kt - 2])
                v0 = sinc(k, lambda e, pa=pa, pb=pb: e.activation(
                    out=pt[0:128, 1024 * pb:1024 * pb + 512], in_=pa,
                    func=AF.Exp, scale=0.125))
                v1 = sinc(k, lambda e, pb_=pb_, pb=pb: e.activation(
                    out=pt[0:128, 1024 * pb + 512:1024 * pb + 1024], in_=pb_,
                    func=AF.Exp, scale=0.125))
                exp_se[kt] = (v0, v1)

            def emit_AV(kt, hp=hp):
                pb = kt % 2
                if kt == 0 and vo_prev is not None:
                    wait(k, "tensor", "ve", vo_prev)
                wait(k, "tensor", "se", exp_se[kt][0])
                mm(k, k.ps["Z0"][0:64, :], vtm(k, kt, 128 * hp, 128 * hp + 64),
                   pt[0:128, 1024 * pb:1024 * pb + 512],
                   start=(kt == 0), stop=(kt == 31), tp=(0, 0))
                wait(k, "tensor", "se", exp_se[kt][1])
                mm(k, k.ps["Z1"][64:128, :], vtm(k, kt, 128 * hp + 64, 128 * hp + 128),
                   pt[0:128, 1024 * pb + 512:1024 * pb + 1024],
                   start=(kt == 0), stop=(kt == 31), tp=(0, 64))
                mm(k, k.ps["Z2"][0:1, :], a(k, "onescol"),
                   pt[0:128, 1024 * pb:1024 * pb + 512],
                   start=(kt == 0), stop=(kt == 31), tp=(0, 0))
                mm(k, k.ps["Z3"][0:1, :], a(k, "onescol"),
                   pt[0:128, 1024 * pb + 512:1024 * pb + 1024],
                   start=(kt == 0), stop=(kt == 31), tp=(0, 0))
                avsum_te[kt] = p.cnt[k.S("te")]

            emit_S(0)
            emit_exp(0)
            emit_S(1)
            emit_exp(1)
            for kt in range(2, 32):
                emit_S(kt)
                if cut != "S":
                    emit_AV(kt - 2)
                emit_exp(kt)
            if cut != "S":
                emit_AV(30)
                emit_AV(31)
            if cut in ("S", "AV"):
                return
            wait(k, "vector", "te")
            fbf = k.sb["f32b"].ap()
            vinc(k, lambda e: e.tensor_copy(out=fbf[0:1, 0:512],
                                            in_=k.ps["Z2"][0:1, :]))
            vinc(k, lambda e: e.tensor_copy(out=fbf[0:1, 512:1024],
                                            in_=k.ps["Z3"][0:1, :]))
            vop(k, lambda e: e.reciprocal(out=smf[0:1, 0:1024],
                                          in_=fbf[0:1, 0:1024]))
            vop(k, lambda e: e.tensor_tensor(
                out=smf[0:1, 1024:2048], in0=fbf[0:1, 0:1024],
                in1=smf[0:1, 0:1024], op=ALU.mult))
            vop(k, lambda e: e.tensor_scalar(
                out=smf[0:1, 1024:2048], in0=smf[0:1, 1024:2048],
                scalar1=-1.0, scalar2=2.0, op0=ALU.mult, op1=ALU.add))
            vop(k, lambda e: e.tensor_tensor(
                out=bcst[0:1, 0:1024], in0=smf[0:1, 0:1024],
                in1=smf[0:1, 1024:2048], op=ALU.mult))
            vrec = vinc(k, lambda e: e.engine_nop())
            if cut == "SUMS":
                return
            wait(k, "tensor", "ve", vrec)
            mm(k, k.ps["Z2"][0:64, 0:512], a(k, "sel1a", 0, 1, 0, 64),
               bcst[0:1, 0:512], start=True, stop=True, tp=(0, 0))
            mm(k, k.ps["Z2"][64:128, 0:512], a(k, "sel1a", 0, 1, 0, 64),
               bcst[0:1, 512:1024], start=True, stop=True, tp=(0, 64))
            wait(k, "vector", "te")
            vb = vinc(k, lambda e: e.tensor_copy(out=bcst[0:128, 4096:4608],
                                                 in_=k.ps["Z2"][:, 0:512]))
            vinc(k, lambda e, hp=hp: e.tensor_tensor(
                out=ost(hp)[0:64, :], in0=k.ps["Z0"][0:64, :],
                in1=bcst[0:64, 4096:4608], op=ALU.mult))
            vo = vinc(k, lambda e, hp=hp: e.tensor_tensor(
                out=ost(hp)[64:128, :], in0=k.ps["Z1"][64:128, :],
                in1=bcst[64:128, 4096:4608], op=ALU.mult))
            vo_prev = vo
        if cut == "BCAST":
            return
        if cut == "HP":
            continue
        wait(k, "tensor", "ve", vo_prev)
        for dc in range(4):
            ps = k.ps["C"] if dc % 2 == 0 else k.ps["D"]
            for hp in range(4):
                mm(k, ps[:, 0:512], wT(k, mi, 3, hp, 128 * dc, 128 * dc + 128),
                   ost(hp), start=(hp == 0), stop=(hp == 3))
            wait(k, "vector", "te")
            bj = 4 * (4 * (mi - 1) + 3) + dc
            res = res_fn(dc, 512 * qc, 512 * qc + 512)
            vz = vinc(k, lambda e, ps=ps, bj=bj, res=res, dc=dc: e.scalar_tensor_tensor(
                out=zst[0:128, 512 * dc:512 * dc + 512], in0=ps[:, 0:512],
                scalar=a(k, "bias4", 0, 128, bj, bj + 1), in1=res,
                op0=ALU.add, op1=ALU.add))
            wait(k, "tensor", "ve", vz)
        if cut == "OPROJ":
            return
        for sub in range(4):
            tq = 4 * qc + sub
            for dc in range(4):
                trans(k, psb(k, "C", 128 * dc, 128 * dc + 128),
                      zst[0:128, 512 * dc + 128 * sub:512 * dc + 128 * sub + 128])
            wait(k, "vector", "te")
            wait(k, "scalar", "te")
            vap = ln_tile(k, psb(k, "C", 0, 512), psb(k, "C", 0, 512),
                          bcst[0:128, 4608:5120], 512,
                          sq_ap=k.sb["f32a"].ap()[0:128, 0:512])
            wait(k, "vector", "se", vap)
            vop(k, lambda e: e.tensor_tensor(
                out=bcst[0:128, 4608:5120], in0=bcst[0:128, 4608:5120],
                in1=gbe_ap(k, 0, 512), op=ALU.mult))
            dst = xdst(k, tq)
            vx = vinc(k, lambda e, dst=dst: e.tensor_tensor(
                out=dst, in0=bcst[0:128, 4608:5120],
                in1=gbe_ap(k, 1024, 1536), op=ALU.add))
            wait(k, "tensor", "ve", vx)
            wait(k, "tensor", "se", vap)


def phase_merge_hso(k):
    d = k.d
    hm = k.sb["f32b"].ap().bitcast(BF16)
    for fc in range(4):
        v0 = sdma(k, hm[0:128, 0:2048],
                  d["hsD"][128 * fc:128 * fc + 128, 0:2048])
        v1 = sdma(k, hm[0:128, 2048:4096],
                  d["hsD"][128 * fc:128 * fc + 128, 2048:4096])
        wait(k, "vector", "ld", v1)
        vop(k, lambda e, fc=fc: e.tensor_scalar(
            out=hso(k, fc), in0=hm[0:128, 2048:4096],
            scalar1=a(k, "maskc", 0, 128, 1, 2), scalar2=None, op0=ALU.mult))
        vm = vinc(k, lambda e, fc=fc: e.scalar_tensor_tensor(
            out=hso(k, fc), in0=hm[0:128, 0:2048],
            scalar=a(k, "maskc", 0, 128, 0, 1), in1=hso(k, fc),
            op0=ALU.mult, op1=ALU.add))
        wait(k, "sync", "ve", vm)


def phase_proj2(k):
    p, d = k.p, k.d
    for c in range(NCH):
        buf = c % 2
        vs = []
        for fc in range(4):
            vs.append(sdma(k, hsring(k, buf, fc),
                           d["hsD"][128 * fc:128 * fc + 128, 512 * c:512 * c + 512]))
        wait(k, "tensor", "ld", vs[-1])
        for dc in range(4):
            ps = k.ps["C"] if dc % 2 == 0 else k.ps["D"]
            for fc in range(4):
                mm(k, ps[:, 0:512], wT(k, 2, 1, fc, 128 * dc, 128 * dc + 128),
                   hsring(k, buf, fc), start=(fc == 0), stop=(fc == 3))
            wait(k, "scalar", "te")
            bj = 4 * 5 + dc
            dst = kfm(k, dc, 512 * c, 512 * c + 512)
            vse = sinc(k, lambda e, dst=dst, ps=ps, bj=bj: e.activation(
                out=dst, in_=ps[:, 0:512], func=AF.Identity,
                bias=a(k, "bias4", 0, 128, bj, bj + 1)))
            wait(k, "tensor", "se", vse)
        for tt in range(4):
            ps = k.ps["C"] if tt % 2 == 0 else k.ps["D"]
            for fc in range(4):
                mm(k, ps[:, 0:512], hsring(k, buf, fc, 128 * tt, 128 * tt + 128),
                   wT(k, 2, 2, fc, 0, 512), start=(fc == 0), stop=(fc == 3))
            wait(k, "vector", "te")
            dst = vtm(k, 4 * c + tt)
            vve = vinc(k, lambda e, dst=dst, ps=ps: e.scalar_tensor_tensor(
                out=dst, in0=ps[:, 0:512], scalar=1.0,
                in1=bvb_ap(k, 512, 1024), op0=ALU.mult, op1=ALU.add))
            wait(k, "tensor", "ve", vve)
        wait(k, "sync", "ve")
        wait(k, "sync", "se")
    proj_fm(k, 2, 0, lambda fc, c0, c1: hso(k, fc, c0, c1), NR, lambda dc, c0, c1: qfm(k, dc, c0, c1))


def phase_ffw_weights(k):
    d = k.d
    stage = k.sb["ptr"].ap()
    for wname, tname, nrow, ncol in (("wff1", "wff1T", DFF, D),
                                     ("wff2", "wff2T", D, DFF)):
        for r in range(nrow // 128):
            for cg in range(ncol // 1024):
                vg = sdma(k, stage[0:128, 0:1024],
                          d[wname][128 * r:128 * r + 128, 1024 * cg:1024 * cg + 1024],
                          "gpsimd")
                wait(k, "tensor", "gd", vg)
                for ic in range(8):
                    trans(k, psb(k, "D", 128 * (ic % 4), 128 * (ic % 4) + 128),
                          stage[0:128, 128 * ic:128 * ic + 128])
                    if ic % 4 == 3:
                        wait(k, "vector", "te")
                        for j in range(4):
                            icj = ic - 3 + j
                            vinc(k, lambda e, icj=icj, j=j: e.tensor_copy(
                                out=stage[0:128, 1024 + 128 * icj:1152 + 128 * icj],
                                in_=psb(k, "D", 128 * j, 128 * j + 128)))
                        wait(k, "tensor", "ve")
                ve = k.p.cnt[k.S("ve")]
                wait(k, "sync", "ve", ve)
                sdma(k, d[tname][128 * r:128 * r + 128, 1024 * cg:1024 * cg + 1024],
                     stage[0:128, 1024:2048])
                wait(k, "gpsimd", "ld")
    wait(k, "sync", "ld")


def phase_attn_ffn(k):
    p, d = k.p, k.d
    build_gbe(k, "gff", "beff", 1024)
    for tq in range(16):
        for half, xsrc in ((0, x1t), (1, x2t)):
            for f in range(4):
                trans(k, psb(k, "D", 128 * f, 128 * f + 128),
                      xsrc(k, tq, 128 * f, 128 * f + 128))
            wait(k, "scalar", "te")
            for f in range(4):
                fc = 4 * half + f
                dst = yfm(k, fc, 128 * tq, 128 * tq + 128)
                sinc(k, lambda e, dst=dst, f=f: e.copy(
                    out=dst, in_=psb(k, "D", 128 * f, 128 * f + 128)))
            wait(k, "tensor", "se")
    wait(k, "vector", "se")
    wait(k, "sync", "se")

    w1r = k.sb["s3"].ap()
    w2r = k.sb["s3"].ap()
    hst = k.sb["f32b"].ap().bitcast(BF16)
    xf = k.sb["f32a"].ap()
    for tb in range(4):
        h1_te = {}
        for fb in range(32):
            buf = fb % 2
            if fb >= 2:
                wait(k, "sync", "te", h1_te[fb - 2])
            v = sdma(k, w1r[0:128, 1024 * buf:1024 * buf + 1024],
                     d["wff1T"][128 * fb:128 * fb + 128, :])
            wait(k, "tensor", "ld", v)
            ps = k.ps["C"] if fb % 2 == 0 else k.ps["D"]
            for yc in range(8):
                mm(k, ps[:, 0:512], w1r[0:128, 1024 * buf + 128 * yc:1024 * buf + 128 * yc + 128],
                   yfm(k, yc, 512 * tb, 512 * tb + 512),
                   start=(yc == 0), stop=(yc == 7))
            h1_te[fb] = k.p.cnt[k.S("te")]
            wait(k, "scalar", "te")
            dst = h1sb(k, fb)
            vse = sinc(k, lambda e, dst=dst, ps=ps, fb=fb: e.activation(
                out=dst, in_=ps[:, 0:512], func=AF.Relu,
                bias=a(k, "bff1c", 0, 128, fb, fb + 1)))
            wait(k, "tensor", "se", vse)
            wait(k, "sync", "se", vse)
        h2_te = {}
        for oc in range(8):
            buf = oc % 2
            if oc >= 2:
                wait(k, "sync", "te", h2_te[oc - 2])
            v = sdma(k, w2r[0:128, 2048 + 4096 * buf:2048 + 4096 * buf + 4096],
                     d["wff2T"][128 * oc:128 * oc + 128, :])
            wait(k, "tensor", "ld", v)
            ps = k.ps["C"] if oc % 2 == 0 else k.ps["D"]
            for ffc in range(32):
                mm(k, ps[:, 0:512],
                   w2r[0:128, 2048 + 4096 * buf + 128 * ffc:2048 + 4096 * buf + 128 * ffc + 128],
                   h1sb(k, ffc), start=(ffc == 0), stop=(ffc == 31))
            h2_te[oc] = k.p.cnt[k.S("te")]
            wait(k, "scalar", "te")
            vse = sinc(k, lambda e, ps=ps, oc=oc: e.activation(
                out=hst[0:128, 4096:4608], in_=ps[:, 0:512], func=AF.Identity,
                bias=a(k, "bff2c", 0, 128, oc, oc + 1)))
            wait(k, "tensor", "se", vse)
            for sub in range(4):
                trans(k, psb(k, "Z0", 128 * sub, 128 * sub + 128),
                      hst[0:128, 4096 + 128 * sub:4096 + 128 * sub + 128])
            wait(k, "vector", "te")
            for sub in range(4):
                tq = 4 * tb + sub
                xs = x1t(k, tq, 128 * oc, 128 * oc + 128) if oc < 4 else \
                     x2t(k, tq, 128 * (oc - 4), 128 * (oc - 4) + 128)
                dst = z3t(k, tq, 128 * oc, 128 * oc + 128)
                vinc(k, lambda e, dst=dst, xs=xs, sub=sub: e.tensor_tensor(
                    out=dst, in0=psb(k, "Z0", 128 * sub, 128 * sub + 128),
                    in1=xs, op=ALU.add))
            wait(k, "tensor", "ve")
            wait(k, "scalar", "ve")
    for tq in range(16):
        v = sdma(k, xf[0:128, 0:1024], d["xq"][128 * tq:128 * tq + 128, :])
        vap = ln_tile(k, z3t(k, tq), z3t(k, tq), hst[0:128, 4096:5120], D)
        wait(k, "vector", "se", vap)
        vop(k, lambda e: e.tensor_tensor(
            out=hst[0:128, 4096:5120], in0=hst[0:128, 4096:5120],
            in1=gbe_ap(k, 0, 1024), op=ALU.mult))
        vop(k, lambda e: e.tensor_tensor(
            out=hst[0:128, 4096:5120], in0=hst[0:128, 4096:5120],
            in1=gbe_ap(k, 1024, 2048), op=ALU.add))
        wait(k, "vector", "ld", v)
        dst = z3t(k, tq)
        vx = vinc(k, lambda e, dst=dst: e.tensor_tensor(
            out=dst, in0=hst[0:128, 4096:5120], in1=xf[0:128, 0:1024], op=ALU.add))
        wait(k, "sync", "ve", vx)
        wait(k, "scalar", "ve", vx)


def quant_ladder(k, zsrc_f32, bca_ps, bcr_ps, width, out_bf):
    p = k.p
    fbap = k.sb["f32b"].ap()
    fbb = k.sb["f32b"].ap().bitcast(BF16)
    z = fbap[0:128, 2304:2304 + width]
    u = fbap[0:128, 2816:2816 + width]
    sg = fbb[0:128, 0:width]
    gslot = lambda j: fbb[0:128, 512 * (j + 1):512 * (j + 1) + width]
    t2 = fbb[0:128, 4096:4096 + width]
    vz = vinc(k, lambda e: e.tensor_tensor(out=z, in0=zsrc_f32, in1=bca_ps,
                                           op=ALU.mult))
    wait(k, "scalar", "ve", vz)
    sinc(k, lambda e: e.activation(out=u, in_=z, func=AF.Abs))
    vsg = sinc(k, lambda e: e.activation(out=sg, in_=z, func=AF.Sign))
    wait(k, "vector", "se", vsg)
    t0, tb = gslot(0), gslot(1)
    vop(k, lambda e: e.tensor_scalar(out=t0, in0=u, scalar1=QMIDS[0], scalar2=None,
                                     op0=ALU.is_gt))
    for i in (1, 2, 3):
        vop(k, lambda e, i=i: e.scalar_tensor_tensor(
            out=t0, in0=u, scalar=QMIDS[i], in1=t0, op0=ALU.is_gt, op1=ALU.add))
    vop(k, lambda e: e.tensor_scalar(out=tb, in0=u, scalar1=QMIDS[6], scalar2=None,
                                     op0=ALU.is_gt))
    for i in (6, 5, 4):
        vop(k, lambda e, i=i: e.scalar_tensor_tensor(
            out=tb, in0=u, scalar=QMIDS[i], in1=tb, op0=ALU.is_gt, op1=ALU.add))
    vop(k, lambda e: e.scalar_tensor_tensor(
        out=t0, in0=tb, scalar=2.0, in1=t0, op0=ALU.mult, op1=ALU.add))
    vop(k, lambda e: e.tensor_tensor(out=t0, in0=t0, in1=sg, op=ALU.mult))
    return vinc(k, lambda e: e.tensor_tensor(out=out_bf, in0=t0, in1=bcr_ps,
                                             op=ALU.mult))


def quant_scales(k, src_f32, width, nblk):
    p = k.p
    sm = k.sb["f32a"].ap()
    bm = sm[0:128, 1408:1408 + nblk]
    rr = sm[0:128, 1496:1496 + nblk]
    rh = sm[0:128, 1584:1584 + nblk]
    tmp = sm[0:128, 1672:1672 + nblk]
    stg = k.sb["ptr"].ap()
    rhiS = stg[0:128, 1536:1536 + nblk]
    rhhiS = stg[0:128, 1664:1664 + nblk]
    rloS = stg[0:128, 1792:1792 + nblk]
    rhloS = stg[0:128, 1920:1920 + nblk]
    v1 = vinc(k, lambda e: e.tensor_reduce(
        out=bm, in_=src_f32.rearrange("p (g w) -> p g w", w=16),
        axis=AX.X, op=ALU.max, apply_absolute_value=True))
    wait(k, "vector", "ve", v1)
    v2 = vinc(k, lambda e: e.tensor_scalar(out=bm, in0=bm, scalar1=6e-12,
                                           scalar2=None, op0=ALU.max))
    wait(k, "vector", "ve", v2)
    v3 = vinc(k, lambda e: e.reciprocal(out=rr, in_=bm))
    vop(k, lambda e: e.tensor_scalar(out=rh, in0=bm, scalar1=1.0 / 12.0,
                                     scalar2=None, op0=ALU.mult))
    wait(k, "vector", "ve", v3)
    v4 = vinc(k, lambda e: e.tensor_tensor(out=tmp, in0=rr, in1=bm, op=ALU.mult))
    wait(k, "vector", "ve", v4)
    v5 = vinc(k, lambda e: e.tensor_scalar(out=tmp, in0=tmp, scalar1=-1.0,
                                           scalar2=2.0, op0=ALU.mult, op1=ALU.add))
    wait(k, "vector", "ve", v5)
    v6 = vinc(k, lambda e: e.tensor_tensor(out=rr, in0=rr, in1=tmp, op=ALU.mult))
    wait(k, "vector", "ve", v6)
    v7 = vinc(k, lambda e: e.tensor_scalar(out=rr, in0=rr, scalar1=6.0,
                                           scalar2=None, op0=ALU.mult))
    vop(k, lambda e: e.tensor_copy(out=rhhiS, in_=rh))
    wait(k, "vector", "ve", v7)
    v8 = vinc(k, lambda e: e.tensor_copy(out=rhiS, in_=rr))
    vop(k, lambda e: e.tensor_tensor(out=bm, in0=rh, in1=rhhiS, op=ALU.subtract))
    wait(k, "vector", "ve", v8)
    v9 = vinc(k, lambda e: e.tensor_tensor(out=tmp, in0=rr, in1=rhiS,
                                           op=ALU.subtract))
    vop(k, lambda e: e.tensor_copy(out=rhloS, in_=bm))
    wait(k, "vector", "ve", v9)
    vr = vinc(k, lambda e: e.tensor_copy(out=rloS, in_=tmp))
    wait(k, "tensor", "ve", vr)
    pD = k.ps["D"].bitcast(BF16)
    trans(k, pD[0:nblk, 0:128], rhiS)
    trans(k, pD[0:nblk, 128:256], rhhiS)
    trans(k, pD[0:nblk, 256:384], rloS)
    trans(k, pD[0:nblk, 384:512], rhloS)
    wait(k, "vector", "te")
    smb = k.sb["f32a"].ap().bitcast(BF16)
    vinc(k, lambda e: e.tensor_copy(out=smb[0:nblk, 3776:3904][0:nblk, 0:128],
                                    in_=pD[0:nblk, 0:128]))
    vinc(k, lambda e: e.tensor_copy(out=smb[0:nblk, 3904:4032][0:nblk, 0:128],
                                    in_=pD[0:nblk, 128:256]))
    vinc(k, lambda e: e.tensor_copy(out=smb[0:nblk, 3520:3648][0:nblk, 0:128],
                                    in_=pD[0:nblk, 256:384]))
    v = vinc(k, lambda e: e.tensor_copy(out=smb[0:nblk, 3648:3776][0:nblk, 0:128],
                                        in_=pD[0:nblk, 384:512]))
    wait(k, "tensor", "ve", v)
    return v


def phase_ffn_weights_quant(k):
    p, d = k.p, k.d
    fa = k.sb["f32a"].ap()
    fb = k.sb["f32b"].ap()
    stage_in = fb[0:128, 0:1024]
    smb = k.sb["f32a"].ap().bitcast(BF16)
    outb = k.sb["ptr"].ap()
    for wname in ("wgate", "wup"):
        for r in range(FFI // 128):
            v = sdma(k, stage_in, d[wname][128 * r:128 * r + 128, :])
            wait(k, "vector", "ld", v)
            wait(k, "scalar", "ld", v)
            vr = quant_scales(k, stage_in, 1024, 64)
            wait(k, "tensor", "ve", vr)
            for ic in range(8):
                trans(k, k.ps["Z2"][:, 128 * (ic % 4):128 * (ic % 4) + 128],
                      stage_in[:, 128 * ic:128 * ic + 128], fp32=True)
                if ic % 4 == 3:
                    wait(k, "vector", "te")
                    for j in range(4):
                        icj = ic - 3 + j
                        vinc(k, lambda e, icj=icj, j=j: e.tensor_copy(
                            out=fa[0:128, 128 * icj:128 * icj + 128],
                            in_=k.ps["Z2"][:, 128 * j:128 * j + 128]))
                    wait(k, "tensor", "ve")
            for ic in range(8):
                mm(k, k.ps["A"][:, 128 * ic:128 * ic + 128],
                   a(k, "sel88", 0, 64, 128 * ic, 128 * ic + 128),
                   smb[0:64, 3776:3904][0:64, 0:128], start=True, stop=False)
                mm(k, k.ps["A"][:, 128 * ic:128 * ic + 128],
                   a(k, "sel88", 0, 64, 128 * ic, 128 * ic + 128),
                   smb[0:64, 3520:3648][0:64, 0:128], start=False, stop=True)
                psr = k.ps["Z0"] if ic < 4 else k.ps["Z1"]
                mm(k, psr[:, 128 * (ic % 4):128 * (ic % 4) + 128],
                   a(k, "sel88", 0, 64, 128 * ic, 128 * ic + 128),
                   smb[0:64, 3904:4032][0:64, 0:128], start=True, stop=False)
                mm(k, psr[:, 128 * (ic % 4):128 * (ic % 4) + 128],
                   a(k, "sel88", 0, 64, 128 * ic, 128 * ic + 128),
                   smb[0:64, 3648:3776][0:64, 0:128], start=False, stop=True)
            wait(k, "vector", "te")
            quant_ladder(k, fa[0:128, 0:512], k.ps["A"][:, 0:512],
                         k.ps["Z0"][:, 0:512], 512, outb[0:128, 0:512])
            vq = quant_ladder(k, fa[0:128, 512:1024], k.ps["A"][:, 512:1024],
                              k.ps["Z1"][:, 0:512], 512, outb[0:128, 512:1024])
            wait(k, "sync", "ve", vq)
            wait(k, "tensor", "ve", vq)
            wait(k, "scalar", "ve", vq)
            tile_fn = wgq_tile if wname == "wgate" else wuq_tile
            for ic in range(8):
                dst = tile_fn(k, ic, 128 * r, 128 * r + 128)
                sinc(k, lambda e, dst=dst, ic=ic: e.copy(
                    out=dst, in_=outb[0:128, 128 * ic:128 * ic + 128]))
            wait(k, "vector", "se")
            wait(k, "vector", "ld")
            wait(k, "scalar", "ld")
        wait(k, "sync", "ld")


def phase_wdown_quant(k):
    p, d = k.p, k.d
    fa = k.sb["f32a"].ap()
    fb = k.sb["f32b"].ap()
    smb = k.sb["f32a"].ap().bitcast(BF16)
    outb = k.sb["ptr"].ap()
    for r in range(D // 128):
        for hf2 in range(2):
            v = sdma(k, fb[0:128, 0:1408],
                     d["wdown"][128 * r:128 * r + 128, 1408 * hf2:1408 * hf2 + 1408])
            wait(k, "vector", "ld", v)
            wait(k, "scalar", "ld", v)
            vr = quant_scales(k, fb[0:128, 0:1408], 1408, 88)
            wait(k, "tensor", "ve", vr)
            for ic in range(11):
                trans(k, k.ps["Z2"][:, 128 * (ic % 4):128 * (ic % 4) + 128],
                      fb[0:128, 128 * ic:128 * ic + 128], fp32=True)
                if ic % 4 == 3 or ic == 10:
                    wait(k, "vector", "te")
                    for j in range(ic % 4 + 1):
                        icj = ic - (ic % 4) + j
                        vinc(k, lambda e, icj=icj, j=j: e.tensor_copy(
                            out=fa[0:128, 128 * icj:128 * icj + 128],
                            in_=k.ps["Z2"][:, 128 * j:128 * j + 128]))
                    wait(k, "tensor", "ve")
            for ic in range(11):
                if ic < 8:
                    psx, col = k.ps["A"], 128 * ic
                    psy, coly = (k.ps["Z0"], 128 * ic) if ic < 4 else (k.ps["Z1"], 128 * (ic - 4))
                else:
                    psx, col = k.ps["C"], 128 * (ic - 8)
                    psy, coly = k.ps["Z2"], 128 * (ic - 8)
                mm(k, psx[:, col:col + 128],
                   a(k, "sel88", 0, 88, 128 * ic, 128 * ic + 128),
                   smb[0:88, 3776:3904][0:88, 0:128], start=True, stop=False)
                mm(k, psx[:, col:col + 128],
                   a(k, "sel88", 0, 88, 128 * ic, 128 * ic + 128),
                   smb[0:88, 3520:3648][0:88, 0:128], start=False, stop=True)
                mm(k, psy[:, coly:coly + 128],
                   a(k, "sel88", 0, 88, 128 * ic, 128 * ic + 128),
                   smb[0:88, 3904:4032][0:88, 0:128], start=True, stop=False)
                mm(k, psy[:, coly:coly + 128],
                   a(k, "sel88", 0, 88, 128 * ic, 128 * ic + 128),
                   smb[0:88, 3648:3776][0:88, 0:128], start=False, stop=True)
            wait(k, "vector", "te")
            quant_ladder(k, fa[0:128, 0:512], k.ps["A"][:, 0:512],
                         k.ps["Z0"][:, 0:512], 512, outb[0:128, 0:512])
            quant_ladder(k, fa[0:128, 512:1024], k.ps["A"][:, 512:1024],
                         k.ps["Z1"][:, 0:512], 512, outb[0:128, 512:1024])
            vq2 = quant_ladder(k, fa[0:128, 1024:1408], k.ps["C"][:, 0:384],
                               k.ps["Z2"][:, 0:384], 384, outb[0:128, 1024:1408])
            wait(k, "sync", "ve", vq2)
            wait(k, "tensor", "ve", vq2)
            wait(k, "scalar", "ve", vq2)
            for ic in range(11):
                dst = wdq_tile(k, 11 * hf2 + ic, 128 * r, 128 * r + 128)
                sinc(k, lambda e, dst=dst, ic=ic: e.copy(
                    out=dst, in_=outb[0:128, 128 * ic:128 * ic + 128]))
            wait(k, "vector", "se")
            wait(k, "vector", "ld")
            wait(k, "scalar", "ld")
    wait(k, "sync", "ld")


def phase_ffn_block(k):
    p, d = k.p, k.d
    fa = k.sb["f32a"].ap()
    fb = k.sb["f32b"].ap()
    smb = k.sb["f32a"].ap().bitcast(BF16)
    ptb = k.sb["ptr"].ap()
    for tq in range(16):
        vap = ln_tile(k, z3t(k, tq), z3t(k, tq), fa[0:128, 0:1024], D,
                      sq_ap=fb[0:128, 2048:3072])
        wait(k, "vector", "se", vap)
        vr = quant_scales(k, fa[0:128, 0:1024], 1024, 64)
        wait(k, "tensor", "ve", vr)
        for ic in range(8):
            trans(k, k.ps["Z2"][:, 128 * (ic % 4):128 * (ic % 4) + 128],
                  fa[0:128, 128 * ic:128 * ic + 128], fp32=True)
            if ic % 4 == 3:
                wait(k, "scalar", "te")
                for j in range(4):
                    icj = ic - 3 + j
                    sinc(k, lambda e, icj=icj, j=j: e.copy(
                        out=fa[0:128, 128 * icj:128 * icj + 128],
                        in_=k.ps["Z2"][:, 128 * j:128 * j + 128]))
                wait(k, "tensor", "se")
        for ic in range(8):
            mm(k, k.ps["A"][:, 128 * ic:128 * ic + 128],
               a(k, "sel88", 0, 64, 128 * ic, 128 * ic + 128),
               smb[0:64, 3776:3904][0:64, 0:128], start=True, stop=False)
            mm(k, k.ps["A"][:, 128 * ic:128 * ic + 128],
               a(k, "sel88", 0, 64, 128 * ic, 128 * ic + 128),
               smb[0:64, 3520:3648][0:64, 0:128], start=False, stop=True)
            psr = k.ps["Z0"] if ic < 4 else k.ps["Z1"]
            mm(k, psr[:, 128 * (ic % 4):128 * (ic % 4) + 128],
               a(k, "sel88", 0, 64, 128 * ic, 128 * ic + 128),
               smb[0:64, 3904:4032][0:64, 0:128], start=True, stop=False)
            mm(k, psr[:, 128 * (ic % 4):128 * (ic % 4) + 128],
               a(k, "sel88", 0, 64, 128 * ic, 128 * ic + 128),
               smb[0:64, 3648:3776][0:64, 0:128], start=False, stop=True)
        wait(k, "vector", "te")
        wait(k, "vector", "se")
        quant_ladder(k, fa[0:128, 0:512], k.ps["A"][:, 0:512],
                     k.ps["Z0"][:, 0:512], 512, ptb[0:128, 0:512])
        vq = quant_ladder(k, fa[0:128, 512:1024], k.ps["A"][:, 512:1024],
                          k.ps["Z1"][:, 0:512], 512, ptb[0:128, 512:1024])
        wait(k, "scalar", "ve", vq)
        for ic in range(8):
            dst = qxfm(k, ic, 128 * tq, 128 * tq + 128)
            sinc(k, lambda e, dst=dst, ic=ic: e.copy(
                out=dst, in_=ptb[0:128, 128 * ic:128 * ic + 128]))
        wait(k, "tensor", "se")
        wait(k, "vector", "se")
    wait(k, "tensor", "gd")
    wait(k, "tensor", "ld")
    for ts in range(16):
        for hf2 in range(2):
            for ci, w in ((0, 512), (1, 512), (2, 384)):
                fo0 = 1408 * hf2 + 512 * ci
                psg, psu = k.ps["C"], k.ps["D"]
                for kc in range(8):
                    mm(k, psg[:, 0:w], qxfm(k, kc, 128 * ts, 128 * ts + 128),
                       wgq_tile(k, kc, fo0, fo0 + w), start=(kc == 0), stop=(kc == 7))
                for kc in range(8):
                    mm(k, psu[:, 0:w], qxfm(k, kc, 128 * ts, 128 * ts + 128),
                       wuq_tile(k, kc, fo0, fo0 + w), start=(kc == 0), stop=(kc == 7))
                wait(k, "scalar", "te")
                vsg = sinc(k, lambda e, w=w: e.activation(
                    out=ptb[0:128, 1024:1024 + w], in_=psg[:, 0:w], func=AF.Silu))
                wait(k, "vector", "se", vsg)
                vh = vinc(k, lambda e, w=w, ci=ci: e.tensor_tensor(
                    out=fa[0:128, 512 * ci:512 * ci + w], in0=psu[:, 0:w],
                    in1=ptb[0:128, 1024:1024 + w], op=ALU.mult))
                wait(k, "tensor", "ve", vh)
                wait(k, "scalar", "ve", vh)
            vr = quant_scales(k, fa[0:128, 0:1408], 1408, 88)
            wait(k, "tensor", "ve", vr)
            for ic in range(11):
                trans(k, k.ps["Z2"][:, 128 * (ic % 4):128 * (ic % 4) + 128],
                      fa[0:128, 128 * ic:128 * ic + 128], fp32=True)
                if ic % 4 == 3 or ic == 10:
                    wait(k, "scalar", "te")
                    for j in range(ic % 4 + 1):
                        icj = ic - (ic % 4) + j
                        sinc(k, lambda e, icj=icj, j=j: e.copy(
                            out=fa[0:128, 128 * icj:128 * icj + 128],
                            in_=k.ps["Z2"][:, 128 * j:128 * j + 128]))
                    wait(k, "tensor", "se")
            for ic in range(11):
                if ic < 8:
                    psx, col = k.ps["A"], 128 * ic
                    psy, coly = (k.ps["Z0"], 128 * ic) if ic < 4 else (k.ps["Z1"], 128 * (ic - 4))
                else:
                    psx, col = k.ps["C"], 128 * (ic - 8)
                    psy, coly = k.ps["Z2"], 128 * (ic - 8)
                mm(k, psx[:, col:col + 128],
                   a(k, "sel88", 0, 88, 128 * ic, 128 * ic + 128),
                   smb[0:88, 3776:3904][0:88, 0:128], start=True, stop=False)
                mm(k, psx[:, col:col + 128],
                   a(k, "sel88", 0, 88, 128 * ic, 128 * ic + 128),
                   smb[0:88, 3520:3648][0:88, 0:128], start=False, stop=True)
                mm(k, psy[:, coly:coly + 128],
                   a(k, "sel88", 0, 88, 128 * ic, 128 * ic + 128),
                   smb[0:88, 3904:4032][0:88, 0:128], start=True, stop=False)
                mm(k, psy[:, coly:coly + 128],
                   a(k, "sel88", 0, 88, 128 * ic, 128 * ic + 128),
                   smb[0:88, 3648:3776][0:88, 0:128], start=False, stop=True)
            wait(k, "vector", "te")
            wait(k, "vector", "se")
            hq0 = 2816 + 1408 * hf2
            quant_ladder(k, fa[0:128, 0:512], k.ps["A"][:, 0:512],
                         k.ps["Z0"][:, 0:512], 512,
                         a(k, "s5", 0, 128, hq0, hq0 + 512))
            quant_ladder(k, fa[0:128, 512:1024], k.ps["A"][:, 512:1024],
                         k.ps["Z1"][:, 0:512], 512,
                         a(k, "s5", 0, 128, hq0 + 512, hq0 + 1024))
            vq2 = quant_ladder(k, fa[0:128, 1024:1408], k.ps["C"][:, 0:384],
                               k.ps["Z2"][:, 0:384], 384,
                               a(k, "s5", 0, 128, hq0 + 1024, hq0 + 1408))
            wait(k, "tensor", "ve", vq2)
        vhq = p.cnt[k.S("ve")]
        wait(k, "sync", "ve", vhq)
        sdma(k, d["hqTD"][128 * ts:128 * ts + 128, :],
             a(k, "s5", 0, 128, 2816, 5632))
        wait(k, "vector", "ld")
    wait(k, "sync", "ld")
    barrier(k)
    phase_wdown_quant(k)
    barrier(k)
    g2_te = {}
    for ts in range(16):
        buf = ts % 2
        if ts >= 2:
            wait(k, "gpsimd", "te", g2_te[ts - 2])
        v = sdma(k, a(k, "s3", 0, 128, 2816 * buf, 2816 * buf + 2816),
                 d["hqTD"][128 * ts:128 * ts + 128, :], "gpsimd")
        wait(k, "tensor", "gd", v)
        for oh in range(2):
            ps = k.ps["C"] if oh == 0 else k.ps["D"]
            for kc in range(22):
                mm(k, ps[:, 0:512], hqring(k, buf, kc),
                   wdq_tile(k, kc, 512 * oh, 512 * oh + 512),
                   start=(kc == 0), stop=(kc == 21))
            wait(k, "vector", "te")
            col = 1024 + 512 * oh
            vf = vinc(k, lambda e, ps=ps, col=col, ts=ts, oh=oh: e.scalar_tensor_tensor(
                out=fa[0:128, col:col + 512], in0=ps[:, 0:512], scalar=1.0,
                in1=z3t(k, ts, 512 * oh, 512 * oh + 512),
                op0=ALU.mult, op1=ALU.add))
            wait(k, "sync", "ve", vf)
            stdma(k, k.out[128 * ts:128 * ts + 128, 512 * oh:512 * oh + 512],
                  fa[0:128, col:col + 512])
            wait(k, "tensor", "ve", vf)
        g2_te[ts] = k.p.cnt[k.S("te")]
    if "qx" in DEBUG:
        wait(k, "gpsimd", "te")
        for kc in range(8):
            sdma(k, d["dbg_qx"][128 * kc:128 * kc + 128, :], qxfm(k, kc), "gpsimd")
    if "hq" in DEBUG:
        wait(k, "gpsimd", "ld")
        for rr in range(4):
            sdma(k, d["dbg_hq"][512 * rr:512 * rr + 512, :],
                 d["hqTD"][512 * rr:512 * rr + 512, :], "gpsimd")
    if DEBUG:
        wait(k, "sync", "gd")
    wait(k, "sync", "st")


STAGE = "full"
ATTCUT = None
ATTMI = 1
DBG = False


def emit_dbg(k):
    if not DBG:
        return
    if STAGE == "P2":
        for dc in range(4):
            sdma(k, k.out[128 * dc:128 * dc + 128, 0:1024],
                 qfm(k, dc, 0, 1024), "gpsimd")
            sdma(k, k.out[512 + 128 * dc:512 + 128 * dc + 128, 0:1024],
                 kfm(k, dc, 0, 1024), "gpsimd")
        for t in range(8):
            sdma(k, k.out[1024 + 128 * t:1152 + 128 * t, 0:512],
                 vtm(k, t), "gpsimd")
        for fc in range(4):
            sdma(k, k.out[1024 + 128 * fc:1152 + 128 * fc, 512:1024],
                 xqfm(k, fc, 0, 512), "gpsimd")
    elif STAGE == "ATT1":
        for t in range(16):
            sdma(k, k.out[128 * t:128 * t + 128, 0:512], x1t(k, t), "gpsimd")
    elif STAGE == "ATT2":
        for t in range(16):
            sdma(k, k.out[128 * t:128 * t + 128, 0:512], x1t(k, t), "gpsimd")
            sdma(k, k.out[128 * t:128 * t + 128, 512:1024], x2t(k, t), "gpsimd")
    elif STAGE == "FFN1":
        for t in range(16):
            sdma(k, k.out[128 * t:128 * t + 128, 0:1024], z3t(k, t), "gpsimd")
    wait(k, "sync", "gd")


def emit_program(k):
    phase_consts(k)
    phase_holt_tables(k)
    build_bvb(k)
    phase_attn_weights(k, 1)
    barrier(k)
    if STAGE == "TABLES":
        return
    phase_xq(k)
    barrier(k)
    phase_stream_x(k)
    barrier(k)
    if STAGE == "P2":
        emit_dbg(k)
        return
    phase_attention(k, 1)
    barrier(k)
    if STAGE == "ATT1":
        emit_dbg(k)
        return
    phase_attn_weights(k, 2)
    phase_merge_hso(k)
    barrier(k)
    if STAGE == "MERGE":
        return
    phase_proj2(k)
    barrier(k)
    if STAGE == "PROJ2":
        return
    phase_attention(k, 2)
    barrier(k)
    if STAGE == "ATT2":
        emit_dbg(k)
        return
    phase_ffw_weights(k)
    barrier(k)
    phase_attn_ffn(k)
    barrier(k)
    if STAGE == "FFN1":
        emit_dbg(k)
        return
    phase_ffn_weights_quant(k)
    barrier(k)
    phase_ffn_block(k)


def build():
    k = K()
    declare(k)
    with k.stack:
        alloc(k)
        for s in ("ld", "gd", "gp", "te", "ve", "se", "st",
                  "bsy", "bte", "bve", "bse", "bgp"):
            k.p.sem(s)
        emit_program(k)
        blk = k.stack.enter_context(k.nc.Block())
        for eng_name, meth in (("sync", blk.sync), ("tensor", blk.tensor),
                               ("vector", blk.vector), ("scalar", blk.scalar),
                               ("gpsimd", blk.gpsimd)):
            def body(e, _ops=k.p.ops[eng_name]):
                for fn in _ops:
                    fn(e)
            meth(body)
    return k


_cache = {}


def kernel(**inputs):
    if "k" not in _cache:
        _cache["k"] = build()
    k = _cache["k"]
    f32 = np.float32
    j_np = np.eye(128, dtype=f32)[::-1].copy()
    sel1a = np.zeros((1, 128), f32); sel1a[0, 0:64] = 1
    sel1b = np.zeros((1, 128), f32); sel1b[0, 64:128] = 1
    def selmat(nblk, ncol):
        s = np.zeros((nblk, ncol), f32)
        for c in range(ncol // 128):
            for pp in range(128):
                g = 8 * c + pp // 16
                if g < nblk:
                    s[g, 128 * c + pp] = 1
        return s
    consts = {
        "idn": np.eye(128, dtype=f32),
        "idnb": np.eye(128, dtype=f32).astype(BF),
        "jmatb": j_np.astype(BF),
        "sel1a": sel1a.astype(BF), "sel1b": sel1b.astype(BF),
        "sel88": selmat(88, 1408).astype(BF),
        "onesf": np.ones((1, 128), f32),
    }
    x_full = np.ascontiguousarray(inputs["L0_attn_x"], dtype=f32)
    wmap = {
        "wq1": "L0_attn_wq1", "bq1": "L0_attn_bq1", "wk1": "L0_attn_wk1",
        "bk1": "L0_attn_bk1", "wv1": "L0_attn_wv1", "bv1": "L0_attn_bv1",
        "wo1": "L0_attn_wo1", "bo1": "L0_attn_bo1",
        "wq2": "L0_attn_wq2", "bq2": "L0_attn_bq2", "wk2": "L0_attn_wk2",
        "bk2": "L0_attn_bk2", "wv2": "L0_attn_wv2", "bv2": "L0_attn_bv2",
        "wo2": "L0_attn_wo2", "bo2": "L0_attn_bo2",
        "g1": "L0_attn_g1", "be1": "L0_attn_be1",
        "g2": "L0_attn_g2", "be2": "L0_attn_be2",
        "wff1": "L0_attn_w_ff1", "bff1": "L0_attn_b_ff1",
        "wff2": "L0_attn_w_ff2", "bff2": "L0_attn_b_ff2",
        "gff": "L0_attn_g_ff", "beff": "L0_attn_be_ff",
        "wgate": "L0_ffn_w_gate", "wup": "L0_ffn_w_up", "wdown": "L0_ffn_w_down",
    }
    shared = {kk: np.ascontiguousarray(inputs[v], dtype=f32) for kk, v in wmap.items()}
    in_maps = []
    for c in range(8):
        bb, half = c // 2, c % 2
        m = dict(consts)
        m.update(shared)
        m["x"] = x_full[bb]
        m["xq"] = np.ascontiguousarray(x_full[bb, half * NR:(half + 1) * NR, :])
        m["mask0"] = np.asarray([1.0 - half], f32)
        in_maps.append(m)
    res = run_bass_kernel_spmd(k.nc, in_maps, core_ids=list(range(8)))
    out = np.empty((B, S, D), f32)
    for c in range(8):
        bb, half = c // 2, c % 2
        out[bb, half * NR:(half + 1) * NR, :] = np.asarray(res.results[c]["out"])
    return out


# version 11 (final)
# speedup vs baseline: 1.5198x; 1.0064x over previous
import numpy as np
import ml_dtypes
import concourse.bass as bass
import concourse.mybir as mybir
from concourse.bass_utils import run_bass_kernel_spmd
from contextlib import ExitStack

F32 = mybir.dt.float32
BF16 = mybir.dt.bfloat16
AF = mybir.ActivationFunctionType
ALU = mybir.AluOpType
AX = mybir.AxisListType
AP = bass.AP
BF = ml_dtypes.bfloat16

B, S, D = 4, 4096, 1024
DH, H, DK = 512, 8, 64
DFF, FFI = 4096, 2816
NR = 2048
TCH, NCH = 512, 8
EPS = 1e-5
ENGS = ("sync", "tensor", "vector", "scalar", "gpsimd")
QMIDS = (0.25, 0.75, 1.25, 1.75, 2.5, 3.5, 5.0)

DEBUG = ()


class Prog:
    def __init__(self, nc, stack):
        self.nc, self.stack = nc, stack
        self.ops = {e: [] for e in ENGS}
        self.cnt, self.sems = {}, {}

    def sem(self, name):
        s = self.stack.enter_context(self.nc.semaphore(name))
        self.cnt[s] = 0
        self.sems[name] = s
        return s

    def on(self, eng, fn):
        self.ops[eng].append(fn)

    def inc(self, eng, fn, sem, n):
        self.cnt[sem] += n
        self.ops[eng].append(lambda e, fn=fn, sem=sem, n=n: fn(e).then_inc(sem, n))
        return self.cnt[sem]

    def wait(self, eng, sem, v):
        self.ops[eng].append(lambda e, sem=sem, v=v: e.wait_ge(sem, v))


class K:
    def __init__(self):
        self.nc = bass.Bass("TRN2", target_bir_lowering=False,
                            detect_race_conditions=False)
        self.stack = ExitStack()
        self.p = Prog(self.nc, self.stack)
        self.d = {}
        self.sb = {}
        self.ps = {}

    def S(self, name):
        return self.p.sems[name]


def din(k, name, shape, dt=F32, kind="ExternalInput"):
    k.d[name] = k.nc.dram_tensor(name, shape, dt, kind=kind).ap()


def declare(k):
    din(k, "x", [S, D])
    din(k, "xq", [NR, D])
    din(k, "mask0", [1])
    for i in (1, 2):
        for w in ("wq", "wk", "wv", "wo"):
            din(k, f"{w}{i}", [DH, DH])
        for bn in ("bq", "bk", "bv", "bo"):
            din(k, f"{bn}{i}", [DH])
        din(k, f"g{i}", [DH]); din(k, f"be{i}", [DH])
    din(k, "wff1", [DFF, D]); din(k, "bff1", [DFF])
    din(k, "wff2", [D, DFF]); din(k, "bff2", [D])
    din(k, "gff", [D]); din(k, "beff", [D])
    din(k, "wgate", [FFI, D]); din(k, "wup", [FFI, D]); din(k, "wdown", [D, FFI])
    din(k, "idn", [128, 128])
    din(k, "idnb", [128, 128], BF16)
    din(k, "jmatb", [128, 128], BF16)
    din(k, "sel1a", [1, 128], BF16)
    din(k, "sel1b", [1, 128], BF16)
    din(k, "sel88", [88, 1408], BF16)
    din(k, "onesf", [1, 128])
    din(k, "hpad", [5120], BF16, "Internal")
    din(k, "hsD", [DH, S], BF16, "Internal")
    din(k, "wff1T", [DFF, D], BF16, "Internal")
    din(k, "wff2T", [D, DFF], BF16, "Internal")
    din(k, "hqTD", [NR, FFI], BF16, "Internal")
    k.out = k.nc.dram_tensor("out", [NR, D], F32, kind="ExternalOutput").ap()
    for nm in DEBUG:
        shp = {"xhq": [DH, NR], "hs": [DH, S], "x1": [NR, DH], "x2": [NR, DH],
               "kfm": [DH, S], "vtm": [S, DH], "qfm": [DH, NR], "o1": [DH, NR],
               "z3": [NR, D], "xres": [NR, D], "qx": [D, NR], "hq": [NR, FFI],
               "wdq": [FFI, D], "wgq": [D, FFI], "wuq": [D, FFI],
               }[nm]
        din(k, "dbg_" + nm, shp, F32, "ExternalOutput")


def alloc(k):
    E = k.stack.enter_context
    def sbt(name, cols, dt=BF16, parts=128):
        k.sb[name] = E(k.nc.sbuf_tensor("sb_" + name, [parts, cols], dt))
    sbt("idn", 128, F32); sbt("idnb", 128); sbt("jmatb", 128)
    sbt("sel1a", 128, BF16, 1); sbt("sel1b", 128, BF16, 33)
    sbt("sel88", 1408, BF16, 88)
    sbt("onesf", 128, F32, 1)
    sbt("onescol", 1, BF16)
    sbt("bias4", 32, F32)
    sbt("bff1c", 32, F32); sbt("bff2c", 8, F32)
    sbt("rowv", 512, F32, 1)
    sbt("maskc", 2, F32)
    sbt("s1", 16384)
    sbt("s2", 16384)
    sbt("s3", 14080)
    sbt("s4", 16384)
    sbt("s5", 8192)
    sbt("s6", 8192)
    sbt("s7", 8192)
    sbt("ptr", 2048)
    sbt("f32a", 2048, F32)
    sbt("f32b", 4160, F32)
    sbt("stats", 16, F32)
    k.ps["A"] = E(k.nc.psum_tensor("psA", [128, 1024], F32))
    for nm in ("C", "D", "Z0", "Z1", "Z2", "Z3"):
        k.ps[nm] = E(k.nc.psum_tensor("ps" + nm, [128, 512], F32))


def a(k, name, p0=0, p1=128, c0=0, c1=None):
    t = k.sb[name].ap()
    if c1 is None:
        c1 = t.shape[-1]
    return t[p0:p1, c0:c1]


def xqfm(k, fc, c0=0, c1=NR):
    return a(k, "s4", 0, 128, 2048 * fc + c0, 2048 * fc + c1)
def qfm(k, dc, c0=0, c1=NR):
    return a(k, "s4", 0, 128, 8192 + 2048 * dc + c0, 8192 + 2048 * dc + c1)
def kfm(k, dc, c0=0, c1=S):
    return a(k, "s1", 0, 128, 4096 * dc + c0, 4096 * dc + c1)
def vtm(k, kt, c0=0, c1=DH):
    return a(k, "s2", 0, 128, 512 * kt + c0, 512 * kt + c1)
def hso(k, fc, c0=0, c1=NR):
    return a(k, "s3", 0, 128, 2048 * fc + c0, 2048 * fc + c1)
def hsring(k, buf, fc, c0=0, c1=512):
    base = 8192 + 2048 * buf + 512 * fc
    return a(k, "s3", 0, 128, base + c0, base + c1)
def xtm_ring(k, buf, tt, c0=0, c1=512):
    base = 2048 * buf + 512 * tt
    return a(k, "s7", 0, 128, base + c0, base + c1)
def xfm_ring(k, buf, fc, c0=0, c1=512):
    base = 8192 + 2048 * buf + 512 * fc
    return a(k, "s3", 0, 128, base + c0, base + c1)
def wT(k, mi, m, ic, o0, o1):
    base = 2048 * m + 512 * ic
    return a(k, "s5", 0, 128, base + o0, base + o1)
def holtT(k, p0, p1, c0, c1):
    return a(k, "s6", p0, p1, c0, c1)
def x1t(k, tt, c0=0, c1=512):
    return a(k, "s6", 0, 128, 512 * tt + c0, 512 * tt + c1)
def x2t(k, tt, c0=0, c1=512):
    return a(k, "s7", 0, 128, 512 * tt + c0, 512 * tt + c1)
QXT = [("s6", 0), ("s6", 2048), ("s6", 4096), ("s6", 6144),
       ("s7", 0), ("s7", 2048), ("s7", 4096), ("s7", 6144)]
def yfm(k, fc, c0=0, c1=NR):
    return a(k, "s1", 0, 128, 2048 * fc + c0, 2048 * fc + c1)
def h1sb(k, fb, c0=0, c1=512):
    return a(k, "s2", 0, 128, 512 * fb + c0, 512 * fb + c1)
def z3t(k, tt, c0=0, c1=D):
    return a(k, "s4", 0, 128, 1024 * tt + c0, 1024 * tt + c1)
def qxfm(k, kc, c0=0, c1=NR):
    nm, base = QXT[kc]
    return a(k, nm, 0, 128, base + c0, base + c1)
def wgq_tile(k, kc, c0=0, c1=FFI):
    if kc < 5:
        return a(k, "s1", 0, 128, FFI * kc + c0, FFI * kc + c1)
    return a(k, "s3", 0, 128, FFI * (kc - 5) + c0, FFI * (kc - 5) + c1)
def wuq_tile(k, kc, c0=0, c1=FFI):
    if kc < 5:
        return a(k, "s2", 0, 128, FFI * kc + c0, FFI * kc + c1)
    if kc < 7:
        return a(k, "s3", 0, 128, 8448 + FFI * (kc - 5) + c0, 8448 + FFI * (kc - 5) + c1)
    return a(k, "s5", 0, 128, c0, c1)
def wdq_tile(k, kc, c0=0, c1=1024):
    if kc < 16:
        return a(k, "s1", 0, 128, 1024 * kc + c0, 1024 * kc + c1)
    return a(k, "s2", 0, 128, 1024 * (kc - 16) + c0, 1024 * (kc - 16) + c1)
def hqring(k, buf, kc, c0=0, c1=128):
    base = 2816 * buf + 128 * kc
    return a(k, "s3", 0, 128, base + c0, base + c1)
def hqstage(k, kc, c0=0, c1=128):
    base = 2816 + 128 * kc
    return a(k, "s5", 0, 128, base + c0, base + c1)


def gbe_ap(k, c0, c1):
    return k.sb["f32b"].ap().bitcast(BF16)[0:128, 5120 + c0:5120 + c1]
def bvb_ap(k, c0, c1):
    return k.sb["f32b"].ap().bitcast(BF16)[0:128, 7168 + c0:7168 + c1]


def barrier(k):
    p = k.p
    names = {"sync": "bsy", "tensor": "bte", "vector": "bve",
             "scalar": "bse", "gpsimd": "bgp"}
    vals = {}
    for eng in ENGS:
        s = p.sems[names[eng]]
        p.cnt[s] += 1
        vals[eng] = p.cnt[s]
        p.ops[eng].append(lambda e, s=s: e.nop().then_inc(s, 1))
    for eng in ENGS:
        for o in ENGS:
            if o != eng:
                p.wait(eng, p.sems[names[o]], vals[o])


def mm(k, out_ap, lhsT, rhs, start=True, stop=True, tp=None):
    return k.p.inc("tensor", lambda e, o=out_ap, l=lhsT, r=rhs, s1=start,
                   s2=stop, tp=tp: e.matmul(o, l, r, start=s1, stop=s2,
                                            tile_position=tp), k.S("te"), 1)


def trans(k, out_ap, in_ap, fp32=False):
    ident = a(k, "idn") if fp32 else a(k, "idnb")
    return k.p.inc("tensor", lambda e, o=out_ap, i=in_ap, d=ident:
                   e.transpose(o, i, d), k.S("te"), 1)


def psb(k, nm, c0, c1):
    return k.ps[nm].bitcast(BF16)[:, c0:c1]


def sdma(k, dst, src, eng="sync", slow=False):
    sem = k.S("ld") if eng == "sync" else k.S("gd")
    if slow:
        return k.p.inc(eng, lambda e, d=dst, s=src: e.dma_start(
            out=d, in_=s, allow_slow_non_contiguous=True), sem, 16)
    return k.p.inc(eng, lambda e, d=dst, s=src: e.dma_start(out=d, in_=s), sem, 16)


def stdma(k, dst, src):
    return k.p.inc("sync", lambda e, d=dst, s=src: e.dma_start(out=d, in_=s),
                   k.S("st"), 16)


def vop(k, fn):
    k.p.on("vector", fn)


def vinc(k, fn):
    return k.p.inc("vector", fn, k.S("ve"), 1)


def sinc(k, fn):
    return k.p.inc("scalar", fn, k.S("se"), 1)


def wait(k, eng, sem, v=None):
    if v is None:
        v = k.p.cnt[k.S(sem)]
    k.p.wait(eng, k.S(sem), v)


def phase_consts(k):
    d = k.d
    sdma(k, a(k, "idn"), d["idn"])
    sdma(k, a(k, "idnb"), d["idnb"])
    sdma(k, a(k, "jmatb"), d["jmatb"])
    sdma(k, a(k, "sel1a", 0, 1), d["sel1a"])
    sdma(k, a(k, "sel1b", 32, 33), d["sel1b"])
    sdma(k, a(k, "sel88", 0, 88), d["sel88"])
    sdma(k, a(k, "onesf", 0, 1), d["onesf"])
    names = [f"{w}{i}" for i in (1, 2) for w in ("bq", "bk", "bv", "bo")]
    for j, nm in enumerate(names):
        sdma(k, a(k, "bias4", 0, 128, 4 * j, 4 * j + 4),
             AP(tensor=d[nm].tensor, offset=0, ap=[[1, 128], [128, 4]]), slow=True)
    sdma(k, a(k, "bff1c"), AP(tensor=d["bff1"].tensor, offset=0, ap=[[1, 128], [128, 32]]), slow=True)
    sdma(k, a(k, "bff2c"), AP(tensor=d["bff2"].tensor, offset=0, ap=[[1, 128], [128, 8]]), slow=True)
    sdma(k, a(k, "rowv", 0, 1, 0, 1), d["mask0"].rearrange("(a b) -> a b", a=1))
    wait(k, "vector", "ld")
    vinc(k, lambda e: e.memset(a(k, "onescol"), 1.0))
    wait(k, "tensor", "ld")
    wait(k, "tensor", "ve")
    mm(k, k.ps["Z1"][:, 0:1], a(k, "onesf", 0, 1), a(k, "rowv", 0, 1, 0, 1))
    wait(k, "vector", "te")
    vmc = vinc(k, lambda e: e.tensor_copy(out=a(k, "maskc", 0, 128, 0, 1),
                                          in_=k.ps["Z1"][:, 0:1]))
    wait(k, "vector", "ve", vmc)
    vop(k, lambda e: e.tensor_scalar(out=a(k, "maskc", 0, 128, 1, 2),
                                     in0=a(k, "maskc", 0, 128, 0, 1),
                                     scalar1=-1.0, scalar2=1.0,
                                     op0=ALU.mult, op1=ALU.add))
    v = vinc(k, lambda e: e.engine_nop())
    wait(k, "tensor", "ve", v)
    wait(k, "sync", "ve", v)


def build_bvb(k):
    d = k.d
    for i in (1, 2):
        v = sdma(k, a(k, "rowv", 0, 1, 0, 512), d[f"bv{i}"].rearrange("(a b) -> a b", a=1))
        wait(k, "tensor", "ld", v)
        mm(k, k.ps["Z1"][:, 0:512], a(k, "onesf", 0, 1), a(k, "rowv", 0, 1, 0, 512))
        wait(k, "vector", "te")
        col = 512 * (i - 1)
        ve = vinc(k, lambda e, col=col: e.tensor_copy(
            out=bvb_ap(k, col, col + 512), in_=k.ps["Z1"][:, 0:512]))
        wait(k, "sync", "ve", ve)
        wait(k, "tensor", "ve", ve)


def build_gbe(k, gname, bename, width):
    d = k.d
    for idx, nm in ((0, gname), (1, bename)):
        for off in range(0, width, 512):
            w = min(512, width - off)
            v = sdma(k, a(k, "rowv", 0, 1, 0, w),
                     d[nm].rearrange("(a b) -> a b", a=1)[:, off:off + w])
            wait(k, "tensor", "ld", v)
            wait(k, "tensor", "ve")
            mm(k, k.ps["Z1"][:, 0:w], a(k, "onesf", 0, 1), a(k, "rowv", 0, 1, 0, w))
            wait(k, "vector", "te")
            col = idx * 1024 + off
            ve = vinc(k, lambda e, col=col, w=w: e.tensor_copy(
                out=gbe_ap(k, col, col + w), in_=k.ps["Z1"][:, 0:w]))
            wait(k, "sync", "ve", ve)
    wait(k, "tensor", "ve")


def phase_holt_tables(k):
    p, d = k.p, k.d
    hf = k.sb["f32b"].ap()
    hb = k.sb["s6"].ap()
    V2C, G2C, W2C = 80, 592, 1104

    def V(fn):
        p.on("vector", fn)

    def VE(fn):
        v = vinc(k, fn)
        wait(k, "vector", "ve", v)
        return v

    VE(lambda e: e.memset(hf[0:1, 6:7], 10.0))
    VE(lambda e: e.reciprocal(out=hf[0:1, 4:5], in_=hf[0:1, 6:7]))
    for _ in range(2):
        VE(lambda e: e.tensor_scalar(out=hf[0:1, 7:8], in0=hf[0:1, 4:5],
                                     scalar1=-10.0, scalar2=2.0,
                                     op0=ALU.mult, op1=ALU.add))
        VE(lambda e: e.tensor_tensor(out=hf[0:1, 4:5], in0=hf[0:1, 4:5],
                                     in1=hf[0:1, 7:8], op=ALU.mult))
    VE(lambda e: e.tensor_tensor(out=hf[0:1, 5:6], in0=hf[0:1, 4:5],
                                 in1=hf[0:1, 4:5], op=ALU.mult))
    VE(lambda e: e.tensor_scalar(out=hf[0:1, 0:1], in0=hf[0:1, 4:5],
                                 scalar1=-1.0, scalar2=1.0, op0=ALU.mult, op1=ALU.add))
    VE(lambda e: e.tensor_copy(out=hf[0:1, 1:2], in_=hf[0:1, 0:1]))
    VE(lambda e: e.tensor_scalar(out=hf[0:1, 2:3], in0=hf[0:1, 5:6],
                                 scalar1=-1.0, scalar2=None, op0=ALU.mult))
    VE(lambda e: e.tensor_scalar(out=hf[0:1, 3:4], in0=hf[0:1, 5:6],
                                 scalar1=-1.0, scalar2=1.0, op0=ALU.mult, op1=ALU.add))
    VE(lambda e: e.tensor_copy(out=hf[0:1, 8:9], in_=hf[0:1, 0:1]))
    VE(lambda e: e.tensor_copy(out=hf[0:1, 9:10], in_=hf[0:1, 2:3]))
    VE(lambda e: e.tensor_copy(out=hf[0:1, 10:11], in_=hf[0:1, 1:2]))
    VE(lambda e: e.tensor_copy(out=hf[0:1, 11:12], in_=hf[0:1, 3:4]))
    vrow = vinc(k, lambda e: e.engine_nop())

    wait(k, "sync", "ve", vrow)
    sdma(k, hf[0:2, 16:18], hf[0:1, 0:4], slow=True)
    sdma(k, hf[0:2, 48:50], hf[0:1, 8:12], slow=True)
    sdma(k, hf[0:2, W2C:W2C + 1], hf[0:1, 4:6], slow=True)
    sdma(k, hf[0:2, V2C:V2C + 1], hf[0:1, 0:2], slow=True)
    vst = p.cnt[k.S("ld")]
    wait(k, "vector", "ld", vst)
    VE(lambda e: e.memset(hf[0:2, G2C:G2C + 1], 0.0))
    VE(lambda e: e.memset(hf[0:1, G2C:G2C + 1], 1.0))
    v0 = vinc(k, lambda e: e.engine_nop())
    wait(k, "tensor", "ve", v0)
    wait(k, "tensor", "ld", vst)

    ps = k.ps["C"]
    smb = k.sb["f32a"].ap().bitcast(BF16)
    def ev(dst, src):
        wait(k, "vector", "te")
        ve = vinc(k, lambda e, dst=dst, src=src: e.tensor_copy(out=dst, in_=src))
        wait(k, "tensor", "ve", ve)

    m, stage = 1, 0
    m512_stage = None
    while m < 4096:
        if m == 512:
            m512_stage = stage
        Pm = hf[0:2, 16 + 2 * stage:18 + 2 * stage]
        Mm = hf[0:2, 48 + 2 * stage:50 + 2 * stage]
        step = min(m, 4096 - m)
        for off in range(0, step, 512):
            w = min(512, step - off)
            mm(k, ps[0:2, 0:w], Mm, hf[0:2, W2C + off:W2C + off + w])
            if m + off < 2048:
                ev(hf[0:2, W2C + m + off:W2C + m + off + w], ps[0:2, 0:w])
            else:
                wait(k, "vector", "te")
                ve = vinc(k, lambda e, w=w: e.tensor_copy(
                    out=smb[0:2, 3520:3520 + w][0:2, 0:w], in_=ps[0:2, 0:w]))
                wait(k, "tensor", "ve", ve)
                wait(k, "gpsimd", "ve", ve)
                sdma(k, d["hpad"][512 + m + off:512 + m + off + w],
                     smb[0:1, 3520:3520 + w], "gpsimd")
                wait(k, "vector", "gd")
        if m < 512:
            stepv = min(m, 512 - m)
            for cc, lhs in ((G2C, Mm), (V2C, Pm)):
                mm(k, ps[0:2, 0:stepv], lhs, hf[0:2, cc:cc + stepv])
                ev(hf[0:2, cc + m:cc + m + stepv], ps[0:2, 0:stepv])
        mm(k, ps[0:2, 0:2], Mm, Pm)
        ev(hf[0:2, 18 + 2 * stage:20 + 2 * stage], ps[0:2, 0:2])
        mm(k, ps[0:2, 0:2], Pm, Mm)
        ev(hf[0:2, 50 + 2 * stage:52 + 2 * stage], ps[0:2, 0:2])
        m *= 2
        stage += 1
    c512 = 48 + 2 * (m512_stage + 1)
    V(lambda e: e.tensor_copy(out=hb[0:2, 4624:4626], in_=hf[0:2, c512:c512 + 2]))
    V(lambda e: e.tensor_copy(out=hb[0:2, 4112:4624], in_=hf[0:2, V2C:V2C + 512]))
    zr = k.sb["f32a"].ap()
    V(lambda e: e.memset(zr[0:1, 0:512], 0.0))
    vz = vinc(k, lambda e: e.engine_nop())
    wait(k, "gpsimd", "ve", vz)
    sdma(k, d["hpad"][0:512], zr[0:1, 0:512], "gpsimd")
    sdma(k, d["hpad"][4608:5120], zr[0:1, 0:512], "gpsimd")
    sdma(k, d["hpad"][512:2560], hf[0:1, W2C:W2C + 2048], "gpsimd")
    wait(k, "sync", "gd")

    hkb = k.sb["ptr"].ap()
    for dst_base in (0, 2048):
        for jsub in range(4):
            off = 512 - 127 - 128 * jsub
            v = sdma(k, hkb[0:128, 0:512],
                     AP(tensor=d["hpad"].tensor, offset=off, ap=[[1, 128], [1, 512]]))
            wait(k, "tensor", "ld", v)
            for tsub in range(4):
                mm(k, k.ps["C"][:, 128 * tsub:128 * tsub + 128], a(k, "jmatb"),
                   hkb[0:128, 128 * tsub:128 * tsub + 128])
            wait(k, "vector", "te")
            col = dst_base + 512 * jsub
            ve = vinc(k, lambda e, col=col: e.tensor_copy(
                out=hb[0:128, col:col + 512], in_=k.ps["C"][:, 0:512]))
            wait(k, "tensor", "ve", ve)
            wait(k, "sync", "ve", ve)
    V(lambda e: e.tensor_copy(out=hb[0:1, 2048:2560], in_=hf[0:1, G2C:G2C + 512]))
    for t in range(4):
        srcc = 128 * (3 - t)
        k.p.inc("tensor", lambda e, srcc=srcc: e.transpose(
            k.ps["C"][:, 0:2], hf[0:2, W2C + srcc:W2C + srcc + 128],
            a(k, "idn", 0, 2, 0, 2)), k.S("te"), 1)
        wait(k, "vector", "te")
        ve = vinc(k, lambda e: e.tensor_copy(out=hkb[0:128, 1024:1026],
                                             in_=k.ps["C"][:, 0:2]))
        wait(k, "tensor", "ve", ve)
        mm(k, k.ps["C"][:, 0:2], a(k, "jmatb"), hkb[0:128, 1024:1026])
        wait(k, "vector", "te")
        for dst0 in (4096, 4104):
            vinc(k, lambda e, dst0=dst0, t=t: e.tensor_copy(
                out=hb[0:128, dst0 + 2 * t:dst0 + 2 * t + 2], in_=k.ps["C"][:, 0:2]))
        wait(k, "tensor", "ve")
    wait(k, "gpsimd", "ve")
    sdma(k, hb[0:1, 4104:4105], hf[0:1, G2C + 511:G2C + 512], "gpsimd", slow=True)
    sdma(k, hb[0:1, 4105:4106], hf[1:2, G2C + 511:G2C + 512], "gpsimd", slow=True)
    wait(k, "vector", "gd")
    wait(k, "tensor", "gd")
    V(lambda e: e.memset(hb[0:2, 4704:5728], 0.0))
    v = vinc(k, lambda e: e.engine_nop())
    wait(k, "tensor", "ve", v)


def sprev(k, c0=0, c1=512):
    return a(k, "s6", 0, 2, 4704 + c0, 4704 + c1)
def snew(k, c0=0, c1=512):
    return a(k, "s6", 0, 2, 5216 + c0, 5216 + c1)


def phase_attn_weights(k, mi):
    d = k.d
    stage = k.sb["ptr"].ap()
    for m, w in enumerate(("wq", "wk", "wv", "wo")):
        wd = d[f"{w}{mi}"]
        for oc in range(4):
            vg = sdma(k, stage[0:128, 0:512], wd[128 * oc:128 * oc + 128, :], "gpsimd")
            wait(k, "tensor", "gd", vg)
            for ic in range(4):
                trans(k, psb(k, "D", 128 * ic, 128 * ic + 128),
                      stage[0:128, 128 * ic:128 * ic + 128])
            wait(k, "vector", "te")
            for ic in range(4):
                dst = wT(k, mi, m, ic, 128 * oc, 128 * oc + 128)
                vinc(k, lambda e, dst=dst, ic=ic: e.tensor_copy(
                    out=dst, in_=psb(k, "D", 128 * ic, 128 * ic + 128)))
            wait(k, "tensor", "ve")
            wait(k, "gpsimd", "ve")


def ln_tile(k, full_src, apply_src, out_bf, width, sq_ap=None):
    st = k.sb["stats"].ap()
    if sq_ap is None:
        sq_ap = k.sb["f32b"].ap()[0:128, 0:width]
    vsq = sinc(k, lambda e: e.activation(out=sq_ap, in_=full_src,
                                         func=AF.Square, accum_out=st[0:128, 2:3]))
    wait(k, "vector", "se", vsq)
    va = vinc(k, lambda e: e.tensor_reduce(out=st[0:128, 3:4], in_=full_src,
                                           axis=AX.X, op=ALU.add))
    vop(k, lambda e: e.tensor_scalar(out=st[0:128, 2:3], in0=st[0:128, 2:3],
                                     scalar1=1.0 / width, scalar2=None, op0=ALU.mult))
    wait(k, "vector", "ve", va)
    vc = vinc(k, lambda e: e.tensor_scalar(out=st[0:128, 0:1], in0=st[0:128, 3:4],
                                           scalar1=1.0 / width, scalar2=None,
                                           op0=ALU.mult))
    wait(k, "vector", "ve", vc)
    vd = vinc(k, lambda e: e.tensor_tensor(out=st[0:128, 3:4], in0=st[0:128, 0:1],
                                           in1=st[0:128, 0:1], op=ALU.mult))
    wait(k, "vector", "ve", vd)
    vv = vinc(k, lambda e: e.scalar_tensor_tensor(
        out=st[0:128, 2:3], in0=st[0:128, 2:3], scalar=EPS, in1=st[0:128, 3:4],
        op0=ALU.add, op1=ALU.subtract))
    wait(k, "scalar", "ve", vv)
    vsr = sinc(k, lambda e: e.activation(out=st[0:128, 3:4], in_=st[0:128, 2:3],
                                         func=AF.Sqrt))
    wait(k, "vector", "se", vsr)
    vh = vinc(k, lambda e: e.reciprocal(out=st[0:128, 1:2], in_=st[0:128, 3:4]))
    wait(k, "vector", "ve", vh)
    vij = vinc(k, lambda e: e.scalar_tensor_tensor(
        out=st[0:128, 3:4], in0=st[0:128, 1:2], scalar=st[0:128, 1:2],
        in1=st[0:128, 2:3], op0=ALU.mult, op1=ALU.mult))
    wait(k, "vector", "ve", vij)
    vk = vinc(k, lambda e: e.tensor_scalar(out=st[0:128, 3:4], in0=st[0:128, 3:4],
                                           scalar1=-0.5, scalar2=1.5,
                                           op0=ALU.mult, op1=ALU.add))
    wait(k, "vector", "ve", vk)
    vl = vinc(k, lambda e: e.tensor_tensor(out=st[0:128, 1:2], in0=st[0:128, 1:2],
                                           in1=st[0:128, 3:4], op=ALU.mult))
    wait(k, "vector", "ve", vl)
    vm = vinc(k, lambda e: e.scalar_tensor_tensor(
        out=st[0:128, 4:5], in0=st[0:128, 0:1], scalar=-1.0, in1=st[0:128, 1:2],
        op0=ALU.mult, op1=ALU.mult))
    wait(k, "scalar", "ve", vm)
    return sinc(k, lambda e: e.activation(
        out=out_bf, in_=apply_src, func=AF.Identity,
        scale=st[0:128, 1:2], bias=st[0:128, 4:5]))


def phase_xq(k):
    d = k.d
    xf = k.sb["f32a"].ap()
    tmb = k.sb["ptr"].ap()
    done = {}
    for tt in range(16):
        buf = tt % 2
        if tt >= 2:
            wait(k, "sync", "se", done[tt - 2][0])
            wait(k, "sync", "se", done[tt - 2][1])
        v = sdma(k, xf[0:128, 1024 * buf:1024 * buf + 1024],
                 d["xq"][128 * tt:128 * tt + 128, :])
        wait(k, "scalar", "ld", v)
        wait(k, "vector", "ld", v)
        vxh = ln_tile(k, xf[0:128, 1024 * buf:1024 * buf + 1024],
                      xf[0:128, 1024 * buf:1024 * buf + 512],
                      tmb[0:128, 512 * buf:512 * buf + 512], D)
        wait(k, "tensor", "se", vxh)
        for fc in range(4):
            trans(k, psb(k, "D", 128 * fc, 128 * fc + 128),
                  tmb[0:128, 512 * buf + 128 * fc:512 * buf + 128 * fc + 128])
        wait(k, "scalar", "te")
        for fc in range(4):
            dst = xqfm(k, fc, 128 * tt, 128 * tt + 128)
            sinc(k, lambda e, dst=dst, fc=fc: e.copy(
                out=dst, in_=psb(k, "D", 128 * fc, 128 * fc + 128)))
        wait(k, "tensor", "se")
        done[tt] = (vxh, k.p.cnt[k.S("se")])
    wait(k, "tensor", "se")
    proj_fm(k, 1, 0, lambda fc, c0, c1: xqfm(k, fc, c0, c1), NR, lambda dc, c0, c1: qfm(k, dc, c0, c1))


def proj_fm(k, mi, m, src_fn, ntok, dst_fn):
    for dc in range(4):
        for t0 in range(0, ntok, 512):
            ps = k.ps["C"] if (t0 // 512) % 2 == 0 else k.ps["D"]
            for fc in range(4):
                mm(k, ps[:, 0:512], wT(k, mi, m, fc, 128 * dc, 128 * dc + 128),
                   src_fn(fc, t0, t0 + 512), start=(fc == 0), stop=(fc == 3))
            wait(k, "scalar", "te")
            bj = 4 * (4 * (mi - 1) + m) + dc
            dst = dst_fn(dc, t0, t0 + 512)
            vse = sinc(k, lambda e, dst=dst, ps=ps, bj=bj: e.activation(
                out=dst, in_=ps[:, 0:512], func=AF.Identity,
                bias=a(k, "bias4", 0, 128, bj, bj + 1)))
            wait(k, "tensor", "se", vse)


def phase_stream_x(k):
    p, d = k.p, k.d
    xf = k.sb["f32a"].ap()
    hb = k.sb["s6"].ap()
    hst = k.sb["f32b"].ap().bitcast(BF16)

    for c in range(NCH):
        blk = c % 2
        for tt in range(4):
            ti = 4 * c + tt
            buf = ti % 2
            v = sdma(k, xf[0:128, 1024 * buf:1024 * buf + 1024],
                     d["x"][128 * ti:128 * ti + 128, :])
            wait(k, "scalar", "ld", v)
            wait(k, "vector", "ld", v)
            vxh = ln_tile(k, xf[0:128, 1024 * buf:1024 * buf + 1024],
                          xf[0:128, 1024 * buf:1024 * buf + 512],
                          xtm_ring(k, blk, tt), D)
            wait(k, "tensor", "se", vxh)
            for fc in range(4):
                trans(k, psb(k, "Z2", 128 * fc, 128 * fc + 128),
                      xtm_ring(k, blk, tt, 128 * fc, 128 * fc + 128))
            wait(k, "scalar", "te")
            for fc in range(4):
                dst = xfm_ring(k, blk, fc, 128 * tt, 128 * tt + 128)
                sinc(k, lambda e, dst=dst, fc=fc: e.copy(
                    out=dst, in_=psb(k, "Z2", 128 * fc, 128 * fc + 128)))
            wait(k, "tensor", "se")
            wait(k, "sync", "se")
        for dc in range(4):
            ps = k.ps["C"] if dc % 2 == 0 else k.ps["D"]
            for fc in range(4):
                mm(k, ps[:, 0:512], wT(k, 1, 1, fc, 128 * dc, 128 * dc + 128),
                   xfm_ring(k, blk, fc, 0, 512), start=(fc == 0), stop=(fc == 3))
            wait(k, "scalar", "te")
            bj = 4 * 1 + dc
            dst = kfm(k, dc, 512 * c, 512 * c + 512)
            vse = sinc(k, lambda e, dst=dst, ps=ps, bj=bj: e.activation(
                out=dst, in_=ps[:, 0:512], func=AF.Identity,
                bias=a(k, "bias4", 0, 128, bj, bj + 1)))
            wait(k, "tensor", "se", vse)
        for tt in range(4):
            ps = k.ps["C"] if tt % 2 == 0 else k.ps["D"]
            for fc in range(4):
                mm(k, ps[:, 0:512], xfm_ring(k, blk, fc, 128 * tt, 128 * tt + 128),
                   wT(k, 1, 2, fc, 0, 512), start=(fc == 0), stop=(fc == 3))
            wait(k, "vector", "te")
            dst = vtm(k, 4 * c + tt)
            vve = vinc(k, lambda e, dst=dst, ps=ps: e.scalar_tensor_tensor(
                out=dst, in0=ps[:, 0:512], scalar=1.0,
                in1=bvb_ap(k, 0, 512), op0=ALU.mult, op1=ALU.add))
            wait(k, "tensor", "ve", vve)
        ut0 = 4104 if c == 0 else 4096
        for js in range(4):
            mm(k, k.ps["Z2"][0:2, 0:512], hb[0:128, ut0 + 2 * js:ut0 + 2 * js + 2],
               xtm_ring(k, blk, js, 0, 512), start=(js == 0), stop=False)
        mm(k, k.ps["Z2"][0:2, 0:512], hb[0:2, 4624:4626], sprev(k),
           start=False, stop=True)
        wait(k, "vector", "te")
        vs = vinc(k, lambda e: e.tensor_copy(out=snew(k), in_=k.ps["Z2"][0:2, 0:512]))
        wait(k, "tensor", "ve", vs)
        mt0 = 2048 if c == 0 else 0
        for fcq in range(4):
            ps = k.ps["Z0"] if fcq % 2 == 0 else k.ps["Z1"]
            for js in range(4):
                mm(k, ps[:, 0:512],
                   xtm_ring(k, blk, js, 128 * fcq, 128 * fcq + 128),
                   hb[0:128, mt0 + 512 * js:mt0 + 512 * js + 512],
                   start=(js == 0), stop=False)
            mm(k, ps[:, 0:512], sprev(k, 128 * fcq, 128 * fcq + 128),
               hb[0:2, 4112:4624], start=False, stop=True)
            wait(k, "scalar", "te")
            vse = sinc(k, lambda e, ps=ps, fcq=fcq: e.copy(
                out=hst[0:128, 4096 + 512 * fcq:4096 + 512 * fcq + 512],
                in_=ps[:, 0:512]))
            wait(k, "sync", "se", vse)
            sdma(k, d["hsD"][128 * fcq:128 * fcq + 128, 512 * c:512 * c + 512],
                 hst[0:128, 4096 + 512 * fcq:4096 + 512 * fcq + 512])
            wait(k, "tensor", "se", vse)
        wait(k, "vector", "te")
        vsp = vinc(k, lambda e: e.tensor_copy(out=sprev(k), in_=snew(k)))
        wait(k, "tensor", "ve", vsp)
        wait(k, "sync", "ve")
        wait(k, "sync", "ld")
        wait(k, "scalar", "ld")
    wait(k, "sync", "ld")


def phase_attention(k, mi):
    p = k.p
    build_gbe(k, f"g{mi}", f"be{mi}", 512)
    pt = k.sb["ptr"].ap()
    smf = k.sb["f32a"].ap()
    smb = k.sb["f32a"].ap().bitcast(BF16)
    bcst = k.sb["f32b"].ap().bitcast(BF16)
    res_fn = (lambda dc, c0, c1: xqfm(k, dc, c0, c1)) if mi == 1 else \
             (lambda dc, c0, c1: hso(k, dc, c0, c1))
    xdst = x1t if mi == 1 else x2t
    zst = k.sb["f32b"].ap().bitcast(BF16)
    ost = lambda hp, c0=0, c1=512: bcst[0:128, 2048 + 512 * hp + c0:2048 + 512 * hp + c1]

    cut = ATTCUT if mi == ATTMI else None
    nqc = 1 if cut else 4
    nhp = 1 if cut in ("S", "AV", "SUMS", "BCAST") else 4
    for qc in range(nqc):
        vo_prev = None
        for hp in range(nhp):
            exp_se = {}
            avsum_te = {}
            S_te = {}

            def psS(kt):
                if kt % 2 == 0:
                    return k.ps["A"][:, 0:512], k.ps["A"][:, 512:1024]
                return k.ps["C"][:, 0:512], k.ps["D"][:, 0:512]

            def emit_S(kt, hp=hp, qc=qc):
                if kt >= 2:
                    wait(k, "tensor", "se", exp_se[kt - 2][1])
                pa, pb_ = psS(kt)
                mm(k, pa, kfm(k, hp, 128 * kt, 128 * kt + 128)[0:64, :],
                   qfm(k, hp, 512 * qc, 512 * qc + 512)[0:64, :],
                   start=True, stop=True, tp=(0, 0))
                mm(k, pb_, kfm(k, hp, 128 * kt, 128 * kt + 128)[64:128, :],
                   qfm(k, hp, 512 * qc, 512 * qc + 512)[64:128, :],
                   start=True, stop=True, tp=(64, 0))
                S_te[kt] = p.cnt[k.S("te")]

            def emit_exp(kt):
                pa, pb_ = psS(kt)
                pb = kt % 2
                wait(k, "scalar", "te", S_te[kt])
                if kt >= 2 and (kt - 2) in avsum_te:
                    wait(k, "scalar", "te", avsum_te[kt - 2])
                v0 = sinc(k, lambda e, pa=pa, pb=pb: e.activation(
                    out=pt[0:128, 1024 * pb:1024 * pb + 512], in_=pa,
                    func=AF.Exp, scale=0.125))
                v1 = sinc(k, lambda e, pb_=pb_, pb=pb: e.activation(
                    out=pt[0:128, 1024 * pb + 512:1024 * pb + 1024], in_=pb_,
                    func=AF.Exp, scale=0.125))
                exp_se[kt] = (v0, v1)

            def emit_AV(kt, hp=hp):
                pb = kt % 2
                if kt == 0 and vo_prev is not None:
                    wait(k, "tensor", "ve", vo_prev)
                wait(k, "tensor", "se", exp_se[kt][0])
                mm(k, k.ps["Z0"][0:64, :], vtm(k, kt, 128 * hp, 128 * hp + 64),
                   pt[0:128, 1024 * pb:1024 * pb + 512],
                   start=(kt == 0), stop=(kt == 31), tp=(0, 0))
                wait(k, "tensor", "se", exp_se[kt][1])
                mm(k, k.ps["Z1"][64:128, :], vtm(k, kt, 128 * hp + 64, 128 * hp + 128),
                   pt[0:128, 1024 * pb + 512:1024 * pb + 1024],
                   start=(kt == 0), stop=(kt == 31), tp=(0, 64))
                mm(k, k.ps["Z2"][0:1, :], a(k, "onescol"),
                   pt[0:128, 1024 * pb:1024 * pb + 512],
                   start=(kt == 0), stop=(kt == 31), tp=(0, 0))
                mm(k, k.ps["Z3"][0:1, :], a(k, "onescol"),
                   pt[0:128, 1024 * pb + 512:1024 * pb + 1024],
                   start=(kt == 0), stop=(kt == 31), tp=(0, 0))
                avsum_te[kt] = p.cnt[k.S("te")]

            emit_S(0)
            emit_exp(0)
            emit_S(1)
            emit_exp(1)
            for kt in range(2, 32):
                emit_S(kt)
                if cut != "S":
                    emit_AV(kt - 2)
                emit_exp(kt)
            if cut != "S":
                emit_AV(30)
                emit_AV(31)
            if cut in ("S", "AV"):
                return
            wait(k, "vector", "te")
            fbf = k.sb["f32b"].ap()
            vinc(k, lambda e: e.tensor_copy(out=fbf[0:1, 0:512],
                                            in_=k.ps["Z2"][0:1, :]))
            vinc(k, lambda e: e.tensor_copy(out=fbf[0:1, 512:1024],
                                            in_=k.ps["Z3"][0:1, :]))
            vop(k, lambda e: e.reciprocal(out=smf[0:1, 0:1024],
                                          in_=fbf[0:1, 0:1024]))
            vop(k, lambda e: e.tensor_tensor(
                out=smf[0:1, 1024:2048], in0=fbf[0:1, 0:1024],
                in1=smf[0:1, 0:1024], op=ALU.mult))
            vop(k, lambda e: e.tensor_scalar(
                out=smf[0:1, 1024:2048], in0=smf[0:1, 1024:2048],
                scalar1=-1.0, scalar2=2.0, op0=ALU.mult, op1=ALU.add))
            vop(k, lambda e: e.tensor_tensor(
                out=bcst[0:1, 0:1024], in0=smf[0:1, 0:1024],
                in1=smf[0:1, 1024:2048], op=ALU.mult))
            vrec = vinc(k, lambda e: e.engine_nop())
            if cut == "SUMS":
                return
            wait(k, "tensor", "ve", vrec)
            mm(k, k.ps["Z2"][0:64, 0:512], a(k, "sel1a", 0, 1, 0, 64),
               bcst[0:1, 0:512], start=True, stop=True, tp=(0, 0))
            mm(k, k.ps["Z2"][64:128, 0:512], a(k, "sel1a", 0, 1, 0, 64),
               bcst[0:1, 512:1024], start=True, stop=True, tp=(0, 64))
            wait(k, "vector", "te")
            vb = vinc(k, lambda e: e.tensor_copy(out=bcst[0:128, 4096:4608],
                                                 in_=k.ps["Z2"][:, 0:512]))
            vinc(k, lambda e, hp=hp: e.tensor_tensor(
                out=ost(hp)[0:64, :], in0=k.ps["Z0"][0:64, :],
                in1=bcst[0:64, 4096:4608], op=ALU.mult))
            vo = vinc(k, lambda e, hp=hp: e.tensor_tensor(
                out=ost(hp)[64:128, :], in0=k.ps["Z1"][64:128, :],
                in1=bcst[64:128, 4096:4608], op=ALU.mult))
            vo_prev = vo
        if cut == "BCAST":
            return
        if cut == "HP":
            continue
        wait(k, "tensor", "ve", vo_prev)
        for dc in range(4):
            ps = k.ps["C"] if dc % 2 == 0 else k.ps["D"]
            for hp in range(4):
                mm(k, ps[:, 0:512], wT(k, mi, 3, hp, 128 * dc, 128 * dc + 128),
                   ost(hp), start=(hp == 0), stop=(hp == 3))
            wait(k, "vector", "te")
            bj = 4 * (4 * (mi - 1) + 3) + dc
            res = res_fn(dc, 512 * qc, 512 * qc + 512)
            vz = vinc(k, lambda e, ps=ps, bj=bj, res=res, dc=dc: e.scalar_tensor_tensor(
                out=zst[0:128, 512 * dc:512 * dc + 512], in0=ps[:, 0:512],
                scalar=a(k, "bias4", 0, 128, bj, bj + 1), in1=res,
                op0=ALU.add, op1=ALU.add))
            wait(k, "tensor", "ve", vz)
        if cut == "OPROJ":
            return
        for sub in range(4):
            tq = 4 * qc + sub
            for dc in range(4):
                trans(k, psb(k, "C", 128 * dc, 128 * dc + 128),
                      zst[0:128, 512 * dc + 128 * sub:512 * dc + 128 * sub + 128])
            wait(k, "vector", "te")
            wait(k, "scalar", "te")
            vap = ln_tile(k, psb(k, "C", 0, 512), psb(k, "C", 0, 512),
                          bcst[0:128, 4608:5120], 512,
                          sq_ap=k.sb["f32a"].ap()[0:128, 0:512])
            wait(k, "vector", "se", vap)
            vop(k, lambda e: e.tensor_tensor(
                out=bcst[0:128, 4608:5120], in0=bcst[0:128, 4608:5120],
                in1=gbe_ap(k, 0, 512), op=ALU.mult))
            dst = xdst(k, tq)
            vx = vinc(k, lambda e, dst=dst: e.tensor_tensor(
                out=dst, in0=bcst[0:128, 4608:5120],
                in1=gbe_ap(k, 1024, 1536), op=ALU.add))
            wait(k, "tensor", "ve", vx)
            wait(k, "tensor", "se", vap)


def phase_merge_hso(k):
    d = k.d
    hm = k.sb["f32b"].ap().bitcast(BF16)
    for fc in range(4):
        v0 = sdma(k, hm[0:128, 0:2048],
                  d["hsD"][128 * fc:128 * fc + 128, 0:2048])
        v1 = sdma(k, hm[0:128, 2048:4096],
                  d["hsD"][128 * fc:128 * fc + 128, 2048:4096])
        wait(k, "vector", "ld", v1)
        vop(k, lambda e, fc=fc: e.tensor_scalar(
            out=hso(k, fc), in0=hm[0:128, 2048:4096],
            scalar1=a(k, "maskc", 0, 128, 1, 2), scalar2=None, op0=ALU.mult))
        vm = vinc(k, lambda e, fc=fc: e.scalar_tensor_tensor(
            out=hso(k, fc), in0=hm[0:128, 0:2048],
            scalar=a(k, "maskc", 0, 128, 0, 1), in1=hso(k, fc),
            op0=ALU.mult, op1=ALU.add))
        wait(k, "sync", "ve", vm)


def phase_proj2(k):
    p, d = k.p, k.d
    for c in range(NCH):
        buf = c % 2
        vs = []
        for fc in range(4):
            vs.append(sdma(k, hsring(k, buf, fc),
                           d["hsD"][128 * fc:128 * fc + 128, 512 * c:512 * c + 512]))
        wait(k, "tensor", "ld", vs[-1])
        for dc in range(4):
            ps = k.ps["C"] if dc % 2 == 0 else k.ps["D"]
            for fc in range(4):
                mm(k, ps[:, 0:512], wT(k, 2, 1, fc, 128 * dc, 128 * dc + 128),
                   hsring(k, buf, fc), start=(fc == 0), stop=(fc == 3))
            wait(k, "scalar", "te")
            bj = 4 * 5 + dc
            dst = kfm(k, dc, 512 * c, 512 * c + 512)
            vse = sinc(k, lambda e, dst=dst, ps=ps, bj=bj: e.activation(
                out=dst, in_=ps[:, 0:512], func=AF.Identity,
                bias=a(k, "bias4", 0, 128, bj, bj + 1)))
            wait(k, "tensor", "se", vse)
        for tt in range(4):
            ps = k.ps["C"] if tt % 2 == 0 else k.ps["D"]
            for fc in range(4):
                mm(k, ps[:, 0:512], hsring(k, buf, fc, 128 * tt, 128 * tt + 128),
                   wT(k, 2, 2, fc, 0, 512), start=(fc == 0), stop=(fc == 3))
            wait(k, "vector", "te")
            dst = vtm(k, 4 * c + tt)
            vve = vinc(k, lambda e, dst=dst, ps=ps: e.scalar_tensor_tensor(
                out=dst, in0=ps[:, 0:512], scalar=1.0,
                in1=bvb_ap(k, 512, 1024), op0=ALU.mult, op1=ALU.add))
            wait(k, "tensor", "ve", vve)
        wait(k, "sync", "ve")
        wait(k, "sync", "se")
    proj_fm(k, 2, 0, lambda fc, c0, c1: hso(k, fc, c0, c1), NR, lambda dc, c0, c1: qfm(k, dc, c0, c1))


def phase_ffw_weights(k):
    d = k.d
    stage = k.sb["ptr"].ap()
    for wname, tname, nrow, ncol in (("wff1", "wff1T", DFF, D),
                                     ("wff2", "wff2T", D, DFF)):
        for r in range(nrow // 128):
            for cg in range(ncol // 1024):
                vg = sdma(k, stage[0:128, 0:1024],
                          d[wname][128 * r:128 * r + 128, 1024 * cg:1024 * cg + 1024],
                          "gpsimd")
                wait(k, "tensor", "gd", vg)
                for ic in range(8):
                    trans(k, psb(k, "D", 128 * (ic % 4), 128 * (ic % 4) + 128),
                          stage[0:128, 128 * ic:128 * ic + 128])
                    if ic % 4 == 3:
                        wait(k, "vector", "te")
                        for j in range(4):
                            icj = ic - 3 + j
                            vinc(k, lambda e, icj=icj, j=j: e.tensor_copy(
                                out=stage[0:128, 1024 + 128 * icj:1152 + 128 * icj],
                                in_=psb(k, "D", 128 * j, 128 * j + 128)))
                        wait(k, "tensor", "ve")
                ve = k.p.cnt[k.S("ve")]
                wait(k, "sync", "ve", ve)
                sdma(k, d[tname][128 * r:128 * r + 128, 1024 * cg:1024 * cg + 1024],
                     stage[0:128, 1024:2048])
                wait(k, "gpsimd", "ld")
    wait(k, "sync", "ld")


def phase_attn_ffn(k):
    p, d = k.p, k.d
    build_gbe(k, "gff", "beff", 1024)
    for tq in range(16):
        for half, xsrc in ((0, x1t), (1, x2t)):
            for f in range(4):
                trans(k, psb(k, "D", 128 * f, 128 * f + 128),
                      xsrc(k, tq, 128 * f, 128 * f + 128))
            wait(k, "scalar", "te")
            for f in range(4):
                fc = 4 * half + f
                dst = yfm(k, fc, 128 * tq, 128 * tq + 128)
                sinc(k, lambda e, dst=dst, f=f: e.copy(
                    out=dst, in_=psb(k, "D", 128 * f, 128 * f + 128)))
            wait(k, "tensor", "se")
    wait(k, "vector", "se")
    wait(k, "sync", "se")

    w1r = k.sb["s3"].ap()
    w2r = k.sb["s3"].ap()
    hst = k.sb["f32b"].ap().bitcast(BF16)
    xf = k.sb["f32a"].ap()
    for tb in range(4):
        h1_te = {}
        for fb in range(32):
            buf = fb % 2
            if fb >= 2:
                wait(k, "sync", "te", h1_te[fb - 2])
            v = sdma(k, w1r[0:128, 1024 * buf:1024 * buf + 1024],
                     d["wff1T"][128 * fb:128 * fb + 128, :])
            wait(k, "tensor", "ld", v)
            ps = k.ps["C"] if fb % 2 == 0 else k.ps["D"]
            for yc in range(8):
                mm(k, ps[:, 0:512], w1r[0:128, 1024 * buf + 128 * yc:1024 * buf + 128 * yc + 128],
                   yfm(k, yc, 512 * tb, 512 * tb + 512),
                   start=(yc == 0), stop=(yc == 7))
            h1_te[fb] = k.p.cnt[k.S("te")]
            wait(k, "scalar", "te")
            dst = h1sb(k, fb)
            vse = sinc(k, lambda e, dst=dst, ps=ps, fb=fb: e.activation(
                out=dst, in_=ps[:, 0:512], func=AF.Relu,
                bias=a(k, "bff1c", 0, 128, fb, fb + 1)))
            wait(k, "tensor", "se", vse)
            wait(k, "sync", "se", vse)
        h2_te = {}
        for oc in range(8):
            buf = oc % 2
            if oc >= 2:
                wait(k, "sync", "te", h2_te[oc - 2])
            v = sdma(k, w2r[0:128, 2048 + 4096 * buf:2048 + 4096 * buf + 4096],
                     d["wff2T"][128 * oc:128 * oc + 128, :])
            wait(k, "tensor", "ld", v)
            ps = k.ps["C"] if oc % 2 == 0 else k.ps["D"]
            for ffc in range(32):
                mm(k, ps[:, 0:512],
                   w2r[0:128, 2048 + 4096 * buf + 128 * ffc:2048 + 4096 * buf + 128 * ffc + 128],
                   h1sb(k, ffc), start=(ffc == 0), stop=(ffc == 31))
            h2_te[oc] = k.p.cnt[k.S("te")]
            wait(k, "scalar", "te")
            vse = sinc(k, lambda e, ps=ps, oc=oc: e.activation(
                out=hst[0:128, 4096:4608], in_=ps[:, 0:512], func=AF.Identity,
                bias=a(k, "bff2c", 0, 128, oc, oc + 1)))
            wait(k, "tensor", "se", vse)
            for sub in range(4):
                trans(k, psb(k, "Z0", 128 * sub, 128 * sub + 128),
                      hst[0:128, 4096 + 128 * sub:4096 + 128 * sub + 128])
            wait(k, "vector", "te")
            for sub in range(4):
                tq = 4 * tb + sub
                xs = x1t(k, tq, 128 * oc, 128 * oc + 128) if oc < 4 else \
                     x2t(k, tq, 128 * (oc - 4), 128 * (oc - 4) + 128)
                dst = z3t(k, tq, 128 * oc, 128 * oc + 128)
                vinc(k, lambda e, dst=dst, xs=xs, sub=sub: e.tensor_tensor(
                    out=dst, in0=psb(k, "Z0", 128 * sub, 128 * sub + 128),
                    in1=xs, op=ALU.add))
            wait(k, "tensor", "ve")
            wait(k, "scalar", "ve")
    for tq in range(16):
        v = sdma(k, xf[0:128, 0:1024], d["xq"][128 * tq:128 * tq + 128, :])
        vap = ln_tile(k, z3t(k, tq), z3t(k, tq), hst[0:128, 4096:5120], D)
        wait(k, "vector", "se", vap)
        vop(k, lambda e: e.tensor_tensor(
            out=hst[0:128, 4096:5120], in0=hst[0:128, 4096:5120],
            in1=gbe_ap(k, 0, 1024), op=ALU.mult))
        vop(k, lambda e: e.tensor_tensor(
            out=hst[0:128, 4096:5120], in0=hst[0:128, 4096:5120],
            in1=gbe_ap(k, 1024, 2048), op=ALU.add))
        wait(k, "vector", "ld", v)
        dst = z3t(k, tq)
        vx = vinc(k, lambda e, dst=dst: e.tensor_tensor(
            out=dst, in0=hst[0:128, 4096:5120], in1=xf[0:128, 0:1024], op=ALU.add))
        wait(k, "sync", "ve", vx)
        wait(k, "scalar", "ve", vx)


def quant_ladder(k, zsrc_f32, bca_ps, bcr_ps, width, out_bf):
    p = k.p
    fbap = k.sb["f32b"].ap()
    fbb = k.sb["f32b"].ap().bitcast(BF16)
    z = fbap[0:128, 2304:2304 + width]
    u = fbap[0:128, 2816:2816 + width]
    sg = fbb[0:128, 0:width]
    gslot = lambda j: fbb[0:128, 512 * (j + 1):512 * (j + 1) + width]
    t2 = fbb[0:128, 4096:4096 + width]
    vz = vinc(k, lambda e: e.tensor_tensor(out=z, in0=zsrc_f32, in1=bca_ps,
                                           op=ALU.mult))
    wait(k, "scalar", "ve", vz)
    sinc(k, lambda e: e.activation(out=u, in_=z, func=AF.Abs))
    vsg = sinc(k, lambda e: e.activation(out=sg, in_=z, func=AF.Sign))
    wait(k, "vector", "se", vsg)
    t0, tb = gslot(0), gslot(1)
    vop(k, lambda e: e.tensor_scalar(out=t0, in0=u, scalar1=QMIDS[0], scalar2=None,
                                     op0=ALU.is_gt))
    for i in (1, 2, 3):
        vop(k, lambda e, i=i: e.scalar_tensor_tensor(
            out=t0, in0=u, scalar=QMIDS[i], in1=t0, op0=ALU.is_gt, op1=ALU.add))
    vop(k, lambda e: e.tensor_scalar(out=tb, in0=u, scalar1=QMIDS[6], scalar2=None,
                                     op0=ALU.is_gt))
    for i in (6, 5, 4):
        vop(k, lambda e, i=i: e.scalar_tensor_tensor(
            out=tb, in0=u, scalar=QMIDS[i], in1=tb, op0=ALU.is_gt, op1=ALU.add))
    vop(k, lambda e: e.scalar_tensor_tensor(
        out=t0, in0=tb, scalar=2.0, in1=t0, op0=ALU.mult, op1=ALU.add))
    vop(k, lambda e: e.tensor_tensor(out=t0, in0=t0, in1=sg, op=ALU.mult))
    return vinc(k, lambda e: e.tensor_tensor(out=out_bf, in0=t0, in1=bcr_ps,
                                             op=ALU.mult))


def quant_scales(k, src_f32, width, nblk):
    p = k.p
    sm = k.sb["f32a"].ap()
    bm = sm[0:128, 1408:1408 + nblk]
    rr = sm[0:128, 1496:1496 + nblk]
    rh = sm[0:128, 1584:1584 + nblk]
    tmp = sm[0:128, 1672:1672 + nblk]
    stg = k.sb["ptr"].ap()
    rhiS = stg[0:128, 1536:1536 + nblk]
    rhhiS = stg[0:128, 1664:1664 + nblk]
    rloS = stg[0:128, 1792:1792 + nblk]
    rhloS = stg[0:128, 1920:1920 + nblk]
    v1 = vinc(k, lambda e: e.tensor_reduce(
        out=bm, in_=src_f32.rearrange("p (g w) -> p g w", w=16),
        axis=AX.X, op=ALU.max, apply_absolute_value=True))
    wait(k, "vector", "ve", v1)
    v2 = vinc(k, lambda e: e.tensor_scalar(out=bm, in0=bm, scalar1=6e-12,
                                           scalar2=None, op0=ALU.max))
    wait(k, "vector", "ve", v2)
    v3 = vinc(k, lambda e: e.reciprocal(out=rr, in_=bm))
    vop(k, lambda e: e.tensor_scalar(out=rh, in0=bm, scalar1=1.0 / 12.0,
                                     scalar2=None, op0=ALU.mult))
    wait(k, "vector", "ve", v3)
    v4 = vinc(k, lambda e: e.tensor_tensor(out=tmp, in0=rr, in1=bm, op=ALU.mult))
    wait(k, "vector", "ve", v4)
    v5 = vinc(k, lambda e: e.tensor_scalar(out=tmp, in0=tmp, scalar1=-1.0,
                                           scalar2=2.0, op0=ALU.mult, op1=ALU.add))
    wait(k, "vector", "ve", v5)
    v6 = vinc(k, lambda e: e.tensor_tensor(out=rr, in0=rr, in1=tmp, op=ALU.mult))
    wait(k, "vector", "ve", v6)
    v7 = vinc(k, lambda e: e.tensor_scalar(out=rr, in0=rr, scalar1=6.0,
                                           scalar2=None, op0=ALU.mult))
    vop(k, lambda e: e.tensor_copy(out=rhhiS, in_=rh))
    wait(k, "vector", "ve", v7)
    v8 = vinc(k, lambda e: e.tensor_copy(out=rhiS, in_=rr))
    vop(k, lambda e: e.tensor_tensor(out=bm, in0=rh, in1=rhhiS, op=ALU.subtract))
    wait(k, "vector", "ve", v8)
    v9 = vinc(k, lambda e: e.tensor_tensor(out=tmp, in0=rr, in1=rhiS,
                                           op=ALU.subtract))
    vop(k, lambda e: e.tensor_copy(out=rhloS, in_=bm))
    wait(k, "vector", "ve", v9)
    vr = vinc(k, lambda e: e.tensor_copy(out=rloS, in_=tmp))
    wait(k, "tensor", "ve", vr)
    pD = k.ps["D"].bitcast(BF16)
    trans(k, pD[0:nblk, 0:128], rhiS)
    trans(k, pD[0:nblk, 128:256], rhhiS)
    trans(k, pD[0:nblk, 256:384], rloS)
    trans(k, pD[0:nblk, 384:512], rhloS)
    wait(k, "vector", "te")
    smb = k.sb["f32a"].ap().bitcast(BF16)
    vinc(k, lambda e: e.tensor_copy(out=smb[0:nblk, 3776:3904][0:nblk, 0:128],
                                    in_=pD[0:nblk, 0:128]))
    vinc(k, lambda e: e.tensor_copy(out=smb[0:nblk, 3904:4032][0:nblk, 0:128],
                                    in_=pD[0:nblk, 128:256]))
    vinc(k, lambda e: e.tensor_copy(out=smb[0:nblk, 3520:3648][0:nblk, 0:128],
                                    in_=pD[0:nblk, 256:384]))
    v = vinc(k, lambda e: e.tensor_copy(out=smb[0:nblk, 3648:3776][0:nblk, 0:128],
                                        in_=pD[0:nblk, 384:512]))
    wait(k, "tensor", "ve", v)
    return v


def phase_ffn_weights_quant(k):
    p, d = k.p, k.d
    fa = k.sb["f32a"].ap()
    fb = k.sb["f32b"].ap()
    stage_in = fb[0:128, 0:1024]
    smb = k.sb["f32a"].ap().bitcast(BF16)
    outb = k.sb["ptr"].ap()
    for wname in ("wgate", "wup"):
        for r in range(FFI // 128):
            v = sdma(k, stage_in, d[wname][128 * r:128 * r + 128, :])
            wait(k, "vector", "ld", v)
            wait(k, "scalar", "ld", v)
            vr = quant_scales(k, stage_in, 1024, 64)
            wait(k, "tensor", "ve", vr)
            for ic in range(8):
                trans(k, k.ps["Z2"][:, 128 * (ic % 4):128 * (ic % 4) + 128],
                      stage_in[:, 128 * ic:128 * ic + 128], fp32=True)
                if ic % 4 == 3:
                    wait(k, "vector", "te")
                    for j in range(4):
                        icj = ic - 3 + j
                        vinc(k, lambda e, icj=icj, j=j: e.tensor_copy(
                            out=fa[0:128, 128 * icj:128 * icj + 128],
                            in_=k.ps["Z2"][:, 128 * j:128 * j + 128]))
                    wait(k, "tensor", "ve")
            for ic in range(8):
                mm(k, k.ps["A"][:, 128 * ic:128 * ic + 128],
                   a(k, "sel88", 0, 64, 128 * ic, 128 * ic + 128),
                   smb[0:64, 3776:3904][0:64, 0:128], start=True, stop=False)
                mm(k, k.ps["A"][:, 128 * ic:128 * ic + 128],
                   a(k, "sel88", 0, 64, 128 * ic, 128 * ic + 128),
                   smb[0:64, 3520:3648][0:64, 0:128], start=False, stop=True)
                psr = k.ps["Z0"] if ic < 4 else k.ps["Z1"]
                mm(k, psr[:, 128 * (ic % 4):128 * (ic % 4) + 128],
                   a(k, "sel88", 0, 64, 128 * ic, 128 * ic + 128),
                   smb[0:64, 3904:4032][0:64, 0:128], start=True, stop=False)
                mm(k, psr[:, 128 * (ic % 4):128 * (ic % 4) + 128],
                   a(k, "sel88", 0, 64, 128 * ic, 128 * ic + 128),
                   smb[0:64, 3648:3776][0:64, 0:128], start=False, stop=True)
            wait(k, "vector", "te")
            quant_ladder(k, fa[0:128, 0:512], k.ps["A"][:, 0:512],
                         k.ps["Z0"][:, 0:512], 512, outb[0:128, 0:512])
            vq = quant_ladder(k, fa[0:128, 512:1024], k.ps["A"][:, 512:1024],
                              k.ps["Z1"][:, 0:512], 512, outb[0:128, 512:1024])
            wait(k, "sync", "ve", vq)
            wait(k, "tensor", "ve", vq)
            wait(k, "scalar", "ve", vq)
            tile_fn = wgq_tile if wname == "wgate" else wuq_tile
            for ic in range(8):
                dst = tile_fn(k, ic, 128 * r, 128 * r + 128)
                sinc(k, lambda e, dst=dst, ic=ic: e.copy(
                    out=dst, in_=outb[0:128, 128 * ic:128 * ic + 128]))
            wait(k, "vector", "se")
            wait(k, "vector", "ld")
            wait(k, "scalar", "ld")
        wait(k, "sync", "ld")


def phase_wdown_quant(k):
    p, d = k.p, k.d
    fa = k.sb["f32a"].ap()
    fb = k.sb["f32b"].ap()
    smb = k.sb["f32a"].ap().bitcast(BF16)
    outb = k.sb["ptr"].ap()
    for r in range(D // 128):
        for hf2 in range(2):
            v = sdma(k, fb[0:128, 0:1408],
                     d["wdown"][128 * r:128 * r + 128, 1408 * hf2:1408 * hf2 + 1408])
            wait(k, "vector", "ld", v)
            wait(k, "scalar", "ld", v)
            vr = quant_scales(k, fb[0:128, 0:1408], 1408, 88)
            wait(k, "tensor", "ve", vr)
            for ic in range(11):
                trans(k, k.ps["Z2"][:, 128 * (ic % 4):128 * (ic % 4) + 128],
                      fb[0:128, 128 * ic:128 * ic + 128], fp32=True)
                if ic % 4 == 3 or ic == 10:
                    wait(k, "vector", "te")
                    for j in range(ic % 4 + 1):
                        icj = ic - (ic % 4) + j
                        vinc(k, lambda e, icj=icj, j=j: e.tensor_copy(
                            out=fa[0:128, 128 * icj:128 * icj + 128],
                            in_=k.ps["Z2"][:, 128 * j:128 * j + 128]))
                    wait(k, "tensor", "ve")
            for ic in range(11):
                if ic < 8:
                    psx, col = k.ps["A"], 128 * ic
                    psy, coly = (k.ps["Z0"], 128 * ic) if ic < 4 else (k.ps["Z1"], 128 * (ic - 4))
                else:
                    psx, col = k.ps["C"], 128 * (ic - 8)
                    psy, coly = k.ps["Z2"], 128 * (ic - 8)
                mm(k, psx[:, col:col + 128],
                   a(k, "sel88", 0, 88, 128 * ic, 128 * ic + 128),
                   smb[0:88, 3776:3904][0:88, 0:128], start=True, stop=False)
                mm(k, psx[:, col:col + 128],
                   a(k, "sel88", 0, 88, 128 * ic, 128 * ic + 128),
                   smb[0:88, 3520:3648][0:88, 0:128], start=False, stop=True)
                mm(k, psy[:, coly:coly + 128],
                   a(k, "sel88", 0, 88, 128 * ic, 128 * ic + 128),
                   smb[0:88, 3904:4032][0:88, 0:128], start=True, stop=False)
                mm(k, psy[:, coly:coly + 128],
                   a(k, "sel88", 0, 88, 128 * ic, 128 * ic + 128),
                   smb[0:88, 3648:3776][0:88, 0:128], start=False, stop=True)
            wait(k, "vector", "te")
            quant_ladder(k, fa[0:128, 0:512], k.ps["A"][:, 0:512],
                         k.ps["Z0"][:, 0:512], 512, outb[0:128, 0:512])
            quant_ladder(k, fa[0:128, 512:1024], k.ps["A"][:, 512:1024],
                         k.ps["Z1"][:, 0:512], 512, outb[0:128, 512:1024])
            vq2 = quant_ladder(k, fa[0:128, 1024:1408], k.ps["C"][:, 0:384],
                               k.ps["Z2"][:, 0:384], 384, outb[0:128, 1024:1408])
            wait(k, "sync", "ve", vq2)
            wait(k, "tensor", "ve", vq2)
            wait(k, "scalar", "ve", vq2)
            for ic in range(11):
                dst = wdq_tile(k, 11 * hf2 + ic, 128 * r, 128 * r + 128)
                sinc(k, lambda e, dst=dst, ic=ic: e.copy(
                    out=dst, in_=outb[0:128, 128 * ic:128 * ic + 128]))
            wait(k, "vector", "se")
            wait(k, "vector", "ld")
            wait(k, "scalar", "ld")
    wait(k, "sync", "ld")


def phase_ffn_block(k):
    p, d = k.p, k.d
    fa = k.sb["f32a"].ap()
    fb = k.sb["f32b"].ap()
    smb = k.sb["f32a"].ap().bitcast(BF16)
    ptb = k.sb["ptr"].ap()
    for tq in range(16):
        vap = ln_tile(k, z3t(k, tq), z3t(k, tq), fa[0:128, 0:1024], D,
                      sq_ap=fb[0:128, 2048:3072])
        wait(k, "vector", "se", vap)
        vr = quant_scales(k, fa[0:128, 0:1024], 1024, 64)
        wait(k, "tensor", "ve", vr)
        for ic in range(8):
            trans(k, k.ps["Z2"][:, 128 * (ic % 4):128 * (ic % 4) + 128],
                  fa[0:128, 128 * ic:128 * ic + 128], fp32=True)
            if ic % 4 == 3:
                wait(k, "scalar", "te")
                for j in range(4):
                    icj = ic - 3 + j
                    sinc(k, lambda e, icj=icj, j=j: e.copy(
                        out=fa[0:128, 128 * icj:128 * icj + 128],
                        in_=k.ps["Z2"][:, 128 * j:128 * j + 128]))
                wait(k, "tensor", "se")
        for ic in range(8):
            mm(k, k.ps["A"][:, 128 * ic:128 * ic + 128],
               a(k, "sel88", 0, 64, 128 * ic, 128 * ic + 128),
               smb[0:64, 3776:3904][0:64, 0:128], start=True, stop=False)
            mm(k, k.ps["A"][:, 128 * ic:128 * ic + 128],
               a(k, "sel88", 0, 64, 128 * ic, 128 * ic + 128),
               smb[0:64, 3520:3648][0:64, 0:128], start=False, stop=True)
            psr = k.ps["Z0"] if ic < 4 else k.ps["Z1"]
            mm(k, psr[:, 128 * (ic % 4):128 * (ic % 4) + 128],
               a(k, "sel88", 0, 64, 128 * ic, 128 * ic + 128),
               smb[0:64, 3904:4032][0:64, 0:128], start=True, stop=False)
            mm(k, psr[:, 128 * (ic % 4):128 * (ic % 4) + 128],
               a(k, "sel88", 0, 64, 128 * ic, 128 * ic + 128),
               smb[0:64, 3648:3776][0:64, 0:128], start=False, stop=True)
        wait(k, "vector", "te")
        wait(k, "vector", "se")
        quant_ladder(k, fa[0:128, 0:512], k.ps["A"][:, 0:512],
                     k.ps["Z0"][:, 0:512], 512, ptb[0:128, 0:512])
        vq = quant_ladder(k, fa[0:128, 512:1024], k.ps["A"][:, 512:1024],
                          k.ps["Z1"][:, 0:512], 512, ptb[0:128, 512:1024])
        wait(k, "scalar", "ve", vq)
        for ic in range(8):
            dst = qxfm(k, ic, 128 * tq, 128 * tq + 128)
            sinc(k, lambda e, dst=dst, ic=ic: e.copy(
                out=dst, in_=ptb[0:128, 128 * ic:128 * ic + 128]))
        wait(k, "tensor", "se")
        wait(k, "vector", "se")
    wait(k, "tensor", "gd")
    wait(k, "tensor", "ld")
    vq2_prev = p.cnt[k.S("ve")]
    for ts in range(16):
        for hf2 in range(2):
            for ci, w in ((0, 512), (1, 512), (2, 384)):
                fo0 = 1408 * hf2 + 512 * ci
                psg, psu = k.ps["C"], k.ps["D"]
                for kc in range(8):
                    mm(k, psg[:, 0:w], qxfm(k, kc, 128 * ts, 128 * ts + 128),
                       wgq_tile(k, kc, fo0, fo0 + w), start=(kc == 0), stop=(kc == 7))
                    mm(k, psu[:, 0:w], qxfm(k, kc, 128 * ts, 128 * ts + 128),
                       wuq_tile(k, kc, fo0, fo0 + w), start=(kc == 0), stop=(kc == 7))
                wait(k, "scalar", "te")
                vsg = sinc(k, lambda e, w=w: e.activation(
                    out=ptb[0:128, 1024:1024 + w], in_=psg[:, 0:w], func=AF.Silu))
                wait(k, "vector", "se", vsg)
                vh = vinc(k, lambda e, w=w, ci=ci: e.tensor_tensor(
                    out=fa[0:128, 512 * ci:512 * ci + w], in0=psu[:, 0:w],
                    in1=ptb[0:128, 1024:1024 + w], op=ALU.mult))
                wait(k, "tensor", "ve", vh)
                wait(k, "scalar", "ve", vh)
            vr = quant_scales(k, fa[0:128, 0:1408], 1408, 88)
            wait(k, "tensor", "ve", vr)
            for ic in range(11):
                trans(k, k.ps["Z2"][:, 128 * (ic % 4):128 * (ic % 4) + 128],
                      fa[0:128, 128 * ic:128 * ic + 128], fp32=True)
                if ic % 4 == 3 or ic == 10:
                    wait(k, "scalar", "te")
                    for j in range(ic % 4 + 1):
                        icj = ic - (ic % 4) + j
                        sinc(k, lambda e, icj=icj, j=j: e.copy(
                            out=fa[0:128, 128 * icj:128 * icj + 128],
                            in_=k.ps["Z2"][:, 128 * j:128 * j + 128]))
                    wait(k, "tensor", "se")
            if vq2_prev is not None:
                wait(k, "tensor", "ve", vq2_prev)
            for ic in range(11):
                if ic < 8:
                    psx, col = k.ps["A"], 128 * ic
                    psy, coly = (k.ps["Z0"], 128 * ic) if ic < 4 else (k.ps["Z1"], 128 * (ic - 4))
                else:
                    psx, col = k.ps["Z2"], 128 * (ic - 8)
                    psy, coly = k.ps["Z3"], 128 * (ic - 8)
                mm(k, psx[:, col:col + 128],
                   a(k, "sel88", 0, 88, 128 * ic, 128 * ic + 128),
                   smb[0:88, 3776:3904][0:88, 0:128], start=True, stop=False)
                mm(k, psx[:, col:col + 128],
                   a(k, "sel88", 0, 88, 128 * ic, 128 * ic + 128),
                   smb[0:88, 3520:3648][0:88, 0:128], start=False, stop=True)
                mm(k, psy[:, coly:coly + 128],
                   a(k, "sel88", 0, 88, 128 * ic, 128 * ic + 128),
                   smb[0:88, 3904:4032][0:88, 0:128], start=True, stop=False)
                mm(k, psy[:, coly:coly + 128],
                   a(k, "sel88", 0, 88, 128 * ic, 128 * ic + 128),
                   smb[0:88, 3648:3776][0:88, 0:128], start=False, stop=True)
            wait(k, "vector", "te")
            wait(k, "vector", "se")
            hq0 = 2816 + 1408 * hf2
            quant_ladder(k, fa[0:128, 0:512], k.ps["A"][:, 0:512],
                         k.ps["Z0"][:, 0:512], 512,
                         a(k, "s5", 0, 128, hq0, hq0 + 512))
            quant_ladder(k, fa[0:128, 512:1024], k.ps["A"][:, 512:1024],
                         k.ps["Z1"][:, 0:512], 512,
                         a(k, "s5", 0, 128, hq0 + 512, hq0 + 1024))
            vq2 = quant_ladder(k, fa[0:128, 1024:1408], k.ps["Z2"][:, 0:384],
                               k.ps["Z3"][:, 0:384], 384,
                               a(k, "s5", 0, 128, hq0 + 1024, hq0 + 1408))
            vq2_prev = vq2
        vhq = p.cnt[k.S("ve")]
        wait(k, "sync", "ve", vhq)
        sdma(k, d["hqTD"][128 * ts:128 * ts + 128, :],
             a(k, "s5", 0, 128, 2816, 5632))
        wait(k, "vector", "ld")
    wait(k, "sync", "ld")
    barrier(k)
    phase_wdown_quant(k)
    barrier(k)
    g2_te = {}
    for ts in range(16):
        buf = ts % 2
        if ts >= 2:
            wait(k, "gpsimd", "te", g2_te[ts - 2])
        v = sdma(k, a(k, "s3", 0, 128, 2816 * buf, 2816 * buf + 2816),
                 d["hqTD"][128 * ts:128 * ts + 128, :], "gpsimd")
        wait(k, "tensor", "gd", v)
        for kc in range(22):
            mm(k, k.ps["C"][:, 0:512], hqring(k, buf, kc),
               wdq_tile(k, kc, 0, 512), start=(kc == 0), stop=(kc == 21))
            mm(k, k.ps["D"][:, 0:512], hqring(k, buf, kc),
               wdq_tile(k, kc, 512, 1024), start=(kc == 0), stop=(kc == 21))
        wait(k, "vector", "te")
        for oh in range(2):
            ps = k.ps["C"] if oh == 0 else k.ps["D"]
            col = 1024 + 512 * oh
            vf = vinc(k, lambda e, ps=ps, col=col, ts=ts, oh=oh: e.scalar_tensor_tensor(
                out=fa[0:128, col:col + 512], in0=ps[:, 0:512], scalar=1.0,
                in1=z3t(k, ts, 512 * oh, 512 * oh + 512),
                op0=ALU.mult, op1=ALU.add))
            wait(k, "sync", "ve", vf)
            stdma(k, k.out[128 * ts:128 * ts + 128, 512 * oh:512 * oh + 512],
                  fa[0:128, col:col + 512])
            wait(k, "tensor", "ve", vf)
        g2_te[ts] = k.p.cnt[k.S("te")]
    if "qx" in DEBUG:
        wait(k, "gpsimd", "te")
        for kc in range(8):
            sdma(k, d["dbg_qx"][128 * kc:128 * kc + 128, :], qxfm(k, kc), "gpsimd")
    if "hq" in DEBUG:
        wait(k, "gpsimd", "ld")
        for rr in range(4):
            sdma(k, d["dbg_hq"][512 * rr:512 * rr + 512, :],
                 d["hqTD"][512 * rr:512 * rr + 512, :], "gpsimd")
    if DEBUG:
        wait(k, "sync", "gd")
    wait(k, "sync", "st")


STAGE = "full"
ATTCUT = None
ATTMI = 1
DBG = False


def emit_dbg(k):
    if not DBG:
        return
    if STAGE == "P2":
        for dc in range(4):
            sdma(k, k.out[128 * dc:128 * dc + 128, 0:1024],
                 qfm(k, dc, 0, 1024), "gpsimd")
            sdma(k, k.out[512 + 128 * dc:512 + 128 * dc + 128, 0:1024],
                 kfm(k, dc, 0, 1024), "gpsimd")
        for t in range(8):
            sdma(k, k.out[1024 + 128 * t:1152 + 128 * t, 0:512],
                 vtm(k, t), "gpsimd")
        for fc in range(4):
            sdma(k, k.out[1024 + 128 * fc:1152 + 128 * fc, 512:1024],
                 xqfm(k, fc, 0, 512), "gpsimd")
    elif STAGE == "ATT1":
        for t in range(16):
            sdma(k, k.out[128 * t:128 * t + 128, 0:512], x1t(k, t), "gpsimd")
    elif STAGE == "ATT2":
        for t in range(16):
            sdma(k, k.out[128 * t:128 * t + 128, 0:512], x1t(k, t), "gpsimd")
            sdma(k, k.out[128 * t:128 * t + 128, 512:1024], x2t(k, t), "gpsimd")
    elif STAGE == "FFN1":
        for t in range(16):
            sdma(k, k.out[128 * t:128 * t + 128, 0:1024], z3t(k, t), "gpsimd")
    wait(k, "sync", "gd")


def emit_program(k):
    phase_consts(k)
    phase_holt_tables(k)
    build_bvb(k)
    phase_attn_weights(k, 1)
    barrier(k)
    if STAGE == "TABLES":
        return
    phase_xq(k)
    barrier(k)
    phase_stream_x(k)
    barrier(k)
    if STAGE == "P2":
        emit_dbg(k)
        return
    phase_attention(k, 1)
    barrier(k)
    if STAGE == "ATT1":
        emit_dbg(k)
        return
    phase_attn_weights(k, 2)
    phase_merge_hso(k)
    barrier(k)
    if STAGE == "MERGE":
        return
    phase_proj2(k)
    barrier(k)
    if STAGE == "PROJ2":
        return
    phase_attention(k, 2)
    barrier(k)
    if STAGE == "ATT2":
        emit_dbg(k)
        return
    phase_ffw_weights(k)
    barrier(k)
    phase_attn_ffn(k)
    barrier(k)
    if STAGE == "FFN1":
        emit_dbg(k)
        return
    phase_ffn_weights_quant(k)
    barrier(k)
    phase_ffn_block(k)


def build():
    k = K()
    declare(k)
    with k.stack:
        alloc(k)
        for s in ("ld", "gd", "gp", "te", "ve", "se", "st",
                  "bsy", "bte", "bve", "bse", "bgp"):
            k.p.sem(s)
        emit_program(k)
        blk = k.stack.enter_context(k.nc.Block())
        for eng_name, meth in (("sync", blk.sync), ("tensor", blk.tensor),
                               ("vector", blk.vector), ("scalar", blk.scalar),
                               ("gpsimd", blk.gpsimd)):
            def body(e, _ops=k.p.ops[eng_name]):
                for fn in _ops:
                    fn(e)
            meth(body)
    return k


_cache = {}


def kernel(**inputs):
    if "k" not in _cache:
        _cache["k"] = build()
    k = _cache["k"]
    f32 = np.float32
    j_np = np.eye(128, dtype=f32)[::-1].copy()
    sel1a = np.zeros((1, 128), f32); sel1a[0, 0:64] = 1
    sel1b = np.zeros((1, 128), f32); sel1b[0, 64:128] = 1
    def selmat(nblk, ncol):
        s = np.zeros((nblk, ncol), f32)
        for c in range(ncol // 128):
            for pp in range(128):
                g = 8 * c + pp // 16
                if g < nblk:
                    s[g, 128 * c + pp] = 1
        return s
    consts = {
        "idn": np.eye(128, dtype=f32),
        "idnb": np.eye(128, dtype=f32).astype(BF),
        "jmatb": j_np.astype(BF),
        "sel1a": sel1a.astype(BF), "sel1b": sel1b.astype(BF),
        "sel88": selmat(88, 1408).astype(BF),
        "onesf": np.ones((1, 128), f32),
    }
    x_full = np.ascontiguousarray(inputs["L0_attn_x"], dtype=f32)
    wmap = {
        "wq1": "L0_attn_wq1", "bq1": "L0_attn_bq1", "wk1": "L0_attn_wk1",
        "bk1": "L0_attn_bk1", "wv1": "L0_attn_wv1", "bv1": "L0_attn_bv1",
        "wo1": "L0_attn_wo1", "bo1": "L0_attn_bo1",
        "wq2": "L0_attn_wq2", "bq2": "L0_attn_bq2", "wk2": "L0_attn_wk2",
        "bk2": "L0_attn_bk2", "wv2": "L0_attn_wv2", "bv2": "L0_attn_bv2",
        "wo2": "L0_attn_wo2", "bo2": "L0_attn_bo2",
        "g1": "L0_attn_g1", "be1": "L0_attn_be1",
        "g2": "L0_attn_g2", "be2": "L0_attn_be2",
        "wff1": "L0_attn_w_ff1", "bff1": "L0_attn_b_ff1",
        "wff2": "L0_attn_w_ff2", "bff2": "L0_attn_b_ff2",
        "gff": "L0_attn_g_ff", "beff": "L0_attn_be_ff",
        "wgate": "L0_ffn_w_gate", "wup": "L0_ffn_w_up", "wdown": "L0_ffn_w_down",
    }
    shared = {kk: np.ascontiguousarray(inputs[v], dtype=f32) for kk, v in wmap.items()}
    in_maps = []
    for c in range(8):
        bb, half = c // 2, c % 2
        m = dict(consts)
        m.update(shared)
        m["x"] = x_full[bb]
        m["xq"] = np.ascontiguousarray(x_full[bb, half * NR:(half + 1) * NR, :])
        m["mask0"] = np.asarray([1.0 - half], f32)
        in_maps.append(m)
    res = run_bass_kernel_spmd(k.nc, in_maps, core_ids=list(range(8)))
    out = np.empty((B, S, D), f32)
    for c in range(8):
        bb, half = c // 2, c % 2
        out[bb, half * NR:(half + 1) * NR, :] = np.asarray(res.results[c]["out"])
    return out
